# version 06
# baseline (speedup 1.0000x reference)
import contextlib
import math
import numpy as np
import ml_dtypes
import concourse.bass as bass
import concourse.mybir as mybir
from concourse.bass_utils import run_bass_kernel_spmd

F32 = mybir.dt.float32
BF16 = mybir.dt.bfloat16
AF = mybir.ActivationFunctionType
ALU = mybir.AluOpType
AX = mybir.AxisListType

T = 4096
D = 1024
NTB = T // 128
EPS = 1e-5
BF16NP = ml_dtypes.bfloat16

N_CORES = 8



class Prog:

    def __init__(self, ctx, nc):
        self.ctx = ctx
        self.ctx0 = ctx
        self.nc = nc
        self.block = None
        self.bar = None
        self.nbar = 0
        self._semcs = {}
        self._banks = []
        self._bank_i = 0

    def sbuf(self, name, shape, dtype, persist=False):
        ctx = self.ctx0 if persist else self.ctx
        return ctx.enter_context(self.nc.sbuf_tensor(name, shape, dtype))

    def push_phase(self):
        st = contextlib.ExitStack()
        self._phase = st
        self.ctx = st

    def pop_phase(self):
        self._phase.close()
        self.ctx = self.ctx0

    def psum(self, name, shape, dtype):
        return self.ctx.enter_context(self.nc.psum_tensor(name, shape, dtype))

    def psum_bank(self, name):
        if len(self._banks) < 8:
            self._banks.append(self.ctx0.enter_context(
                self.nc.psum_tensor(f"bank{len(self._banks)}", [128, 512], F32)))
            self._bank_i = len(self._banks) - 1
            return self._banks[self._bank_i]
        self._bank_i = (self._bank_i + 1) % 8
        return self._banks[self._bank_i]

    def semc(self, name):
        if name not in self._semcs:
            self._semcs[name] = SemC(self.ctx0.enter_context(self.nc.semaphore(name)))
        return self._semcs[name]

    def sem(self, name):
        return self.ctx0.enter_context(self.nc.semaphore(name))

    def barrier(self):
        self.nbar += 1
        k = self.nbar
        b = self.block
        for section in (b.tensor, b.vector, b.scalar, b.gpsimd, b.sync):
            @section
            def _(e):
                e.nop().then_inc(self.bar, 1)
                e.wait_ge(self.bar, 5 * k)

    def tape_init(self):
        self.ops = {k: [] for k in ("tensor", "vector", "scalar", "gpsimd", "sync")}

    def emit(self, eng, fn):
        self.ops[eng].append(fn)

    def tape_barrier(self):
        self.nbar += 1
        k = self.nbar
        for eng in self.ops:
            def f(e, k=k):
                e.drain()
                e.nop().then_inc(self.bar, 1)
                e.wait_ge(self.bar, 5 * k)
            self.emit(eng, f)

    def flush(self):
        b = self.block
        for name, section in (("tensor", b.tensor), ("vector", b.vector),
                              ("scalar", b.scalar), ("gpsimd", b.gpsimd),
                              ("sync", b.sync)):
            ops = self.ops[name]

            @section
            def _(e, _ops=ops):
                for fn in _ops:
                    fn(e)


class SemC:

    def __init__(self, handle):
        self.h = handle
        self.n = 0

    def inc(self, k=1):
        self.n += k
        return self.n



def emit_pre(p, x, xnf, xn2f, lnw_rep, lnb_rep, ident_hbm, mu_hbm, rstd_hbm,
             affine=True):
    nc, block = p.nc, p.block
    NQ = 2 if affine else 1

    xb = [p.sbuf(f"pre_xb{i}", [128, D], F32) for i in range(2)]
    xc = p.sbuf("pre_xc", [128, D], F32)
    sq = p.sbuf("pre_sq", [128, D], F32)
    xn = p.sbuf("pre_xn", [128, D], F32)
    xnbf = p.sbuf("pre_xnbf", [128, D], BF16)
    xn2bf = p.sbuf("pre_xn2bf", [128, D], BF16)
    xc2 = p.sbuf("pre_xc2", [128, D], F32)
    st1 = p.sbuf("pre_st1", [128, 8], F32)
    epsc = p.sbuf("pre_eps", [128, 1], F32)
    mu_sb = p.sbuf("pre_mu", [128, NTB], F32)
    rstd_sb = p.sbuf("pre_rstd", [128, NTB], F32)
    lnw = p.sbuf("pre_lnw", [128, D], F32)
    lnb = p.sbuf("pre_lnb", [128, D], F32)
    ident = p.sbuf("ident_sb", [128, 128], BF16)
    stg = p.sbuf("pre_stg", [128, 8, 128], BF16)
    stg2 = p.sbuf("pre_stg2", [128, 8, 128], BF16)
    ptb = [p.psum_bank(f"pre_pt{i}") for i in range(2)]
    pt = [b.ap().bitcast(BF16)[:, 0:128] for b in ptb]

    s_xld = [p.sem("pre_xld0"), p.sem("pre_xld1")]
    s_cld = p.sem("pre_cld")
    s_cldg = p.sem("pre_cldg")
    s_v2s = p.sem("pre_v2s")
    s_s2v = p.sem("pre_s2v")
    s_cast = p.sem("pre_cast")
    s_tr = p.sem("pre_tr")
    s_ev = p.sem("pre_ev")
    s_out = p.sem("pre_out")
    s_xcons = p.sem("pre_xcons")

    NT = 8 * NQ

    @block.sync
    def _(s_):
        if affine:
            s_.dma_start(out=lnw[:], in_=lnw_rep[:, :]).then_inc(s_cld, 16)
            s_.dma_start(out=lnb[:], in_=lnb_rep[:, :]).then_inc(s_cld, 16)
        for tb in range(NTB):
            if tb >= 2:
                s_.wait_ge(s_xcons, tb - 1)
            s_.dma_start(out=xb[tb % 2][:], in_=x[tb * 128:(tb + 1) * 128, :]
                         ).then_inc(s_xld[tb % 2], 16)

    CLD = 32 if affine else 0

    @block.vector
    def _(v):
        def guard(instr):
            v.drain()

        v.memset(epsc[:], EPS)
        if affine:
            v.wait_ge(s_cld, CLD)
        for tb in range(NTB):
            b = xb[tb % 2]
            v.wait_ge(s_xld[tb % 2], 16 * (tb // 2 + 1))
            guard(v.reduce_sum(out=st1[:, 0:1], in_=b[:], axis=AX.X))
            v.tensor_scalar(out=st1[:, 0:1], in0=st1[:, 0:1], scalar1=-1.0 / D,
                            scalar2=None, op0=ALU.mult)
            v.drain()
            v.tensor_scalar(out=xc[:], in0=b[:], scalar1=st1[:, 0:1], scalar2=None,
                            op0=ALU.add).then_inc(s_v2s, 1)
            v.tensor_copy(out=mu_sb[:, tb:tb + 1], in_=st1[:, 0:1]).then_inc(s_xcons, 1)
            v.wait_ge(s_s2v, NQ * tb + 1)
            guard(v.reciprocal(out=st1[:, 3:4], in_=st1[:, 2:3]))
            v.tensor_copy(out=rstd_sb[:, tb:tb + 1], in_=st1[:, 3:4])
            v.tensor_scalar(out=xn[:], in0=xc[:], scalar1=st1[:, 3:4], scalar2=None,
                            op0=ALU.mult)
            v.tensor_copy(out=xnbf[:], in_=xn[:]).then_inc(s_cast, 1)
            if affine:
                guard(v.reduce_sum(out=st1[:, 4:5], in_=xn[:], axis=AX.X))
                v.tensor_scalar(out=st1[:, 4:5], in0=st1[:, 4:5], scalar1=-1.0 / D,
                                scalar2=None, op0=ALU.mult)
                v.drain()
                v.tensor_scalar(out=xc2[:], in0=xn[:], scalar1=st1[:, 4:5],
                                scalar2=None, op0=ALU.add).then_inc(s_v2s, 1)
                v.wait_ge(s_s2v, 2 * tb + 2)
                guard(v.reciprocal(out=st1[:, 7:8], in_=st1[:, 6:7]))
                v.tensor_scalar(out=xc2[:], in0=xc2[:], scalar1=st1[:, 7:8],
                                scalar2=None, op0=ALU.mult)
                v.drain()
                v.tensor_tensor(out=xc2[:], in0=xc2[:], in1=lnw[:], op=ALU.mult)
                v.drain()
                v.tensor_tensor(out=xn2bf[:], in0=xc2[:], in1=lnb[:],
                                op=ALU.add).then_inc(s_cast, 1)
            v.wait_ge(s_out, 16 * NQ * tb)
            for c in range(NT):
                v.wait_ge(s_tr, NT * tb + c + 1)
                psrc = pt[c % 2]
                dst = stg if c < 8 else stg2
                cc = c if c < 8 else c - 8
                v.tensor_copy(out=dst[:, cc, :], in_=psrc[:]).then_inc(s_ev, 1)

    @block.scalar
    def _(sc):
        for tb in range(NTB):
            sc.wait_ge(s_v2s, NQ * tb + 1)
            sc.activation(out=sq[:], in_=xc[:], func=AF.Square, accum_out=st1[:, 1:2])
            sc.drain()
            sc.activation(out=st1[:, 2:3], in_=st1[:, 1:2], func=AF.Sqrt,
                          scale=1.0 / D, bias=epsc[:, 0:1]).then_inc(s_s2v, 1)
            if affine:
                sc.wait_ge(s_v2s, 2 * tb + 2)
                sc.activation(out=sq[:], in_=xc2[:], func=AF.Square,
                              accum_out=st1[:, 5:6])
                sc.drain()
                sc.activation(out=st1[:, 6:7], in_=st1[:, 5:6], func=AF.Sqrt,
                              scale=1.0 / D, bias=epsc[:, 0:1]).then_inc(s_s2v, 1)

    @block.tensor
    def _(t):
        t.wait_ge(s_cldg, 16)
        for tb in range(NTB):
            for c in range(NT):
                t.wait_ge(s_cast, NQ * tb + (1 if c < 8 else 2))
                if tb > 0 or c >= 2:
                    t.wait_ge(s_ev, NT * tb + c - 1)
                srct = xnbf if c < 8 else xn2bf
                cc = c if c < 8 else c - 8
                t.transpose(pt[c % 2][:], srct[:, cc * 128:(cc + 1) * 128],
                            ident[:]).then_inc(s_tr, 1)

    @block.gpsimd
    def _(g):
        g.dma_start(out=ident[:], in_=ident_hbm[:, :]).then_inc(s_cldg, 16)
        for tb in range(NTB):
            g.wait_ge(s_ev, NT * tb + 8)
            g.dma_start(out=xnf[:, :, tb * 128:(tb + 1) * 128],
                        in_=stg[:, :, :]).then_inc(s_out, 16)
            if affine:
                g.wait_ge(s_ev, NT * tb + 16)
                g.dma_start(out=xn2f[:, :, tb * 128:(tb + 1) * 128],
                            in_=stg2[:, :, :]).then_inc(s_out, 16)
        g.dma_start(out=mu_hbm[:, :], in_=mu_sb[:]).then_inc(s_out, 16)
        g.dma_start(out=rstd_hbm[:, :], in_=rstd_sb[:]).then_inc(s_out, 16)
        g.wait_ge(s_out, 16 * (NQ * NTB + 2))


def build_pre(affine=True):
    ctx = contextlib.ExitStack()
    with ctx:
        nc = bass.Bass("TRN2", target_bir_lowering=False, debug=False)
        p = Prog(ctx, nc)
        x = nc.dram_tensor("x", [T, D], F32, kind="ExternalInput").ap()
        lnw = nc.dram_tensor("lnw_rep", [128, D], F32, kind="ExternalInput").ap()
        lnb = nc.dram_tensor("lnb_rep", [128, D], F32, kind="ExternalInput").ap()
        ident = nc.dram_tensor("ident", [128, 128], F32, kind="ExternalInput").ap()
        xnf = nc.dram_tensor("xnf", [128, 8, T], BF16, kind="ExternalOutput").ap()
        if affine:
            xn2f = nc.dram_tensor("xn2f", [128, 8, T], BF16, kind="ExternalOutput").ap()
        else:
            xn2f = None
        mu = nc.dram_tensor("mu", [128, NTB], F32, kind="ExternalOutput").ap()
        rstd = nc.dram_tensor("rstd", [128, NTB], F32, kind="ExternalOutput").ap()
        p.bar = p.sem("bar")
        with nc.Block() as block:
            p.block = block
            emit_pre(p, x, xnf, xn2f, lnw, lnb, ident, mu, rstd, affine=affine)
        return nc






def emit_moe(p, xnf, x, brA, mu_hbm, rstd_hbm, rwT, rb, w1sT, w2sT, b1s, b2s,
             ones1, x1_out, n_exp=9):
    nc = p.nc
    E = p.emit

    rwt = p.sbuf("moe_rwt", [128, 8, 8], BF16)
    rbt = p.sbuf("moe_rbt", [1, 8], BF16)
    ones_r = p.sbuf("moe_ones", [1, 128], BF16)
    mu_sb = p.sbuf("moe_mu", [128, NTB], F32)
    rstd_sb = p.sbuf("moe_rstd", [128, NTB], F32)
    comb = p.sbuf("moe_comb", [128, NTB, 8], F32)
    xnr = [p.sbuf(f"moe_xnr{i}", [128, 8, 128], BF16) for i in range(2)]
    lg = p.sbuf("moe_lg", [128, 8], F32)
    rst = p.sbuf("moe_rst", [128, 8], F32)
    eprob = p.sbuf("moe_eprob", [128, 8], F32)
    msk = p.sbuf("moe_msk", [128, 8], F32)
    w1q = [p.sbuf(f"moe_w1q{i}", [128, 8, 1024], BF16) for i in range(2)]
    w2q = [p.sbuf(f"moe_w2q{i}", [128, 8, 1024], BF16) for i in range(2)]
    b1t = [p.sbuf(f"moe_b1t{i}", [128, 32], F32) for i in range(2)]
    b2t = [p.sbuf(f"moe_b2t{i}", [1, 1024], BF16) for i in range(2)]
    xnt = [p.sbuf(f"moe_xnt{i}", [128, 8, 512], BF16) for i in range(2)]
    ht = [p.sbuf(f"moe_ht{i}", [128, 8, 512], BF16) for i in range(2)]
    acc = p.sbuf("moe_acc", [128, 16, 1024], F32)
    xcb = [p.sbuf(f"moe_xcb{i}", [128, D], F32) for i in range(2)]
    bab = [p.sbuf(f"moe_bab{i}", [128, D], F32) for i in range(2)]
    x1b = [p.sbuf(f"moe_x1b{i}", [128, D], F32) for i in range(2)]
    xn_s = p.sbuf("moe_xn_s", [128, D], F32)

    p_r = [p.psum_bank(f"moe_pr{i}") for i in range(2)]
    p_h = [p.psum_bank(f"moe_ph{i}") for i in range(2)]
    p_o = [p.psum_bank(f"moe_po{i}") for i in range(2)]

    s_cld = SemC(p.sem("moe_cld"))
    s_cld2 = SemC(p.sem("moe_cld2"))
    s_rxl = [SemC(p.sem("moe_rxl0")), SemC(p.sem("moe_rxl1"))]
    s_rmm = SemC(p.sem("moe_rmm"))
    s_rlev = SemC(p.sem("moe_rlev"))
    s_v2s = SemC(p.sem("moe_v2s"))
    s_s2v = SemC(p.sem("moe_s2v"))
    s_cdone = SemC(p.sem("moe_cdone"))
    s_wld = [SemC(p.sem("moe_wld0")), SemC(p.sem("moe_wld1"))]
    s_xld = [SemC(p.sem("moe_xld0")), SemC(p.sem("moe_xld1"))]
    s_hgrp = SemC(p.sem("moe_hgrp"))
    s_hev = SemC(p.sem("moe_hev"))
    s_ogrp = SemC(p.sem("moe_ogrp"))
    s_oev = SemC(p.sem("moe_oev"))
    s_cxl = [SemC(p.sem("moe_cxl0")), SemC(p.sem("moe_cxl1"))]
    s_cv = SemC(p.sem("moe_cv"))
    s_x1o = [SemC(p.sem("moe_x1o0")), SemC(p.sem("moe_x1o1"))]

    def f(g):
        g.dma_start(out=rwt[:], in_=rwT.rearrange("(c p) e -> p c e", p=128)).then_inc(s_cld.h, 16)
        g.dma_start(out=rbt[:], in_=rb[:, :]).then_inc(s_cld.h, 16)
        g.dma_start(out=ones_r[:], in_=ones1[:, :]).then_inc(s_cld.h, 16)
    E("gpsimd", f)
    s_cld.inc(48)

    def f(s_):
        s_.dma_start(out=mu_sb[:], in_=mu_hbm[:, :]).then_inc(s_cld2.h, 16)
        s_.dma_start(out=rstd_sb[:], in_=rstd_hbm[:, :]).then_inc(s_cld2.h, 16)
    E("sync", f)
    s_cld2.inc(32)
    CLD = s_cld.n
    CLD2 = s_cld2.n

    for tb in range(NTB):
        par = tb % 2
        need_cdone = s_cdone.n - 1
        xl = s_rxl[par].inc(16)

        def f(s_, tb=tb, par=par, need=need_cdone):
            s_.wait_ge(s_cld.h, CLD)
            if need > 0:
                s_.wait_ge(s_cdone.h, need)
            s_.dma_start(out=xnr[par][:],
                         in_=xnf[:, :, tb * 128:(tb + 1) * 128]).then_inc(s_rxl[par].h, 16)
        E("sync", f)

        need_rlev = s_rlev.n - 1
        mm = s_rmm.inc(1)

        def f(t, tb=tb, par=par, xl=xl, need=need_rlev):
            t.wait_ge(s_cld.h, CLD)
            t.wait_ge(s_rxl[par].h, xl)
            if need > 0:
                t.wait_ge(s_rlev.h, need)
            pr = p_r[par].ap()
            for c in range(8):
                t.matmul(pr[:, 0:8], xnr[par][:, c, :], rwt[:, c, :],
                         start=(c == 0), stop=False)
            t.matmul(pr[:, 0:8], ones_r[:], rbt[:, :], start=False,
                     stop=True).then_inc(s_rmm.h, 1)
        E("tensor", f)

        rlev = s_rlev.inc(1)
        v2s = s_v2s.inc(1)
        s2v_t = s_s2v.n + 1
        cd = s_cdone.inc(1)

        def f(v, tb=tb, par=par, mm=mm, s2v_t=s2v_t):
            v.wait_ge(s_rmm.h, mm)
            v.reduce_max(out=rst[:, 0:1], in_=p_r[par].ap()[:, 0:8], axis=AX.X)
            v.drain()
            v.tensor_scalar(out=rst[:, 1:2], in0=rst[:, 0:1], scalar1=-1.0,
                            scalar2=None, op0=ALU.mult)
            v.drain().then_inc(s_v2s.h, 1)
            v.wait_ge(s_s2v.h, s2v_t)
            v.tensor_scalar(out=msk[:], in0=eprob[:], scalar1=1.0, scalar2=None,
                            op0=ALU.is_ge)
            v.drain()
            v.scalar_tensor_tensor(out=msk[:], in0=msk[:], scalar=-2.0, in1=eprob[:],
                                   op0=ALU.mult, op1=ALU.add)
            v.drain()
            v.reduce_max(out=rst[:, 2:3], in_=msk[:], axis=AX.X)
            v.drain()
            v.tensor_scalar(out=msk[:], in0=eprob[:], scalar1=rst[:, 2:3],
                            scalar2=None, op0=ALU.is_ge)
            v.tensor_scalar(out=rst[:, 3:4], in0=rst[:, 2:3], scalar1=1.0,
                            scalar2=None, op0=ALU.add)
            v.drain()
            v.reciprocal(out=rst[:, 4:5], in_=rst[:, 3:4])
            v.drain()
            v.tensor_tensor(out=msk[:], in0=msk[:], in1=eprob[:], op=ALU.mult)
            v.drain()
            v.tensor_scalar(out=comb[:, tb, :], in0=msk[:], scalar1=rst[:, 4:5],
                            scalar2=None, op0=ALU.mult).then_inc(s_cdone.h, 1)
        E("vector", f)

        def f(sc, par=par, v2s=v2s):
            sc.wait_ge(s_v2s.h, v2s)
            sc.activation(out=eprob[:], in_=p_r[par].ap()[:, 0:8], func=AF.Exp,
                          bias=rst[:, 1:2]).then_inc(s_rlev.h, 1)
            sc.drain().then_inc(s_s2v.h, 1)
        E("scalar", f)
        s_s2v.inc(1)

    p.tape_barrier()

    TPH = 16
    n_eq = 0
    n_t = 0
    n_b = 0
    n_og = 0
    for H in range(2):
        eq_list = [(H, e, q) for e in range(n_exp) for q in range(4)]
        pending = None

        def emit_pending():
            nonlocal pending, n_og
            if pending is None:
                return
            key, it, og0 = pending
            _emit_moe_outgroups(p, key, it, ht, w1q, w2q, b1t, b2t, ones_r, p_o,
                                acc, comb, s_hev, s_oev, s_ogrp, og0)
            n_og += 8
            s_ogrp.inc(8)
            s_oev.inc(8)
            pending = None

        for (H_, e, q) in eq_list:
            idx_eq = n_eq
            n_eq += 1
            par = idx_eq % 2
            epar = e % 2
            need_og = (idx_eq - 1) * 32

            def f(g, e=e, q=q, par=par, epar=epar, need=need_og):
                if need > 0:
                    g.wait_ge(s_ogrp.h, need)
                g.dma_start(out=w1q[par][:],
                            in_=w1sT[e, q, :, :, :]).then_inc(s_wld[par].h, 16)
                if q == 0:
                    g.dma_start(out=b1t[epar][:], in_=b1s[e, :, :]).then_inc(s_wld[par].h, 16)
            E("gpsimd", f)

            def f(s_, e=e, q=q, par=par, epar=epar, need=need_og):
                if need > 0:
                    s_.wait_ge(s_ogrp.h, need)
                s_.dma_start(out=w2q[par][:],
                             in_=w2sT[e, q, :, :, :]).then_inc(s_wld[par].h, 16)
                if q == 0:
                    s_.dma_start(out=b2t[epar][:], in_=b2s[e, :, :]).then_inc(s_wld[par].h, 16)
            E("sync", f)
            s_wld[par].inc(32 + (32 if q == 0 else 0))
            wld_after = s_wld[par].n

            for tau in range(4):
                idx_t = n_t
                n_t += 1
                tpar = idx_t % 2
                need_hgrp = (idx_t - 1) * 8

                def f(s_, H=H, tau=tau, tpar=tpar, need=need_hgrp):
                    if need > 0:
                        s_.wait_ge(s_hgrp.h, need)
                    s_.dma_start(out=xnt[tpar][:],
                                 in_=xnf[:, :, H * 2048 + tau * 512:H * 2048 + (tau + 1) * 512]
                                 ).then_inc(s_xld[tpar].h, 16)
                E("sync", f)
                xld_after = s_xld[tpar].inc(16)

                def f(t, par=par, tpar=tpar, wld=wld_after, xld=xld_after, gb0=n_b):
                    t.wait_ge(s_wld[par].h, wld)
                    t.wait_ge(s_xld[tpar].h, xld)
                    for b in range(8):
                        gb = gb0 + b
                        if gb >= 2:
                            t.wait_ge(s_hev.h, gb - 1)
                        ph = p_h[gb % 2].ap()
                        for c in range(8):
                            mm = t.matmul(ph[:, 0:512], w1q[par][:, c, b * 128:(b + 1) * 128],
                                          xnt[tpar][:, c, :], start=(c == 0),
                                          stop=(c == 7))
                        mm.then_inc(s_hgrp.h, 1)
                E("tensor", f)
                gb0_this = n_b
                n_b += 8
                s_hgrp.inc(8)

                def f(sc, q=q, par=par, epar=epar, tpar=tpar, gb0=gb0_this, wld=wld_after,
                      need_og=(idx_t - 1) * 8):
                    sc.wait_ge(s_wld[par].h, wld)
                    if need_og > 0:
                        sc.wait_ge(s_ogrp.h, need_og)
                    for b in range(8):
                        gb = gb0 + b
                        sc.wait_ge(s_hgrp.h, gb + 1)
                        ph = p_h[gb % 2].ap()
                        sc.activation(out=ht[tpar][:, b, :], in_=ph[:, 0:512],
                                      func=AF.Gelu_apprx_tanh,
                                      bias=b1t[epar][:, q * 8 + b:q * 8 + b + 1]
                                      ).then_inc(s_hev.h, 1)
                E("scalar", f)
                s_hev.inc(8)

                emit_pending()
                pending = ((H, e, q, tau), idx_t, n_og)

        emit_pending()
        p.tape_barrier()

        for tl in range(TPH):
            tb = H * TPH + tl
            par = tl % 2
            need_cv = s_cv.n - 1

            def f(s_, tb=tb, par=par, need=need_cv):
                if need > 0:
                    s_.wait_ge(s_cv.h, need)
                s_.dma_start(out=xcb[par][:],
                             in_=x[tb * 128:(tb + 1) * 128, :]).then_inc(s_cxl[par].h, 16)
                s_.dma_start(out=bab[par][:],
                             in_=brA[tb * 128:(tb + 1) * 128, :]).then_inc(s_cxl[par].h, 16)
            E("sync", f)
            cxl = s_cxl[par].inc(32)
            need_out = s_x1o[par].n

            def f(v, tb=tb, tl=tl, par=par, cxl=cxl, need_out=need_out):
                v.wait_ge(s_cld2.h, CLD2)
                v.wait_ge(s_cxl[par].h, cxl)
                if need_out > 0:
                    v.wait_ge(s_x1o[par].h, need_out)
                v.tensor_scalar(out=xn_s[:], in0=xcb[par][:], scalar1=mu_sb[:, tb:tb + 1],
                                scalar2=rstd_sb[:, tb:tb + 1], op0=ALU.add, op1=ALU.mult)
                v.drain()
                v.tensor_tensor(out=xn_s[:], in0=xn_s[:], in1=acc[:, tl, :], op=ALU.add)
                v.drain()
                v.tensor_tensor(out=xn_s[:], in0=xn_s[:], in1=xcb[par][:], op=ALU.add)
                v.drain()
                v.tensor_tensor(out=x1b[par][:], in0=xn_s[:], in1=bab[par][:], op=ALU.add)
                v.drain().then_inc(s_cv.h, 1)
            E("vector", f)
            cv = s_cv.inc(1)

            def f(g, tb=tb, par=par, cv=cv):
                g.wait_ge(s_cv.h, cv)
                g.dma_start(out=x1_out[tb * 128:(tb + 1) * 128, :],
                            in_=x1b[par][:]).then_inc(s_x1o[par].h, 16)
            E("gpsimd", f)
            s_x1o[par].inc(16)

        p.tape_barrier()

    t0_, t1_ = s_x1o[0].n, s_x1o[1].n

    def f(g):
        g.wait_ge(s_x1o[0].h, t0_)
        g.wait_ge(s_x1o[1].h, t1_)
    E("gpsimd", f)


def _emit_moe_outgroups(p, key, idx_t, ht, w1q, w2q, b1t, b2t, ones_r, p_o,
                        acc, comb, s_hev, s_oev, s_ogrp, og0):
    (H, e, q, tau) = key
    idx_eq = idx_t // 4
    par = idx_eq % 2
    epar = e % 2
    tpar = idx_t % 2
    hev_need = (idx_t + 1) * 8

    def f(t, q=q, par=par, epar=epar, tpar=tpar, og0=og0, hev=hev_need):
        t.wait_ge(s_hev.h, hev)
        for og in range(8):
            tl_, dt = og // 2, og % 2
            gog = og0 + og
            if gog >= 2:
                t.wait_ge(s_oev.h, gog - 1)
            po = p_o[gog % 2].ap()
            for b in range(8):
                mm = t.matmul(po[:, 0:512],
                              ht[tpar][:, b, tl_ * 128:(tl_ + 1) * 128],
                              w2q[par][:, b, dt * 512:(dt + 1) * 512],
                              start=(b == 0), stop=(q != 3 and b == 7))
            if q == 3:
                mm = t.matmul(po[:, 0:512], ones_r[:], b2t[epar][:, dt * 512:(dt + 1) * 512],
                              start=False, stop=True)
            mm.then_inc(s_ogrp.h, 1)
    p.emit("tensor", f)

    ogrp_base = s_ogrp.n

    def f(v, e=e, q=q, tau=tau, H=H, og0=og0, ogrp_base=ogrp_base):
        for og in range(8):
            tl_, dt = og // 2, og % 2
            gog = og0 + og
            v.wait_ge(s_ogrp.h, ogrp_base + og + 1)
            po = p_o[gog % 2].ap()
            tl_abs = tau * 4 + tl_
            tb_abs = H * 16 + tl_abs
            dst = acc[:, tl_abs, dt * 512:(dt + 1) * 512]
            if e == 0 and q == 0:
                v.tensor_scalar(out=dst, in0=po[:, 0:512], scalar1=1.0,
                                scalar2=None, op0=ALU.mult).then_inc(s_oev.h, 1)
            elif e == 0:
                v.scalar_tensor_tensor(out=dst, in0=po[:, 0:512], scalar=1.0,
                                       in1=dst, op0=ALU.mult,
                                       op1=ALU.add).then_inc(s_oev.h, 1)
            else:
                v.scalar_tensor_tensor(out=dst, in0=po[:, 0:512],
                                       scalar=comb[:, tb_abs, e - 1:e],
                                       in1=dst, op0=ALU.mult,
                                       op1=ALU.add).then_inc(s_oev.h, 1)
    p.emit("vector", f)


def build_moe(n_exp=9):
    ctx = contextlib.ExitStack()
    with ctx:
        nc = bass.Bass("TRN2", target_bir_lowering=False, debug=False)
        p = Prog(ctx, nc)
        xnf = nc.dram_tensor("xnf", [128, 8, T], BF16, kind="ExternalInput").ap()
        x = nc.dram_tensor("x", [T, D], F32, kind="ExternalInput").ap()
        brA = nc.dram_tensor("brA", [T, D], F32, kind="ExternalInput").ap()
        mu = nc.dram_tensor("mu", [128, NTB], F32, kind="ExternalInput").ap()
        rstd = nc.dram_tensor("rstd", [128, NTB], F32, kind="ExternalInput").ap()
        rwT = nc.dram_tensor("rwT", [1024, 8], F32, kind="ExternalInput").ap()
        rb = nc.dram_tensor("rb", [1, 8], F32, kind="ExternalInput").ap()
        w1sT = nc.dram_tensor("w1sT", [9, 8, 128, 4096], F32, kind="ExternalInput").ap()
        w2sT = nc.dram_tensor("w2sT", [9, 32, 128, 1024], F32, kind="ExternalInput").ap()
        b1s = nc.dram_tensor("b1s", [9, 128, 32], F32, kind="ExternalInput").ap()
        b2s = nc.dram_tensor("b2s", [9, 1, 1024], F32, kind="ExternalInput").ap()
        ones1 = nc.dram_tensor("ones1", [1, 128], F32, kind="ExternalInput").ap()
        x1 = nc.dram_tensor("x1", [T, D], F32, kind="ExternalOutput").ap()
        p.bar = p.sem("bar")
        p.tape_init()
        emit_moe(p, xnf, x, brA, mu, rstd, rwT, rb, w1sT, w2sT, b1s, b2s, ones1,
                 x1, n_exp=n_exp)
        with nc.Block() as block:
            p.block = block
            p.flush()
        return nc


def moe_host_layouts(inp):
    w1 = np.concatenate([inp["L0_ffn_shared_w1"][None], inp["L0_ffn_spec_w1"]], 0)
    w2 = np.concatenate([inp["L0_ffn_shared_w2"][None], inp["L0_ffn_spec_w2"]], 0)
    b1 = np.concatenate([inp["L0_ffn_shared_b1"][None], inp["L0_ffn_spec_b1"]], 0)
    b2 = np.concatenate([inp["L0_ffn_shared_b2"][None], inp["L0_ffn_spec_b2"]], 0)
    w1sT = np.ascontiguousarray(
        w1.transpose(0, 2, 1).reshape(9, 8, 128, 4, 1024).transpose(0, 3, 2, 1, 4))
    w2sT = np.ascontiguousarray(
        w2.transpose(0, 2, 1).reshape(9, 4, 8, 128, 1024).transpose(0, 1, 3, 2, 4))
    b1s = np.ascontiguousarray(b1.reshape(9, 32, 128).transpose(0, 2, 1))
    b2s = np.ascontiguousarray(b2.reshape(9, 1, 1024))
    rwT = np.ascontiguousarray(inp["L0_ffn_router_w"].T)
    rb = np.ascontiguousarray(inp["L0_ffn_router_b"].reshape(1, 8))
    ones1 = np.ones((1, 128), np.float32)
    return dict(w1sT=w1sT, w2sT=w2sT, b1s=b1s, b2s=b2s, rwT=rwT, rb=rb, ones1=ones1)



NH = 32
HD = 64
DIN = 2048
SCW = 256
CPS = SCW // 128
NSC = T // SCW
NCH = 32


def emit_mamba(p, xn2f, wzT, wxT, wdT, woT, convw, convb, dtb4, alog, dvec,
               normw, triu_i, tril_s, ones128, ones1f, identf, identd,
               ltdram, brA_out, dbg=None):
    nc = p.nc
    E = p.emit

    wz = p.sbuf("ma_wz", [128, 8, 2048], BF16)
    wx = p.sbuf("ma_wx", [128, 8, 2304], BF16)
    wd = p.sbuf("ma_wd", [128, 8, 32], BF16)
    wo = p.sbuf("ma_wo", [128, 16, 1024], BF16)
    cw = p.sbuf("ma_cw", [128, 18, 4], F32)
    cb = p.sbuf("ma_cb", [128, 18], F32)
    dtbr = p.sbuf("ma_dtbr", [128, CPS * 32], F32)
    nwr = p.sbuf("ma_nwr", [128, 2048], BF16)
    tri1 = p.sbuf("ma_tri1", [128, 128], F32)
    tri2 = p.sbuf("ma_tri2", [128, 128], F32)
    on128 = p.sbuf("ma_on128", [128, 128], F32)
    on1f = p.sbuf("ma_on1f", [1, 128], F32)
    non1f = p.sbuf("ma_non1f", [1, 128], F32)
    idf = p.sbuf("ma_idf", [128, 128], F32)
    idb = p.sbuf("ma_idb", [128, 128], BF16)
    alog_s = p.sbuf("ma_alog", [1, 32], F32)
    arow = p.sbuf("ma_arow", [1, 128], F32)
    arep = p.sbuf("ma_arep", [128, 128], F32)
    dvec_s = p.sbuf("ma_dvec", [1, 32], F32)
    drow = p.sbuf("ma_drow", [1, 32], BF16)
    drep = p.sbuf("ma_drep", [128, 32], BF16)
    epsc = p.sbuf("ma_eps", [128, 1], F32)

    xn2t = p.sbuf("ma_xn2t", [128, 8, SCW], BF16)
    z_sil = p.sbuf("ma_zsil", [128, CPS, 2048], BF16)
    xbce = [p.sbuf(f"ma_xbce{i}", [128, SCW + 3], F32) for i in range(2)]
    xbcc = [p.sbuf(f"ma_xbcc{i}", [128, SCW], F32) for i in range(2)]
    halo = p.sbuf("ma_halo", [128, 18, 3], F32)
    xs_fm = p.sbuf("ma_xsfm", [128, 16, SCW], BF16)
    b_fm = p.sbuf("ma_bfm", [128, SCW], BF16)
    c_fm = p.sbuf("ma_cfm", [128, SCW], BF16)
    dt_raw = p.sbuf("ma_dtraw", [128, CPS, 32], F32)
    dtp = p.sbuf("ma_dtp", [128, CPS * 32], F32)
    dtp_bf = p.sbuf("ma_dtpbf", [128, CPS, 32], BF16)
    adt = p.sbuf("ma_adt", [128, CPS * 32], F32)
    sp1 = p.sbuf("ma_sp1", [128, CPS * 32], F32)
    sp2 = p.sbuf("ma_sp2", [128, CPS * 32], F32)
    ltok = p.sbuf("ma_ltok", [128, 32], F32)
    etok = p.sbuf("ma_etok", [128, CPS, 32], F32)
    lam = p.sbuf("ma_lam", [128, CPS, 32], F32)
    w2bf = p.sbuf("ma_w2bf", [128, CPS, 32], BF16)
    w2f = p.sbuf("ma_w2f", [128, 32], F32)
    ltt = p.sbuf("ma_ltt", [32, 128], F32)
    pstr = p.sbuf("ma_pstr", [1, 4096], F32)
    g2m = p.sbuf("ma_g2m", [128, 128], BF16)
    btok = p.sbuf("ma_btok", [128, 128], BF16)
    xtok = p.sbuf("ma_xtok", [128, 2048], BF16)
    xw1 = p.sbuf("ma_xw1", [128, 2048], BF16)
    xw2 = p.sbuf("ma_xw2", [128, 2048], BF16)
    decf = p.sbuf("ma_decf", [128, 512], F32)
    decb = p.sbuf("ma_decb", [128, 512], BF16)
    sct = [p.sbuf(f"ma_sct{i}", [128, 512], BF16) for i in range(2)]
    ych = p.sbuf("ma_ych", [128, 2048], F32)
    ssb = p.sbuf("ma_ssb", [128, 2048], F32)
    sbf = p.sbuf("ma_sbf", [128, 2048], BF16)
    ysq = p.sbuf("ma_ysq", [128, 2048], F32)
    gfm = p.sbuf("ma_gfm", [128, 16, 128], BF16)
    oout = [p.sbuf("ma_oout0", [128, 1024], F32)]

    pb = [p.psum_bank(f"ma_p{i}") for i in range(8)]
    pbb = [b.ap().bitcast(BF16) for b in pb]

    s_w = SemC(p.sem("ma_w"))
    s_ws = SemC(p.sem("ma_ws"))
    s_x = [SemC(p.sem(f"ma_x{i}")) for i in range(2)]
    s_pmm = SemC(p.sem("ma_pmm"))
    s_pev = SemC(p.sem("ma_pev"))
    s_v2s = SemC(p.sem("ma_v2s"))
    s_s2v = SemC(p.sem("ma_s2v"))
    s_vg = SemC(p.sem("ma_vg"))
    s_sg = SemC(p.sem("ma_sg"))
    s_pg = SemC(p.sem("ma_pg"))
    s_tg = SemC(p.sem("ma_tg"))
    s_lts = [SemC(p.sem(f"ma_lts{i}")) for i in range(2)]
    s_ltl = SemC(p.sem("ma_ltl"))
    s_oo = [SemC(p.sem(f"ma_oo{i}")) for i in range(2)]
    s_t34 = SemC(p.sem("ma_t34"))
    s_e34 = SemC(p.sem("ma_e34"))

    def tW(t, val):
        if val > 0:
            t.wait_ge(s_tg.h, val)

    def f(g):
        g.dma_start(out=wz[:], in_=wzT.rearrange("(c p) z -> p c z", p=128)).then_inc(s_w.h, 16)
        g.dma_start(out=wx[:], in_=wxT.rearrange("(c p) z -> p c z", p=128)).then_inc(s_w.h, 16)
        g.dma_start(out=wd[:], in_=wdT.rearrange("(c p) z -> p c z", p=128)).then_inc(s_w.h, 16)
        g.dma_start(out=wo[:], in_=woT.rearrange("(c p) z -> p c z", p=128)).then_inc(s_w.h, 16)
        g.dma_start(out=idb[:], in_=identd[:, :]).then_inc(s_w.h, 16)
        g.dma_start(out=drow[:], in_=dvec[:, :]).then_inc(s_w.h, 16)
        g.dma_start(out=nwr[:], in_=normw[:, :]).then_inc(s_w.h, 16)
    E("gpsimd", f)
    s_w.inc(112)
    W0 = s_w.n

    def f(s_):
        for dst, srct in ((cw, convw), (cb, convb), (dtbr, dtb4),
                          (tri1, triu_i), (tri2, tril_s), (on128, ones128),
                          (on1f, ones1f), (idf, identf), (alog_s, alog),
                          (dvec_s, dvec)):
            s_.dma_start(out=dst[:], in_=srct[:, :] if srct.ndim == 2 else srct[:, :, :]).then_inc(s_ws.h, 16)
    E("sync", f)
    s_ws.inc(16 * 10)
    WS0 = s_ws.n

    def f(sc):
        sc.wait_ge(s_ws.h, WS0)
        sc.activation(out=arow[:, 0:32], in_=alog_s[:], func=AF.Exp, scale=1.0)
        sc.drain()
        sc.mul(out=arow[:, 0:32], in_=arow[:, 0:32], mul=-1.0)
        sc.drain()
        for j in range(1, 4):
            sc.copy(out=arow[:, 32 * j:32 * (j + 1)], in_=arow[:, 0:32])
        sc.drain().then_inc(s_sg.h, 1)
    E("scalar", f)
    s_sg.inc(1)
    SG_A = s_sg.n

    def f(t):
        t.wait_ge(s_sg.h, SG_A)
        t.wait_ge(s_w.h, W0)
        t.matmul(pb[2].ap()[:, 0:128], on1f[:], arow[:], start=True,
                 stop=True).then_inc(s_tg.h, 1)
    E("tensor", f)
    s_tg.inc(1)
    TG_A = s_tg.n

    def f(v):
        v.memset(epsc[:], EPS)
        v.memset(ssb[:], 0.0)
        v.memset(sbf[:], 0.0)
        v.memset(halo[:], 0.0)
        v.wait_ge(s_tg.h, TG_A)
        v.tensor_copy(out=arep[:], in_=pb[2].ap()[:, 0:128])
        v.drain().then_inc(s_vg.h, 1)
    E("vector", f)
    s_vg.inc(1)
    VG_A = s_vg.n

    onesb = p.sbuf("ma_onesb", [1, 128], BF16)

    def f(v):
        v.tensor_copy(out=onesb[:], in_=on1f[:])
        v.drain().then_inc(s_vg.h, 1)
    E("vector", f)
    s_vg.inc(1)
    VG_OB = s_vg.n

    def f(t):
        t.wait_ge(s_vg.h, VG_OB)
        t.matmul(pb[7].ap()[:, 0:32], onesb[:], drow[:], start=True, stop=True).then_inc(s_tg.h, 1)
    E("tensor", f)
    s_tg.inc(1)
    TG_D = s_tg.n

    def f(v):
        v.wait_ge(s_tg.h, TG_D)
        v.tensor_copy(out=drep[:], in_=pb[7].ap()[:, 0:32])
        v.drain().then_inc(s_vg.h, 1)
    E("vector", f)
    s_vg.inc(1)

    p.tape_barrier()

    p.ma = dict(pgrp=0, t34=0, ldiff_done={}, sil={}, zs={}, dt_ready={},
                strip_ld={}, lblk={})
    ctxd = dict(locals())

    def f(v):
        v.tensor_scalar(out=non1f[:], in0=on1f[:], scalar1=-1.0, scalar2=None,
                        op0=ALU.mult)
        v.drain().then_inc(s_vg.h, 1)
    E("vector", f)
    s_vg.inc(1)

    for S in range(NSC):
        _mamba_stage1(p, S, ctxd)
        for cloc in range(CPS):
            g = CPS * S + cloc
            _mamba_scan_chunk(p, S, cloc, g, ctxd)
        p.tape_barrier()

    o0, o1 = s_oo[0].n, s_oo[1].n

    def f(s_):
        s_.wait_ge(s_oo[0].h, o0)
        s_.wait_ge(s_oo[1].h, o1)
    E("sync", f)


def _mamba_stage1(p, S, L):
    E = p.emit
    xn2t, wz, wx, wd = L["xn2t"], L["wz"], L["wx"], L["wd"]
    pb = L["pb"]
    z_sil, xbce, xbcc, halo = L["z_sil"], L["xbce"], L["xbcc"], L["halo"]
    xs_fm, b_fm, c_fm = L["xs_fm"], L["b_fm"], L["c_fm"]
    dt_raw, dtp, dtp_bf, adt = L["dt_raw"], L["dtp"], L["dtp_bf"], L["adt"]
    sp1, sp2 = L["sp1"], L["sp2"]
    ltok, etok, lam, w2bf, w2f = L["ltok"], L["etok"], L["lam"], L["w2bf"], L["w2f"]
    ltt = L["ltt"]
    cw, cb, dtbr, arep = L["cw"], L["cb"], L["dtbr"], L["arep"]
    tri1, tri2, on128, idf = L["tri1"], L["tri2"], L["on128"], L["idf"]
    xn2f, ltdram = L["xn2f"], L["ltdram"]
    s_x, s_pmm, s_pev = L["s_x"], L["s_pmm"], L["s_pev"]
    s_vg, s_sg, s_pg = L["s_vg"], L["s_sg"], L["s_pg"]
    s_v2s, s_s2v = L["s_v2s"], L["s_s2v"]
    s_tg = L["s_tg"]
    s_lts = L["s_lts"]
    s_w, s_ws = L["s_w"], L["s_ws"]
    W0, WS0 = L["W0"], L["WS0"]

    xl = s_x[S % 2].inc(16)

    def f(s_, S=S):
        s_.wait_ge(s_w.h, W0)
        s_.dma_start(out=xn2t[:], in_=xn2f[:, :, S * SCW:(S + 1) * SCW]
                     ).then_inc(s_x[S % 2].h, 16)
    E("sync", f)

    groups = ([("z", i // 4, i % 4) for i in range(4 * CPS)]
              + [("x", cb_, None) for cb_ in range(18)]
              + [("dt", tb, None) for tb in range(CPS)])
    pg0 = p.ma["pgrp"]
    for gi, (kind, a1, a2) in enumerate(groups):
        gidx = pg0 + gi
        bank = pb[gidx % 2]
        need_ev = gidx - 1

        def f(t, kind=kind, a1=a1, a2=a2, bank=bank, xl=xl, need=need_ev, S=S):
            t.wait_ge(s_x[S % 2].h, xl)
            if need > 0:
                t.wait_ge(s_pev.h, need)
            pa = bank.ap()
            if kind == "z":
                for c in range(8):
                    mm = t.matmul(pa[:, 0:512], xn2t[:, c, a1 * 128:(a1 + 1) * 128],
                                  wz[:, c, a2 * 512:(a2 + 1) * 512],
                                  start=(c == 0), stop=(c == 7))
            elif kind == "x":
                for c in range(8):
                    mm = t.matmul(pa[:, 0:SCW], wx[:, c, a1 * 128:(a1 + 1) * 128],
                                  xn2t[:, c, :], start=(c == 0), stop=(c == 7))
            else:
                for c in range(8):
                    mm = t.matmul(pa[:, 0:32], xn2t[:, c, a1 * 128:(a1 + 1) * 128],
                                  wd[:, c, :], start=(c == 0), stop=(c == 7))
            mm.then_inc(s_pmm.h, 1)
        E("tensor", f)
        pmm = s_pmm.inc(1)

        if kind == "z":
            def f(sc, a1=a1, a2=a2, bank=bank, pmm=pmm):
                sc.wait_ge(s_pmm.h, pmm)
                sc.activation(out=z_sil[:, a1, a2 * 512:(a2 + 1) * 512],
                              in_=bank.ap()[:, 0:512],
                              func=AF.Silu).then_inc(s_pev.h, 1)
            E("scalar", f)
            s_pev.inc(1)
        elif kind == "x":
            cc = a1
            par = cc % 2
            conv_prev = p.ma.get("conv_pg", {}).get(cc - 2, 0)
            sil_prev = p.ma.get("sil_sg", {}).get(cc - 2, 0)

            def f(v, cc=cc, par=par, bank=bank, pmm=pmm, cprev=conv_prev):
                v.wait_ge(s_pmm.h, pmm)
                if cprev > 0:
                    v.wait_ge(s_pg.h, cprev)
                v.tensor_copy(out=xbce[par][:, 0:3], in_=halo[:, cc, :])
                v.tensor_copy(out=xbce[par][:, 3:SCW + 3],
                              in_=bank.ap()[:, 0:SCW])
                v.drain()
                v.tensor_copy(out=halo[:, cc, :], in_=xbce[par][:, SCW:SCW + 3])
                v.drain().then_inc(s_pev.h, 1)
            E("vector", f)
            ev = s_pev.inc(1)

            def f(v, cc=cc, par=par, ev=ev, sprev=sil_prev):
                v.wait_ge(s_pev.h, ev)
                if sprev > 0:
                    v.wait_ge(s_sg.h, sprev)
                for k in range(4):
                    win = xbce[par][:, k:k + SCW]
                    wcol = cw[:, cc, k:k + 1]
                    if k == 0:
                        v.tensor_scalar(out=xbcc[par][:, :], in0=win, scalar1=wcol,
                                        scalar2=None, op0=ALU.mult)
                    else:
                        v.scalar_tensor_tensor(out=xbcc[par][:, :], in0=win,
                                               scalar=wcol, in1=xbcc[par][:, :],
                                               op0=ALU.mult, op1=ALU.add)
                    v.drain()
                v.nop().then_inc(s_pg.h, 1)
            E("vector", f)
            p.ma.setdefault("conv_pg", {})[cc] = s_pg.inc(1)

            def f(sc, cc=cc, par=par, pg=p.ma["conv_pg"][cc]):
                sc.wait_ge(s_pg.h, pg)
                sc.wait_ge(s_ws.h, WS0)
                if cc < 16:
                    dst = xs_fm[:, cc, :]
                elif cc == 16:
                    dst = b_fm[:, :]
                else:
                    dst = c_fm[:, :]
                sc.activation(out=dst, in_=xbcc[par][:, :], func=AF.Silu,
                              bias=cb[:, cc:cc + 1]).then_inc(s_sg.h, 1)
            E("scalar", f)
            p.ma.setdefault("sil_sg", {})[cc] = s_sg.inc(1)
        else:
            def f(v, a1=a1, bank=bank, pmm=pmm):
                v.wait_ge(s_pmm.h, pmm)
                v.tensor_copy(out=dt_raw[:, a1, :],
                              in_=bank.ap()[:, 0:32]).then_inc(s_pev.h, 1)
            E("vector", f)
            s_pev.inc(1)
    p.ma["pgrp"] += len(groups)
    pev_all = s_pev.n
    p.ma["sil"][S] = s_sg.n

    def f(v, pev=pev_all):
        v.wait_ge(s_pev.h, pev)
        v.tensor_tensor(out=sp1[:], in0=dt_raw[:].rearrange("p a b -> p (a b)"),
                        in1=dtbr[:], op=ALU.add)
        v.drain()
        v.tensor_scalar(out=dtp[:], in0=sp1[:], scalar1=0.0, scalar2=None,
                        op0=ALU.max)
        v.drain().then_inc(s_v2s.h, 1)
    E("vector", f)
    v2s = s_v2s.inc(1)

    def f(sc, v2s=v2s):
        sc.wait_ge(s_v2s.h, v2s)
        sc.activation(out=sp2[:], in_=sp1[:], func=AF.Abs)
        sc.drain()
        sc.activation(out=sp2[:], in_=sp2[:], func=AF.Exp, scale=-1.0)
        sc.drain()
        sc.activation(out=sp2[:], in_=sp2[:], func=AF.Ln, bias=1.0)
        sc.drain().then_inc(s_s2v.h, 1)
    E("scalar", f)
    s2v = s_s2v.inc(1)

    def f(v, s2v=s2v):
        v.wait_ge(s_s2v.h, s2v)
        v.tensor_tensor(out=dtp[:], in0=dtp[:], in1=sp2[:], op=ALU.add)
        v.drain()
        v.tensor_tensor(out=adt[:], in0=dtp[:], in1=arep[:, 0:CPS * 32], op=ALU.mult)
        v.tensor_copy(out=dtp_bf[:, :, :],
                      in_=dtp[:].rearrange("p (a b) -> p a b", a=CPS, b=32))
        v.drain().then_inc(s_vg.h, 1)
    E("vector", f)
    dt_vg = s_vg.inc(1)
    p.ma["dt_ready"][S] = dt_vg

    for cloc in range(CPS):
        gch = CPS * S + cloc

        def f(t, cloc=cloc, dt_vg=dt_vg, pv=p.ma.get("ltev_vg", 0),
              ps=p.ma.get("lexp_sg", 0)):
            t.wait_ge(s_vg.h, dt_vg)
            if pv > 0:
                t.wait_ge(s_vg.h, pv)
            if ps > 0:
                t.wait_ge(s_sg.h, ps)
            asl = adt[:, cloc * 32:(cloc + 1) * 32]
            t.matmul(pb[2].ap()[:, 0:32], tri1[:], asl, start=True,
                     stop=True).then_inc(s_tg.h, 1)
            t.matmul(pb[7].ap()[:, 0:32], tri2[:], asl, start=True, stop=False)
            t.matmul(pb[7].ap()[:, 32:64], on128[:], asl, start=False,
                     stop=True).then_inc(s_tg.h, 1)
        E("tensor", f)
        tg_l = s_tg.inc(2)

        def f(v, tg=tg_l - 1):
            v.wait_ge(s_tg.h, tg)
            v.tensor_copy(out=ltok[:], in_=pb[2].ap()[:, 0:32])
            v.drain().then_inc(s_vg.h, 1)
        E("vector", f)
        ltok_vg = s_vg.inc(1)

        def f(sc, cloc=cloc, tg=tg_l, w2prev=p.ma.get("w2_vg", 0)):
            sc.wait_ge(s_tg.h, tg - 1)
            sc.activation(out=etok[:, cloc, :], in_=pb[2].ap()[:, 0:32], func=AF.Exp)
            sc.drain()
            sc.wait_ge(s_tg.h, tg)
            if w2prev > 0:
                sc.wait_ge(s_vg.h, w2prev)
            sc.activation(out=w2f[:, :], in_=pb[7].ap()[:, 0:32], func=AF.Exp)
            sc.drain()
            sc.activation(out=lam[:, cloc, :], in_=pb[7].ap()[:, 32:64], func=AF.Exp)
            sc.drain().then_inc(s_sg.h, 1)
        E("scalar", f)
        lexp_sg = s_sg.inc(1)
        p.ma["lexp_sg"] = lexp_sg

        def f(v, cloc=cloc, sg=lexp_sg):
            v.wait_ge(s_sg.h, sg)
            v.tensor_tensor(out=w2bf[:, cloc, :], in0=w2f[:, :],
                            in1=dtp[:, cloc * 32:(cloc + 1) * 32], op=ALU.mult)
            v.drain().then_inc(s_vg.h, 1)
        E("vector", f)
        p.ma["w2_vg"] = s_vg.inc(1)

        def f(t, vg=ltok_vg, sg=lexp_sg):
            t.wait_ge(s_vg.h, vg)
            t.wait_ge(s_sg.h, sg)
            t.transpose(pb[2].ap()[0:32, 0:128], ltok[:], idf[:]).then_inc(s_tg.h, 1)
        E("tensor", f)
        tg_t = s_tg.inc(1)

        def f(v, tg=tg_t, pstore=p.ma.get("lt_store", None)):
            v.wait_ge(s_tg.h, tg)
            if pstore is not None:
                v.wait_ge(pstore[0], pstore[1])
            v.tensor_copy(out=ltt[:], in_=pb[2].ap()[0:32, 0:128])
            v.drain().then_inc(s_vg.h, 1)
        E("vector", f)
        ltev_vg = s_vg.inc(1)
        p.ma["ltev_vg"] = ltev_vg
        p.ma["lblk"][gch] = (tg_t, ltev_vg)

        def f(gp, gch=gch, vg=ltev_vg):
            gp.wait_ge(s_vg.h, vg)
            gp.dma_start(out=ltdram[gch, 0, :, :], in_=ltt[:]).then_inc(s_lts[gch % 2].h, 16)
        E("gpsimd", f)
        s_lts[gch % 2].inc(16)
        p.ma["strip_ld"][gch] = s_lts[gch % 2].n
        p.ma["lt_store"] = (s_lts[gch % 2].h, s_lts[gch % 2].n)


def _mamba_scan_chunk(p, S, cloc, g, L):
    E = p.emit
    pb, pbb = L["pb"], L["pbb"]
    xs_fm, b_fm, c_fm = L["xs_fm"], L["b_fm"], L["c_fm"]
    btok, xtok, xw1, xw2 = L["btok"], L["xtok"], L["xw1"], L["xw2"]
    dtp_bf, w2bf, etok, lam = L["dtp_bf"], L["w2bf"], L["etok"], L["lam"]
    pstr = L["pstr"]
    decf, decb, sct, g2m = L["decf"], L["decb"], L["sct"], L["g2m"]
    ych, ssb, sbf = L["ych"], L["ssb"], L["sbf"]
    ysq = L["ysq"]
    gfm, oout = L["gfm"], L["oout"]
    gbf = L["xw1"]
    z_sil, nwr, drep, epsc = L["z_sil"], L["nwr"], L["drep"], L["epsc"]
    tri1, idb, on1f, idf, wo = L["tri1"], L["idb"], L["on1f"], L["idf"], L["wo"]
    non1f = L["non1f"]
    ltdram, brA_out = L["ltdram"], L["brA_out"]
    s_tg, s_vg, s_sg = L["s_tg"], L["s_vg"], L["s_sg"]
    s_t34, s_e34 = L["s_t34"], L["s_e34"]
    s_v2s, s_s2v = L["s_v2s"], L["s_s2v"]
    s_lts, s_ltl, s_oo = L["s_lts"], L["s_ltl"], L["s_oo"]
    sil_sg = p.ma["sil"][S]
    dt_vg = p.ma["dt_ready"][S]
    par = g % 2

    need_ld = p.ma["ldiff_done"].get(g - 1, 0)
    st_cnt = p.ma["strip_ld"][g]

    def f(s_, g=g, par=par, need=need_ld, st=st_cnt):
        s_.wait_ge(s_lts[par].h, st)
        if need > 0:
            s_.wait_ge(s_t34.h, need)
        s_.dma_start(out=pstr[:, :],
                     in_=ltdram[g, 0, :, :].rearrange("a b -> (a b)").unsqueeze(0)
                     ).then_inc(s_ltl.h, 16)
    E("sync", f)
    ltl = s_ltl.inc(16)

    def t34_op(body, extra_waits=()):
        i = p.ma["t34"]
        p.ma["t34"] += 1
        bank = pb[3 + (i % 2)]
        bb = pbb[3 + (i % 2)]
        need = i - 1

        def f(t, bank=bank, bb=bb, need=need, ws=tuple(extra_waits)):
            for (sh, sv) in ws:
                t.wait_ge(sh, sv)
            if need > 0:
                t.wait_ge(s_e34.h, need)
            body(t, bank.ap(), bb)
        E("tensor", f)
        s_t34.inc(1)
        return s_t34.n, bank, bb

    def ev34(engine, body, tcount, extra_waits=()):
        def f(e, tcount=tcount, ws=tuple(extra_waits)):
            e.wait_ge(s_t34.h, tcount)
            for (sh, sv) in ws:
                e.wait_ge(sh, sv)
            body(e)
        E(engine, f)
        s_e34.inc(1)
        return s_e34.n

    def body(t, pa, bb):
        t.transpose(bb[:, 0:128], b_fm[:, cloc * 128:(cloc + 1) * 128],
                    idb[:]).then_inc(s_t34.h, 1)
    tc, _, bb_ = t34_op(body, extra_waits=[(s_sg.h, sil_sg)])

    def b(v, bb_=bb_, btfree=p.ma.get("xw_free", 0)):
        if btfree > 0:
            v.wait_ge(s_tg.h, btfree)
        v.tensor_copy(out=btok[:, :], in_=bb_[:, 0:128]).then_inc(s_e34.h, 1)
    ev34("vector", b, tc)

    for cc in range(16):
        def body(t, pa, bb, cc=cc):
            t.transpose(bb[:, 0:128], xs_fm[:, cc, cloc * 128:(cloc + 1) * 128],
                        idb[:]).then_inc(s_t34.h, 1)
        tc, _, bb_ = t34_op(body)

        def b(v, cc=cc, bb_=bb_):
            v.tensor_copy(out=xtok[:, cc * 128:(cc + 1) * 128],
                          in_=bb_[:, 0:128]).then_inc(s_e34.h, 1)
        ev34("vector", b, tc)
    xtok_ev = s_e34.n

    def f(v, cloc=cloc, ev=xtok_ev, dvg=dt_vg, w2vg=p.ma["w2_vg"],
          xwfree=p.ma.get("xw_free", 0)):
        v.wait_ge(s_e34.h, ev)
        if xwfree > 0:
            v.wait_ge(s_tg.h, xwfree)
        v.wait_ge(s_vg.h, max(dvg, w2vg))
        xv = xtok[:].rearrange("p (h d) -> p h d", h=32)
        v.tensor_tensor(out=xw1[:].rearrange("p (h d) -> p h d", h=32), in0=xv,
                        in1=dtp_bf[:, cloc, :].unsqueeze(2).broadcast_to([128, 32, 64]),
                        op=ALU.mult)
        v.tensor_tensor(out=xw2[:].rearrange("p (h d) -> p h d", h=32), in0=xv,
                        in1=w2bf[:, cloc, :].unsqueeze(2).broadcast_to([128, 32, 64]),
                        op=ALU.mult)
        v.drain().then_inc(s_vg.h, 1)
    E("vector", f)
    xw_vg = s_vg.inc(1)

    def f(t, cloc=cloc, sg=sil_sg, lblk=p.ma["lblk"][CPS * S + CPS - 1]):
        t.wait_ge(s_sg.h, sg)
        t.wait_ge(s_vg.h, lblk[1])
        t.matmul(pb[2].ap()[:, 0:128], b_fm[:, cloc * 128:(cloc + 1) * 128],
                 c_fm[:, cloc * 128:(cloc + 1) * 128], start=True,
                 stop=True).then_inc(s_tg.h, 1)
    E("tensor", f)
    g2_tg = s_tg.inc(1)

    def f(v, tg=g2_tg):
        v.wait_ge(s_tg.h, tg)
        v.tensor_tensor(out=g2m[:], in0=pb[2].ap()[:, 0:128], in1=tri1[:],
                        op=ALU.mult)
        v.drain().then_inc(s_vg.h, 1)
    E("vector", f)
    g2_vg = s_vg.inc(1)

    sbf_ready = p.ma.get("sbf_vg", 0)
    pb5_free0 = p.ma.get("ds_ev", 0)
    for j in range(4):
        evn = pb5_free0 if j == 0 else p.ma.get("yev", 0)

        def f(t, j=j, sr=sbf_ready, evn=evn, sg=sil_sg):
            t.wait_ge(s_sg.h, sg)
            if sr > 0:
                t.wait_ge(s_vg.h, sr)
            if evn > 0:
                t.wait_ge(s_vg.h, evn)
            t.matmul(pb[5].ap()[:, 0:512], c_fm[:, cloc * 128:(cloc + 1) * 128],
                     sbf[:, j * 512:(j + 1) * 512], start=True,
                     stop=True).then_inc(s_tg.h, 1)
        E("tensor", f)
        tgy = s_tg.inc(1)

        def f(v, j=j, tgy=tgy, cloc=cloc):
            v.wait_ge(s_tg.h, tgy)
            v.tensor_tensor(out=ych[:, j * 512:(j + 1) * 512]
                            .rearrange("p (a b) -> p a b", a=8),
                            in0=pb[5].ap()[:, 0:512].rearrange("p (a b) -> p a b", a=8),
                            in1=etok[:, cloc, j * 8:(j + 1) * 8].unsqueeze(2)
                            .broadcast_to([128, 8, 64]),
                            op=ALU.mult)
            v.drain().then_inc(s_vg.h, 1)
        E("vector", f)
        p.ma["yev"] = s_vg.inc(1)

    for hg in range(8):
        def body(t, pa, bb, hg=hg, ltl=ltl):
            t.wait_ge(s_ltl.h, ltl)
            for j in range(4):
                h = 4 * hg + j
                sl = slice(j * 128, (j + 1) * 128)
                t.matmul(pa[:, sl], on1f[:], pstr[:, h * 128:(h + 1) * 128],
                         start=True, stop=False)
                mm = t.matmul(pa[:, sl], pstr[:, h * 128:(h + 1) * 128],
                              non1f[:], start=False, stop=True)
            mm.then_inc(s_t34.h, 1)
        tc, bank, _ = t34_op(body)

        def b(v, bank=bank, prev_s2v=s_s2v.n):
            if prev_s2v > 0:
                v.wait_ge(s_s2v.h, prev_s2v)
            v.tensor_scalar(out=decf[:], in0=bank.ap()[:, 0:512], scalar1=0.0,
                            scalar2=None, op0=ALU.min)
            v.drain().then_inc(s_e34.h, 1)
        ev34("vector", b, tc)

        def f(v):
            v.nop().then_inc(s_v2s.h, 1)
        E("vector", f)
        v2s = s_v2s.inc(1)

        def f(sc, v2s=v2s, prev_sct=s_vg.n):
            sc.wait_ge(s_v2s.h, v2s)
            sc.wait_ge(s_vg.h, prev_sct)
            sc.activation(out=decb[:], in_=decf[:], func=AF.Exp)
            sc.drain().then_inc(s_s2v.h, 1)
        E("scalar", f)
        s2v = s_s2v.inc(1)

        sfree = p.ma.get("yintra_mm", {}).get(hg - 2 if hg >= 2 else -99,
                                              p.ma.get("yintra_last", 0))

        def f(v, hg=hg, s2v=s2v, gvg=g2_vg, sfree=sfree):
            v.wait_ge(s_s2v.h, s2v)
            v.wait_ge(s_vg.h, gvg)
            if sfree > 0:
                v.wait_ge(s_tg.h, sfree)
            v.tensor_tensor(out=sct[hg % 2][:].rearrange("p (a b) -> p a b", a=4),
                            in0=decb[:].rearrange("p (a b) -> p a b", a=4),
                            in1=g2m[:].unsqueeze(1).broadcast_to([128, 4, 128]),
                            op=ALU.mult)
            v.drain().then_inc(s_vg.h, 1)
        E("vector", f)
        sct_vg = s_vg.inc(1)

        j = hg // 2
        pb6_free = p.ma.get("yintra_ev", 0) if hg % 2 == 0 else 0

        def f(t, hg=hg, sv=sct_vg, xwv=xw_vg, pbf=pb6_free):
            t.wait_ge(s_vg.h, max(sv, xwv))
            if pbf > 0 and hg % 2 == 0:
                t.wait_ge(s_vg.h, pbf)
            base = (hg % 2) * 256
            for hh in range(4):
                h = 4 * hg + hh
                mm = t.matmul(pb[6].ap()[:, base + hh * 64:base + (hh + 1) * 64],
                              sct[hg % 2][:, hh * 128:(hh + 1) * 128],
                              xw1[:, h * 64:(h + 1) * 64],
                              start=(hg % 2 == 0 and hh == 0),
                              stop=(hg % 2 == 1 and hh == 3))
            mm.then_inc(s_tg.h, 1)
        E("tensor", f)
        p.ma.setdefault("yintra_mm", {})[hg] = s_tg.inc(1)

        if hg % 2 == 1:
            def f(v, j=j, tgi=p.ma["yintra_mm"][hg]):
                v.wait_ge(s_tg.h, tgi)
                v.scalar_tensor_tensor(out=ych[:, j * 512:(j + 1) * 512],
                                       in0=pb[6].ap()[:, 0:512], scalar=1.0,
                                       in1=ych[:, j * 512:(j + 1) * 512],
                                       op0=ALU.mult, op1=ALU.add)
                v.drain().then_inc(s_vg.h, 1)
            E("vector", f)
            p.ma["yintra_ev"] = s_vg.inc(1)
    y_vg = s_vg.n
    p.ma["yintra_last"] = p.ma["yintra_mm"][7]
    p.ma["ldiff_done"][g] = s_t34.n

    def f(v, ev=xtok_ev, yv=y_vg):
        v.wait_ge(s_vg.h, yv)
        v.tensor_tensor(out=ysq[:].rearrange("p (h d) -> p h d", h=32),
                        in0=xtok[:].rearrange("p (h d) -> p h d", h=32),
                        in1=drep[:].unsqueeze(2).broadcast_to([128, 32, 64]),
                        op=ALU.mult)
        v.drain()
        v.tensor_tensor(out=ych[:], in0=ych[:], in1=ysq[:], op=ALU.add)
        v.drain().then_inc(s_vg.h, 1)
    E("vector", f)
    yd_vg = s_vg.inc(1)

    for j in range(4):
        def f(t, j=j, xwv=xw_vg, yev=p.ma["yev"], dsev=p.ma.get("ds_ev", 0)):
            t.wait_ge(s_vg.h, xwv)
            t.wait_ge(s_vg.h, yev)
            if dsev > 0:
                t.wait_ge(s_vg.h, dsev)
            t.matmul(pb[5].ap()[:, 0:512], btok[:, :],
                     xw2[:, j * 512:(j + 1) * 512], start=True,
                     stop=True).then_inc(s_tg.h, 1)
        E("tensor", f)
        tgd = s_tg.inc(1)

        def f(v, j=j, tgd=tgd, cloc=cloc):
            v.wait_ge(s_tg.h, tgd)
            v.tensor_tensor(out=ssb[:, j * 512:(j + 1) * 512]
                            .rearrange("p (a b) -> p a b", a=8),
                            in0=ssb[:, j * 512:(j + 1) * 512]
                            .rearrange("p (a b) -> p a b", a=8),
                            in1=lam[:, cloc, j * 8:(j + 1) * 8].unsqueeze(2)
                            .broadcast_to([128, 8, 64]),
                            op=ALU.mult)
            v.drain()
            v.scalar_tensor_tensor(out=ssb[:, j * 512:(j + 1) * 512],
                                   in0=pb[5].ap()[:, 0:512], scalar=1.0,
                                   in1=ssb[:, j * 512:(j + 1) * 512],
                                   op0=ALU.mult, op1=ALU.add)
            v.drain().then_inc(s_vg.h, 1)
        E("vector", f)
        p.ma["ds_ev"] = s_vg.inc(1)
    p.ma["xw_free"] = s_tg.n

    def f(v):
        v.tensor_copy(out=sbf[:], in_=ssb[:])
        v.drain().then_inc(s_vg.h, 1)
    E("vector", f)
    p.ma["sbf_vg"] = s_vg.inc(1)

    def f(v, cloc=cloc, yv=yd_vg):
        v.wait_ge(s_vg.h, yv)
        v.tensor_tensor(out=ych[:], in0=ych[:], in1=z_sil[:, cloc, :], op=ALU.mult)
        v.drain().then_inc(s_v2s.h, 1)
    E("vector", f)
    v2s_g = s_v2s.inc(1)

    def f(sc, v2s=v2s_g):
        sc.wait_ge(s_v2s.h, v2s)
        sc.activation(out=ysq[:], in_=ych[:], func=AF.Square,
                      accum_out=L["w2f"][:, 0:1])
        sc.drain()
        sc.activation(out=L["w2f"][:, 1:2], in_=L["w2f"][:, 0:1], func=AF.Sqrt,
                      scale=1.0 / 2048, bias=epsc[:, 0:1])
        sc.drain().then_inc(s_s2v.h, 1)
    E("scalar", f)
    s2v_g = s_s2v.inc(1)

    def f(v, s2v=s2v_g):
        v.wait_ge(s_s2v.h, s2v)
        v.reciprocal(out=L["w2f"][:, 2:3], in_=L["w2f"][:, 1:2])
        v.drain()
        v.tensor_scalar(out=ych[:], in0=ych[:], scalar1=L["w2f"][:, 2:3],
                        scalar2=None, op0=ALU.mult)
        v.drain()
        v.tensor_tensor(out=gbf[:], in0=ych[:], in1=nwr[:], op=ALU.mult)
        v.drain().then_inc(s_vg.h, 1)
    E("vector", f)
    gbf_vg = s_vg.inc(1)

    for cc in range(16):
        def body(t, pa, bb, cc=cc, gv=gbf_vg):
            t.wait_ge(s_vg.h, gv)
            t.transpose(bb[:, 0:128], gbf[:, cc * 128:(cc + 1) * 128],
                        idb[:]).then_inc(s_t34.h, 1)
        tc, _, bb_ = t34_op(body)

        def b(v, cc=cc, bb_=bb_):
            v.tensor_copy(out=gfm[:, cc, :], in_=bb_[:, 0:128]).then_inc(s_e34.h, 1)
        ev34("vector", b, tc)
    gfm_ev = s_e34.n

    opar = 0
    for dt_ in range(2):
        def body(t, pa, bb, dt_=dt_, gv=gfm_ev, oo=p.ma.get("oo_done", {}).get((g - 2, dt_), 0)):
            t.wait_ge(s_e34.h, gv)
            for cc in range(16):
                mm = t.matmul(pa[:, 0:512], gfm[:, cc, :],
                              wo[:, cc, dt_ * 512:(dt_ + 1) * 512],
                              start=(cc == 0), stop=(cc == 15))
            mm.then_inc(s_t34.h, 1)
        tc, bank, _ = t34_op(body)

        def b(v, dt_=dt_, bank=bank, opar=opar, pre=s_oo[opar].n):
            if pre > 0:
                v.wait_ge(s_oo[opar].h, pre)
            v.tensor_copy(out=oout[opar][:, dt_ * 512:(dt_ + 1) * 512],
                          in_=bank.ap()[:, 0:512]).then_inc(s_e34.h, 1)
        ev34("vector", b, tc)
    oev = s_e34.n

    def f(s_, g=g, opar=opar, oev=oev):
        s_.wait_ge(s_e34.h, oev)
        s_.dma_start(out=brA_out[g * 128:(g + 1) * 128, :],
                     in_=oout[opar][:]).then_inc(s_oo[opar].h, 16)
    E("sync", f)
    s_oo[opar].inc(16)


def build_mamba():
    ctx = contextlib.ExitStack()
    with ctx:
        nc = bass.Bass("TRN2", target_bir_lowering=False, debug=False)
        p = Prog(ctx, nc)
        d = {}
        d["xn2f"] = nc.dram_tensor("xn2f", [128, 8, T], BF16, kind="ExternalInput").ap()
        for nm, shape in [("wzT", [1024, 2048]), ("wxT", [1024, 2304]),
                          ("wdT", [1024, 32]), ("woT", [2048, 1024]),
                          ("convw", [128, 18, 4]), ("convb", [128, 18]),
                          ("dtb4", [128, CPS * 32]), ("alog", [1, 32]),
                          ("dvec", [1, 32]), ("normw", [128, 2048]),
                          ("triu_i", [128, 128]), ("tril_s", [128, 128]),
                          ("ones128", [128, 128]), ("ones1f", [1, 128]),
                          ("identf", [128, 128]), ("identd", [128, 128])]:
            d[nm] = nc.dram_tensor(nm, shape, F32, kind="ExternalInput").ap()
        d["ltdram"] = nc.dram_tensor("ltdram", [32, 2, 32, 128], F32, kind="Internal").ap()
        d["brA_out"] = nc.dram_tensor("brA", [T, D], F32, kind="ExternalOutput").ap()
        p.bar = p.sem("bar")
        p.tape_init()
        emit_mamba(p, **d)
        with nc.Block() as block:
            p.block = block
            p.flush()
        return nc


def mamba_host_layouts(inp, pref="L0_attn_"):
    ipw = inp[pref + "in_proj_w"]
    e = np.eye(128, dtype=np.float32)
    return dict(
        wzT=np.ascontiguousarray(ipw[0:2048].T),
        wxT=np.ascontiguousarray(ipw[2048:4352].T),
        wdT=np.ascontiguousarray(ipw[4352:4384].T),
        woT=np.ascontiguousarray(inp[pref + "out_proj_w"].T),
        convw=np.ascontiguousarray(inp[pref + "conv_w"].reshape(18, 128, 4).transpose(1, 0, 2)),
        convb=np.ascontiguousarray(inp[pref + "conv_b"].reshape(18, 128).T),
        dtb4=np.ascontiguousarray(np.broadcast_to(np.tile(inp[pref + "dt_bias"], CPS), (128, CPS * 32))).astype(np.float32),
        alog=inp[pref + "A_log"].reshape(1, 32).astype(np.float32),
        dvec=inp[pref + "D"].reshape(1, 32).astype(np.float32),
        normw=np.ascontiguousarray(np.broadcast_to(inp[pref + "norm_w"], (128, 2048))).astype(np.float32),
        triu_i=np.triu(np.ones((128, 128), np.float32)),
        tril_s=np.tril(np.ones((128, 128), np.float32), -1),
        ones128=np.ones((128, 128), np.float32),
        ones1f=np.ones((1, 128), np.float32),
        identf=e, identd=e,
    )



def emit_l1attn(p, xn1f, wqT, wkT, wvT, woT, bq, bk, bv, bo, cosT, sinT,
                ones1f, identd, cmaskT, colones, qf, kf, vf, aof, brA1):
    nc = p.nc
    E = p.emit
    SCALE = 0.125

    wo = p.sbuf("la_wo", [128, 8, 1024], BF16)
    bqs = p.sbuf("la_bq", [1, 1024], BF16)
    bks = p.sbuf("la_bk", [1, 1024], BF16)
    bvs = p.sbuf("la_bv", [1, 1024], BF16)
    bos = p.sbuf("la_bo", [1, 1024], BF16)
    on1b = p.sbuf("la_on1b", [1, 128], BF16)
    onecol = p.sbuf("la_onecol", [64, 1], BF16)
    idb = p.sbuf("la_idb", [128, 128], BF16)
    cmask = p.sbuf("la_cmask", [128, 4, 512], BF16)
    onescol = p.sbuf("la_onescol", [128, 1], BF16)

    stA = contextlib.ExitStack()

    def sbA(nm, shape, dt):
        return stA.enter_context(nc.sbuf_tensor(nm, shape, dt))

    wq = sbA("la_wq", [128, 8, 1024], BF16)
    wk = sbA("la_wk", [128, 8, 1024], BF16)
    wv = sbA("la_wv", [128, 8, 1024], BF16)
    coss = sbA("la_cos", [128, 32, 32], F32)
    sins = sbA("la_sin", [128, 32, 32], F32)
    xt = [sbA(f"la_xt{i}", [128, 8, 128], BF16) for i in range(2)]
    qtok = sbA("la_qtok", [128, 1024], F32)
    rsc = sbA("la_rsc", [128, 4, 512], F32)
    rot = sbA("la_rot", [128, 1024], BF16)
    stg = sbA("la_stg", [128, 8, 128], BF16)

    pb = [p.psum_bank(f"la_p{i}") for i in range(8)]
    pbb = [b.ap().bitcast(BF16) for b in pb]

    s_w = SemC(p.sem("la_w"))
    s_ws = SemC(p.sem("la_ws"))
    s_x = [SemC(p.sem(f"la_x{i}")) for i in range(2)]
    s_mm = SemC(p.sem("la_mm"))
    s_ev = SemC(p.sem("la_ev"))
    s_vg = SemC(p.sem("la_vg"))
    s_sg = SemC(p.sem("la_sg"))
    s_st = {(w_, i): SemC(p.sem(f"la_st{w_}{i}")) for w_ in range(3) for i in range(2)}
    s_hl = SemC(p.sem("la_hl"))
    s_ao = SemC(p.sem("la_ao"))
    s_al = [SemC(p.sem(f"la_al{i}")) for i in range(2)]
    s_oo = [SemC(p.sem(f"la_oo{i}")) for i in range(2)]

    def f(g):
        for dst, srct in ((wq, wqT), (wk, wkT), (wv, wvT), (wo, woT)):
            g.dma_start(out=dst[:], in_=srct.rearrange("(c p) z -> p c z", p=128)
                        ).then_inc(s_w.h, 16)
        for dst, srct in ((bqs, bq), (bks, bk), (bvs, bv), (bos, bo)):
            g.dma_start(out=dst[:], in_=srct[:, :]).then_inc(s_w.h, 16)
        g.dma_start(out=idb[:], in_=identd[:, :]).then_inc(s_w.h, 16)
        g.dma_start(out=on1b[:], in_=ones1f[:, :]).then_inc(s_w.h, 16)
        g.dma_start(out=onecol[:], in_=ones1f[:, 0:64].rearrange("a b -> b a")
                    ).then_inc(s_w.h, 16)
        g.dma_start(out=cmask[:], in_=cmaskT[:, :, :]).then_inc(s_w.h, 16)
        g.dma_start(out=onescol[:], in_=colones[:, :]).then_inc(s_w.h, 16)
    E("gpsimd", f)
    s_w.inc(16 * 13)
    W0 = s_w.n

    def f(s_):
        s_.dma_start(out=coss[:], in_=cosT[:, :, :]).then_inc(s_ws.h, 16)
        s_.dma_start(out=sins[:], in_=sinT[:, :, :]).then_inc(s_ws.h, 16)
    E("sync", f)
    s_ws.inc(32)
    WS0 = s_ws.n

    s_tev = SemC(p.sem("la_tev"))
    ntr = [0]
    nqkv = [0]
    xt_used = {0: 0, 1: 0}
    last_store = [None]
    last_vstore = [None]

    for tb in range(NTB):
        par = tb % 2

        def f(s_, tb=tb, par=par, need=xt_used[par]):
            s_.wait_ge(s_w.h, W0)
            if need > 0:
                s_.wait_ge(s_mm.h, need)
            s_.dma_start(out=xt[par][:], in_=xn1f[:, :, tb * 128:(tb + 1) * 128]
                         ).then_inc(s_x[par].h, 16)
        E("sync", f)
        xl = s_x[par].inc(16)

        for wi, (wt, bt, dst_hbm) in enumerate(((wq, bqs, qf), (wk, bks, kf),
                                                (wv, bvs, vf))):
            for half in range(2):
                gi = nqkv[0]
                nqkv[0] += 1
                bank = gi % 4

                def f(t, wt=wt, bt=bt, par=par, half=half, xl=xl, bank=bank,
                      evn=s_ev.n - 3):
                    t.wait_ge(s_x[par].h, xl)
                    if evn > 0:
                        t.wait_ge(s_ev.h, evn)
                    pa = pb[bank].ap()
                    for c in range(8):
                        t.matmul(pa[:, 0:512], xt[par][:, c, :],
                                 wt[:, c, half * 512:(half + 1) * 512],
                                 start=(c == 0), stop=False)
                    t.matmul(pa[:, 0:512], on1b[:],
                             bt[:, half * 512:(half + 1) * 512],
                             start=False, stop=True).then_inc(s_mm.h, 1)
                E("tensor", f)
                mm = s_mm.inc(1)

                def f(v, half=half, mm=mm, bank=bank):
                    v.wait_ge(s_mm.h, mm)
                    v.tensor_copy(out=qtok[:, half * 512:(half + 1) * 512],
                                  in_=pb[bank].ap()[:, 0:512]).then_inc(s_ev.h, 1)
                E("vector", f)
                s_ev.inc(1)
            xt_used[par] = s_mm.n

            def f(v, wi=wi, tb=tb, ev=s_ev.n, lvs=last_vstore[0]):
                v.wait_ge(s_ws.h, WS0)
                v.wait_ge(s_ev.h, ev)
                if wi == 0 and lvs is not None:
                    v.wait_ge(lvs[0], lvs[1])
                if wi < 2:
                    qv = qtok[:].rearrange("p (h i two) -> p h i two", h=16, i=32)
                    cb_ = coss[:, tb, :].unsqueeze(1).broadcast_to([128, 16, 32])
                    sb_ = sins[:, tb, :].unsqueeze(1).broadcast_to([128, 16, 32])
                    A = rsc[:, 0, :].rearrange("p (h i) -> p h i", h=16)
                    B = rsc[:, 1, :].rearrange("p (h i) -> p h i", h=16)
                    C = rsc[:, 2, :].rearrange("p (h i) -> p h i", h=16)
                    Dd = rsc[:, 3, :].rearrange("p (h i) -> p h i", h=16)
                    rv = rot[:].rearrange("p (h i two) -> p h i two", h=16, i=32)
                    v.tensor_tensor(out=A, in0=qv[:, :, :, 0], in1=cb_, op=ALU.mult)
                    v.tensor_tensor(out=C, in0=qv[:, :, :, 0], in1=sb_, op=ALU.mult)
                    v.tensor_tensor(out=B, in0=qv[:, :, :, 1], in1=sb_, op=ALU.mult)
                    v.tensor_tensor(out=Dd, in0=qv[:, :, :, 1], in1=cb_, op=ALU.mult)
                    v.drain()
                    v.tensor_tensor(out=rv[:, :, :, 0], in0=A, in1=B, op=ALU.subtract)
                    v.tensor_tensor(out=rv[:, :, :, 1], in0=C, in1=Dd, op=ALU.add)
                else:
                    v.tensor_copy(out=rot[:], in_=qtok[:])
                v.drain().then_inc(s_vg.h, 1)
            E("vector", f)
            rope_vg = s_vg.inc(1)

            if wi < 2:
                for c in range(8):
                    i = ntr[0]
                    ntr[0] += 1
                    bank = 4 + (i % 2)

                    def f(t, c=c, bank=bank, vg=rope_vg, need=i - 1):
                        t.wait_ge(s_vg.h, vg)
                        if need > 0:
                            t.wait_ge(s_tev.h, need)
                        t.transpose(pbb[bank][:, 0:128], rot[:, c * 128:(c + 1) * 128],
                                    idb[:]).then_inc(s_mm.h, 1)
                    E("tensor", f)
                    trm = s_mm.inc(1)

                    def f(v, c=c, bank=bank, trm=trm, ls=last_store[0]):
                        v.wait_ge(s_mm.h, trm)
                        if c == 0 and ls is not None:
                            v.wait_ge(ls[0], ls[1])
                        v.tensor_copy(out=stg[:, c, :],
                                      in_=pbb[bank][:, 0:128]).then_inc(s_tev.h, 1)
                    E("vector", f)
                    s_tev.inc(1)
                stv = s_tev.n

                def f(g, tb=tb, dst=dst_hbm, stv=stv, wi=wi, par=par):
                    g.wait_ge(s_tev.h, stv)
                    g.dma_start(out=dst[:, :, tb * 128:(tb + 1) * 128],
                                in_=stg[:]).then_inc(s_st[(wi, par)].h, 16)
                E("gpsimd", f)
                s_st[(wi, par)].inc(16)
                last_store[0] = (s_st[(wi, par)].h, s_st[(wi, par)].n)
            else:
                def f(g, tb=tb, vg=rope_vg, par=par):
                    g.wait_ge(s_vg.h, vg)
                    g.dma_start(out=vf[:, tb, :],
                                in_=rot[:]).then_inc(s_st[(2, par)].h, 16)
                E("gpsimd", f)
                s_st[(2, par)].inc(16)
                last_vstore[0] = (s_st[(2, par)].h, s_st[(2, par)].n)

    p.tape_barrier()
    p.ma["t34"] = 0

    st_tot = [(s.h, s.n) for s in s_st.values()]

    def f(s_, st_tot=st_tot):
        for hh, nn in st_tot:
            s_.wait_ge(hh, nn)
    E("sync", f)
    p.tape_barrier()

    stA.close()
    stB = contextlib.ExitStack()

    def sbB(nm, shape, dt):
        return stB.enter_context(nc.sbuf_tensor(nm, shape, dt))

    qext = [sbB(f"la_qext{i}", [65, T], BF16) for i in range(2)]
    kext = [sbB(f"la_kext{i}", [65, T], BF16) for i in range(2)]
    sqf = sbB("la_sqf", [64, T], BF16)
    vsb = sbB("la_vsb", [128, 32, 16, 65], BF16)
    pstrip = sbB("la_pstrip", [128, 32, 512], BF16)
    r2row = sbB("la_r2", [1, T], BF16)
    brow = sbB("la_brow", [1, T], BF16)
    cells = sbB("la_cells", [1, 8], F32)
    rden = sbB("la_rden", [1, 512], F32)
    rdenb = sbB("la_rdenb", [1, 512], BF16)
    rdrep = sbB("la_rdrep", [64, 512], F32)
    aoh = [sbB(f"la_aoh{i}", [64, T], BF16) for i in range(2)]

    s_vl = s_ws
    vstg = sbB("la_vstg", [128, 2, 1024], BF16)

    for i in range(16):
        used = p.ma.get("vstg_used", 0)

        def f(g, i=i, used=used):
            if used > 0:
                g.wait_ge(s_vg.h, used)
            g.dma_start(out=vstg[:], in_=vf[:, i * 2:(i + 1) * 2, :]
                        ).then_inc(s_vl.h, 16)
        E("gpsimd", f)
        vl = s_vl.inc(16)

        def f(v, i=i, vl=vl):
            v.wait_ge(s_vl.h, vl)
            for hh in range(16):
                v.tensor_copy(out=vsb[:, i * 2:(i + 1) * 2, hh, 0:64],
                              in_=vstg[:, :, hh * 64:(hh + 1) * 64])
            v.drain().then_inc(s_vg.h, 1)
        E("vector", f)
        p.ma["vstg_used"] = s_vg.inc(1)

    def f(v):
        for hh in range(16):
            v.memset(vsb[:, :, hh, 64:65], 1.0)
        v.drain().then_inc(s_vg.h, 1)
    E("vector", f)
    ones_vg = s_vg.inc(1)
    VL0 = s_vl.n

    s_se = SemC(p.sem("la_se"))
    QT = 8
    for h in range(16):
        par = h % 2
        hp = (h % 2) * 64
        hc = h // 2
        qe = qext[par]
        ke = kext[par]
        ao = aoh[par]
        need_mm = p.ma.get(("qk_used", par), 0)

        def f(s_, hp=hp, hc=hc, par=par, need=need_mm):
            if need > 0:
                s_.wait_ge(s_mm.h, need)
            s_.dma_start(out=qext[par][0:64, :], in_=qf[hp:hp + 64, hc, :]).then_inc(s_hl.h, 16)
            s_.dma_start(out=kext[par][0:64, :], in_=kf[hp:hp + 64, hc, :]).then_inc(s_hl.h, 16)
        E("sync", f)
        hl = s_hl.inc(32)

        def f(v, qe=qe, hl=hl):
            v.wait_ge(s_hl.h, hl)
            v.tensor_tensor(out=sqf[:, :], in0=qe[0:64, :], in1=qe[0:64, :],
                            op=ALU.mult)
            v.drain().then_inc(s_vg.h, 1)
        E("vector", f)
        sq_vg = s_vg.inc(1)

        for i in range(8):
            bank = 6 + (i % 2)

            def f(t, i=i, bank=bank, vg=sq_vg, evp=s_vg.n - 1):
                t.wait_ge(s_vg.h, vg)
                if i >= 2:
                    t.wait_ge(s_vg.h, evp)
                t.matmul(pb[bank].ap()[0:1, 0:512], onecol[:],
                         sqf[:, i * 512:(i + 1) * 512], start=True,
                         stop=True).then_inc(s_mm.h, 1)
            E("tensor", f)
            mmi = s_mm.inc(1)

            def f(v, i=i, bank=bank, mmi=mmi):
                v.wait_ge(s_mm.h, mmi)
                v.tensor_copy(out=r2row[:, i * 512:(i + 1) * 512],
                              in_=pb[bank].ap()[0:1, 0:512]).then_inc(s_vg.h, 1)
            E("vector", f)
            s_vg.inc(1)
        rq_ev = s_vg.n

        def f(v, ke=ke, ev=rq_ev):
            v.wait_ge(s_vg.h, ev)
            v.tensor_tensor(out=sqf[:, :], in0=ke[0:64, :], in1=ke[0:64, :],
                            op=ALU.mult)
            v.drain().then_inc(s_vg.h, 1)
        E("vector", f)
        sqk_vg = s_vg.inc(1)

        for i in range(8):
            bank = 6 + (i % 2)

            def f(t, i=i, bank=bank, vg=sqk_vg, evp=s_vg.n - 1):
                t.wait_ge(s_vg.h, vg)
                if i >= 2:
                    t.wait_ge(s_vg.h, evp)
                t.matmul(pb[bank].ap()[0:1, 0:512], onecol[:],
                         sqf[:, i * 512:(i + 1) * 512], start=True,
                         stop=True).then_inc(s_mm.h, 1)
            E("tensor", f)
            mmi = s_mm.inc(1)

            def f(v, i=i, bank=bank, mmi=mmi):
                v.wait_ge(s_mm.h, mmi)
                v.tensor_copy(out=brow[:, i * 512:(i + 1) * 512],
                              in_=pb[bank].ap()[0:1, 0:512]).then_inc(s_vg.h, 1)
            E("vector", f)
            s_vg.inc(1)

        def f(v):
            v.drain()
            v.reduce_max(out=cells[:, 0:1], in_=brow[:, :], axis=AX.X)
            v.drain().then_inc(s_vg.h, 1)
        E("vector", f)
        rk_ev = s_vg.inc(1)

        def f(sc, ev=rk_ev):
            sc.wait_ge(s_vg.h, ev)
            sc.activation(out=cells[:, 1:2], in_=cells[:, 0:1], func=AF.Sqrt)
            sc.drain()
            sc.mul(out=cells[:, 2:3], in_=cells[:, 1:2], mul=-1.0)
            sc.drain()
            sc.activation(out=brow[:, :], in_=r2row[:, :], func=AF.Sqrt)
            sc.drain().then_inc(s_sg.h, 1)
        E("scalar", f)
        b_sg = s_sg.inc(1)

        def f(v, qe=qe, ke=ke, sg=b_sg):
            v.wait_ge(s_sg.h, sg)
            v.tensor_scalar(out=qe[64:65, :], in0=brow[:, :], scalar1=cells[:, 2:3],
                            scalar2=None, op0=ALU.mult)
            v.memset(ke[64:65, :], 1.0)
            v.drain().then_inc(s_vg.h, 1)
        E("vector", f)
        bnd_vg = s_vg.inc(1)

        for qt in range(QT):
            khi = 4 * (qt + 1)
            for kb in range(khi):
                bank = kb % 2

                def f(t, qe=qe, ke=ke, qt=qt, kb=kb, bank=bank, bvg=bnd_vg,
                      sen=s_se.n - 1):
                    t.wait_ge(s_vg.h, bvg)
                    if sen > 0:
                        t.wait_ge(s_se.h, sen)
                    t.matmul(pb[bank].ap()[:, 0:512], ke[:, kb * 128:(kb + 1) * 128],
                             qe[:, qt * 512:(qt + 1) * 512], start=True,
                             stop=True).then_inc(s_mm.h, 1)
                E("tensor", f)
                smm = s_mm.inc(1)

                def f(sc, kb=kb, bank=bank, smm=smm,
                      avp=p.ma.get("av_mm", 0)):
                    sc.wait_ge(s_mm.h, smm)
                    if kb == 0 and avp > 0:
                        sc.wait_ge(s_mm.h, avp)
                    sc.activation(out=pstrip[:, kb, :], in_=pb[bank].ap()[:, 0:512],
                                  func=AF.Exp, scale=SCALE).then_inc(s_se.h, 1)
                E("scalar", f)
                s_se.inc(1)
            se_all = s_se.n

            def f(v, qt=qt, khi=khi, se=se_all):
                v.wait_ge(s_se.h, se)
                for j in range(4):
                    kb = 4 * qt + j
                    v.tensor_tensor(out=pstrip[:, kb, :], in0=pstrip[:, kb, :],
                                    in1=cmask[:, j, :], op=ALU.mult)
                v.drain().then_inc(s_vg.h, 1)
            E("vector", f)
            msk_vg = s_vg.inc(1)

            def f(t, h=h, khi=khi, mv=msk_vg, avp=p.ma.get("av_ev", 0),
                  vl=VL0, ov=ones_vg):
                t.wait_ge(s_vl.h, vl)
                t.wait_ge(s_vg.h, ov)
                t.wait_ge(s_vg.h, mv)
                if avp > 0:
                    t.wait_ge(s_vg.h, avp)
                for kb in range(khi):
                    mm = t.matmul(pb[4].ap()[0:65, 0:512],
                                  vsb[:, kb, h, :],
                                  pstrip[:, kb, :], start=(kb == 0),
                                  stop=(kb == khi - 1))
                mm.then_inc(s_mm.h, 1)
            E("tensor", f)
            av_mm = s_mm.inc(1)
            p.ma["av_mm"] = av_mm

            def f(v, mm=av_mm):
                v.wait_ge(s_mm.h, mm)
                v.tensor_copy(out=rden[:, :], in_=pb[4].ap()[64:65, 0:512])
                v.drain()
                v.reciprocal(out=rden[:, :], in_=rden[:, :])
                v.drain()
                v.tensor_copy(out=rdenb[:, :], in_=rden[:, :])
                v.drain().then_inc(s_vg.h, 1)
            E("vector", f)
            p.ma["dn_ev"] = s_vg.inc(1)
            dn_ev = p.ma["dn_ev"]

            def f(t, dv=dn_ev, rp=p.ma.get("rdrep_ev", 0)):
                t.wait_ge(s_vg.h, dv)
                if rp > 0:
                    t.wait_ge(s_vg.h, rp)
                t.matmul(pb[3].ap()[0:64, 0:512], on1b[:, 0:64], rdenb[:, :],
                         start=True, stop=True).then_inc(s_mm.h, 1)
            E("tensor", f)
            rr_mm = s_mm.inc(1)

            def f(v, mm=rr_mm):
                v.wait_ge(s_mm.h, mm)
                v.tensor_copy(out=rdrep[:, :], in_=pb[3].ap()[0:64, 0:512])
                v.drain().then_inc(s_vg.h, 1)
            E("vector", f)
            p.ma["rdrep_ev"] = s_vg.inc(1)

            def f(v, ao=ao, qt=qt, mm=av_mm, rp=p.ma["rdrep_ev"],
                  aofree=p.ma.get(("aoh_store", par), 0)):
                v.wait_ge(s_mm.h, mm)
                v.wait_ge(s_vg.h, rp)
                if qt == 0 and aofree > 0:
                    v.wait_ge(s_ao.h, aofree)
                v.tensor_tensor(out=ao[:, qt * 512:(qt + 1) * 512],
                                in0=pb[4].ap()[0:64, 0:512], in1=rdrep[:, :],
                                op=ALU.mult)
                v.drain().then_inc(s_vg.h, 1)
            E("vector", f)
            p.ma["av_ev"] = s_vg.inc(1)

        def f(g, ao=ao, hp=hp, hc=hc, avl=p.ma["av_ev"]):
            g.wait_ge(s_vg.h, avl)
            g.dma_start(out=aof[hp:hp + 64, hc, :], in_=ao[:, :]).then_inc(s_ao.h, 16)
        E("gpsimd", f)
        s_ao.inc(16)
        p.ma[("aoh_store", par)] = s_ao.n
        p.ma[("qk_used", par)] = s_mm.n
        p.ma["head_done"] = s_ao.n
    p.tape_barrier()

    stB.close()
    aot = [p.sbuf(f"la_aot{i}", [128, 8, 128], BF16) for i in range(2)]
    oott = [p.sbuf(f"la_oot{i}", [128, 1024], F32) for i in range(2)]
    ao_done = s_ao.n
    for tb in range(NTB):
        par = tb % 2

        def f(s_, tb=tb, par=par, need=p.ma.get("oo_mm", 0), ad=ao_done):
            s_.wait_ge(s_ao.h, ad)
            if need > 0:
                s_.wait_ge(s_mm.h, need)
            s_.dma_start(out=aot[par][:], in_=aof[:, :, tb * 128:(tb + 1) * 128]
                         ).then_inc(s_al[par].h, 16)
        E("sync", f)
        al = s_al[par].inc(16)

        for dt_ in range(2):
            bank = (2 * tb + dt_) % 2

            def f(t, par=par, dt_=dt_, al=al, bank=bank, evn=p.ma.get("oo_ev", 0) - 1):
                t.wait_ge(s_al[par].h, al)
                if evn > 0:
                    t.wait_ge(s_vg.h, evn)
                pa = pb[bank].ap()
                for c in range(8):
                    t.matmul(pa[:, 0:512], aot[par][:, c, :],
                             wo[:, c, dt_ * 512:(dt_ + 1) * 512],
                             start=(c == 0), stop=False)
                t.matmul(pa[:, 0:512], on1b[:], bos[:, dt_ * 512:(dt_ + 1) * 512],
                         start=False, stop=True).then_inc(s_mm.h, 1)
            E("tensor", f)
            mm = s_mm.inc(1)

            def f(v, par=par, dt_=dt_, mm=mm, bank=bank,
                  oprev=s_oo[par].n if dt_ == 0 else 0):
                v.wait_ge(s_mm.h, mm)
                if dt_ == 0 and oprev > 0:
                    v.wait_ge(s_oo[par].h, oprev)
                v.tensor_copy(out=oott[par][:, dt_ * 512:(dt_ + 1) * 512],
                              in_=pb[bank].ap()[:, 0:512]).then_inc(s_vg.h, 1)
            E("vector", f)
            p.ma["oo_ev"] = s_vg.inc(1)
        p.ma["oo_mm"] = s_mm.n

        def f(g, tb=tb, par=par, ev=p.ma["oo_ev"]):
            g.wait_ge(s_vg.h, ev)
            g.dma_start(out=brA1[tb * 128:(tb + 1) * 128, :],
                        in_=oott[par][:]).then_inc(s_oo[par].h, 16)
        E("gpsimd", f)
        s_oo[par].inc(16)

    o0, o1 = s_oo[0].n, s_oo[1].n

    def f(g):
        g.wait_ge(s_oo[0].h, o0)
        g.wait_ge(s_oo[1].h, o1)
    E("gpsimd", f)


def build_l1attn():
    ctx = contextlib.ExitStack()
    with ctx:
        nc = bass.Bass("TRN2", target_bir_lowering=False, debug=False)
        p = Prog(ctx, nc)
        p.ma = {"t34": 0}
        d = {}
        d["xn1f"] = nc.dram_tensor("xn1f", [128, 8, T], BF16, kind="ExternalInput").ap()
        for nm, shape in [("wqT", [1024, 1024]), ("wkT", [1024, 1024]),
                          ("wvT", [1024, 1024]), ("woT", [1024, 1024]),
                          ("bq", [1, 1024]), ("bk", [1, 1024]), ("bv", [1, 1024]),
                          ("bo", [1, 1024]), ("cosT", [128, 32, 32]),
                          ("sinT", [128, 32, 32]), ("ones1f", [1, 128]),
                          ("identd", [128, 128]), ("cmaskT", [128, 4, 512]),
                          ("colones", [128, 1])]:
            d[nm] = nc.dram_tensor(nm, shape, F32, kind="ExternalInput").ap()
        d["qf"] = nc.dram_tensor("qf", [128, 8, T], BF16, kind="Internal").ap()
        d["kf"] = nc.dram_tensor("kf", [128, 8, T], BF16, kind="Internal").ap()
        d["vf"] = nc.dram_tensor("vf", [128, 32, 1024], BF16, kind="Internal").ap()
        d["aof"] = nc.dram_tensor("aof", [128, 8, T], BF16, kind="Internal").ap()
        d["brA1"] = nc.dram_tensor("brA1", [T, D], F32, kind="ExternalOutput").ap()
        p.bar = p.sem("bar")
        p.tape_init()
        emit_l1attn(p, **d)
        with nc.Block() as block:
            p.block = block
            p.flush()
        return nc


def l1attn_host_layouts(inp):
    t = np.arange(T, dtype=np.float64)
    inv = 1.0 / (10000.0 ** (np.arange(0, 64, 2, dtype=np.float64) / 64))
    ang = np.outer(t, inv)
    cosT = np.ascontiguousarray(np.cos(ang).astype(np.float32).reshape(32, 128, 32).transpose(1, 0, 2))
    sinT = np.ascontiguousarray(np.sin(ang).astype(np.float32).reshape(32, 128, 32).transpose(1, 0, 2))
    cm = np.zeros((4, 128, 512), np.float32)
    for j in range(4):
        s = np.arange(128)[:, None] + 128 * j
        tt = np.arange(512)[None, :]
        cm[j] = (s <= tt).astype(np.float32)
    cmaskT = np.ascontiguousarray(cm.transpose(1, 0, 2))
    return dict(
        wqT=np.ascontiguousarray(inp["L1_attn_Wq"].T),
        wkT=np.ascontiguousarray(inp["L1_attn_Wk"].T),
        wvT=np.ascontiguousarray(inp["L1_attn_Wv"].T),
        woT=np.ascontiguousarray(inp["L1_attn_Wo"].T),
        bq=inp["L1_attn_bq"].reshape(1, -1).astype(np.float32),
        bk=inp["L1_attn_bk"].reshape(1, -1).astype(np.float32),
        bv=inp["L1_attn_bv"].reshape(1, -1).astype(np.float32),
        bo=inp["L1_attn_bo"].reshape(1, -1).astype(np.float32),
        cosT=cosT, sinT=sinT, cmaskT=cmaskT,
        ones1f=np.ones((1, 128), np.float32),
        colones=np.ones((128, 1), np.float32),
        identd=np.eye(128, dtype=np.float32),
    )



def _compose_bfly(p, mats_hbm, R_sb, pb, pbb, idb, s_w, W0, s_mm, s_vg, tag):
    E = p.emit
    NMB = 2
    mbuf = [p.sbuf(f"bf_m{tag}{i}", [128, 128], BF16) for i in range(NMB)]
    s_ml = [p.semc(f"bf_ml{i}") for i in range(NMB)]
    base = s_vg.n
    mm_used = {}
    gi = 0
    for s in range(6, -1, -1):
        for c in range(32):
            par = gi % NMB
            used = mm_used.get(par, 0)

            def f(q, c=c, s=s, par=par, used=used):
                if used > 0:
                    q.wait_ge(s_mm.h, used)
                q.dma_start(out=mbuf[par][:], in_=mats_hbm[s, c, :, :]
                             ).then_inc(s_ml[par].h, 16)
            E("gpsimd" if par == 0 else "sync", f)
            ml = s_ml[par].inc(16)

            def f(t, c=c, s=s, par=par, ml=ml, gi=gi, base=base):
                t.wait_ge(s_w.h, W0)
                t.wait_ge(s_ml[par].h, ml)
                if gi >= 2:
                    t.wait_ge(s_vg.h, base + gi - 1)
                if s < 6:
                    t.wait_ge(s_vg.h, base + (5 - s) * 32 + c + 1)
                rhs = idb[:] if s == 6 else R_sb[:, c, :]
                t.matmul(pb[gi % 2].ap()[:, 0:128], mbuf[par][:], rhs, start=True,
                         stop=True).then_inc(s_mm.h, 1)
            E("tensor", f)
            mm = s_mm.inc(1)
            mm_used[par] = mm

            def f(v, c=c, gi=gi, mm=mm):
                v.wait_ge(s_mm.h, mm)
                v.tensor_copy(out=R_sb[:, c, :],
                              in_=pb[gi % 2].ap()[:, 0:128]).then_inc(s_vg.h, 1)
            E("vector", f)
            s_vg.inc(1)
            gi += 1


def _emit_cross_stages(p, hb, tbuf, coefA, coefB, pairs, s_vg, ws, pre=None,
                       outw=512):
    E = p.emit
    s_sgx = p.semc("bf_sgx")
    for st in range(5):
        src = hb[st % 2]
        dst = hb[(st + 1) % 2]
        prev_vg = s_vg.n
        for grp in range(4):
            def fs(sc, st=st, src=src, grp=grp, prev_vg=prev_vg):
                sc.wait_ge(ws[0], ws[1])
                if grp == 0:
                    if st == 0 and pre is not None:
                        sc.wait_ge(pre[0], pre[1])
                    if prev_vg > 0:
                        sc.wait_ge(s_vg.h, prev_vg)
                for c in range(grp * 8, grp * 8 + 8):
                    lo, hi = pairs[st][c]
                    sc.activation(out=tbuf[:, c, :], in_=src[:, hi, :],
                                  func=AF.Copy, scale=coefB[:, st, c:c + 1],
                                  bias=0.0)
                sc.drain().then_inc(s_sgx.h, 1)
            E("scalar", fs)
            sgx = s_sgx.inc(1)

            def fv(v, st=st, src=src, dst=dst, grp=grp, sgx=sgx):
                v.wait_ge(s_sgx.h, sgx)
                if grp == 0:
                    if st == 0 and pre is not None:
                        v.wait_ge(pre[0], pre[1])
                    v.wait_ge(ws[0], ws[1])
                for c in range(grp * 8, grp * 8 + 8):
                    lo, hi = pairs[st][c]
                    v.scalar_tensor_tensor(out=dst[:, c, :], in0=src[:, lo, :],
                                           scalar=coefA[:, st, c:c + 1],
                                           in1=tbuf[:, c, :], op0=ALU.mult,
                                           op1=ALU.add)
                v.drain().then_inc(s_vg.h, 1)
            E("vector", fv)
            s_vg.inc(1)
    return (5 % 2)


def emit_l1f1(p, xn1f, weT, be, upM, upA, upB, ubias, identd, ones1f, hgf):
    nc = p.nc
    E = p.emit
    we = p.sbuf("f1_we", [128, 8, 4096], BF16)
    bes = p.sbuf("f1_be", [128, 32], F32)
    idb = p.sbuf("f1_idb", [128, 128], BF16)
    on1b = p.sbuf("f1_on1b", [1, 128], BF16)
    Rup = p.sbuf("f1_rup", [128, 32, 128], BF16)
    cA = p.sbuf("f1_ca", [128, 5, 32], F32)
    cB = p.sbuf("f1_cb", [128, 5, 32], F32)
    ub = p.sbuf("f1_ub", [128, 32], F32)
    xt = [p.sbuf(f"f1_xt{i}", [128, 8, 512], BF16) for i in range(2)]
    ha = p.sbuf("f1_ha", [128, 32, 512], BF16)
    hb2 = p.sbuf("f1_hb", [128, 32, 512], BF16)
    tbuf = p.sbuf("f1_tb", [128, 32, 512], BF16)
    hg = hb2

    pb = [p.psum_bank(f"f1_p{i}") for i in range(4)]
    pbb = [b.ap().bitcast(BF16) for b in pb]

    s_w = SemC(p.sem("f1_w"))
    s_ws = SemC(p.sem("f1_ws"))
    s_x = [SemC(p.sem(f"f1_x{i}")) for i in range(2)]
    s_mm = SemC(p.sem("f1_mm"))
    s_ev = SemC(p.sem("f1_ev"))
    s_vg = SemC(p.sem("f1_vg"))
    s_sg = SemC(p.sem("f1_sg"))
    s_hs = [SemC(p.sem(f"f1_hs{i}")) for i in range(2)]

    def f(g):
        g.dma_start(out=we[:], in_=weT.rearrange("(c p) z -> p c z", p=128)).then_inc(s_w.h, 16)
        g.dma_start(out=bes[:], in_=be[:, :]).then_inc(s_w.h, 16)
        g.dma_start(out=idb[:], in_=identd[:, :]).then_inc(s_w.h, 16)
        g.dma_start(out=on1b[:], in_=ones1f[:, :]).then_inc(s_w.h, 16)
    E("gpsimd", f)
    s_w.inc(64)
    W0 = s_w.n

    def f(s_):
        s_.dma_start(out=cA[:], in_=upA[:, :, :]).then_inc(s_ws.h, 16)
        s_.dma_start(out=cB[:], in_=upB[:, :, :]).then_inc(s_ws.h, 16)
        s_.dma_start(out=ub[:], in_=ubias[:, :]).then_inc(s_ws.h, 16)
    E("sync", f)
    s_ws.inc(48)
    WS0 = s_ws.n

    _compose_bfly(p, upM, Rup, pb, pbb, idb, s_w, W0, s_mm, s_vg, "u")
    comp_vg = s_vg.n
    p.tape_barrier()

    pairs = p.bfly_pairs
    s_wv = SemC(p.sem("f1_wv"))

    for tt in range(8):
        par = tt % 2

        def f(s_, tt=tt, par=par, need=p.ma.get("f1_xt", {}).get(par, 0)):
            s_.wait_ge(s_w.h, W0)
            if need > 0:
                s_.wait_ge(s_mm.h, need)
            s_.dma_start(out=xt[par][:], in_=xn1f[:, :, tt * 512:(tt + 1) * 512]
                         ).then_inc(s_x[par].h, 16)
        E("sync", f)
        xl = s_x[par].inc(16)

        prev_gelu = p.ma.get("f1_gelu", 0)
        for c in range(32):
            bank = c % 2

            def f(t, c=c, par=par, bank=bank, xl=xl, evn=s_ev.n - 1):
                t.wait_ge(s_x[par].h, xl)
                if evn > 0:
                    t.wait_ge(s_ev.h, evn)
                pa = pb[bank].ap()
                for d in range(8):
                    t.matmul(pa[:, 0:512], we[:, d, c * 128:(c + 1) * 128],
                             xt[par][:, d, :], start=(d == 0),
                             stop=(d == 7)).then_inc(s_mm.h, 1) if d == 7 else                         t.matmul(pa[:, 0:512], we[:, d, c * 128:(c + 1) * 128],
                                 xt[par][:, d, :], start=(d == 0), stop=False)
            E("tensor", f)
            mm = s_mm.inc(1)

            def f(v, c=c, bank=bank, mm=mm, pg=prev_gelu):
                v.wait_ge(s_mm.h, mm)
                v.wait_ge(s_ws.h, WS0)
                if pg > 0 and c == 0:
                    v.wait_ge(s_sg.h, pg)
                v.tensor_scalar(out=ha[:, c, :], in0=pb[bank].ap()[:, 0:512],
                                scalar1=bes[:, c:c + 1], scalar2=None,
                                op0=ALU.add).then_inc(s_ev.h, 1)
            E("vector", f)
            s_ev.inc(1)
        p.ma.setdefault("f1_xt", {})[par] = s_mm.n
        h0ev = s_ev.n

        for c in range(32):
            bank = 2 + (c % 2)

            def f(t, c=c, bank=bank, ev=h0ev, wvn=s_wv.n - 1):
                t.wait_ge(s_ev.h, ev)
                if wvn > 0:
                    t.wait_ge(s_wv.h, wvn)
                t.matmul(pb[bank].ap()[:, 0:512], Rup[:, c, :], ha[:, c, :],
                         start=True, stop=True).then_inc(s_mm.h, 1)
            E("tensor", f)
            mm = s_mm.inc(1)

            def f(v, c=c, bank=bank, mm=mm, ls=p.ma.get("f1_hs_last", None)):
                v.wait_ge(s_mm.h, mm)
                if c == 0 and ls is not None:
                    v.wait_ge(ls[0], ls[1])
                v.tensor_copy(out=hb2[:, c, :],
                              in_=pb[bank].ap()[:, 0:512]).then_inc(s_wv.h, 1)
            E("vector", f)
            s_wv.inc(1)

        _emit_cross_stages(p, [hb2, ha], tbuf, cA, cB, pairs, s_vg, (s_ws.h, WS0),
                           pre=(s_wv.h, s_wv.n))
        cross_vg = s_vg.n

        def f(sc, tt=tt, par=par, cv=cross_vg, hs=p.ma.get(("f1_hs", par), 0)):
            sc.wait_ge(s_vg.h, cv)
            if hs > 0:
                sc.wait_ge(s_hs[par].h, hs)
            for c in range(32):
                sc.activation(out=hg[:, c, :], in_=ha[:, c, :],
                              func=AF.Gelu, bias=ub[:, c:c + 1])
            sc.drain().then_inc(s_sg.h, 1)
        E("scalar", f)
        p.ma["f1_gelu"] = s_sg.inc(1)

        def f(g, tt=tt, par=par, sg=p.ma["f1_gelu"]):
            g.wait_ge(s_sg.h, sg)
            g.dma_start(out=hgf[:, :, tt * 512:(tt + 1) * 512],
                        in_=hg[:]).then_inc(s_hs[par].h, 16)
        E("gpsimd", f)
        p.ma[("f1_hs", par)] = s_hs[par].inc(16)
        p.ma["f1_hs_last"] = (s_hs[par].h, s_hs[par].n)

    h0_, h1_ = s_hs[0].n, s_hs[1].n

    def f(g):
        g.wait_ge(s_hs[0].h, h0_)
        g.wait_ge(s_hs[1].h, h1_)
    E("gpsimd", f)


def emit_l1f2(p, hgf, dnM, dnA, dnB, wcT, bcon, dbias, x1h, brA1h, identd,
              ones1f, out_h):
    nc = p.nc
    E = p.emit
    wc = p.sbuf("f2_wc", [128, 32, 1024], BF16)
    bcs = p.sbuf("f2_bc", [1, 1024], BF16)
    dbs = p.sbuf("f2_db", [1, 1024], BF16)
    idb = p.sbuf("f2_idb", [128, 128], BF16)
    on1b = p.sbuf("f2_on1b", [1, 128], BF16)
    Rdn = p.sbuf("f2_rdn", [128, 32, 128], BF16)
    cA = p.sbuf("f2_ca", [128, 5, 32], F32)
    cB = p.sbuf("f2_cb", [128, 5, 32], F32)
    ha = p.sbuf("f2_ha", [128, 32, 512], BF16)
    hb2 = p.sbuf("f2_hb", [128, 32, 512], BF16)
    tbuf = p.sbuf("f2_tb", [128, 32, 512], BF16)
    x1t = [p.sbuf(f"f2_x1t{i}", [128, 1024], F32) for i in range(2)]
    bat = [p.sbuf(f"f2_bat{i}", [128, 1024], F32) for i in range(2)]
    ot = [p.sbuf(f"f2_ot{i}", [128, 1024], F32) for i in range(2)]
    osc = p.sbuf("f2_osc", [128, 512], F32)

    pb = [p.psum_bank(f"f2_p{i}") for i in range(4)]
    pbb = [b.ap().bitcast(BF16) for b in pb]

    s_w = SemC(p.sem("f2_w"))
    s_ws = SemC(p.sem("f2_ws"))
    s_h = [SemC(p.sem(f"f2_h{i}")) for i in range(2)]
    s_xb = [SemC(p.sem(f"f2_xb{i}")) for i in range(2)]
    s_mm = SemC(p.sem("f2_mm"))
    s_wv = SemC(p.sem("f2_wv"))
    s_vg = SemC(p.sem("f2_vg"))
    s_oo = [SemC(p.sem(f"f2_oo{i}")) for i in range(2)]

    def f(g):
        g.dma_start(out=wc[:], in_=wcT.rearrange("(c p) z -> p c z", p=128)).then_inc(s_w.h, 16)
        g.dma_start(out=bcs[:], in_=bcon[:, :]).then_inc(s_w.h, 16)
        g.dma_start(out=dbs[:], in_=dbias[:, :]).then_inc(s_w.h, 16)
        g.dma_start(out=idb[:], in_=identd[:, :]).then_inc(s_w.h, 16)
        g.dma_start(out=on1b[:], in_=ones1f[:, :]).then_inc(s_w.h, 16)
    E("gpsimd", f)
    s_w.inc(80)
    W0 = s_w.n

    def f(s_):
        s_.dma_start(out=cA[:], in_=dnA[:, :, :]).then_inc(s_ws.h, 16)
        s_.dma_start(out=cB[:], in_=dnB[:, :, :]).then_inc(s_ws.h, 16)
    E("sync", f)
    s_ws.inc(32)
    WS0 = s_ws.n

    _compose_bfly(p, dnM, Rdn, pb, pbb, idb, s_w, W0, s_mm, s_vg, "d")
    p.tape_barrier()

    pairs = p.bfly_pairs_dn

    for tt in range(8):
        par = tt % 2

        def f(s_, tt=tt, par=par, need=p.ma.get("f2_omm", 0),
              cvp=p.ma.get("f2_crossv", 0)):
            s_.wait_ge(s_w.h, W0)
            if need > 0:
                s_.wait_ge(s_mm.h, need)
            if cvp > 0:
                s_.wait_ge(s_vg.h, cvp)
            s_.dma_start(out=ha[:], in_=hgf[:, :, tt * 512:(tt + 1) * 512]
                         ).then_inc(s_h[par].h, 16)
        E("sync", f)
        hl = s_h[par].inc(16)

        prev_out = p.ma.get("f2_omm", 0)
        for c in range(32):
            bank = c % 2

            def f(t, c=c, par=par, bank=bank, hl=hl, wvn=s_wv.n - 1):
                t.wait_ge(s_h[par].h, hl)
                if wvn > 0:
                    t.wait_ge(s_wv.h, wvn)
                t.matmul(pb[bank].ap()[:, 0:512], Rdn[:, c, :], ha[:, c, :],
                         start=True, stop=True).then_inc(s_mm.h, 1)
            E("tensor", f)
            mm = s_mm.inc(1)

            def f(v, c=c, bank=bank, mm=mm, po=prev_out):
                v.wait_ge(s_mm.h, mm)
                if c == 0 and po > 0:
                    v.wait_ge(s_mm.h, po)
                v.tensor_copy(out=hb2[:, c, :],
                              in_=pb[bank].ap()[:, 0:512]).then_inc(s_wv.h, 1)
            E("vector", f)
            s_wv.inc(1)

        _emit_cross_stages(p, [hb2, ha], tbuf, cA, cB, pairs, s_vg, (s_ws.h, WS0),
                           pre=(s_wv.h, s_wv.n))
        cross_vg = s_vg.n
        p.ma["f2_crossv"] = cross_vg

        for tb in range(4):
            tba = tt * 4 + tb
            bpar = tba % 2

            def f(s_, tba=tba, bpar=bpar, need=p.ma.get("f2_cons", 0)):
                if need > 0:
                    s_.wait_ge(s_vg.h, need)
                s_.dma_start(out=x1t[bpar][:], in_=x1h[tba * 128:(tba + 1) * 128, :]
                             ).then_inc(s_xb[bpar].h, 16)
                s_.dma_start(out=bat[bpar][:], in_=brA1h[tba * 128:(tba + 1) * 128, :]
                             ).then_inc(s_xb[bpar].h, 16)
            E("sync", f)
            xbl = s_xb[bpar].inc(32)

            for dt_ in range(2):
                bank = 2 + ((2 * tba + dt_) % 2)

                def f(t, tb=tb, dt_=dt_, bank=bank, cv=cross_vg,
                      evn=p.ma.get("f2_oev", 0) - 1):
                    t.wait_ge(s_vg.h, cv)
                    if evn > 0:
                        t.wait_ge(s_vg.h, evn)
                    pa = pb[bank].ap()
                    for c in range(32):
                        t.matmul(pa[:, 0:512], ha[:, c, tb * 128:(tb + 1) * 128],
                                 wc[:, c, dt_ * 512:(dt_ + 1) * 512],
                                 start=(c == 0), stop=False)
                    t.matmul(pa[:, 0:512], on1b[:], bcs[:, dt_ * 512:(dt_ + 1) * 512],
                             start=False, stop=False)
                    t.matmul(pa[:, 0:512], on1b[:], dbs[:, dt_ * 512:(dt_ + 1) * 512],
                             start=False, stop=True).then_inc(s_mm.h, 1)
                E("tensor", f)
                mm = s_mm.inc(1)

                def f(v, dt_=dt_, bpar=bpar, bank=bank, mm=mm, xbl=xbl,
                      oprev=s_oo[bpar].n if dt_ == 0 else 0):
                    v.wait_ge(s_mm.h, mm)
                    v.wait_ge(s_xb[bpar].h, xbl)
                    if dt_ == 0 and oprev > 0:
                        v.wait_ge(s_oo[bpar].h, oprev)
                    sl = slice(dt_ * 512, (dt_ + 1) * 512)
                    v.scalar_tensor_tensor(out=osc[:, :], in0=pb[bank].ap()[:, 0:512],
                                           scalar=1.0, in1=x1t[bpar][:, sl],
                                           op0=ALU.mult, op1=ALU.add)
                    v.drain()
                    v.tensor_tensor(out=ot[bpar][:, sl], in0=osc[:, :],
                                    in1=bat[bpar][:, sl], op=ALU.add)
                    v.drain().then_inc(s_vg.h, 1)
                E("vector", f)
                p.ma["f2_oev"] = s_vg.inc(1)
            p.ma["f2_omm"] = s_mm.n
            p.ma["f2_cons"] = p.ma["f2_oev"]

            def f(g, tba=tba, bpar=bpar, ev=p.ma["f2_oev"]):
                g.wait_ge(s_vg.h, ev)
                g.dma_start(out=out_h[tba * 128:(tba + 1) * 128, :],
                            in_=ot[bpar][:]).then_inc(s_oo[bpar].h, 16)
            E("gpsimd", f)
            s_oo[bpar].inc(16)

    o0, o1 = s_oo[0].n, s_oo[1].n

    def f(g):
        g.wait_ge(s_oo[0].h, o0)
        g.wait_ge(s_oo[1].h, o1)
    E("gpsimd", f)


def bfly_host_layouts(up_w, dn_w):
    res = {}
    pairs = None
    for name, w in (("up", up_w), ("dn", dn_w)):
        M = np.zeros((7, 32, 128, 128), np.float32)
        A = np.zeros((5, 32, 128), np.float32)
        B = np.zeros((5, 32, 128), np.float32)
        prs = [[None] * 32 for _ in range(5)]
        for stage in range(12):
            s_ = 2 ** stage
            nb = 4096 // (2 * s_)
            idx = np.arange(4096).reshape(nb, 2, s_)
            ww = w[stage].reshape(nb, s_, 2, 2)
            a_idx = idx[:, 0, :]
            b_idx = idx[:, 1, :]
            if stage <= 6:
                for kb in range(nb):
                    for si in range(s_):
                        a = int(a_idx[kb, si]); b = int(b_idx[kb, si])
                        c = a // 128
                        M[stage, c, a % 128, a % 128] = ww[kb, si, 0, 0]
                        M[stage, c, a % 128, b % 128] = ww[kb, si, 0, 1]
                        M[stage, c, b % 128, a % 128] = ww[kb, si, 1, 0]
                        M[stage, c, b % 128, b % 128] = ww[kb, si, 1, 1]
            else:
                st = stage - 7
                aa = a_idx.ravel(); bb = b_idx.ravel()
                wf = ww.reshape(-1, 2, 2)
                ca = aa // 128; pa = aa % 128
                cb_ = bb // 128
                for i in range(len(aa)):
                    A[st, ca[i], pa[i]] = wf[i, 0, 0]
                    B[st, ca[i], pa[i]] = wf[i, 0, 1]
                    A[st, cb_[i], pa[i]] = wf[i, 1, 0]
                    B[st, cb_[i], pa[i]] = wf[i, 1, 1]
                    prs[st][ca[i]] = (int(ca[i]), int(cb_[i]))
                    prs[st][cb_[i]] = (int(ca[i]), int(cb_[i]))
        res[name + "M"] = M
        res[name + "A"] = np.ascontiguousarray(A.transpose(2, 0, 1))
        res[name + "B"] = np.ascontiguousarray(B.transpose(2, 0, 1))
        pairs = prs
    res["pairs"] = pairs
    return res


def build_l1f1(pairs):
    ctx = contextlib.ExitStack()
    with ctx:
        nc = bass.Bass("TRN2", target_bir_lowering=False, debug=False)
        p = Prog(ctx, nc)
        p.ma = {"t34": 0}
        p.bfly_pairs = pairs
        d = {}
        d["xn1f"] = nc.dram_tensor("xn1f", [128, 8, T], BF16, kind="ExternalInput").ap()
        for nm, shape in [("weT", [1024, 4096]), ("be", [128, 32]),
                          ("upM", [7, 32, 128, 128]), ("upA", [128, 5, 32]),
                          ("upB", [128, 5, 32]), ("ubias", [128, 32]),
                          ("identd", [128, 128]), ("ones1f", [1, 128])]:
            d[nm] = nc.dram_tensor(nm, shape, F32, kind="ExternalInput").ap()
        d["hgf"] = nc.dram_tensor("hgf", [128, 32, T], BF16, kind="ExternalOutput").ap()
        p.bar = p.sem("bar")
        p.tape_init()
        emit_l1f1(p, **d)
        with nc.Block() as block:
            p.block = block
            p.flush()
        return nc


def build_l1f2(pairs):
    ctx = contextlib.ExitStack()
    with ctx:
        nc = bass.Bass("TRN2", target_bir_lowering=False, debug=False)
        p = Prog(ctx, nc)
        p.ma = {"t34": 0}
        p.bfly_pairs_dn = pairs
        d = {}
        d["hgf"] = nc.dram_tensor("hgf", [128, 32, T], BF16, kind="ExternalInput").ap()
        for nm, shape in [("dnM", [7, 32, 128, 128]), ("dnA", [128, 5, 32]),
                          ("dnB", [128, 5, 32]), ("wcT", [4096, 1024]),
                          ("bcon", [1, 1024]), ("dbias", [1, 1024]),
                          ("x1h", [T, D]), ("brA1h", [T, D]),
                          ("identd", [128, 128]), ("ones1f", [1, 128])]:
            d[nm] = nc.dram_tensor(nm, shape, F32, kind="ExternalInput").ap()
        d["out_h"] = nc.dram_tensor("out", [T, D], F32, kind="ExternalOutput").ap()
        p.bar = p.sem("bar")
        p.tape_init()
        emit_l1f2(p, **d)
        with nc.Block() as block:
            p.block = block
            p.flush()
        return nc


def l1ffn_host_layouts(inp):
    bf = bfly_host_layouts(inp["L1_ffn_up_weights"], inp["L1_ffn_down_weights"])
    return dict(
        weT=np.ascontiguousarray(inp["L1_ffn_W_exp"]),
        be=np.ascontiguousarray(inp["L1_ffn_b_exp"].reshape(32, 128).T),
        upM=bf["upM"], upA=bf["upA"], upB=bf["upB"],
        dnM=bf["dnM"], dnA=bf["dnA"], dnB=bf["dnB"],
        ubias=np.ascontiguousarray(inp["L1_ffn_up_bias"].reshape(32, 128).T),
        wcT=np.ascontiguousarray(inp["L1_ffn_W_con"]),
        bcon=inp["L1_ffn_b_con"].reshape(1, -1).astype(np.float32),
        dbias=inp["L1_ffn_down_bias"].reshape(1, -1).astype(np.float32),
        identd=np.eye(128, dtype=np.float32),
        ones1f=np.ones((1, 128), np.float32),
        pairs=bf["pairs"],
    )


_BUILD_CACHE = {}


def _cached(key, fn):
    if key not in _BUILD_CACHE:
        _BUILD_CACHE[key] = fn()
    return _BUILD_CACHE[key]


def emit_pre_tape(p, x, xnf, xn2f, lnw_rep, lnb_rep, ident_hbm, mu_hbm,
                  rstd_hbm, affine=True):
    E = p.emit
    NQ = 2 if affine else 1
    NT = 8 * NQ
    tag = "pa" if affine else "pb"

    xb = [p.sbuf(f"{tag}_xb{i}", [128, D], F32) for i in range(2)]
    xc = p.sbuf(f"{tag}_xc", [128, D], F32)
    sq = p.sbuf(f"{tag}_sq", [128, D], F32)
    xn = p.sbuf(f"{tag}_xn", [128, D], F32)
    xnbf = p.sbuf(f"{tag}_xnbf", [128, D], BF16)
    xn2bf = p.sbuf(f"{tag}_xn2bf", [128, D], BF16)
    xc2 = p.sbuf(f"{tag}_xc2", [128, D], F32)
    st1 = p.sbuf(f"{tag}_st1", [128, 8], F32)
    epsc = p.sbuf(f"{tag}_eps", [128, 1], F32)
    mu_sb = p.sbuf(f"{tag}_mu", [128, NTB], F32)
    rstd_sb = p.sbuf(f"{tag}_rstd", [128, NTB], F32)
    lnw = p.sbuf(f"{tag}_lnw", [128, D], F32)
    lnb = p.sbuf(f"{tag}_lnb", [128, D], F32)
    ident = p.sbuf(f"{tag}_id", [128, 128], BF16)
    stg = p.sbuf(f"{tag}_stg", [128, 8, 128], BF16)
    stg2 = p.sbuf(f"{tag}_stg2", [128, 8, 128], BF16)
    ptb = [p.psum_bank(f"{tag}_pt{i}") for i in range(2)]
    pt = [b.ap().bitcast(BF16)[:, 0:128] for b in ptb]

    s_xld = [p.semc("pre_xld0"), p.semc("pre_xld1")]
    s_cld = p.semc("pre_cld")
    s_cldg = p.semc("pre_cldg")
    s_v2s = p.semc("pre_v2s")
    s_s2v = p.semc("pre_s2v")
    s_cast = p.semc("pre_cast")
    s_tr = p.semc("pre_tr")
    s_ev = p.semc("pre_ev")
    s_out = p.semc("pre_out")
    s_xcons = p.semc("pre_xcons")

    def f(g):
        g.dma_start(out=ident[:], in_=ident_hbm[:, :]).then_inc(s_cldg.h, 16)
    E("gpsimd", f)
    CLDG = s_cldg.inc(16)

    if affine:
        def f(s_):
            s_.dma_start(out=lnw[:], in_=lnw_rep[:, :]).then_inc(s_cld.h, 16)
            s_.dma_start(out=lnb[:], in_=lnb_rep[:, :]).then_inc(s_cld.h, 16)
        E("sync", f)
        s_cld.inc(32)
    CLD = s_cld.n

    xld_t = {}
    for tb in range(NTB):
        par = tb % 2
        need = s_xcons.n - 1

        def f(s_, tb=tb, par=par, need=need):
            if need > 0:
                s_.wait_ge(s_xcons.h, need)
            s_.dma_start(out=xb[par][:], in_=x[tb * 128:(tb + 1) * 128, :]
                         ).then_inc(s_xld[par].h, 16)
        E("sync", f)
        xld_t[tb] = s_xld[par].inc(16)

        v2s_1 = s_v2s.n + 1
        s2v_1 = s_s2v.n + 1
        cast_1 = s_cast.n + 1
        out_need = s_out.n

        def f(v, tb=tb, par=par, xl=xld_t[tb], s2v_1=s2v_1, on=out_need,
              tr0=s_tr.n):
            if tb == 0:
                v.memset(epsc[:], EPS)
                if affine:
                    v.wait_ge(s_cld.h, CLD)
            b = xb[par]
            v.wait_ge(s_xld[par].h, xl)
            v.drain()
            v.reduce_sum(out=st1[:, 0:1], in_=b[:], axis=AX.X)
            v.drain()
            v.tensor_scalar(out=st1[:, 0:1], in0=st1[:, 0:1], scalar1=-1.0 / D,
                            scalar2=None, op0=ALU.mult)
            v.drain()
            v.tensor_scalar(out=xc[:], in0=b[:], scalar1=st1[:, 0:1], scalar2=None,
                            op0=ALU.add).then_inc(s_v2s.h, 1)
            v.tensor_copy(out=mu_sb[:, tb:tb + 1], in_=st1[:, 0:1])
            v.drain().then_inc(s_xcons.h, 1)
            v.wait_ge(s_s2v.h, s2v_1)
            v.reciprocal(out=st1[:, 3:4], in_=st1[:, 2:3])
            v.drain()
            v.tensor_copy(out=rstd_sb[:, tb:tb + 1], in_=st1[:, 3:4])
            v.tensor_scalar(out=xn[:], in0=xc[:], scalar1=st1[:, 3:4], scalar2=None,
                            op0=ALU.mult)
            v.drain()
            v.tensor_copy(out=xnbf[:], in_=xn[:]).then_inc(s_cast.h, 1)
            if affine:
                v.reduce_sum(out=st1[:, 4:5], in_=xn[:], axis=AX.X)
                v.drain()
                v.tensor_scalar(out=st1[:, 4:5], in0=st1[:, 4:5], scalar1=-1.0 / D,
                                scalar2=None, op0=ALU.mult)
                v.drain()
                v.tensor_scalar(out=xc2[:], in0=xn[:], scalar1=st1[:, 4:5],
                                scalar2=None, op0=ALU.add).then_inc(s_v2s.h, 1)
                v.wait_ge(s_s2v.h, s2v_1 + 1)
                v.reciprocal(out=st1[:, 7:8], in_=st1[:, 6:7])
                v.drain()
                v.tensor_scalar(out=xc2[:], in0=xc2[:], scalar1=st1[:, 7:8],
                                scalar2=None, op0=ALU.mult)
                v.drain()
                v.tensor_tensor(out=xc2[:], in0=xc2[:], in1=lnw[:], op=ALU.mult)
                v.drain()
                v.tensor_tensor(out=xn2bf[:], in0=xc2[:], in1=lnb[:],
                                op=ALU.add).then_inc(s_cast.h, 1)
            if on > 0:
                v.wait_ge(s_out.h, on)
            for c in range(NT):
                v.wait_ge(s_tr.h, tr0 + c + 1)
                psrc = pt[c % 2]
                dst = stg if c < 8 else stg2
                cc = c if c < 8 else c - 8
                v.tensor_copy(out=dst[:, cc, :], in_=psrc[:]).then_inc(s_ev.h, 1)
        E("vector", f)
        s_v2s.inc(NQ)
        s_cast.inc(NQ)
        s_xcons.inc(1)

        def f(sc, v2s_1=v2s_1):
            sc.wait_ge(s_v2s.h, v2s_1)
            sc.activation(out=sq[:], in_=xc[:], func=AF.Square, accum_out=st1[:, 1:2])
            sc.drain()
            sc.activation(out=st1[:, 2:3], in_=st1[:, 1:2], func=AF.Sqrt,
                          scale=1.0 / D, bias=epsc[:, 0:1]).then_inc(s_s2v.h, 1)
            if affine:
                sc.wait_ge(s_v2s.h, v2s_1 + 1)
                sc.activation(out=sq[:], in_=xc2[:], func=AF.Square,
                              accum_out=st1[:, 5:6])
                sc.drain()
                sc.activation(out=st1[:, 6:7], in_=st1[:, 5:6], func=AF.Sqrt,
                              scale=1.0 / D, bias=epsc[:, 0:1]).then_inc(s_s2v.h, 1)
        E("scalar", f)
        s_s2v.inc(NQ)

        def f(t, tb=tb, cast_1=cast_1, ev0=s_ev.n, tr0=s_tr.n, cldg=CLDG):
            if tb == 0:
                t.wait_ge(s_cldg.h, cldg)
            for c in range(NT):
                t.wait_ge(s_cast.h, cast_1 + (0 if c < 8 else 1))
                if tb > 0 or c >= 2:
                    t.wait_ge(s_ev.h, ev0 + c - NT - 1 + NT)
                t.transpose(pt[c % 2][:], (xnbf if c < 8 else xn2bf)
                            [:, (c % 8) * 128:((c % 8) + 1) * 128],
                            ident[:]).then_inc(s_tr.h, 1)
        E("tensor", f)
        s_tr.inc(NT)
        ev_mid = s_ev.n + 8
        s_ev.inc(NT)
        ev_all = s_ev.n

        def f(g, tb=tb, em=ev_mid, ea=ev_all):
            g.wait_ge(s_ev.h, em)
            g.dma_start(out=xnf[:, :, tb * 128:(tb + 1) * 128],
                        in_=stg[:, :, :]).then_inc(s_out.h, 16)
            if affine:
                g.wait_ge(s_ev.h, ea)
                g.dma_start(out=xn2f[:, :, tb * 128:(tb + 1) * 128],
                            in_=stg2[:, :, :]).then_inc(s_out.h, 16)
        E("gpsimd", f)
        s_out.inc(16 * NQ)

    def f(g):
        g.dma_start(out=mu_hbm[:, :], in_=mu_sb[:]).then_inc(s_out.h, 16)
        g.dma_start(out=rstd_hbm[:, :], in_=rstd_sb[:]).then_inc(s_out.h, 16)
    E("gpsimd", f)
    s_out.inc(32)
    tot = s_out.n

    def f(g):
        g.wait_ge(s_out.h, tot)
    E("gpsimd", f)
    return mu_sb, rstd_sb


def build_fused(pairs):
    ctx = contextlib.ExitStack()
    with ctx:
        nc = bass.Bass("TRN2", target_bir_lowering=False, debug=False)
        p = Prog(ctx, nc)
        p.ma = {"t34": 0}
        p.bfly_pairs = pairs
        p.bfly_pairs_dn = pairs

        def EI(nm, shape, dt=F32):
            return nc.dram_tensor(nm, shape, dt, kind="ExternalInput").ap()

        def IN(nm, shape, dt):
            return nc.dram_tensor(nm, shape, dt, kind="Internal").ap()

        x = EI("x", [T, D])
        lnw = EI("lnw_rep", [128, D]); lnb = EI("lnb_rep", [128, D])
        ident = EI("ident", [128, 128])
        identb = EI("identb", [128, 128], BF16)
        ma = {nm: EI("m_" + nm, shape, dt) for nm, shape, dt in [
            ("wzT", [1024, 2048], BF16), ("wxT", [1024, 2304], BF16),
            ("wdT", [1024, 32], BF16),
            ("woT", [2048, 1024], BF16), ("convw", [128, 18, 4], F32),
            ("convb", [128, 18], F32),
            ("dtb4", [128, CPS * 32], F32), ("alog", [1, 32], F32),
            ("dvec", [1, 32], F32),
            ("normw", [128, 2048], BF16), ("triu_i", [128, 128], F32),
            ("tril_s", [128, 128], F32),
            ("ones128", [128, 128], F32), ("ones1f", [1, 128], F32)]}
        mo = {nm: EI("q_" + nm, shape, dt) for nm, shape, dt in [
            ("rwT", [1024, 8], BF16), ("rb", [1, 8], BF16),
            ("w1sT", [9, 4, 128, 8, 1024], BF16),
            ("w2sT", [9, 4, 128, 8, 1024], BF16), ("b1s", [9, 128, 32], F32),
            ("b2s", [9, 1, 1024], BF16)]}
        la = {nm: EI("a_" + nm, shape, dt) for nm, shape, dt in [
            ("wqT", [1024, 1024], BF16), ("wkT", [1024, 1024], BF16),
            ("wvT", [1024, 1024], BF16),
            ("woT", [1024, 1024], BF16), ("bq", [1, 1024], BF16),
            ("bk", [1, 1024], BF16),
            ("bv", [1, 1024], BF16), ("bo", [1, 1024], BF16),
            ("cosT", [128, 32, 32], F32),
            ("sinT", [128, 32, 32], F32), ("cmaskT", [128, 4, 512], BF16),
            ("colones", [128, 1], BF16)]}
        ff = {nm: EI("f_" + nm, shape, dt) for nm, shape, dt in [
            ("weT", [1024, 4096], BF16), ("be", [128, 32], F32),
            ("upM", [7, 32, 128, 128], BF16),
            ("upA", [128, 5, 32], F32), ("upB", [128, 5, 32], F32),
            ("ubias", [128, 32], F32),
            ("dnM", [7, 32, 128, 128], BF16), ("dnA", [128, 5, 32], F32),
            ("dnB", [128, 5, 32], F32),
            ("wcT", [4096, 1024], BF16), ("bcon", [1, 1024], BF16),
            ("dbias", [1, 1024], BF16)]}

        xnf = IN("i_xnf", [128, 8, T], BF16)
        xn2f = IN("i_xn2f", [128, 8, T], BF16)
        xn1f = IN("i_xn1f", [128, 8, T], BF16)
        mu1 = IN("i_mu1", [128, NTB], F32)
        rstd1 = IN("i_rstd1", [128, NTB], F32)
        mu2 = IN("i_mu2", [128, NTB], F32)
        rstd2 = IN("i_rstd2", [128, NTB], F32)
        brA = IN("i_brA", [T, D], F32)
        x1 = IN("i_x1", [T, D], F32)
        brA1 = IN("i_brA1", [T, D], F32)
        qf = IN("i_qf", [128, 8, T], BF16)
        kf = IN("i_kf", [128, 8, T], BF16)
        vf = IN("i_vf", [128, 32, 1024], BF16)
        aof = IN("i_aof", [128, 8, T], BF16)
        hgf = IN("i_hgf", [128, 32, T], BF16)
        ltdram = IN("i_lt", [32, 2, 32, 128], F32)
        out_h = nc.dram_tensor("out", [T, D], F32, kind="ExternalOutput").ap()

        p.bar = p.sem("bar")
        p.tape_init()

        p.push_phase()
        emit_pre_tape(p, x, xnf, xn2f, lnw, lnb, identb, mu1, rstd1, affine=True)
        p.tape_barrier()
        p.pop_phase()

        p.push_phase()
        emit_mamba(p, xn2f=xn2f, identf=ident, identd=identb, ltdram=ltdram,
                   brA_out=brA, **ma)
        p.tape_barrier()
        p.pop_phase()

        p.push_phase()
        emit_moe(p, xnf, x, brA, mu1, rstd1, mo["rwT"], mo["rb"], mo["w1sT"],
                 mo["w2sT"], mo["b1s"], mo["b2s"], ma["ones1f"], x1, n_exp=9)
        p.tape_barrier()
        p.pop_phase()

        p.push_phase()
        emit_pre_tape(p, x1, xn1f, None, lnw, lnb, identb, mu2, rstd2, affine=False)
        p.tape_barrier()
        p.pop_phase()

        p.push_phase()
        emit_l1attn(p, xn1f, la["wqT"], la["wkT"], la["wvT"], la["woT"], la["bq"],
                    la["bk"], la["bv"], la["bo"], la["cosT"], la["sinT"],
                    ma["ones1f"], identb, la["cmaskT"], la["colones"], qf, kf, vf,
                    aof, brA1)
        p.tape_barrier()
        p.pop_phase()

        p.push_phase()
        emit_l1f1(p, xn1f, ff["weT"], ff["be"], ff["upM"], ff["upA"], ff["upB"],
                  ff["ubias"], identb, ma["ones1f"], hgf)
        p.tape_barrier()
        p.pop_phase()

        p.push_phase()
        emit_l1f2(p, hgf, ff["dnM"], ff["dnA"], ff["dnB"], ff["wcT"], ff["bcon"],
                  ff["dbias"], x1, brA1, identb, ma["ones1f"], out_h)
        p.pop_phase()

        with nc.Block() as block:
            p.block = block
            p.flush()
        return nc


def fused_in_map(inp, b, lays=None):
    if lays is None:
        lays = {}
        lays["m"] = mamba_host_layouts(inp)
        lays["q"] = moe_host_layouts(inp)
        lays["a"] = l1attn_host_layouts(inp)
        lays["f"] = l1ffn_host_layouts(inp)
    m, q, a, f = lays["m"], lays["q"], lays["a"], lays["f"]
    im = {"x": np.ascontiguousarray(inp["L0_attn_x"][b])}
    im["lnw_rep"] = np.ascontiguousarray(np.broadcast_to(inp["L0_attn_ln_w"], (128, D))).astype(np.float32)
    im["lnb_rep"] = np.ascontiguousarray(np.broadcast_to(inp["L0_attn_ln_b"], (128, D))).astype(np.float32)
    im["ident"] = np.eye(128, dtype=np.float32)
    im["identb"] = np.eye(128, dtype=BF16NP)
    bf_m = ("wzT", "wxT", "wdT", "woT", "normw")
    bf_q = ("rwT", "rb", "w1sT", "w2sT", "b2s")
    bf_a = ("wqT", "wkT", "wvT", "woT", "bq", "bk", "bv", "bo", "cmaskT",
            "colones")
    bf_f = ("weT", "upM", "dnM", "wcT", "bcon", "dbias")

    def pick(dct, nm, bfset):
        arr = dct[nm]
        if nm in bfset and arr.dtype != BF16NP:
            arr = np.ascontiguousarray(arr).astype(BF16NP)
            dct[nm] = arr
        return arr

    for nm in ("wzT", "wxT", "wdT", "woT", "convw", "convb", "dtb4", "alog",
               "dvec", "normw", "triu_i", "tril_s", "ones128", "ones1f"):
        im["m_" + nm] = pick(m, nm, bf_m)
    for nm in ("rwT", "rb", "w1sT", "w2sT", "b1s", "b2s"):
        im["q_" + nm] = pick(q, nm, bf_q)
    for nm in ("wqT", "wkT", "wvT", "woT", "bq", "bk", "bv", "bo", "cosT",
               "sinT", "cmaskT", "colones"):
        im["a_" + nm] = pick(a, nm, bf_a)
    for nm in ("weT", "be", "upM", "upA", "upB", "ubias", "dnM", "dnA", "dnB",
               "wcT", "bcon", "dbias"):
        im["f_" + nm] = pick(f, nm, bf_f)
    return im, lays


def kernel(**inputs):
    lay_f = l1ffn_host_layouts(inputs)
    pairs = lay_f["pairs"]
    nc = build_fused(pairs)
    lays = None
    in_maps = []
    for b in range(8):
        im, lays = fused_in_map(inputs, b, lays)
        in_maps.append(im)
    res = run_bass_kernel_spmd(nc, in_maps, core_ids=list(range(N_CORES)))
    return np.stack([np.asarray(r["out"]) for r in res.results], 0).astype(np.float32)


# version 07
# speedup vs baseline: 1.0346x; 1.0346x over previous
import contextlib
import math
import numpy as np
import ml_dtypes
import concourse.bass as bass
import concourse.mybir as mybir
from concourse.bass_utils import run_bass_kernel_spmd

F32 = mybir.dt.float32
BF16 = mybir.dt.bfloat16
AF = mybir.ActivationFunctionType
ALU = mybir.AluOpType
AX = mybir.AxisListType

T = 4096
D = 1024
NTB = T // 128
EPS = 1e-5
BF16NP = ml_dtypes.bfloat16

N_CORES = 8



class Prog:

    def __init__(self, ctx, nc):
        self.ctx = ctx
        self.ctx0 = ctx
        self.nc = nc
        self.block = None
        self.bar = None
        self.nbar = 0
        self._semcs = {}
        self._banks = []
        self._bank_i = 0

    def sbuf(self, name, shape, dtype, persist=False):
        ctx = self.ctx0 if persist else self.ctx
        return ctx.enter_context(self.nc.sbuf_tensor(name, shape, dtype))

    def push_phase(self):
        st = contextlib.ExitStack()
        self._phase = st
        self.ctx = st

    def pop_phase(self):
        self._phase.close()
        self.ctx = self.ctx0

    def psum(self, name, shape, dtype):
        return self.ctx.enter_context(self.nc.psum_tensor(name, shape, dtype))

    def psum_bank(self, name):
        if len(self._banks) < 8:
            self._banks.append(self.ctx0.enter_context(
                self.nc.psum_tensor(f"bank{len(self._banks)}", [128, 512], F32)))
            self._bank_i = len(self._banks) - 1
            return self._banks[self._bank_i]
        self._bank_i = (self._bank_i + 1) % 8
        return self._banks[self._bank_i]

    def semc(self, name):
        if name not in self._semcs:
            self._semcs[name] = SemC(self.ctx0.enter_context(self.nc.semaphore(name)))
        return self._semcs[name]

    def sem(self, name):
        return self.ctx0.enter_context(self.nc.semaphore(name))

    def barrier(self):
        self.nbar += 1
        k = self.nbar
        b = self.block
        for section in (b.tensor, b.vector, b.scalar, b.gpsimd, b.sync):
            @section
            def _(e):
                e.nop().then_inc(self.bar, 1)
                e.wait_ge(self.bar, 5 * k)

    def tape_init(self):
        self.ops = {k: [] for k in ("tensor", "vector", "scalar", "gpsimd", "sync")}

    def emit(self, eng, fn):
        self.ops[eng].append(fn)

    def tape_barrier(self):
        self.nbar += 1
        k = self.nbar
        for eng in self.ops:
            def f(e, k=k):
                e.drain()
                e.nop().then_inc(self.bar, 1)
                e.wait_ge(self.bar, 5 * k)
            self.emit(eng, f)

    def flush(self):
        b = self.block
        for name, section in (("tensor", b.tensor), ("vector", b.vector),
                              ("scalar", b.scalar), ("gpsimd", b.gpsimd),
                              ("sync", b.sync)):
            ops = self.ops[name]

            @section
            def _(e, _ops=ops):
                for fn in _ops:
                    fn(e)


class SemC:

    def __init__(self, handle):
        self.h = handle
        self.n = 0

    def inc(self, k=1):
        self.n += k
        return self.n



def emit_pre(p, x, xnf, xn2f, lnw_rep, lnb_rep, ident_hbm, mu_hbm, rstd_hbm,
             affine=True):
    nc, block = p.nc, p.block
    NQ = 2 if affine else 1

    xb = [p.sbuf(f"pre_xb{i}", [128, D], F32) for i in range(2)]
    xc = p.sbuf("pre_xc", [128, D], F32)
    sq = p.sbuf("pre_sq", [128, D], F32)
    xn = p.sbuf("pre_xn", [128, D], F32)
    xnbf = p.sbuf("pre_xnbf", [128, D], BF16)
    xn2bf = p.sbuf("pre_xn2bf", [128, D], BF16)
    xc2 = p.sbuf("pre_xc2", [128, D], F32)
    st1 = p.sbuf("pre_st1", [128, 8], F32)
    epsc = p.sbuf("pre_eps", [128, 1], F32)
    mu_sb = p.sbuf("pre_mu", [128, NTB], F32)
    rstd_sb = p.sbuf("pre_rstd", [128, NTB], F32)
    lnw = p.sbuf("pre_lnw", [128, D], F32)
    lnb = p.sbuf("pre_lnb", [128, D], F32)
    ident = p.sbuf("ident_sb", [128, 128], BF16)
    stg = p.sbuf("pre_stg", [128, 8, 128], BF16)
    stg2 = p.sbuf("pre_stg2", [128, 8, 128], BF16)
    ptb = [p.psum_bank(f"pre_pt{i}") for i in range(2)]
    pt = [b.ap().bitcast(BF16)[:, 0:128] for b in ptb]

    s_xld = [p.sem("pre_xld0"), p.sem("pre_xld1")]
    s_cld = p.sem("pre_cld")
    s_cldg = p.sem("pre_cldg")
    s_v2s = p.sem("pre_v2s")
    s_s2v = p.sem("pre_s2v")
    s_cast = p.sem("pre_cast")
    s_tr = p.sem("pre_tr")
    s_ev = p.sem("pre_ev")
    s_out = p.sem("pre_out")
    s_xcons = p.sem("pre_xcons")

    NT = 8 * NQ

    @block.sync
    def _(s_):
        if affine:
            s_.dma_start(out=lnw[:], in_=lnw_rep[:, :]).then_inc(s_cld, 16)
            s_.dma_start(out=lnb[:], in_=lnb_rep[:, :]).then_inc(s_cld, 16)
        for tb in range(NTB):
            if tb >= 2:
                s_.wait_ge(s_xcons, tb - 1)
            s_.dma_start(out=xb[tb % 2][:], in_=x[tb * 128:(tb + 1) * 128, :]
                         ).then_inc(s_xld[tb % 2], 16)

    CLD = 32 if affine else 0

    @block.vector
    def _(v):
        def guard(instr):
            v.drain()

        v.memset(epsc[:], EPS)
        if affine:
            v.wait_ge(s_cld, CLD)
        for tb in range(NTB):
            b = xb[tb % 2]
            v.wait_ge(s_xld[tb % 2], 16 * (tb // 2 + 1))
            guard(v.reduce_sum(out=st1[:, 0:1], in_=b[:], axis=AX.X))
            v.tensor_scalar(out=st1[:, 0:1], in0=st1[:, 0:1], scalar1=-1.0 / D,
                            scalar2=None, op0=ALU.mult)
            v.drain()
            v.tensor_scalar(out=xc[:], in0=b[:], scalar1=st1[:, 0:1], scalar2=None,
                            op0=ALU.add).then_inc(s_v2s, 1)
            v.tensor_copy(out=mu_sb[:, tb:tb + 1], in_=st1[:, 0:1]).then_inc(s_xcons, 1)
            v.wait_ge(s_s2v, NQ * tb + 1)
            guard(v.reciprocal(out=st1[:, 3:4], in_=st1[:, 2:3]))
            v.tensor_copy(out=rstd_sb[:, tb:tb + 1], in_=st1[:, 3:4])
            v.tensor_scalar(out=xn[:], in0=xc[:], scalar1=st1[:, 3:4], scalar2=None,
                            op0=ALU.mult)
            v.tensor_copy(out=xnbf[:], in_=xn[:]).then_inc(s_cast, 1)
            if affine:
                guard(v.reduce_sum(out=st1[:, 4:5], in_=xn[:], axis=AX.X))
                v.tensor_scalar(out=st1[:, 4:5], in0=st1[:, 4:5], scalar1=-1.0 / D,
                                scalar2=None, op0=ALU.mult)
                v.drain()
                v.tensor_scalar(out=xc2[:], in0=xn[:], scalar1=st1[:, 4:5],
                                scalar2=None, op0=ALU.add).then_inc(s_v2s, 1)
                v.wait_ge(s_s2v, 2 * tb + 2)
                guard(v.reciprocal(out=st1[:, 7:8], in_=st1[:, 6:7]))
                v.tensor_scalar(out=xc2[:], in0=xc2[:], scalar1=st1[:, 7:8],
                                scalar2=None, op0=ALU.mult)
                v.drain()
                v.tensor_tensor(out=xc2[:], in0=xc2[:], in1=lnw[:], op=ALU.mult)
                v.drain()
                v.tensor_tensor(out=xn2bf[:], in0=xc2[:], in1=lnb[:],
                                op=ALU.add).then_inc(s_cast, 1)
            v.wait_ge(s_out, 16 * NQ * tb)
            for c in range(NT):
                v.wait_ge(s_tr, NT * tb + c + 1)
                psrc = pt[c % 2]
                dst = stg if c < 8 else stg2
                cc = c if c < 8 else c - 8
                v.tensor_copy(out=dst[:, cc, :], in_=psrc[:]).then_inc(s_ev, 1)

    @block.scalar
    def _(sc):
        for tb in range(NTB):
            sc.wait_ge(s_v2s, NQ * tb + 1)
            sc.activation(out=sq[:], in_=xc[:], func=AF.Square, accum_out=st1[:, 1:2])
            sc.drain()
            sc.activation(out=st1[:, 2:3], in_=st1[:, 1:2], func=AF.Sqrt,
                          scale=1.0 / D, bias=epsc[:, 0:1]).then_inc(s_s2v, 1)
            if affine:
                sc.wait_ge(s_v2s, 2 * tb + 2)
                sc.activation(out=sq[:], in_=xc2[:], func=AF.Square,
                              accum_out=st1[:, 5:6])
                sc.drain()
                sc.activation(out=st1[:, 6:7], in_=st1[:, 5:6], func=AF.Sqrt,
                              scale=1.0 / D, bias=epsc[:, 0:1]).then_inc(s_s2v, 1)

    @block.tensor
    def _(t):
        t.wait_ge(s_cldg, 16)
        for tb in range(NTB):
            for c in range(NT):
                t.wait_ge(s_cast, NQ * tb + (1 if c < 8 else 2))
                if tb > 0 or c >= 2:
                    t.wait_ge(s_ev, NT * tb + c - 1)
                srct = xnbf if c < 8 else xn2bf
                cc = c if c < 8 else c - 8
                t.transpose(pt[c % 2][:], srct[:, cc * 128:(cc + 1) * 128],
                            ident[:]).then_inc(s_tr, 1)

    @block.gpsimd
    def _(g):
        g.dma_start(out=ident[:], in_=ident_hbm[:, :]).then_inc(s_cldg, 16)
        for tb in range(NTB):
            g.wait_ge(s_ev, NT * tb + 8)
            g.dma_start(out=xnf[:, :, tb * 128:(tb + 1) * 128],
                        in_=stg[:, :, :]).then_inc(s_out, 16)
            if affine:
                g.wait_ge(s_ev, NT * tb + 16)
                g.dma_start(out=xn2f[:, :, tb * 128:(tb + 1) * 128],
                            in_=stg2[:, :, :]).then_inc(s_out, 16)
        g.dma_start(out=mu_hbm[:, :], in_=mu_sb[:]).then_inc(s_out, 16)
        g.dma_start(out=rstd_hbm[:, :], in_=rstd_sb[:]).then_inc(s_out, 16)
        g.wait_ge(s_out, 16 * (NQ * NTB + 2))


def build_pre(affine=True):
    ctx = contextlib.ExitStack()
    with ctx:
        nc = bass.Bass("TRN2", target_bir_lowering=False, debug=False)
        p = Prog(ctx, nc)
        x = nc.dram_tensor("x", [T, D], F32, kind="ExternalInput").ap()
        lnw = nc.dram_tensor("lnw_rep", [128, D], F32, kind="ExternalInput").ap()
        lnb = nc.dram_tensor("lnb_rep", [128, D], F32, kind="ExternalInput").ap()
        ident = nc.dram_tensor("ident", [128, 128], F32, kind="ExternalInput").ap()
        xnf = nc.dram_tensor("xnf", [128, 8, T], BF16, kind="ExternalOutput").ap()
        if affine:
            xn2f = nc.dram_tensor("xn2f", [128, 8, T], BF16, kind="ExternalOutput").ap()
        else:
            xn2f = None
        mu = nc.dram_tensor("mu", [128, NTB], F32, kind="ExternalOutput").ap()
        rstd = nc.dram_tensor("rstd", [128, NTB], F32, kind="ExternalOutput").ap()
        p.bar = p.sem("bar")
        with nc.Block() as block:
            p.block = block
            emit_pre(p, x, xnf, xn2f, lnw, lnb, ident, mu, rstd, affine=affine)
        return nc






def emit_moe(p, xnf, x, brA, mu_hbm, rstd_hbm, rwT, rb, w1sT, w2sT, b1s, b2s,
             ones1, x1_out, n_exp=9):
    nc = p.nc
    E = p.emit

    rwt = p.sbuf("moe_rwt", [128, 8, 8], BF16)
    rbt = p.sbuf("moe_rbt", [1, 8], BF16)
    ones_r = p.sbuf("moe_ones", [1, 128], BF16)
    mu_sb = p.sbuf("moe_mu", [128, NTB], F32)
    rstd_sb = p.sbuf("moe_rstd", [128, NTB], F32)
    comb = p.sbuf("moe_comb", [128, NTB, 8], F32)
    xnr = [p.sbuf(f"moe_xnr{i}", [128, 8, 128], BF16) for i in range(2)]
    lg = p.sbuf("moe_lg", [128, 8], F32)
    rst = p.sbuf("moe_rst", [128, 8], F32)
    eprob = p.sbuf("moe_eprob", [128, 8], F32)
    msk = p.sbuf("moe_msk", [128, 8], F32)
    w1q = [p.sbuf(f"moe_w1q{i}", [128, 8, 1024], BF16) for i in range(2)]
    w2q = [p.sbuf(f"moe_w2q{i}", [128, 8, 1024], BF16) for i in range(2)]
    b1t = [p.sbuf(f"moe_b1t{i}", [128, 32], F32) for i in range(2)]
    b2t = [p.sbuf(f"moe_b2t{i}", [1, 1024], BF16) for i in range(2)]
    xnt = [p.sbuf(f"moe_xnt{i}", [128, 8, 512], BF16) for i in range(2)]
    ht = [p.sbuf(f"moe_ht{i}", [128, 8, 512], BF16) for i in range(2)]
    acc = p.sbuf("moe_acc", [128, 16, 1024], F32)
    xcb = [p.sbuf(f"moe_xcb{i}", [128, D], F32) for i in range(2)]
    bab = [p.sbuf(f"moe_bab{i}", [128, D], F32) for i in range(2)]
    x1b = [p.sbuf(f"moe_x1b{i}", [128, D], F32) for i in range(2)]
    xn_s = p.sbuf("moe_xn_s", [128, D], F32)

    p_r = [p.psum_bank(f"moe_pr{i}") for i in range(2)]
    p_h = [p.psum_bank(f"moe_ph{i}") for i in range(2)]
    p_o = [p.psum_bank(f"moe_po{i}") for i in range(2)]

    s_cld = SemC(p.sem("moe_cld"))
    s_cld2 = SemC(p.sem("moe_cld2"))
    s_rxl = [SemC(p.sem("moe_rxl0")), SemC(p.sem("moe_rxl1"))]
    s_rmm = SemC(p.sem("moe_rmm"))
    s_rlev = SemC(p.sem("moe_rlev"))
    s_v2s = SemC(p.sem("moe_v2s"))
    s_s2v = SemC(p.sem("moe_s2v"))
    s_cdone = SemC(p.sem("moe_cdone"))
    s_wld = [SemC(p.sem("moe_wld0")), SemC(p.sem("moe_wld1"))]
    s_xld = [SemC(p.sem("moe_xld0")), SemC(p.sem("moe_xld1"))]
    s_hgrp = SemC(p.sem("moe_hgrp"))
    s_hev = SemC(p.sem("moe_hev"))
    s_ogrp = SemC(p.sem("moe_ogrp"))
    s_oev = SemC(p.sem("moe_oev"))
    s_cxl = [SemC(p.sem("moe_cxl0")), SemC(p.sem("moe_cxl1"))]
    s_cv = SemC(p.sem("moe_cv"))
    s_x1o = [SemC(p.sem("moe_x1o0")), SemC(p.sem("moe_x1o1"))]

    def f(g):
        g.dma_start(out=rwt[:], in_=rwT.rearrange("(c p) e -> p c e", p=128)).then_inc(s_cld.h, 16)
        g.dma_start(out=rbt[:], in_=rb[:, :]).then_inc(s_cld.h, 16)
        g.dma_start(out=ones_r[:], in_=ones1[:, :]).then_inc(s_cld.h, 16)
    E("gpsimd", f)
    s_cld.inc(48)

    def f(s_):
        s_.dma_start(out=mu_sb[:], in_=mu_hbm[:, :]).then_inc(s_cld2.h, 16)
        s_.dma_start(out=rstd_sb[:], in_=rstd_hbm[:, :]).then_inc(s_cld2.h, 16)
    E("sync", f)
    s_cld2.inc(32)
    CLD = s_cld.n
    CLD2 = s_cld2.n

    for tb in range(NTB):
        par = tb % 2
        need_cdone = s_cdone.n - 1
        xl = s_rxl[par].inc(16)

        def f(s_, tb=tb, par=par, need=need_cdone):
            s_.wait_ge(s_cld.h, CLD)
            if need > 0:
                s_.wait_ge(s_cdone.h, need)
            s_.dma_start(out=xnr[par][:],
                         in_=xnf[:, :, tb * 128:(tb + 1) * 128]).then_inc(s_rxl[par].h, 16)
        E("sync", f)

        need_rlev = s_rlev.n - 1
        mm = s_rmm.inc(1)

        def f(t, tb=tb, par=par, xl=xl, need=need_rlev):
            t.wait_ge(s_cld.h, CLD)
            t.wait_ge(s_rxl[par].h, xl)
            if need > 0:
                t.wait_ge(s_rlev.h, need)
            pr = p_r[par].ap()
            for c in range(8):
                t.matmul(pr[:, 0:8], xnr[par][:, c, :], rwt[:, c, :],
                         start=(c == 0), stop=False)
            t.matmul(pr[:, 0:8], ones_r[:], rbt[:, :], start=False,
                     stop=True).then_inc(s_rmm.h, 1)
        E("tensor", f)

        rlev = s_rlev.inc(1)
        v2s = s_v2s.inc(1)
        s2v_t = s_s2v.n + 1
        cd = s_cdone.inc(1)

        def f(v, tb=tb, par=par, mm=mm, s2v_t=s2v_t):
            v.wait_ge(s_rmm.h, mm)
            v.reduce_max(out=rst[:, 0:1], in_=p_r[par].ap()[:, 0:8], axis=AX.X)
            v.drain()
            v.tensor_scalar(out=rst[:, 1:2], in0=rst[:, 0:1], scalar1=-1.0,
                            scalar2=None, op0=ALU.mult)
            v.drain().then_inc(s_v2s.h, 1)
            v.wait_ge(s_s2v.h, s2v_t)
            v.tensor_scalar(out=msk[:], in0=eprob[:], scalar1=1.0, scalar2=None,
                            op0=ALU.is_ge)
            v.drain()
            v.scalar_tensor_tensor(out=msk[:], in0=msk[:], scalar=-2.0, in1=eprob[:],
                                   op0=ALU.mult, op1=ALU.add)
            v.drain()
            v.reduce_max(out=rst[:, 2:3], in_=msk[:], axis=AX.X)
            v.drain()
            v.tensor_scalar(out=msk[:], in0=eprob[:], scalar1=rst[:, 2:3],
                            scalar2=None, op0=ALU.is_ge)
            v.tensor_scalar(out=rst[:, 3:4], in0=rst[:, 2:3], scalar1=1.0,
                            scalar2=None, op0=ALU.add)
            v.drain()
            v.reciprocal(out=rst[:, 4:5], in_=rst[:, 3:4])
            v.drain()
            v.tensor_tensor(out=msk[:], in0=msk[:], in1=eprob[:], op=ALU.mult)
            v.drain()
            v.tensor_scalar(out=comb[:, tb, :], in0=msk[:], scalar1=rst[:, 4:5],
                            scalar2=None, op0=ALU.mult).then_inc(s_cdone.h, 1)
        E("vector", f)

        def f(sc, par=par, v2s=v2s):
            sc.wait_ge(s_v2s.h, v2s)
            sc.activation(out=eprob[:], in_=p_r[par].ap()[:, 0:8], func=AF.Exp,
                          bias=rst[:, 1:2]).then_inc(s_rlev.h, 1)
            sc.drain().then_inc(s_s2v.h, 1)
        E("scalar", f)
        s_s2v.inc(1)

    p.tape_barrier()

    TPH = 16
    n_eq = 0
    n_t = 0
    n_b = 0
    n_og = 0
    for H in range(2):
        eq_list = [(H, e, q) for e in range(n_exp) for q in range(4)]
        pending = None

        def emit_pending():
            nonlocal pending, n_og
            if pending is None:
                return
            key, it, og0 = pending
            _emit_moe_outgroups(p, key, it, ht, w1q, w2q, b1t, b2t, ones_r, p_o,
                                acc, comb, s_hev, s_oev, s_ogrp, og0)
            n_og += 8
            s_ogrp.inc(8)
            s_oev.inc(8)
            pending = None

        for (H_, e, q) in eq_list:
            idx_eq = n_eq
            n_eq += 1
            par = idx_eq % 2
            epar = e % 2
            need_og = (idx_eq - 1) * 32

            def f(g, e=e, q=q, par=par, epar=epar, need=need_og):
                if need > 0:
                    g.wait_ge(s_ogrp.h, need)
                g.dma_start(out=w1q[par][:],
                            in_=w1sT[e, q, :, :, :]).then_inc(s_wld[par].h, 16)
                if q == 0:
                    g.dma_start(out=b1t[epar][:], in_=b1s[e, :, :]).then_inc(s_wld[par].h, 16)
            E("gpsimd", f)

            def f(s_, e=e, q=q, par=par, epar=epar, need=need_og):
                if need > 0:
                    s_.wait_ge(s_ogrp.h, need)
                s_.dma_start(out=w2q[par][:],
                             in_=w2sT[e, q, :, :, :]).then_inc(s_wld[par].h, 16)
                if q == 0:
                    s_.dma_start(out=b2t[epar][:], in_=b2s[e, :, :]).then_inc(s_wld[par].h, 16)
            E("sync", f)
            s_wld[par].inc(32 + (32 if q == 0 else 0))
            wld_after = s_wld[par].n

            for tau in range(4):
                idx_t = n_t
                n_t += 1
                tpar = idx_t % 2
                need_hgrp = (idx_t - 1) * 8

                def f(s_, H=H, tau=tau, tpar=tpar, need=need_hgrp):
                    if need > 0:
                        s_.wait_ge(s_hgrp.h, need)
                    s_.dma_start(out=xnt[tpar][:],
                                 in_=xnf[:, :, H * 2048 + tau * 512:H * 2048 + (tau + 1) * 512]
                                 ).then_inc(s_xld[tpar].h, 16)
                E("sync", f)
                xld_after = s_xld[tpar].inc(16)

                def f(t, par=par, tpar=tpar, wld=wld_after, xld=xld_after, gb0=n_b):
                    t.wait_ge(s_wld[par].h, wld)
                    t.wait_ge(s_xld[tpar].h, xld)
                    for b in range(8):
                        gb = gb0 + b
                        if gb >= 2:
                            t.wait_ge(s_hev.h, gb - 1)
                        ph = p_h[gb % 2].ap()
                        for c in range(8):
                            mm = t.matmul(ph[:, 0:512], w1q[par][:, c, b * 128:(b + 1) * 128],
                                          xnt[tpar][:, c, :], start=(c == 0),
                                          stop=(c == 7))
                        mm.then_inc(s_hgrp.h, 1)
                E("tensor", f)
                gb0_this = n_b
                n_b += 8
                s_hgrp.inc(8)

                def f(sc, q=q, par=par, epar=epar, tpar=tpar, gb0=gb0_this, wld=wld_after,
                      need_og=(idx_t - 1) * 8):
                    sc.wait_ge(s_wld[par].h, wld)
                    if need_og > 0:
                        sc.wait_ge(s_ogrp.h, need_og)
                    for b in range(8):
                        gb = gb0 + b
                        sc.wait_ge(s_hgrp.h, gb + 1)
                        ph = p_h[gb % 2].ap()
                        sc.activation(out=ht[tpar][:, b, :], in_=ph[:, 0:512],
                                      func=AF.Gelu_apprx_tanh,
                                      bias=b1t[epar][:, q * 8 + b:q * 8 + b + 1]
                                      ).then_inc(s_hev.h, 1)
                E("scalar", f)
                s_hev.inc(8)

                emit_pending()
                pending = ((H, e, q, tau), idx_t, n_og)

        emit_pending()
        p.tape_barrier()

        for tl in range(TPH):
            tb = H * TPH + tl
            par = tl % 2
            need_cv = s_cv.n - 1

            def f(s_, tb=tb, par=par, need=need_cv):
                if need > 0:
                    s_.wait_ge(s_cv.h, need)
                s_.dma_start(out=xcb[par][:],
                             in_=x[tb * 128:(tb + 1) * 128, :]).then_inc(s_cxl[par].h, 16)
                s_.dma_start(out=bab[par][:],
                             in_=brA[tb * 128:(tb + 1) * 128, :]).then_inc(s_cxl[par].h, 16)
            E("sync", f)
            cxl = s_cxl[par].inc(32)
            need_out = s_x1o[par].n

            def f(v, tb=tb, tl=tl, par=par, cxl=cxl, need_out=need_out):
                v.wait_ge(s_cld2.h, CLD2)
                v.wait_ge(s_cxl[par].h, cxl)
                if need_out > 0:
                    v.wait_ge(s_x1o[par].h, need_out)
                v.tensor_scalar(out=xn_s[:], in0=xcb[par][:], scalar1=mu_sb[:, tb:tb + 1],
                                scalar2=rstd_sb[:, tb:tb + 1], op0=ALU.add, op1=ALU.mult)
                v.drain()
                v.tensor_tensor(out=xn_s[:], in0=xn_s[:], in1=acc[:, tl, :], op=ALU.add)
                v.drain()
                v.tensor_tensor(out=xn_s[:], in0=xn_s[:], in1=xcb[par][:], op=ALU.add)
                v.drain()
                v.tensor_tensor(out=x1b[par][:], in0=xn_s[:], in1=bab[par][:], op=ALU.add)
                v.drain().then_inc(s_cv.h, 1)
            E("vector", f)
            cv = s_cv.inc(1)

            def f(g, tb=tb, par=par, cv=cv):
                g.wait_ge(s_cv.h, cv)
                g.dma_start(out=x1_out[tb * 128:(tb + 1) * 128, :],
                            in_=x1b[par][:]).then_inc(s_x1o[par].h, 16)
            E("gpsimd", f)
            s_x1o[par].inc(16)

        p.tape_barrier()

    t0_, t1_ = s_x1o[0].n, s_x1o[1].n

    def f(g):
        g.wait_ge(s_x1o[0].h, t0_)
        g.wait_ge(s_x1o[1].h, t1_)
    E("gpsimd", f)


def _emit_moe_outgroups(p, key, idx_t, ht, w1q, w2q, b1t, b2t, ones_r, p_o,
                        acc, comb, s_hev, s_oev, s_ogrp, og0):
    (H, e, q, tau) = key
    idx_eq = idx_t // 4
    par = idx_eq % 2
    epar = e % 2
    tpar = idx_t % 2
    hev_need = (idx_t + 1) * 8

    def f(t, q=q, par=par, epar=epar, tpar=tpar, og0=og0, hev=hev_need):
        t.wait_ge(s_hev.h, hev)
        for og in range(8):
            tl_, dt = og // 2, og % 2
            gog = og0 + og
            if gog >= 2:
                t.wait_ge(s_oev.h, gog - 1)
            po = p_o[gog % 2].ap()
            for b in range(8):
                mm = t.matmul(po[:, 0:512],
                              ht[tpar][:, b, tl_ * 128:(tl_ + 1) * 128],
                              w2q[par][:, b, dt * 512:(dt + 1) * 512],
                              start=(b == 0), stop=(q != 3 and b == 7))
            if q == 3:
                mm = t.matmul(po[:, 0:512], ones_r[:], b2t[epar][:, dt * 512:(dt + 1) * 512],
                              start=False, stop=True)
            mm.then_inc(s_ogrp.h, 1)
    p.emit("tensor", f)

    ogrp_base = s_ogrp.n

    def f(v, e=e, q=q, tau=tau, H=H, og0=og0, ogrp_base=ogrp_base):
        for og in range(8):
            tl_, dt = og // 2, og % 2
            gog = og0 + og
            v.wait_ge(s_ogrp.h, ogrp_base + og + 1)
            po = p_o[gog % 2].ap()
            tl_abs = tau * 4 + tl_
            tb_abs = H * 16 + tl_abs
            dst = acc[:, tl_abs, dt * 512:(dt + 1) * 512]
            if e == 0 and q == 0:
                v.tensor_scalar(out=dst, in0=po[:, 0:512], scalar1=1.0,
                                scalar2=None, op0=ALU.mult).then_inc(s_oev.h, 1)
            elif e == 0:
                v.scalar_tensor_tensor(out=dst, in0=po[:, 0:512], scalar=1.0,
                                       in1=dst, op0=ALU.mult,
                                       op1=ALU.add).then_inc(s_oev.h, 1)
            else:
                v.scalar_tensor_tensor(out=dst, in0=po[:, 0:512],
                                       scalar=comb[:, tb_abs, e - 1:e],
                                       in1=dst, op0=ALU.mult,
                                       op1=ALU.add).then_inc(s_oev.h, 1)
    p.emit("vector", f)


def build_moe(n_exp=9):
    ctx = contextlib.ExitStack()
    with ctx:
        nc = bass.Bass("TRN2", target_bir_lowering=False, debug=False)
        p = Prog(ctx, nc)
        xnf = nc.dram_tensor("xnf", [128, 8, T], BF16, kind="ExternalInput").ap()
        x = nc.dram_tensor("x", [T, D], F32, kind="ExternalInput").ap()
        brA = nc.dram_tensor("brA", [T, D], F32, kind="ExternalInput").ap()
        mu = nc.dram_tensor("mu", [128, NTB], F32, kind="ExternalInput").ap()
        rstd = nc.dram_tensor("rstd", [128, NTB], F32, kind="ExternalInput").ap()
        rwT = nc.dram_tensor("rwT", [1024, 8], F32, kind="ExternalInput").ap()
        rb = nc.dram_tensor("rb", [1, 8], F32, kind="ExternalInput").ap()
        w1sT = nc.dram_tensor("w1sT", [9, 8, 128, 4096], F32, kind="ExternalInput").ap()
        w2sT = nc.dram_tensor("w2sT", [9, 32, 128, 1024], F32, kind="ExternalInput").ap()
        b1s = nc.dram_tensor("b1s", [9, 128, 32], F32, kind="ExternalInput").ap()
        b2s = nc.dram_tensor("b2s", [9, 1, 1024], F32, kind="ExternalInput").ap()
        ones1 = nc.dram_tensor("ones1", [1, 128], F32, kind="ExternalInput").ap()
        x1 = nc.dram_tensor("x1", [T, D], F32, kind="ExternalOutput").ap()
        p.bar = p.sem("bar")
        p.tape_init()
        emit_moe(p, xnf, x, brA, mu, rstd, rwT, rb, w1sT, w2sT, b1s, b2s, ones1,
                 x1, n_exp=n_exp)
        with nc.Block() as block:
            p.block = block
            p.flush()
        return nc


def moe_host_layouts(inp):
    w1 = np.concatenate([inp["L0_ffn_shared_w1"][None], inp["L0_ffn_spec_w1"]], 0)
    w2 = np.concatenate([inp["L0_ffn_shared_w2"][None], inp["L0_ffn_spec_w2"]], 0)
    b1 = np.concatenate([inp["L0_ffn_shared_b1"][None], inp["L0_ffn_spec_b1"]], 0)
    b2 = np.concatenate([inp["L0_ffn_shared_b2"][None], inp["L0_ffn_spec_b2"]], 0)
    w1sT = np.ascontiguousarray(
        w1.transpose(0, 2, 1).reshape(9, 8, 128, 4, 1024).transpose(0, 3, 2, 1, 4))
    w2sT = np.ascontiguousarray(
        w2.transpose(0, 2, 1).reshape(9, 4, 8, 128, 1024).transpose(0, 1, 3, 2, 4))
    b1s = np.ascontiguousarray(b1.reshape(9, 32, 128).transpose(0, 2, 1))
    b2s = np.ascontiguousarray(b2.reshape(9, 1, 1024))
    rwT = np.ascontiguousarray(inp["L0_ffn_router_w"].T)
    rb = np.ascontiguousarray(inp["L0_ffn_router_b"].reshape(1, 8))
    ones1 = np.ones((1, 128), np.float32)
    return dict(w1sT=w1sT, w2sT=w2sT, b1s=b1s, b2s=b2s, rwT=rwT, rb=rb, ones1=ones1)



NH = 32
HD = 64
DIN = 2048
SCW = 256
CPS = SCW // 128
NSC = T // SCW
NCH = 32


def emit_mamba(p, xn2f, wzT, wxT, wdT, woT, convw, convb, dtb4, alog, dvec,
               normw, triu_i, tril_s, ones128, ones1f, identf, identd,
               ltdram, brA_out, dbg=None):
    nc = p.nc
    E = p.emit

    wz = p.sbuf("ma_wz", [128, 8, 2048], BF16)
    wx = p.sbuf("ma_wx", [128, 8, 2304], BF16)
    wd = p.sbuf("ma_wd", [128, 8, 32], BF16)
    wo = p.sbuf("ma_wo", [128, 16, 1024], BF16)
    cw = p.sbuf("ma_cw", [128, 18, 4], F32)
    cb = p.sbuf("ma_cb", [128, 18], F32)
    dtbr = p.sbuf("ma_dtbr", [128, CPS * 32], F32)
    nwr = p.sbuf("ma_nwr", [128, 2048], BF16)
    tri1 = p.sbuf("ma_tri1", [128, 128], F32)
    tri2 = p.sbuf("ma_tri2", [128, 128], F32)
    on128 = p.sbuf("ma_on128", [128, 128], F32)
    on1f = p.sbuf("ma_on1f", [1, 128], F32)
    non1f = p.sbuf("ma_non1f", [1, 128], F32)
    idf = p.sbuf("ma_idf", [128, 128], F32)
    idb = p.sbuf("ma_idb", [128, 128], BF16)
    alog_s = p.sbuf("ma_alog", [1, 32], F32)
    arow = p.sbuf("ma_arow", [1, 128], F32)
    arep = p.sbuf("ma_arep", [128, 128], F32)
    dvec_s = p.sbuf("ma_dvec", [1, 32], F32)
    drow = p.sbuf("ma_drow", [1, 32], BF16)
    drep = p.sbuf("ma_drep", [128, 32], BF16)
    epsc = p.sbuf("ma_eps", [128, 1], F32)

    xn2t = p.sbuf("ma_xn2t", [128, 8, SCW], BF16)
    z_sil = p.sbuf("ma_zsil", [128, CPS, 2048], BF16)
    xbce = [p.sbuf(f"ma_xbce{i}", [128, SCW + 3], F32) for i in range(2)]
    xbcc = [p.sbuf(f"ma_xbcc{i}", [128, SCW], F32) for i in range(2)]
    halo = p.sbuf("ma_halo", [128, 18, 3], F32)
    xs_fm = p.sbuf("ma_xsfm", [128, 16, SCW], BF16)
    b_fm = p.sbuf("ma_bfm", [128, SCW], BF16)
    c_fm = p.sbuf("ma_cfm", [128, SCW], BF16)
    dt_raw = p.sbuf("ma_dtraw", [128, CPS, 32], F32)
    dtp = p.sbuf("ma_dtp", [128, CPS * 32], F32)
    dtp_bf = p.sbuf("ma_dtpbf", [128, CPS, 32], BF16)
    adt = p.sbuf("ma_adt", [128, CPS * 32], F32)
    sp1 = p.sbuf("ma_sp1", [128, CPS * 32], F32)
    sp2 = p.sbuf("ma_sp2", [128, CPS * 32], F32)
    ltok = p.sbuf("ma_ltok", [128, 32], F32)
    etok = p.sbuf("ma_etok", [128, CPS, 32], F32)
    lam = p.sbuf("ma_lam", [128, CPS, 32], F32)
    w2bf = p.sbuf("ma_w2bf", [128, CPS, 32], BF16)
    w2f = p.sbuf("ma_w2f", [128, 32], F32)
    ltt = p.sbuf("ma_ltt", [32, 128], F32)
    pstr = p.sbuf("ma_pstr", [1, 4096], F32)
    g2m = p.sbuf("ma_g2m", [128, 128], BF16)
    btok = p.sbuf("ma_btok", [128, 128], BF16)
    xtok = p.sbuf("ma_xtok", [128, 2048], BF16)
    xw1 = p.sbuf("ma_xw1", [128, 2048], BF16)
    xw2 = p.sbuf("ma_xw2", [128, 2048], BF16)
    decf = p.sbuf("ma_decf", [128, 512], F32)
    decb = p.sbuf("ma_decb", [128, 512], BF16)
    sct = [p.sbuf(f"ma_sct{i}", [128, 512], BF16) for i in range(2)]
    ych = p.sbuf("ma_ych", [128, 2048], F32)
    ssb = p.sbuf("ma_ssb", [128, 2048], F32)
    sbf = p.sbuf("ma_sbf", [128, 2048], BF16)
    ysq = p.sbuf("ma_ysq", [128, 2048], F32)
    gfm = p.sbuf("ma_gfm", [128, 16, 128], BF16)
    oout = [p.sbuf("ma_oout0", [128, 1024], F32)]

    pb = [p.psum_bank(f"ma_p{i}") for i in range(8)]
    pbb = [b.ap().bitcast(BF16) for b in pb]

    s_w = SemC(p.sem("ma_w"))
    s_ws = SemC(p.sem("ma_ws"))
    s_x = [SemC(p.sem(f"ma_x{i}")) for i in range(2)]
    s_pmm = SemC(p.sem("ma_pmm"))
    s_pev = SemC(p.sem("ma_pev"))
    s_v2s = SemC(p.sem("ma_v2s"))
    s_s2v = SemC(p.sem("ma_s2v"))
    s_vg = SemC(p.sem("ma_vg"))
    s_sg = SemC(p.sem("ma_sg"))
    s_pg = SemC(p.sem("ma_pg"))
    s_tg = SemC(p.sem("ma_tg"))
    s_lts = [SemC(p.sem(f"ma_lts{i}")) for i in range(2)]
    s_ltl = SemC(p.sem("ma_ltl"))
    s_oo = [SemC(p.sem(f"ma_oo{i}")) for i in range(2)]
    s_t34 = SemC(p.sem("ma_t34"))
    s_e34 = SemC(p.sem("ma_e34"))

    def tW(t, val):
        if val > 0:
            t.wait_ge(s_tg.h, val)

    def f(g):
        g.dma_start(out=wz[:], in_=wzT.rearrange("(c p) z -> p c z", p=128)).then_inc(s_w.h, 16)
        g.dma_start(out=wx[:], in_=wxT.rearrange("(c p) z -> p c z", p=128)).then_inc(s_w.h, 16)
        g.dma_start(out=wd[:], in_=wdT.rearrange("(c p) z -> p c z", p=128)).then_inc(s_w.h, 16)
        g.dma_start(out=wo[:], in_=woT.rearrange("(c p) z -> p c z", p=128)).then_inc(s_w.h, 16)
        g.dma_start(out=idb[:], in_=identd[:, :]).then_inc(s_w.h, 16)
        g.dma_start(out=drow[:], in_=dvec[:, :]).then_inc(s_w.h, 16)
        g.dma_start(out=nwr[:], in_=normw[:, :]).then_inc(s_w.h, 16)
    E("gpsimd", f)
    s_w.inc(112)
    W0 = s_w.n

    def f(s_):
        for dst, srct in ((cw, convw), (cb, convb), (dtbr, dtb4),
                          (tri1, triu_i), (tri2, tril_s), (on128, ones128),
                          (on1f, ones1f), (idf, identf), (alog_s, alog),
                          (dvec_s, dvec)):
            s_.dma_start(out=dst[:], in_=srct[:, :] if srct.ndim == 2 else srct[:, :, :]).then_inc(s_ws.h, 16)
    E("sync", f)
    s_ws.inc(16 * 10)
    WS0 = s_ws.n

    def f(sc):
        sc.wait_ge(s_ws.h, WS0)
        sc.activation(out=arow[:, 0:32], in_=alog_s[:], func=AF.Exp, scale=1.0)
        sc.drain()
        sc.mul(out=arow[:, 0:32], in_=arow[:, 0:32], mul=-1.0)
        sc.drain()
        for j in range(1, 4):
            sc.copy(out=arow[:, 32 * j:32 * (j + 1)], in_=arow[:, 0:32])
        sc.drain().then_inc(s_sg.h, 1)
    E("scalar", f)
    s_sg.inc(1)
    SG_A = s_sg.n

    def f(t):
        t.wait_ge(s_sg.h, SG_A)
        t.wait_ge(s_w.h, W0)
        t.matmul(pb[2].ap()[:, 0:128], on1f[:], arow[:], start=True,
                 stop=True).then_inc(s_tg.h, 1)
    E("tensor", f)
    s_tg.inc(1)
    TG_A = s_tg.n

    def f(v):
        v.memset(epsc[:], EPS)
        v.memset(ssb[:], 0.0)
        v.memset(sbf[:], 0.0)
        v.memset(halo[:], 0.0)
        v.wait_ge(s_tg.h, TG_A)
        v.tensor_copy(out=arep[:], in_=pb[2].ap()[:, 0:128])
        v.drain().then_inc(s_vg.h, 1)
    E("vector", f)
    s_vg.inc(1)
    VG_A = s_vg.n

    onesb = p.sbuf("ma_onesb", [1, 128], BF16)

    def f(v):
        v.tensor_copy(out=onesb[:], in_=on1f[:])
        v.drain().then_inc(s_vg.h, 1)
    E("vector", f)
    s_vg.inc(1)
    VG_OB = s_vg.n

    def f(t):
        t.wait_ge(s_vg.h, VG_OB)
        t.matmul(pb[7].ap()[:, 0:32], onesb[:], drow[:], start=True, stop=True).then_inc(s_tg.h, 1)
    E("tensor", f)
    s_tg.inc(1)
    TG_D = s_tg.n

    def f(v):
        v.wait_ge(s_tg.h, TG_D)
        v.tensor_copy(out=drep[:], in_=pb[7].ap()[:, 0:32])
        v.drain().then_inc(s_vg.h, 1)
    E("vector", f)
    s_vg.inc(1)

    p.tape_barrier()

    p.ma = dict(pgrp=0, t34=0, ldiff_done={}, sil={}, zs={}, dt_ready={},
                strip_ld={}, lblk={})
    ctxd = dict(locals())

    def f(v):
        v.tensor_scalar(out=non1f[:], in0=on1f[:], scalar1=-1.0, scalar2=None,
                        op0=ALU.mult)
        v.drain().then_inc(s_vg.h, 1)
    E("vector", f)
    s_vg.inc(1)

    for S in range(NSC):
        _mamba_stage1(p, S, ctxd)
        for cloc in range(CPS):
            g = CPS * S + cloc
            _mamba_scan_chunk(p, S, cloc, g, ctxd)
        p.tape_barrier()

    o0, o1 = s_oo[0].n, s_oo[1].n

    def f(s_):
        s_.wait_ge(s_oo[0].h, o0)
        s_.wait_ge(s_oo[1].h, o1)
    E("sync", f)


def _mamba_stage1(p, S, L):
    E = p.emit
    xn2t, wz, wx, wd = L["xn2t"], L["wz"], L["wx"], L["wd"]
    pb = L["pb"]
    z_sil, xbce, xbcc, halo = L["z_sil"], L["xbce"], L["xbcc"], L["halo"]
    xs_fm, b_fm, c_fm = L["xs_fm"], L["b_fm"], L["c_fm"]
    dt_raw, dtp, dtp_bf, adt = L["dt_raw"], L["dtp"], L["dtp_bf"], L["adt"]
    sp1, sp2 = L["sp1"], L["sp2"]
    ltok, etok, lam, w2bf, w2f = L["ltok"], L["etok"], L["lam"], L["w2bf"], L["w2f"]
    ltt = L["ltt"]
    cw, cb, dtbr, arep = L["cw"], L["cb"], L["dtbr"], L["arep"]
    tri1, tri2, on128, idf = L["tri1"], L["tri2"], L["on128"], L["idf"]
    xn2f, ltdram = L["xn2f"], L["ltdram"]
    s_x, s_pmm, s_pev = L["s_x"], L["s_pmm"], L["s_pev"]
    s_vg, s_sg, s_pg = L["s_vg"], L["s_sg"], L["s_pg"]
    s_v2s, s_s2v = L["s_v2s"], L["s_s2v"]
    s_tg = L["s_tg"]
    s_lts = L["s_lts"]
    s_w, s_ws = L["s_w"], L["s_ws"]
    W0, WS0 = L["W0"], L["WS0"]

    xl = s_x[S % 2].inc(16)

    def f(s_, S=S):
        s_.wait_ge(s_w.h, W0)
        s_.dma_start(out=xn2t[:], in_=xn2f[:, :, S * SCW:(S + 1) * SCW]
                     ).then_inc(s_x[S % 2].h, 16)
    E("sync", f)

    groups = ([("z", i // 4, i % 4) for i in range(4 * CPS)]
              + [("x", cb_, None) for cb_ in range(18)]
              + [("dt", tb, None) for tb in range(CPS)])
    pg0 = p.ma["pgrp"]
    for gi, (kind, a1, a2) in enumerate(groups):
        gidx = pg0 + gi
        bank = pb[gidx % 2]
        need_ev = gidx - 1

        def f(t, kind=kind, a1=a1, a2=a2, bank=bank, xl=xl, need=need_ev, S=S):
            t.wait_ge(s_x[S % 2].h, xl)
            if need > 0:
                t.wait_ge(s_pev.h, need)
            pa = bank.ap()
            if kind == "z":
                for c in range(8):
                    mm = t.matmul(pa[:, 0:512], xn2t[:, c, a1 * 128:(a1 + 1) * 128],
                                  wz[:, c, a2 * 512:(a2 + 1) * 512],
                                  start=(c == 0), stop=(c == 7))
            elif kind == "x":
                for c in range(8):
                    mm = t.matmul(pa[:, 0:SCW], wx[:, c, a1 * 128:(a1 + 1) * 128],
                                  xn2t[:, c, :], start=(c == 0), stop=(c == 7))
            else:
                for c in range(8):
                    mm = t.matmul(pa[:, 0:32], xn2t[:, c, a1 * 128:(a1 + 1) * 128],
                                  wd[:, c, :], start=(c == 0), stop=(c == 7))
            mm.then_inc(s_pmm.h, 1)
        E("tensor", f)
        pmm = s_pmm.inc(1)

        if kind == "z":
            def f(sc, a1=a1, a2=a2, bank=bank, pmm=pmm):
                sc.wait_ge(s_pmm.h, pmm)
                sc.activation(out=z_sil[:, a1, a2 * 512:(a2 + 1) * 512],
                              in_=bank.ap()[:, 0:512],
                              func=AF.Silu).then_inc(s_pev.h, 1)
            E("scalar", f)
            s_pev.inc(1)
        elif kind == "x":
            cc = a1
            par = cc % 2
            conv_prev = p.ma.get("conv_pg", {}).get(cc - 2, 0)
            sil_prev = p.ma.get("sil_sg", {}).get(cc - 2, 0)

            def f(v, cc=cc, par=par, bank=bank, pmm=pmm, cprev=conv_prev):
                v.wait_ge(s_pmm.h, pmm)
                if cprev > 0:
                    v.wait_ge(s_pg.h, cprev)
                v.tensor_copy(out=xbce[par][:, 0:3], in_=halo[:, cc, :])
                v.tensor_copy(out=xbce[par][:, 3:SCW + 3],
                              in_=bank.ap()[:, 0:SCW])
                v.drain()
                v.tensor_copy(out=halo[:, cc, :], in_=xbce[par][:, SCW:SCW + 3])
                v.drain().then_inc(s_pev.h, 1)
            E("vector", f)
            ev = s_pev.inc(1)

            def f(v, cc=cc, par=par, ev=ev, sprev=sil_prev):
                v.wait_ge(s_pev.h, ev)
                if sprev > 0:
                    v.wait_ge(s_sg.h, sprev)
                for k in range(4):
                    win = xbce[par][:, k:k + SCW]
                    wcol = cw[:, cc, k:k + 1]
                    if k == 0:
                        v.tensor_scalar(out=xbcc[par][:, :], in0=win, scalar1=wcol,
                                        scalar2=None, op0=ALU.mult)
                    else:
                        v.scalar_tensor_tensor(out=xbcc[par][:, :], in0=win,
                                               scalar=wcol, in1=xbcc[par][:, :],
                                               op0=ALU.mult, op1=ALU.add)
                    v.drain()
                v.nop().then_inc(s_pg.h, 1)
            E("vector", f)
            p.ma.setdefault("conv_pg", {})[cc] = s_pg.inc(1)

            def f(sc, cc=cc, par=par, pg=p.ma["conv_pg"][cc]):
                sc.wait_ge(s_pg.h, pg)
                sc.wait_ge(s_ws.h, WS0)
                if cc < 16:
                    dst = xs_fm[:, cc, :]
                elif cc == 16:
                    dst = b_fm[:, :]
                else:
                    dst = c_fm[:, :]
                sc.activation(out=dst, in_=xbcc[par][:, :], func=AF.Silu,
                              bias=cb[:, cc:cc + 1]).then_inc(s_sg.h, 1)
            E("scalar", f)
            p.ma.setdefault("sil_sg", {})[cc] = s_sg.inc(1)
        else:
            def f(v, a1=a1, bank=bank, pmm=pmm):
                v.wait_ge(s_pmm.h, pmm)
                v.tensor_copy(out=dt_raw[:, a1, :],
                              in_=bank.ap()[:, 0:32]).then_inc(s_pev.h, 1)
            E("vector", f)
            s_pev.inc(1)
    p.ma["pgrp"] += len(groups)
    pev_all = s_pev.n
    p.ma["sil"][S] = s_sg.n

    def f(v, pev=pev_all):
        v.wait_ge(s_pev.h, pev)
        v.tensor_tensor(out=sp1[:], in0=dt_raw[:].rearrange("p a b -> p (a b)"),
                        in1=dtbr[:], op=ALU.add)
        v.drain()
        v.tensor_scalar(out=dtp[:], in0=sp1[:], scalar1=0.0, scalar2=None,
                        op0=ALU.max)
        v.drain().then_inc(s_v2s.h, 1)
    E("vector", f)
    v2s = s_v2s.inc(1)

    def f(sc, v2s=v2s):
        sc.wait_ge(s_v2s.h, v2s)
        sc.activation(out=sp2[:], in_=sp1[:], func=AF.Abs)
        sc.drain()
        sc.activation(out=sp2[:], in_=sp2[:], func=AF.Exp, scale=-1.0)
        sc.drain()
        sc.activation(out=sp2[:], in_=sp2[:], func=AF.Ln, bias=1.0)
        sc.drain().then_inc(s_s2v.h, 1)
    E("scalar", f)
    s2v = s_s2v.inc(1)

    def f(v, s2v=s2v):
        v.wait_ge(s_s2v.h, s2v)
        v.tensor_tensor(out=dtp[:], in0=dtp[:], in1=sp2[:], op=ALU.add)
        v.drain()
        v.tensor_tensor(out=adt[:], in0=dtp[:], in1=arep[:, 0:CPS * 32], op=ALU.mult)
        v.tensor_copy(out=dtp_bf[:, :, :],
                      in_=dtp[:].rearrange("p (a b) -> p a b", a=CPS, b=32))
        v.drain().then_inc(s_vg.h, 1)
    E("vector", f)
    dt_vg = s_vg.inc(1)
    p.ma["dt_ready"][S] = dt_vg

    for cloc in range(CPS):
        gch = CPS * S + cloc

        def f(t, cloc=cloc, dt_vg=dt_vg, pv=p.ma.get("ltev_vg", 0),
              ps=p.ma.get("lexp_sg", 0)):
            t.wait_ge(s_vg.h, dt_vg)
            if pv > 0:
                t.wait_ge(s_vg.h, pv)
            if ps > 0:
                t.wait_ge(s_sg.h, ps)
            asl = adt[:, cloc * 32:(cloc + 1) * 32]
            t.matmul(pb[2].ap()[:, 0:32], tri1[:], asl, start=True,
                     stop=True).then_inc(s_tg.h, 1)
            t.matmul(pb[7].ap()[:, 0:32], tri2[:], asl, start=True, stop=False)
            t.matmul(pb[7].ap()[:, 32:64], on128[:], asl, start=False,
                     stop=True).then_inc(s_tg.h, 1)
        E("tensor", f)
        tg_l = s_tg.inc(2)

        def f(v, tg=tg_l - 1):
            v.wait_ge(s_tg.h, tg)
            v.tensor_copy(out=ltok[:], in_=pb[2].ap()[:, 0:32])
            v.drain().then_inc(s_vg.h, 1)
        E("vector", f)
        ltok_vg = s_vg.inc(1)

        def f(sc, cloc=cloc, tg=tg_l, w2prev=p.ma.get("w2_vg", 0)):
            sc.wait_ge(s_tg.h, tg - 1)
            sc.activation(out=etok[:, cloc, :], in_=pb[2].ap()[:, 0:32], func=AF.Exp)
            sc.drain()
            sc.wait_ge(s_tg.h, tg)
            if w2prev > 0:
                sc.wait_ge(s_vg.h, w2prev)
            sc.activation(out=w2f[:, :], in_=pb[7].ap()[:, 0:32], func=AF.Exp)
            sc.drain()
            sc.activation(out=lam[:, cloc, :], in_=pb[7].ap()[:, 32:64], func=AF.Exp)
            sc.drain().then_inc(s_sg.h, 1)
        E("scalar", f)
        lexp_sg = s_sg.inc(1)
        p.ma["lexp_sg"] = lexp_sg

        def f(v, cloc=cloc, sg=lexp_sg):
            v.wait_ge(s_sg.h, sg)
            v.tensor_tensor(out=w2bf[:, cloc, :], in0=w2f[:, :],
                            in1=dtp[:, cloc * 32:(cloc + 1) * 32], op=ALU.mult)
            v.drain().then_inc(s_vg.h, 1)
        E("vector", f)
        p.ma["w2_vg"] = s_vg.inc(1)

        def f(t, vg=ltok_vg, sg=lexp_sg):
            t.wait_ge(s_vg.h, vg)
            t.wait_ge(s_sg.h, sg)
            t.transpose(pb[2].ap()[0:32, 0:128], ltok[:], idf[:]).then_inc(s_tg.h, 1)
        E("tensor", f)
        tg_t = s_tg.inc(1)

        def f(v, tg=tg_t, pstore=p.ma.get("lt_store", None)):
            v.wait_ge(s_tg.h, tg)
            if pstore is not None:
                v.wait_ge(pstore[0], pstore[1])
            v.tensor_copy(out=ltt[:], in_=pb[2].ap()[0:32, 0:128])
            v.drain().then_inc(s_vg.h, 1)
        E("vector", f)
        ltev_vg = s_vg.inc(1)
        p.ma["ltev_vg"] = ltev_vg
        p.ma["lblk"][gch] = (tg_t, ltev_vg)

        def f(gp, gch=gch, vg=ltev_vg):
            gp.wait_ge(s_vg.h, vg)
            gp.dma_start(out=ltdram[gch, 0, :, :], in_=ltt[:]).then_inc(s_lts[gch % 2].h, 16)
        E("gpsimd", f)
        s_lts[gch % 2].inc(16)
        p.ma["strip_ld"][gch] = s_lts[gch % 2].n
        p.ma["lt_store"] = (s_lts[gch % 2].h, s_lts[gch % 2].n)


def _mamba_scan_chunk(p, S, cloc, g, L):
    E = p.emit
    pb, pbb = L["pb"], L["pbb"]
    xs_fm, b_fm, c_fm = L["xs_fm"], L["b_fm"], L["c_fm"]
    btok, xtok, xw1, xw2 = L["btok"], L["xtok"], L["xw1"], L["xw2"]
    dtp_bf, w2bf, etok, lam = L["dtp_bf"], L["w2bf"], L["etok"], L["lam"]
    pstr = L["pstr"]
    decf, decb, sct, g2m = L["decf"], L["decb"], L["sct"], L["g2m"]
    ych, ssb, sbf = L["ych"], L["ssb"], L["sbf"]
    ysq = L["ysq"]
    gfm, oout = L["gfm"], L["oout"]
    gbf = L["xw1"]
    z_sil, nwr, drep, epsc = L["z_sil"], L["nwr"], L["drep"], L["epsc"]
    tri1, idb, on1f, idf, wo = L["tri1"], L["idb"], L["on1f"], L["idf"], L["wo"]
    non1f = L["non1f"]
    ltdram, brA_out = L["ltdram"], L["brA_out"]
    s_tg, s_vg, s_sg = L["s_tg"], L["s_vg"], L["s_sg"]
    s_t34, s_e34 = L["s_t34"], L["s_e34"]
    s_v2s, s_s2v = L["s_v2s"], L["s_s2v"]
    s_lts, s_ltl, s_oo = L["s_lts"], L["s_ltl"], L["s_oo"]
    sil_sg = p.ma["sil"][S]
    dt_vg = p.ma["dt_ready"][S]
    par = g % 2

    need_ld = p.ma["ldiff_done"].get(g - 1, 0)
    st_cnt = p.ma["strip_ld"][g]

    def f(s_, g=g, par=par, need=need_ld, st=st_cnt):
        s_.wait_ge(s_lts[par].h, st)
        if need > 0:
            s_.wait_ge(s_t34.h, need)
        s_.dma_start(out=pstr[:, :],
                     in_=ltdram[g, 0, :, :].rearrange("a b -> (a b)").unsqueeze(0)
                     ).then_inc(s_ltl.h, 16)
    E("sync", f)
    ltl = s_ltl.inc(16)

    def t34_op(body, extra_waits=()):
        i = p.ma["t34"]
        p.ma["t34"] += 1
        bank = pb[3 + (i % 2)]
        bb = pbb[3 + (i % 2)]
        need = i - 1

        def f(t, bank=bank, bb=bb, need=need, ws=tuple(extra_waits)):
            for (sh, sv) in ws:
                t.wait_ge(sh, sv)
            if need > 0:
                t.wait_ge(s_e34.h, need)
            body(t, bank.ap(), bb)
        E("tensor", f)
        s_t34.inc(1)
        return s_t34.n, bank, bb

    def ev34(engine, body, tcount, extra_waits=()):
        def f(e, tcount=tcount, ws=tuple(extra_waits)):
            e.wait_ge(s_t34.h, tcount)
            for (sh, sv) in ws:
                e.wait_ge(sh, sv)
            body(e)
        E(engine, f)
        s_e34.inc(1)
        return s_e34.n

    def body(t, pa, bb):
        t.transpose(bb[:, 0:128], b_fm[:, cloc * 128:(cloc + 1) * 128],
                    idb[:]).then_inc(s_t34.h, 1)
    tc, _, bb_ = t34_op(body, extra_waits=[(s_sg.h, sil_sg)])

    def b(v, bb_=bb_, btfree=p.ma.get("xw_free", 0)):
        if btfree > 0:
            v.wait_ge(s_tg.h, btfree)
        v.tensor_copy(out=btok[:, :], in_=bb_[:, 0:128]).then_inc(s_e34.h, 1)
    ev34("vector", b, tc)

    for cc in range(16):
        def body(t, pa, bb, cc=cc):
            t.transpose(bb[:, 0:128], xs_fm[:, cc, cloc * 128:(cloc + 1) * 128],
                        idb[:]).then_inc(s_t34.h, 1)
        tc, _, bb_ = t34_op(body)

        def b(v, cc=cc, bb_=bb_):
            v.tensor_copy(out=xtok[:, cc * 128:(cc + 1) * 128],
                          in_=bb_[:, 0:128]).then_inc(s_e34.h, 1)
        ev34("vector", b, tc)
    xtok_ev = s_e34.n

    def f(v, cloc=cloc, ev=xtok_ev, dvg=dt_vg, w2vg=p.ma["w2_vg"],
          xwfree=p.ma.get("xw_free", 0)):
        v.wait_ge(s_e34.h, ev)
        if xwfree > 0:
            v.wait_ge(s_tg.h, xwfree)
        v.wait_ge(s_vg.h, max(dvg, w2vg))
        xv = xtok[:].rearrange("p (h d) -> p h d", h=32)
        v.tensor_tensor(out=xw1[:].rearrange("p (h d) -> p h d", h=32), in0=xv,
                        in1=dtp_bf[:, cloc, :].unsqueeze(2).broadcast_to([128, 32, 64]),
                        op=ALU.mult)
        v.tensor_tensor(out=xw2[:].rearrange("p (h d) -> p h d", h=32), in0=xv,
                        in1=w2bf[:, cloc, :].unsqueeze(2).broadcast_to([128, 32, 64]),
                        op=ALU.mult)
        v.drain().then_inc(s_vg.h, 1)
    E("vector", f)
    xw_vg = s_vg.inc(1)

    def f(t, cloc=cloc, sg=sil_sg, lblk=p.ma["lblk"][CPS * S + CPS - 1]):
        t.wait_ge(s_sg.h, sg)
        t.wait_ge(s_vg.h, lblk[1])
        t.matmul(pb[2].ap()[:, 0:128], b_fm[:, cloc * 128:(cloc + 1) * 128],
                 c_fm[:, cloc * 128:(cloc + 1) * 128], start=True,
                 stop=True).then_inc(s_tg.h, 1)
    E("tensor", f)
    g2_tg = s_tg.inc(1)

    def f(v, tg=g2_tg):
        v.wait_ge(s_tg.h, tg)
        v.tensor_tensor(out=g2m[:], in0=pb[2].ap()[:, 0:128], in1=tri1[:],
                        op=ALU.mult)
        v.drain().then_inc(s_vg.h, 1)
    E("vector", f)
    g2_vg = s_vg.inc(1)

    sbf_ready = p.ma.get("sbf_vg", 0)
    pb5_free0 = p.ma.get("ds_ev", 0)
    for j in range(4):
        evn = pb5_free0 if j == 0 else p.ma.get("yev", 0)

        def f(t, j=j, sr=sbf_ready, evn=evn, sg=sil_sg):
            t.wait_ge(s_sg.h, sg)
            if sr > 0:
                t.wait_ge(s_vg.h, sr)
            if evn > 0:
                t.wait_ge(s_vg.h, evn)
            t.matmul(pb[5].ap()[:, 0:512], c_fm[:, cloc * 128:(cloc + 1) * 128],
                     sbf[:, j * 512:(j + 1) * 512], start=True,
                     stop=True).then_inc(s_tg.h, 1)
        E("tensor", f)
        tgy = s_tg.inc(1)

        def f(v, j=j, tgy=tgy, cloc=cloc):
            v.wait_ge(s_tg.h, tgy)
            v.tensor_tensor(out=ych[:, j * 512:(j + 1) * 512]
                            .rearrange("p (a b) -> p a b", a=8),
                            in0=pb[5].ap()[:, 0:512].rearrange("p (a b) -> p a b", a=8),
                            in1=etok[:, cloc, j * 8:(j + 1) * 8].unsqueeze(2)
                            .broadcast_to([128, 8, 64]),
                            op=ALU.mult)
            v.drain().then_inc(s_vg.h, 1)
        E("vector", f)
        p.ma["yev"] = s_vg.inc(1)

    for hg in range(8):
        def body(t, pa, bb, hg=hg, ltl=ltl):
            t.wait_ge(s_ltl.h, ltl)
            for j in range(4):
                h = 4 * hg + j
                sl = slice(j * 128, (j + 1) * 128)
                t.matmul(pa[:, sl], on1f[:], pstr[:, h * 128:(h + 1) * 128],
                         start=True, stop=False)
                mm = t.matmul(pa[:, sl], pstr[:, h * 128:(h + 1) * 128],
                              non1f[:], start=False, stop=True)
            mm.then_inc(s_t34.h, 1)
        tc, bank, _ = t34_op(body)

        def b(v, bank=bank, prev_s2v=s_s2v.n):
            if prev_s2v > 0:
                v.wait_ge(s_s2v.h, prev_s2v)
            v.tensor_scalar(out=decf[:], in0=bank.ap()[:, 0:512], scalar1=0.0,
                            scalar2=None, op0=ALU.min)
            v.drain().then_inc(s_e34.h, 1)
        ev34("vector", b, tc)

        def f(v):
            v.nop().then_inc(s_v2s.h, 1)
        E("vector", f)
        v2s = s_v2s.inc(1)

        def f(sc, v2s=v2s, prev_sct=s_vg.n):
            sc.wait_ge(s_v2s.h, v2s)
            sc.wait_ge(s_vg.h, prev_sct)
            sc.activation(out=decb[:], in_=decf[:], func=AF.Exp)
            sc.drain().then_inc(s_s2v.h, 1)
        E("scalar", f)
        s2v = s_s2v.inc(1)

        sfree = p.ma.get("yintra_mm", {}).get(hg - 2 if hg >= 2 else -99,
                                              p.ma.get("yintra_last", 0))

        def f(v, hg=hg, s2v=s2v, gvg=g2_vg, sfree=sfree):
            v.wait_ge(s_s2v.h, s2v)
            v.wait_ge(s_vg.h, gvg)
            if sfree > 0:
                v.wait_ge(s_tg.h, sfree)
            v.tensor_tensor(out=sct[hg % 2][:].rearrange("p (a b) -> p a b", a=4),
                            in0=decb[:].rearrange("p (a b) -> p a b", a=4),
                            in1=g2m[:].unsqueeze(1).broadcast_to([128, 4, 128]),
                            op=ALU.mult)
            v.drain().then_inc(s_vg.h, 1)
        E("vector", f)
        sct_vg = s_vg.inc(1)

        j = hg // 2
        pb6_free = p.ma.get("yintra_ev", 0) if hg % 2 == 0 else 0

        def f(t, hg=hg, sv=sct_vg, xwv=xw_vg, pbf=pb6_free):
            t.wait_ge(s_vg.h, max(sv, xwv))
            if pbf > 0 and hg % 2 == 0:
                t.wait_ge(s_vg.h, pbf)
            base = (hg % 2) * 256
            for hh in range(4):
                h = 4 * hg + hh
                mm = t.matmul(pb[6].ap()[:, base + hh * 64:base + (hh + 1) * 64],
                              sct[hg % 2][:, hh * 128:(hh + 1) * 128],
                              xw1[:, h * 64:(h + 1) * 64],
                              start=(hg % 2 == 0 and hh == 0),
                              stop=(hg % 2 == 1 and hh == 3))
            mm.then_inc(s_tg.h, 1)
        E("tensor", f)
        p.ma.setdefault("yintra_mm", {})[hg] = s_tg.inc(1)

        if hg % 2 == 1:
            def f(v, j=j, tgi=p.ma["yintra_mm"][hg]):
                v.wait_ge(s_tg.h, tgi)
                v.scalar_tensor_tensor(out=ych[:, j * 512:(j + 1) * 512],
                                       in0=pb[6].ap()[:, 0:512], scalar=1.0,
                                       in1=ych[:, j * 512:(j + 1) * 512],
                                       op0=ALU.mult, op1=ALU.add)
                v.drain().then_inc(s_vg.h, 1)
            E("vector", f)
            p.ma["yintra_ev"] = s_vg.inc(1)
    y_vg = s_vg.n
    p.ma["yintra_last"] = p.ma["yintra_mm"][7]
    p.ma["ldiff_done"][g] = s_t34.n

    def f(v, ev=xtok_ev, yv=y_vg):
        v.wait_ge(s_vg.h, yv)
        v.tensor_tensor(out=ysq[:].rearrange("p (h d) -> p h d", h=32),
                        in0=xtok[:].rearrange("p (h d) -> p h d", h=32),
                        in1=drep[:].unsqueeze(2).broadcast_to([128, 32, 64]),
                        op=ALU.mult)
        v.drain()
        v.tensor_tensor(out=ych[:], in0=ych[:], in1=ysq[:], op=ALU.add)
        v.drain().then_inc(s_vg.h, 1)
    E("vector", f)
    yd_vg = s_vg.inc(1)

    for j in range(4):
        def f(t, j=j, xwv=xw_vg, yev=p.ma["yev"], dsev=p.ma.get("ds_ev", 0)):
            t.wait_ge(s_vg.h, xwv)
            t.wait_ge(s_vg.h, yev)
            if dsev > 0:
                t.wait_ge(s_vg.h, dsev)
            t.matmul(pb[5].ap()[:, 0:512], btok[:, :],
                     xw2[:, j * 512:(j + 1) * 512], start=True,
                     stop=True).then_inc(s_tg.h, 1)
        E("tensor", f)
        tgd = s_tg.inc(1)

        def f(v, j=j, tgd=tgd, cloc=cloc):
            v.wait_ge(s_tg.h, tgd)
            v.tensor_tensor(out=ssb[:, j * 512:(j + 1) * 512]
                            .rearrange("p (a b) -> p a b", a=8),
                            in0=ssb[:, j * 512:(j + 1) * 512]
                            .rearrange("p (a b) -> p a b", a=8),
                            in1=lam[:, cloc, j * 8:(j + 1) * 8].unsqueeze(2)
                            .broadcast_to([128, 8, 64]),
                            op=ALU.mult)
            v.drain()
            v.scalar_tensor_tensor(out=ssb[:, j * 512:(j + 1) * 512],
                                   in0=pb[5].ap()[:, 0:512], scalar=1.0,
                                   in1=ssb[:, j * 512:(j + 1) * 512],
                                   op0=ALU.mult, op1=ALU.add)
            v.drain().then_inc(s_vg.h, 1)
        E("vector", f)
        p.ma["ds_ev"] = s_vg.inc(1)
    p.ma["xw_free"] = s_tg.n

    def f(v):
        v.tensor_copy(out=sbf[:], in_=ssb[:])
        v.drain().then_inc(s_vg.h, 1)
    E("vector", f)
    p.ma["sbf_vg"] = s_vg.inc(1)

    def f(v, cloc=cloc, yv=yd_vg):
        v.wait_ge(s_vg.h, yv)
        v.tensor_tensor(out=ych[:], in0=ych[:], in1=z_sil[:, cloc, :], op=ALU.mult)
        v.drain().then_inc(s_v2s.h, 1)
    E("vector", f)
    v2s_g = s_v2s.inc(1)

    def f(sc, v2s=v2s_g):
        sc.wait_ge(s_v2s.h, v2s)
        sc.activation(out=ysq[:], in_=ych[:], func=AF.Square,
                      accum_out=L["w2f"][:, 0:1])
        sc.drain()
        sc.activation(out=L["w2f"][:, 1:2], in_=L["w2f"][:, 0:1], func=AF.Sqrt,
                      scale=1.0 / 2048, bias=epsc[:, 0:1])
        sc.drain().then_inc(s_s2v.h, 1)
    E("scalar", f)
    s2v_g = s_s2v.inc(1)

    def f(v, s2v=s2v_g):
        v.wait_ge(s_s2v.h, s2v)
        v.reciprocal(out=L["w2f"][:, 2:3], in_=L["w2f"][:, 1:2])
        v.drain()
        v.tensor_scalar(out=ych[:], in0=ych[:], scalar1=L["w2f"][:, 2:3],
                        scalar2=None, op0=ALU.mult)
        v.drain()
        v.tensor_tensor(out=gbf[:], in0=ych[:], in1=nwr[:], op=ALU.mult)
        v.drain().then_inc(s_vg.h, 1)
    E("vector", f)
    gbf_vg = s_vg.inc(1)

    for cc in range(16):
        def body(t, pa, bb, cc=cc, gv=gbf_vg):
            t.wait_ge(s_vg.h, gv)
            t.transpose(bb[:, 0:128], gbf[:, cc * 128:(cc + 1) * 128],
                        idb[:]).then_inc(s_t34.h, 1)
        tc, _, bb_ = t34_op(body)

        def b(v, cc=cc, bb_=bb_):
            v.tensor_copy(out=gfm[:, cc, :], in_=bb_[:, 0:128]).then_inc(s_e34.h, 1)
        ev34("vector", b, tc)
    gfm_ev = s_e34.n

    opar = 0
    for dt_ in range(2):
        def body(t, pa, bb, dt_=dt_, gv=gfm_ev, oo=p.ma.get("oo_done", {}).get((g - 2, dt_), 0)):
            t.wait_ge(s_e34.h, gv)
            for cc in range(16):
                mm = t.matmul(pa[:, 0:512], gfm[:, cc, :],
                              wo[:, cc, dt_ * 512:(dt_ + 1) * 512],
                              start=(cc == 0), stop=(cc == 15))
            mm.then_inc(s_t34.h, 1)
        tc, bank, _ = t34_op(body)

        def b(v, dt_=dt_, bank=bank, opar=opar, pre=s_oo[opar].n):
            if pre > 0:
                v.wait_ge(s_oo[opar].h, pre)
            v.tensor_copy(out=oout[opar][:, dt_ * 512:(dt_ + 1) * 512],
                          in_=bank.ap()[:, 0:512]).then_inc(s_e34.h, 1)
        ev34("vector", b, tc)
    oev = s_e34.n

    def f(s_, g=g, opar=opar, oev=oev):
        s_.wait_ge(s_e34.h, oev)
        s_.dma_start(out=brA_out[g * 128:(g + 1) * 128, :],
                     in_=oout[opar][:]).then_inc(s_oo[opar].h, 16)
    E("sync", f)
    s_oo[opar].inc(16)


def build_mamba():
    ctx = contextlib.ExitStack()
    with ctx:
        nc = bass.Bass("TRN2", target_bir_lowering=False, debug=False)
        p = Prog(ctx, nc)
        d = {}
        d["xn2f"] = nc.dram_tensor("xn2f", [128, 8, T], BF16, kind="ExternalInput").ap()
        for nm, shape in [("wzT", [1024, 2048]), ("wxT", [1024, 2304]),
                          ("wdT", [1024, 32]), ("woT", [2048, 1024]),
                          ("convw", [128, 18, 4]), ("convb", [128, 18]),
                          ("dtb4", [128, CPS * 32]), ("alog", [1, 32]),
                          ("dvec", [1, 32]), ("normw", [128, 2048]),
                          ("triu_i", [128, 128]), ("tril_s", [128, 128]),
                          ("ones128", [128, 128]), ("ones1f", [1, 128]),
                          ("identf", [128, 128]), ("identd", [128, 128])]:
            d[nm] = nc.dram_tensor(nm, shape, F32, kind="ExternalInput").ap()
        d["ltdram"] = nc.dram_tensor("ltdram", [32, 2, 32, 128], F32, kind="Internal").ap()
        d["brA_out"] = nc.dram_tensor("brA", [T, D], F32, kind="ExternalOutput").ap()
        p.bar = p.sem("bar")
        p.tape_init()
        emit_mamba(p, **d)
        with nc.Block() as block:
            p.block = block
            p.flush()
        return nc


def mamba_host_layouts(inp, pref="L0_attn_"):
    ipw = inp[pref + "in_proj_w"]
    e = np.eye(128, dtype=np.float32)
    return dict(
        wzT=np.ascontiguousarray(ipw[0:2048].T),
        wxT=np.ascontiguousarray(ipw[2048:4352].T),
        wdT=np.ascontiguousarray(ipw[4352:4384].T),
        woT=np.ascontiguousarray(inp[pref + "out_proj_w"].T),
        convw=np.ascontiguousarray(inp[pref + "conv_w"].reshape(18, 128, 4).transpose(1, 0, 2)),
        convb=np.ascontiguousarray(inp[pref + "conv_b"].reshape(18, 128).T),
        dtb4=np.ascontiguousarray(np.broadcast_to(np.tile(inp[pref + "dt_bias"], CPS), (128, CPS * 32))).astype(np.float32),
        alog=inp[pref + "A_log"].reshape(1, 32).astype(np.float32),
        dvec=inp[pref + "D"].reshape(1, 32).astype(np.float32),
        normw=np.ascontiguousarray(np.broadcast_to(inp[pref + "norm_w"], (128, 2048))).astype(np.float32),
        triu_i=np.triu(np.ones((128, 128), np.float32)),
        tril_s=np.tril(np.ones((128, 128), np.float32), -1),
        ones128=np.ones((128, 128), np.float32),
        ones1f=np.ones((1, 128), np.float32),
        identf=e, identd=e,
    )



def emit_l1attn(p, xn1f, wqT, wkT, wvT, woT, bq, bk, bv, bo, cosT, sinT,
                ones1f, identd, cmaskT, colones, qf, kf, vf, aof, brA1):
    nc = p.nc
    E = p.emit
    SCALE = 0.125

    wo = p.sbuf("la_wo", [128, 8, 1024], BF16)
    bqs = p.sbuf("la_bq", [1, 1024], BF16)
    bks = p.sbuf("la_bk", [1, 1024], BF16)
    bvs = p.sbuf("la_bv", [1, 1024], BF16)
    bos = p.sbuf("la_bo", [1, 1024], BF16)
    on1b = p.sbuf("la_on1b", [1, 128], BF16)
    onecol = p.sbuf("la_onecol", [64, 1], BF16)
    idb = p.sbuf("la_idb", [128, 128], BF16)
    cmask = p.sbuf("la_cmask", [128, 4, 512], BF16)
    onescol = p.sbuf("la_onescol", [128, 1], BF16)

    stA = contextlib.ExitStack()

    def sbA(nm, shape, dt):
        return stA.enter_context(nc.sbuf_tensor(nm, shape, dt))

    wq = sbA("la_wq", [128, 8, 1024], BF16)
    wk = sbA("la_wk", [128, 8, 1024], BF16)
    wv = sbA("la_wv", [128, 8, 1024], BF16)
    coss = sbA("la_cos", [128, 32, 32], F32)
    sins = sbA("la_sin", [128, 32, 32], F32)
    xt = [sbA(f"la_xt{i}", [128, 8, 128], BF16) for i in range(2)]
    qtok = sbA("la_qtok", [128, 1024], F32)
    rsc = sbA("la_rsc", [128, 4, 512], F32)
    rot = sbA("la_rot", [128, 1024], BF16)
    stg = sbA("la_stg", [128, 8, 128], BF16)

    pb = [p.psum_bank(f"la_p{i}") for i in range(8)]
    pbb = [b.ap().bitcast(BF16) for b in pb]

    s_w = SemC(p.sem("la_w"))
    s_ws = SemC(p.sem("la_ws"))
    s_x = [SemC(p.sem(f"la_x{i}")) for i in range(2)]
    s_mm = SemC(p.sem("la_mm"))
    s_ev = SemC(p.sem("la_ev"))
    s_vg = SemC(p.sem("la_vg"))
    s_sg = SemC(p.sem("la_sg"))
    s_st = {(w_, i): SemC(p.sem(f"la_st{w_}{i}")) for w_ in range(3) for i in range(2)}
    s_hl = SemC(p.sem("la_hl"))
    s_ao = SemC(p.sem("la_ao"))
    s_al = [SemC(p.sem(f"la_al{i}")) for i in range(2)]
    s_oo = [SemC(p.sem(f"la_oo{i}")) for i in range(2)]

    def f(g):
        for dst, srct in ((wq, wqT), (wk, wkT), (wv, wvT), (wo, woT)):
            g.dma_start(out=dst[:], in_=srct.rearrange("(c p) z -> p c z", p=128)
                        ).then_inc(s_w.h, 16)
        for dst, srct in ((bqs, bq), (bks, bk), (bvs, bv), (bos, bo)):
            g.dma_start(out=dst[:], in_=srct[:, :]).then_inc(s_w.h, 16)
        g.dma_start(out=idb[:], in_=identd[:, :]).then_inc(s_w.h, 16)
        g.dma_start(out=on1b[:], in_=ones1f[:, :]).then_inc(s_w.h, 16)
        g.dma_start(out=onecol[:], in_=ones1f[:, 0:64].rearrange("a b -> b a")
                    ).then_inc(s_w.h, 16)
        g.dma_start(out=cmask[:], in_=cmaskT[:, :, :]).then_inc(s_w.h, 16)
        g.dma_start(out=onescol[:], in_=colones[:, :]).then_inc(s_w.h, 16)
    E("gpsimd", f)
    s_w.inc(16 * 13)
    W0 = s_w.n

    def f(s_):
        s_.dma_start(out=coss[:], in_=cosT[:, :, :]).then_inc(s_ws.h, 16)
        s_.dma_start(out=sins[:], in_=sinT[:, :, :]).then_inc(s_ws.h, 16)
    E("sync", f)
    s_ws.inc(32)
    WS0 = s_ws.n

    s_tev = SemC(p.sem("la_tev"))
    ntr = [0]
    nqkv = [0]
    xt_used = {0: 0, 1: 0}
    last_store = [None]
    last_vstore = [None]

    for tb in range(NTB):
        par = tb % 2

        def f(s_, tb=tb, par=par, need=xt_used[par]):
            s_.wait_ge(s_w.h, W0)
            if need > 0:
                s_.wait_ge(s_mm.h, need)
            s_.dma_start(out=xt[par][:], in_=xn1f[:, :, tb * 128:(tb + 1) * 128]
                         ).then_inc(s_x[par].h, 16)
        E("sync", f)
        xl = s_x[par].inc(16)

        for wi, (wt, bt, dst_hbm) in enumerate(((wq, bqs, qf), (wk, bks, kf),
                                                (wv, bvs, vf))):
            for half in range(2):
                gi = nqkv[0]
                nqkv[0] += 1
                bank = gi % 4

                def f(t, wt=wt, bt=bt, par=par, half=half, xl=xl, bank=bank,
                      evn=s_ev.n - 3):
                    t.wait_ge(s_x[par].h, xl)
                    if evn > 0:
                        t.wait_ge(s_ev.h, evn)
                    pa = pb[bank].ap()
                    for c in range(8):
                        t.matmul(pa[:, 0:512], xt[par][:, c, :],
                                 wt[:, c, half * 512:(half + 1) * 512],
                                 start=(c == 0), stop=False)
                    t.matmul(pa[:, 0:512], on1b[:],
                             bt[:, half * 512:(half + 1) * 512],
                             start=False, stop=True).then_inc(s_mm.h, 1)
                E("tensor", f)
                mm = s_mm.inc(1)

                def f(v, half=half, mm=mm, bank=bank):
                    v.wait_ge(s_mm.h, mm)
                    v.tensor_copy(out=qtok[:, half * 512:(half + 1) * 512],
                                  in_=pb[bank].ap()[:, 0:512]).then_inc(s_ev.h, 1)
                E("vector", f)
                s_ev.inc(1)
            xt_used[par] = s_mm.n

            def f(v, wi=wi, tb=tb, ev=s_ev.n, lvs=last_vstore[0]):
                v.wait_ge(s_ws.h, WS0)
                v.wait_ge(s_ev.h, ev)
                if wi == 0 and lvs is not None:
                    v.wait_ge(lvs[0], lvs[1])
                if wi < 2:
                    qv = qtok[:].rearrange("p (h i two) -> p h i two", h=16, i=32)
                    cb_ = coss[:, tb, :].unsqueeze(1).broadcast_to([128, 16, 32])
                    sb_ = sins[:, tb, :].unsqueeze(1).broadcast_to([128, 16, 32])
                    A = rsc[:, 0, :].rearrange("p (h i) -> p h i", h=16)
                    B = rsc[:, 1, :].rearrange("p (h i) -> p h i", h=16)
                    C = rsc[:, 2, :].rearrange("p (h i) -> p h i", h=16)
                    Dd = rsc[:, 3, :].rearrange("p (h i) -> p h i", h=16)
                    rv = rot[:].rearrange("p (h i two) -> p h i two", h=16, i=32)
                    v.tensor_tensor(out=A, in0=qv[:, :, :, 0], in1=cb_, op=ALU.mult)
                    v.tensor_tensor(out=C, in0=qv[:, :, :, 0], in1=sb_, op=ALU.mult)
                    v.tensor_tensor(out=B, in0=qv[:, :, :, 1], in1=sb_, op=ALU.mult)
                    v.tensor_tensor(out=Dd, in0=qv[:, :, :, 1], in1=cb_, op=ALU.mult)
                    v.drain()
                    v.tensor_tensor(out=rv[:, :, :, 0], in0=A, in1=B, op=ALU.subtract)
                    v.tensor_tensor(out=rv[:, :, :, 1], in0=C, in1=Dd, op=ALU.add)
                else:
                    v.tensor_copy(out=rot[:], in_=qtok[:])
                v.drain().then_inc(s_vg.h, 1)
            E("vector", f)
            rope_vg = s_vg.inc(1)

            if wi < 2:
                for c in range(8):
                    i = ntr[0]
                    ntr[0] += 1
                    bank = 4 + (i % 2)

                    def f(t, c=c, bank=bank, vg=rope_vg, need=i - 1):
                        t.wait_ge(s_vg.h, vg)
                        if need > 0:
                            t.wait_ge(s_tev.h, need)
                        t.transpose(pbb[bank][:, 0:128], rot[:, c * 128:(c + 1) * 128],
                                    idb[:]).then_inc(s_mm.h, 1)
                    E("tensor", f)
                    trm = s_mm.inc(1)

                    def f(v, c=c, bank=bank, trm=trm, ls=last_store[0]):
                        v.wait_ge(s_mm.h, trm)
                        if c == 0 and ls is not None:
                            v.wait_ge(ls[0], ls[1])
                        v.tensor_copy(out=stg[:, c, :],
                                      in_=pbb[bank][:, 0:128]).then_inc(s_tev.h, 1)
                    E("vector", f)
                    s_tev.inc(1)
                stv = s_tev.n

                def f(g, tb=tb, dst=dst_hbm, stv=stv, wi=wi, par=par):
                    g.wait_ge(s_tev.h, stv)
                    g.dma_start(out=dst[:, :, tb * 128:(tb + 1) * 128],
                                in_=stg[:]).then_inc(s_st[(wi, par)].h, 16)
                E("gpsimd", f)
                s_st[(wi, par)].inc(16)
                last_store[0] = (s_st[(wi, par)].h, s_st[(wi, par)].n)
            else:
                def f(g, tb=tb, vg=rope_vg, par=par):
                    g.wait_ge(s_vg.h, vg)
                    g.dma_start(out=vf[:, tb, :],
                                in_=rot[:]).then_inc(s_st[(2, par)].h, 16)
                E("gpsimd", f)
                s_st[(2, par)].inc(16)
                last_vstore[0] = (s_st[(2, par)].h, s_st[(2, par)].n)

    p.tape_barrier()
    p.ma["t34"] = 0

    st_tot = [(s.h, s.n) for s in s_st.values()]

    def f(s_, st_tot=st_tot):
        for hh, nn in st_tot:
            s_.wait_ge(hh, nn)
    E("sync", f)
    p.tape_barrier()

    stA.close()
    stB = contextlib.ExitStack()

    def sbB(nm, shape, dt):
        return stB.enter_context(nc.sbuf_tensor(nm, shape, dt))

    qext = [sbB(f"la_qext{i}", [65, T], BF16) for i in range(2)]
    kext = [sbB(f"la_kext{i}", [65, T], BF16) for i in range(2)]
    sqf = sbB("la_sqf", [64, T], BF16)
    vsb = sbB("la_vsb", [128, 32, 16, 65], BF16)
    pstrip = sbB("la_pstrip", [128, 32, 512], BF16)
    r2row = sbB("la_r2", [1, T], BF16)
    brow = sbB("la_brow", [1, T], BF16)
    cells = sbB("la_cells", [1, 8], F32)
    rden = sbB("la_rden", [1, 512], F32)
    rdenb = sbB("la_rdenb", [1, 512], BF16)
    rdrep = sbB("la_rdrep", [64, 512], F32)
    aoh = [sbB(f"la_aoh{i}", [64, T], BF16) for i in range(2)]

    s_vl = s_ws
    vstg = sbB("la_vstg", [128, 2, 1024], BF16)
    vrep = []

    def f(v):
        for hh in range(16):
            v.memset(vsb[:, :, hh, 64:65], 1.0)
        v.drain().then_inc(s_vg.h, 1)
    E("vector", f)
    ones_vg = s_vg.inc(1)

    for i in range(16):
        used = p.ma.get("vstg_used", 0)

        def f(g, i=i, used=used):
            if used > 0:
                g.wait_ge(s_vg.h, used)
            g.dma_start(out=vstg[:], in_=vf[:, i * 2:(i + 1) * 2, :]
                        ).then_inc(s_vl.h, 16)
        E("gpsimd", f)
        vl = s_vl.inc(16)

        def f(v, i=i, vl=vl):
            v.wait_ge(s_vl.h, vl)
            for hh in range(16):
                v.tensor_copy(out=vsb[:, i * 2:(i + 1) * 2, hh, 0:64],
                              in_=vstg[:, :, hh * 64:(hh + 1) * 64])
            v.drain().then_inc(s_vg.h, 1)
        E("vector", f)
        p.ma["vstg_used"] = s_vg.inc(1)
        vrep.append(p.ma["vstg_used"])
    VL0 = s_vl.n

    s_se = SemC(p.sem("la_se"))
    QT = 8
    bnd = {}

    def emit_qk_dma(h):
        par = h % 2
        hp = (h % 2) * 64
        hc = h // 2
        need = p.ma.get(("qk_used", par), 0)

        def f(s_, hp=hp, hc=hc, par=par, need=need):
            if need > 0:
                s_.wait_ge(s_mm.h, need)
            s_.dma_start(out=qext[par][0:64, :], in_=qf[hp:hp + 64, hc, :]).then_inc(s_hl.h, 16)
            s_.dma_start(out=kext[par][0:64, :], in_=kf[hp:hp + 64, hc, :]).then_inc(s_hl.h, 16)
        E("sync", f)
        return s_hl.inc(32)

    def emit_bounds(h, hl):
        par = h % 2
        qe = qext[par]
        ke = kext[par]
        prev_mm = p.ma.get("bnd_mm", 0)

        def f(v, qe=qe, hl=hl, pm=prev_mm):
            v.wait_ge(s_hl.h, hl)
            if pm > 0:
                v.wait_ge(s_mm.h, pm)
            v.tensor_tensor(out=sqf[:, :], in0=qe[0:64, :], in1=qe[0:64, :],
                            op=ALU.mult)
            v.drain().then_inc(s_vg.h, 1)
        E("vector", f)
        sq_vg = s_vg.inc(1)

        for i in range(8):
            bank = 6 + (i % 2)

            def f(t, i=i, bank=bank, vg=sq_vg, evp=s_vg.n - 1):
                t.wait_ge(s_vg.h, vg)
                if i >= 2:
                    t.wait_ge(s_vg.h, evp)
                t.matmul(pb[bank].ap()[0:1, 0:512], onecol[:],
                         sqf[:, i * 512:(i + 1) * 512], start=True,
                         stop=True).then_inc(s_mm.h, 1)
            E("tensor", f)
            mmi = s_mm.inc(1)

            def f(v, i=i, bank=bank, mmi=mmi):
                v.wait_ge(s_mm.h, mmi)
                v.tensor_copy(out=r2row[:, i * 512:(i + 1) * 512],
                              in_=pb[bank].ap()[0:1, 0:512]).then_inc(s_vg.h, 1)
            E("vector", f)
            s_vg.inc(1)
        rq_ev = s_vg.n

        def f(v, ke=ke, ev=rq_ev):
            v.wait_ge(s_vg.h, ev)
            v.tensor_tensor(out=sqf[:, :], in0=ke[0:64, :], in1=ke[0:64, :],
                            op=ALU.mult)
            v.drain().then_inc(s_vg.h, 1)
        E("vector", f)
        sqk_vg = s_vg.inc(1)

        for i in range(8):
            bank = 6 + (i % 2)

            def f(t, i=i, bank=bank, vg=sqk_vg, evp=s_vg.n - 1):
                t.wait_ge(s_vg.h, vg)
                if i >= 2:
                    t.wait_ge(s_vg.h, evp)
                t.matmul(pb[bank].ap()[0:1, 0:512], onecol[:],
                         sqf[:, i * 512:(i + 1) * 512], start=True,
                         stop=True).then_inc(s_mm.h, 1)
            E("tensor", f)
            mmi = s_mm.inc(1)

            def f(v, i=i, bank=bank, mmi=mmi):
                v.wait_ge(s_mm.h, mmi)
                v.tensor_copy(out=brow[:, i * 512:(i + 1) * 512],
                              in_=pb[bank].ap()[0:1, 0:512]).then_inc(s_vg.h, 1)
            E("vector", f)
            s_vg.inc(1)
        p.ma["bnd_mm"] = s_mm.n

        def f(v):
            v.drain()
            v.reduce_max(out=cells[:, 0:1], in_=brow[:, :], axis=AX.X)
            v.drain().then_inc(s_vg.h, 1)
        E("vector", f)
        rk_ev = s_vg.inc(1)

        def f(sc, ev=rk_ev):
            sc.wait_ge(s_vg.h, ev)
            sc.activation(out=cells[:, 1:2], in_=cells[:, 0:1], func=AF.Sqrt)
            sc.drain()
            sc.mul(out=cells[:, 2:3], in_=cells[:, 1:2], mul=-1.0)
            sc.drain()
            sc.activation(out=brow[:, :], in_=r2row[:, :], func=AF.Sqrt)
            sc.drain().then_inc(s_sg.h, 1)
        E("scalar", f)
        b_sg = s_sg.inc(1)

        def f(v, qe=qe, ke=ke, sg=b_sg):
            v.wait_ge(s_sg.h, sg)
            v.tensor_scalar(out=qe[64:65, :], in0=brow[:, :], scalar1=cells[:, 2:3],
                            scalar2=None, op0=ALU.mult)
            v.memset(ke[64:65, :], 1.0)
            v.drain().then_inc(s_vg.h, 1)
        E("vector", f)
        bnd[h] = s_vg.inc(1)

    hl0 = emit_qk_dma(0)
    hl1 = emit_qk_dma(1)
    emit_bounds(0, hl0)
    emit_bounds(1, hl1)

    for h in range(16):
        par = h % 2
        hp = (h % 2) * 64
        hc = h // 2
        qe = qext[par]
        ke = kext[par]
        ao = aoh[par]
        bnd_vg = bnd[h]

        for qt in range(QT):
            khi = 4 * (qt + 1)
            for kb in range(khi):
                bank = p.ma.get("gstrip", 0) % 3
                p.ma["gstrip"] = p.ma.get("gstrip", 0) + 1

                def f(t, qe=qe, ke=ke, qt=qt, kb=kb, bank=bank, bvg=bnd_vg,
                      sen=s_se.n - 2):
                    t.wait_ge(s_vg.h, bvg)
                    if sen > 0:
                        t.wait_ge(s_se.h, sen)
                    t.matmul(pb[bank].ap()[:, 0:512], ke[:, kb * 128:(kb + 1) * 128],
                             qe[:, qt * 512:(qt + 1) * 512], start=True,
                             stop=True).then_inc(s_mm.h, 1)
                E("tensor", f)
                smm = s_mm.inc(1)

                def f(sc, kb=kb, bank=bank, smm=smm,
                      avp=p.ma.get("av_mm", 0)):
                    sc.wait_ge(s_mm.h, smm)
                    if kb == 0 and avp > 0:
                        sc.wait_ge(s_mm.h, avp)
                    sc.activation(out=pstrip[:, kb, :], in_=pb[bank].ap()[:, 0:512],
                                  func=AF.Exp, scale=SCALE).then_inc(s_se.h, 1)
                E("scalar", f)
                s_se.inc(1)
            se_all = s_se.n

            def f(v, qt=qt, khi=khi, se=se_all):
                v.wait_ge(s_se.h, se)
                for j in range(4):
                    kb = 4 * qt + j
                    v.tensor_tensor(out=pstrip[:, kb, :], in0=pstrip[:, kb, :],
                                    in1=cmask[:, j, :], op=ALU.mult)
                v.drain().then_inc(s_vg.h, 1)
            E("vector", f)
            msk_vg = s_vg.inc(1)

            def f(t, h=h, khi=khi, mv=msk_vg, avp=p.ma.get("av_ev", 0),
                  vr=vrep[min((khi - 1) // 2, 15)], ov=ones_vg):
                t.wait_ge(s_vg.h, ov)
                t.wait_ge(s_vg.h, vr)
                t.wait_ge(s_vg.h, mv)
                if avp > 0:
                    t.wait_ge(s_vg.h, avp)
                for kb in range(khi):
                    mm = t.matmul(pb[4].ap()[0:65, 0:512],
                                  vsb[:, kb, h, :],
                                  pstrip[:, kb, :], start=(kb == 0),
                                  stop=(kb == khi - 1))
                mm.then_inc(s_mm.h, 1)
            E("tensor", f)
            av_mm = s_mm.inc(1)
            p.ma["av_mm"] = av_mm

            def f(v, mm=av_mm):
                v.wait_ge(s_mm.h, mm)
                v.tensor_copy(out=rden[:, :], in_=pb[4].ap()[64:65, 0:512])
                v.drain()
                v.reciprocal(out=rden[:, :], in_=rden[:, :])
                v.drain()
                v.tensor_copy(out=rdenb[:, :], in_=rden[:, :])
                v.drain().then_inc(s_vg.h, 1)
            E("vector", f)
            p.ma["dn_ev"] = s_vg.inc(1)
            dn_ev = p.ma["dn_ev"]

            def f(t, dv=dn_ev, rp=p.ma.get("rdrep_ev", 0)):
                t.wait_ge(s_vg.h, dv)
                if rp > 0:
                    t.wait_ge(s_vg.h, rp)
                t.matmul(pb[3].ap()[0:64, 0:512], on1b[:, 0:64], rdenb[:, :],
                         start=True, stop=True).then_inc(s_mm.h, 1)
            E("tensor", f)
            rr_mm = s_mm.inc(1)

            def f(v, mm=rr_mm):
                v.wait_ge(s_mm.h, mm)
                v.tensor_copy(out=rdrep[:, :], in_=pb[3].ap()[0:64, 0:512])
                v.drain().then_inc(s_vg.h, 1)
            E("vector", f)
            p.ma["rdrep_ev"] = s_vg.inc(1)

            def f(v, ao=ao, qt=qt, mm=av_mm, rp=p.ma["rdrep_ev"],
                  aofree=p.ma.get(("aoh_store", par), 0)):
                v.wait_ge(s_mm.h, mm)
                v.wait_ge(s_vg.h, rp)
                if qt == 0 and aofree > 0:
                    v.wait_ge(s_ao.h, aofree)
                v.tensor_tensor(out=ao[:, qt * 512:(qt + 1) * 512],
                                in0=pb[4].ap()[0:64, 0:512], in1=rdrep[:, :],
                                op=ALU.mult)
                v.drain().then_inc(s_vg.h, 1)
            E("vector", f)
            p.ma["av_ev"] = s_vg.inc(1)

        def f(g, ao=ao, hp=hp, hc=hc, avl=p.ma["av_ev"]):
            g.wait_ge(s_vg.h, avl)
            g.dma_start(out=aof[hp:hp + 64, hc, :], in_=ao[:, :]).then_inc(s_ao.h, 16)
        E("gpsimd", f)
        s_ao.inc(16)
        p.ma[("aoh_store", par)] = s_ao.n
        p.ma[("qk_used", par)] = s_mm.n
        p.ma["head_done"] = s_ao.n
        if h + 2 < 16:
            hln = emit_qk_dma(h + 2)
            emit_bounds(h + 2, hln)
    p.tape_barrier()

    stB.close()
    aot = [p.sbuf(f"la_aot{i}", [128, 8, 128], BF16) for i in range(2)]
    oott = [p.sbuf(f"la_oot{i}", [128, 1024], F32) for i in range(2)]
    ao_done = s_ao.n
    for tb in range(NTB):
        par = tb % 2

        def f(s_, tb=tb, par=par, need=p.ma.get("oo_mm", 0), ad=ao_done):
            s_.wait_ge(s_ao.h, ad)
            if need > 0:
                s_.wait_ge(s_mm.h, need)
            s_.dma_start(out=aot[par][:], in_=aof[:, :, tb * 128:(tb + 1) * 128]
                         ).then_inc(s_al[par].h, 16)
        E("sync", f)
        al = s_al[par].inc(16)

        for dt_ in range(2):
            bank = (2 * tb + dt_) % 2

            def f(t, par=par, dt_=dt_, al=al, bank=bank, evn=p.ma.get("oo_ev", 0) - 1):
                t.wait_ge(s_al[par].h, al)
                if evn > 0:
                    t.wait_ge(s_vg.h, evn)
                pa = pb[bank].ap()
                for c in range(8):
                    t.matmul(pa[:, 0:512], aot[par][:, c, :],
                             wo[:, c, dt_ * 512:(dt_ + 1) * 512],
                             start=(c == 0), stop=False)
                t.matmul(pa[:, 0:512], on1b[:], bos[:, dt_ * 512:(dt_ + 1) * 512],
                         start=False, stop=True).then_inc(s_mm.h, 1)
            E("tensor", f)
            mm = s_mm.inc(1)

            def f(v, par=par, dt_=dt_, mm=mm, bank=bank,
                  oprev=s_oo[par].n if dt_ == 0 else 0):
                v.wait_ge(s_mm.h, mm)
                if dt_ == 0 and oprev > 0:
                    v.wait_ge(s_oo[par].h, oprev)
                v.tensor_copy(out=oott[par][:, dt_ * 512:(dt_ + 1) * 512],
                              in_=pb[bank].ap()[:, 0:512]).then_inc(s_vg.h, 1)
            E("vector", f)
            p.ma["oo_ev"] = s_vg.inc(1)
        p.ma["oo_mm"] = s_mm.n

        def f(g, tb=tb, par=par, ev=p.ma["oo_ev"]):
            g.wait_ge(s_vg.h, ev)
            g.dma_start(out=brA1[tb * 128:(tb + 1) * 128, :],
                        in_=oott[par][:]).then_inc(s_oo[par].h, 16)
        E("gpsimd", f)
        s_oo[par].inc(16)

    o0, o1 = s_oo[0].n, s_oo[1].n

    def f(g):
        g.wait_ge(s_oo[0].h, o0)
        g.wait_ge(s_oo[1].h, o1)
    E("gpsimd", f)


def build_l1attn():
    ctx = contextlib.ExitStack()
    with ctx:
        nc = bass.Bass("TRN2", target_bir_lowering=False, debug=False)
        p = Prog(ctx, nc)
        p.ma = {"t34": 0}
        d = {}
        d["xn1f"] = nc.dram_tensor("xn1f", [128, 8, T], BF16, kind="ExternalInput").ap()
        for nm, shape in [("wqT", [1024, 1024]), ("wkT", [1024, 1024]),
                          ("wvT", [1024, 1024]), ("woT", [1024, 1024]),
                          ("bq", [1, 1024]), ("bk", [1, 1024]), ("bv", [1, 1024]),
                          ("bo", [1, 1024]), ("cosT", [128, 32, 32]),
                          ("sinT", [128, 32, 32]), ("ones1f", [1, 128]),
                          ("identd", [128, 128]), ("cmaskT", [128, 4, 512]),
                          ("colones", [128, 1])]:
            d[nm] = nc.dram_tensor(nm, shape, F32, kind="ExternalInput").ap()
        d["qf"] = nc.dram_tensor("qf", [128, 8, T], BF16, kind="Internal").ap()
        d["kf"] = nc.dram_tensor("kf", [128, 8, T], BF16, kind="Internal").ap()
        d["vf"] = nc.dram_tensor("vf", [128, 32, 1024], BF16, kind="Internal").ap()
        d["aof"] = nc.dram_tensor("aof", [128, 8, T], BF16, kind="Internal").ap()
        d["brA1"] = nc.dram_tensor("brA1", [T, D], F32, kind="ExternalOutput").ap()
        p.bar = p.sem("bar")
        p.tape_init()
        emit_l1attn(p, **d)
        with nc.Block() as block:
            p.block = block
            p.flush()
        return nc


def l1attn_host_layouts(inp):
    t = np.arange(T, dtype=np.float64)
    inv = 1.0 / (10000.0 ** (np.arange(0, 64, 2, dtype=np.float64) / 64))
    ang = np.outer(t, inv)
    cosT = np.ascontiguousarray(np.cos(ang).astype(np.float32).reshape(32, 128, 32).transpose(1, 0, 2))
    sinT = np.ascontiguousarray(np.sin(ang).astype(np.float32).reshape(32, 128, 32).transpose(1, 0, 2))
    cm = np.zeros((4, 128, 512), np.float32)
    for j in range(4):
        s = np.arange(128)[:, None] + 128 * j
        tt = np.arange(512)[None, :]
        cm[j] = (s <= tt).astype(np.float32)
    cmaskT = np.ascontiguousarray(cm.transpose(1, 0, 2))
    return dict(
        wqT=np.ascontiguousarray(inp["L1_attn_Wq"].T),
        wkT=np.ascontiguousarray(inp["L1_attn_Wk"].T),
        wvT=np.ascontiguousarray(inp["L1_attn_Wv"].T),
        woT=np.ascontiguousarray(inp["L1_attn_Wo"].T),
        bq=inp["L1_attn_bq"].reshape(1, -1).astype(np.float32),
        bk=inp["L1_attn_bk"].reshape(1, -1).astype(np.float32),
        bv=inp["L1_attn_bv"].reshape(1, -1).astype(np.float32),
        bo=inp["L1_attn_bo"].reshape(1, -1).astype(np.float32),
        cosT=cosT, sinT=sinT, cmaskT=cmaskT,
        ones1f=np.ones((1, 128), np.float32),
        colones=np.ones((128, 1), np.float32),
        identd=np.eye(128, dtype=np.float32),
    )



def _compose_bfly(p, mats_hbm, R_sb, pb, pbb, idb, s_w, W0, s_mm, s_vg, tag):
    E = p.emit
    NMB = 2
    mbuf = [p.sbuf(f"bf_m{tag}{i}", [128, 128], BF16) for i in range(NMB)]
    s_ml = [p.semc(f"bf_ml{i}") for i in range(NMB)]
    base = s_vg.n
    mm_used = {}
    gi = 0
    for s in range(6, -1, -1):
        for c in range(32):
            par = gi % NMB
            used = mm_used.get(par, 0)

            def f(q, c=c, s=s, par=par, used=used):
                if used > 0:
                    q.wait_ge(s_mm.h, used)
                q.dma_start(out=mbuf[par][:], in_=mats_hbm[s, c, :, :]
                             ).then_inc(s_ml[par].h, 16)
            E("gpsimd" if par == 0 else "sync", f)
            ml = s_ml[par].inc(16)

            def f(t, c=c, s=s, par=par, ml=ml, gi=gi, base=base):
                t.wait_ge(s_w.h, W0)
                t.wait_ge(s_ml[par].h, ml)
                if gi >= 2:
                    t.wait_ge(s_vg.h, base + gi - 1)
                if s < 6:
                    t.wait_ge(s_vg.h, base + (5 - s) * 32 + c + 1)
                rhs = idb[:] if s == 6 else R_sb[:, c, :]
                t.matmul(pb[gi % 2].ap()[:, 0:128], mbuf[par][:], rhs, start=True,
                         stop=True).then_inc(s_mm.h, 1)
            E("tensor", f)
            mm = s_mm.inc(1)
            mm_used[par] = mm

            def f(v, c=c, gi=gi, mm=mm):
                v.wait_ge(s_mm.h, mm)
                v.tensor_copy(out=R_sb[:, c, :],
                              in_=pb[gi % 2].ap()[:, 0:128]).then_inc(s_vg.h, 1)
            E("vector", f)
            s_vg.inc(1)
            gi += 1


def _emit_cross_stages(p, hb, tbuf, coefA, coefB, pairs, s_vg, ws, pre=None,
                       outw=512):
    E = p.emit
    s_sgx = p.semc("bf_sgx")
    for st in range(5):
        src = hb[st % 2]
        dst = hb[(st + 1) % 2]
        prev_vg = s_vg.n
        for grp in range(4):
            def fs(sc, st=st, src=src, grp=grp, prev_vg=prev_vg):
                sc.wait_ge(ws[0], ws[1])
                if grp == 0:
                    if st == 0 and pre is not None:
                        sc.wait_ge(pre[0], pre[1])
                    if prev_vg > 0:
                        sc.wait_ge(s_vg.h, prev_vg)
                for c in range(grp * 8, grp * 8 + 8):
                    lo, hi = pairs[st][c]
                    sc.activation(out=tbuf[:, c, :], in_=src[:, hi, :],
                                  func=AF.Copy, scale=coefB[:, st, c:c + 1],
                                  bias=0.0)
                sc.drain().then_inc(s_sgx.h, 1)
            E("scalar", fs)
            sgx = s_sgx.inc(1)

            def fv(v, st=st, src=src, dst=dst, grp=grp, sgx=sgx):
                v.wait_ge(s_sgx.h, sgx)
                if grp == 0:
                    if st == 0 and pre is not None:
                        v.wait_ge(pre[0], pre[1])
                    v.wait_ge(ws[0], ws[1])
                for c in range(grp * 8, grp * 8 + 8):
                    lo, hi = pairs[st][c]
                    v.scalar_tensor_tensor(out=dst[:, c, :], in0=src[:, lo, :],
                                           scalar=coefA[:, st, c:c + 1],
                                           in1=tbuf[:, c, :], op0=ALU.mult,
                                           op1=ALU.add)
                v.drain().then_inc(s_vg.h, 1)
            E("vector", fv)
            s_vg.inc(1)
    return (5 % 2)


def emit_l1f1(p, xn1f, weT, be, upM, upA, upB, ubias, identd, ones1f, hgf):
    nc = p.nc
    E = p.emit
    we = p.sbuf("f1_we", [128, 8, 4096], BF16)
    bes = p.sbuf("f1_be", [128, 32], F32)
    idb = p.sbuf("f1_idb", [128, 128], BF16)
    on1b = p.sbuf("f1_on1b", [1, 128], BF16)
    Rup = p.sbuf("f1_rup", [128, 32, 128], BF16)
    cA = p.sbuf("f1_ca", [128, 5, 32], F32)
    cB = p.sbuf("f1_cb", [128, 5, 32], F32)
    ub = p.sbuf("f1_ub", [128, 32], F32)
    xt = [p.sbuf(f"f1_xt{i}", [128, 8, 512], BF16) for i in range(2)]
    ha = p.sbuf("f1_ha", [128, 32, 512], BF16)
    hb2 = p.sbuf("f1_hb", [128, 32, 512], BF16)
    tbuf = p.sbuf("f1_tb", [128, 32, 512], BF16)
    hg = hb2

    pb = [p.psum_bank(f"f1_p{i}") for i in range(4)]
    pbb = [b.ap().bitcast(BF16) for b in pb]

    s_w = SemC(p.sem("f1_w"))
    s_ws = SemC(p.sem("f1_ws"))
    s_x = [SemC(p.sem(f"f1_x{i}")) for i in range(2)]
    s_mm = SemC(p.sem("f1_mm"))
    s_ev = SemC(p.sem("f1_ev"))
    s_vg = SemC(p.sem("f1_vg"))
    s_sg = SemC(p.sem("f1_sg"))
    s_hs = [SemC(p.sem(f"f1_hs{i}")) for i in range(2)]

    def f(g):
        g.dma_start(out=we[:], in_=weT.rearrange("(c p) z -> p c z", p=128)).then_inc(s_w.h, 16)
        g.dma_start(out=bes[:], in_=be[:, :]).then_inc(s_w.h, 16)
        g.dma_start(out=idb[:], in_=identd[:, :]).then_inc(s_w.h, 16)
        g.dma_start(out=on1b[:], in_=ones1f[:, :]).then_inc(s_w.h, 16)
    E("gpsimd", f)
    s_w.inc(64)
    W0 = s_w.n

    def f(s_):
        s_.dma_start(out=cA[:], in_=upA[:, :, :]).then_inc(s_ws.h, 16)
        s_.dma_start(out=cB[:], in_=upB[:, :, :]).then_inc(s_ws.h, 16)
        s_.dma_start(out=ub[:], in_=ubias[:, :]).then_inc(s_ws.h, 16)
    E("sync", f)
    s_ws.inc(48)
    WS0 = s_ws.n

    _compose_bfly(p, upM, Rup, pb, pbb, idb, s_w, W0, s_mm, s_vg, "u")
    comp_vg = s_vg.n
    p.tape_barrier()

    pairs = p.bfly_pairs
    s_wv = SemC(p.sem("f1_wv"))

    for tt in range(8):
        par = tt % 2

        def f(s_, tt=tt, par=par, need=p.ma.get("f1_xt", {}).get(par, 0)):
            s_.wait_ge(s_w.h, W0)
            if need > 0:
                s_.wait_ge(s_mm.h, need)
            s_.dma_start(out=xt[par][:], in_=xn1f[:, :, tt * 512:(tt + 1) * 512]
                         ).then_inc(s_x[par].h, 16)
        E("sync", f)
        xl = s_x[par].inc(16)

        prev_gelu = p.ma.get("f1_gelu", 0)
        for c in range(32):
            bank = c % 2

            def f(t, c=c, par=par, bank=bank, xl=xl, evn=s_ev.n - 1):
                t.wait_ge(s_x[par].h, xl)
                if evn > 0:
                    t.wait_ge(s_ev.h, evn)
                pa = pb[bank].ap()
                for d in range(8):
                    t.matmul(pa[:, 0:512], we[:, d, c * 128:(c + 1) * 128],
                             xt[par][:, d, :], start=(d == 0),
                             stop=(d == 7)).then_inc(s_mm.h, 1) if d == 7 else                         t.matmul(pa[:, 0:512], we[:, d, c * 128:(c + 1) * 128],
                                 xt[par][:, d, :], start=(d == 0), stop=False)
            E("tensor", f)
            mm = s_mm.inc(1)

            def f(v, c=c, bank=bank, mm=mm, pg=prev_gelu):
                v.wait_ge(s_mm.h, mm)
                v.wait_ge(s_ws.h, WS0)
                if pg > 0 and c == 0:
                    v.wait_ge(s_sg.h, pg)
                v.tensor_scalar(out=ha[:, c, :], in0=pb[bank].ap()[:, 0:512],
                                scalar1=bes[:, c:c + 1], scalar2=None,
                                op0=ALU.add).then_inc(s_ev.h, 1)
            E("vector", f)
            s_ev.inc(1)
        p.ma.setdefault("f1_xt", {})[par] = s_mm.n
        h0ev = s_ev.n

        for c in range(32):
            bank = 2 + (c % 2)

            def f(t, c=c, bank=bank, ev=h0ev, wvn=s_wv.n - 1):
                t.wait_ge(s_ev.h, ev)
                if wvn > 0:
                    t.wait_ge(s_wv.h, wvn)
                t.matmul(pb[bank].ap()[:, 0:512], Rup[:, c, :], ha[:, c, :],
                         start=True, stop=True).then_inc(s_mm.h, 1)
            E("tensor", f)
            mm = s_mm.inc(1)

            def f(v, c=c, bank=bank, mm=mm, ls=p.ma.get("f1_hs_last", None)):
                v.wait_ge(s_mm.h, mm)
                if c == 0 and ls is not None:
                    v.wait_ge(ls[0], ls[1])
                v.tensor_copy(out=hb2[:, c, :],
                              in_=pb[bank].ap()[:, 0:512]).then_inc(s_wv.h, 1)
            E("vector", f)
            s_wv.inc(1)

        _emit_cross_stages(p, [hb2, ha], tbuf, cA, cB, pairs, s_vg, (s_ws.h, WS0),
                           pre=(s_wv.h, s_wv.n))
        cross_vg = s_vg.n

        def f(sc, tt=tt, par=par, cv=cross_vg, hs=p.ma.get(("f1_hs", par), 0)):
            sc.wait_ge(s_vg.h, cv)
            if hs > 0:
                sc.wait_ge(s_hs[par].h, hs)
            for c in range(32):
                sc.activation(out=hg[:, c, :], in_=ha[:, c, :],
                              func=AF.Gelu, bias=ub[:, c:c + 1])
            sc.drain().then_inc(s_sg.h, 1)
        E("scalar", f)
        p.ma["f1_gelu"] = s_sg.inc(1)

        def f(g, tt=tt, par=par, sg=p.ma["f1_gelu"]):
            g.wait_ge(s_sg.h, sg)
            g.dma_start(out=hgf[:, :, tt * 512:(tt + 1) * 512],
                        in_=hg[:]).then_inc(s_hs[par].h, 16)
        E("gpsimd", f)
        p.ma[("f1_hs", par)] = s_hs[par].inc(16)
        p.ma["f1_hs_last"] = (s_hs[par].h, s_hs[par].n)

    h0_, h1_ = s_hs[0].n, s_hs[1].n

    def f(g):
        g.wait_ge(s_hs[0].h, h0_)
        g.wait_ge(s_hs[1].h, h1_)
    E("gpsimd", f)


def emit_l1f2(p, hgf, dnM, dnA, dnB, wcT, bcon, dbias, x1h, brA1h, identd,
              ones1f, out_h):
    nc = p.nc
    E = p.emit
    wc = p.sbuf("f2_wc", [128, 32, 1024], BF16)
    bcs = p.sbuf("f2_bc", [1, 1024], BF16)
    dbs = p.sbuf("f2_db", [1, 1024], BF16)
    idb = p.sbuf("f2_idb", [128, 128], BF16)
    on1b = p.sbuf("f2_on1b", [1, 128], BF16)
    Rdn = p.sbuf("f2_rdn", [128, 32, 128], BF16)
    cA = p.sbuf("f2_ca", [128, 5, 32], F32)
    cB = p.sbuf("f2_cb", [128, 5, 32], F32)
    ha = p.sbuf("f2_ha", [128, 32, 512], BF16)
    hb2 = p.sbuf("f2_hb", [128, 32, 512], BF16)
    tbuf = p.sbuf("f2_tb", [128, 32, 512], BF16)
    x1t = [p.sbuf(f"f2_x1t{i}", [128, 1024], F32) for i in range(2)]
    bat = [p.sbuf(f"f2_bat{i}", [128, 1024], F32) for i in range(2)]
    ot = [p.sbuf(f"f2_ot{i}", [128, 1024], F32) for i in range(2)]
    osc = p.sbuf("f2_osc", [128, 512], F32)

    pb = [p.psum_bank(f"f2_p{i}") for i in range(4)]
    pbb = [b.ap().bitcast(BF16) for b in pb]

    s_w = SemC(p.sem("f2_w"))
    s_ws = SemC(p.sem("f2_ws"))
    s_h = [SemC(p.sem(f"f2_h{i}")) for i in range(2)]
    s_xb = [SemC(p.sem(f"f2_xb{i}")) for i in range(2)]
    s_mm = SemC(p.sem("f2_mm"))
    s_wv = SemC(p.sem("f2_wv"))
    s_vg = SemC(p.sem("f2_vg"))
    s_oo = [SemC(p.sem(f"f2_oo{i}")) for i in range(2)]

    def f(g):
        g.dma_start(out=wc[:], in_=wcT.rearrange("(c p) z -> p c z", p=128)).then_inc(s_w.h, 16)
        g.dma_start(out=bcs[:], in_=bcon[:, :]).then_inc(s_w.h, 16)
        g.dma_start(out=dbs[:], in_=dbias[:, :]).then_inc(s_w.h, 16)
        g.dma_start(out=idb[:], in_=identd[:, :]).then_inc(s_w.h, 16)
        g.dma_start(out=on1b[:], in_=ones1f[:, :]).then_inc(s_w.h, 16)
    E("gpsimd", f)
    s_w.inc(80)
    W0 = s_w.n

    def f(s_):
        s_.dma_start(out=cA[:], in_=dnA[:, :, :]).then_inc(s_ws.h, 16)
        s_.dma_start(out=cB[:], in_=dnB[:, :, :]).then_inc(s_ws.h, 16)
    E("sync", f)
    s_ws.inc(32)
    WS0 = s_ws.n

    _compose_bfly(p, dnM, Rdn, pb, pbb, idb, s_w, W0, s_mm, s_vg, "d")
    p.tape_barrier()

    pairs = p.bfly_pairs_dn

    for tt in range(8):
        par = tt % 2

        def f(s_, tt=tt, par=par, need=p.ma.get("f2_omm", 0),
              cvp=p.ma.get("f2_crossv", 0)):
            s_.wait_ge(s_w.h, W0)
            if need > 0:
                s_.wait_ge(s_mm.h, need)
            if cvp > 0:
                s_.wait_ge(s_vg.h, cvp)
            s_.dma_start(out=ha[:], in_=hgf[:, :, tt * 512:(tt + 1) * 512]
                         ).then_inc(s_h[par].h, 16)
        E("sync", f)
        hl = s_h[par].inc(16)

        prev_out = p.ma.get("f2_omm", 0)
        for c in range(32):
            bank = c % 2

            def f(t, c=c, par=par, bank=bank, hl=hl, wvn=s_wv.n - 1):
                t.wait_ge(s_h[par].h, hl)
                if wvn > 0:
                    t.wait_ge(s_wv.h, wvn)
                t.matmul(pb[bank].ap()[:, 0:512], Rdn[:, c, :], ha[:, c, :],
                         start=True, stop=True).then_inc(s_mm.h, 1)
            E("tensor", f)
            mm = s_mm.inc(1)

            def f(v, c=c, bank=bank, mm=mm, po=prev_out):
                v.wait_ge(s_mm.h, mm)
                if c == 0 and po > 0:
                    v.wait_ge(s_mm.h, po)
                v.tensor_copy(out=hb2[:, c, :],
                              in_=pb[bank].ap()[:, 0:512]).then_inc(s_wv.h, 1)
            E("vector", f)
            s_wv.inc(1)

        _emit_cross_stages(p, [hb2, ha], tbuf, cA, cB, pairs, s_vg, (s_ws.h, WS0),
                           pre=(s_wv.h, s_wv.n))
        cross_vg = s_vg.n
        p.ma["f2_crossv"] = cross_vg

        for tb in range(4):
            tba = tt * 4 + tb
            bpar = tba % 2

            def f(s_, tba=tba, bpar=bpar, need=p.ma.get("f2_cons", 0)):
                if need > 0:
                    s_.wait_ge(s_vg.h, need)
                s_.dma_start(out=x1t[bpar][:], in_=x1h[tba * 128:(tba + 1) * 128, :]
                             ).then_inc(s_xb[bpar].h, 16)
                s_.dma_start(out=bat[bpar][:], in_=brA1h[tba * 128:(tba + 1) * 128, :]
                             ).then_inc(s_xb[bpar].h, 16)
            E("sync", f)
            xbl = s_xb[bpar].inc(32)

            for dt_ in range(2):
                bank = 2 + ((2 * tba + dt_) % 2)

                def f(t, tb=tb, dt_=dt_, bank=bank, cv=cross_vg,
                      evn=p.ma.get("f2_oev", 0) - 1):
                    t.wait_ge(s_vg.h, cv)
                    if evn > 0:
                        t.wait_ge(s_vg.h, evn)
                    pa = pb[bank].ap()
                    for c in range(32):
                        t.matmul(pa[:, 0:512], ha[:, c, tb * 128:(tb + 1) * 128],
                                 wc[:, c, dt_ * 512:(dt_ + 1) * 512],
                                 start=(c == 0), stop=False)
                    t.matmul(pa[:, 0:512], on1b[:], bcs[:, dt_ * 512:(dt_ + 1) * 512],
                             start=False, stop=False)
                    t.matmul(pa[:, 0:512], on1b[:], dbs[:, dt_ * 512:(dt_ + 1) * 512],
                             start=False, stop=True).then_inc(s_mm.h, 1)
                E("tensor", f)
                mm = s_mm.inc(1)

                def f(v, dt_=dt_, bpar=bpar, bank=bank, mm=mm, xbl=xbl,
                      oprev=s_oo[bpar].n if dt_ == 0 else 0):
                    v.wait_ge(s_mm.h, mm)
                    v.wait_ge(s_xb[bpar].h, xbl)
                    if dt_ == 0 and oprev > 0:
                        v.wait_ge(s_oo[bpar].h, oprev)
                    sl = slice(dt_ * 512, (dt_ + 1) * 512)
                    v.scalar_tensor_tensor(out=osc[:, :], in0=pb[bank].ap()[:, 0:512],
                                           scalar=1.0, in1=x1t[bpar][:, sl],
                                           op0=ALU.mult, op1=ALU.add)
                    v.drain()
                    v.tensor_tensor(out=ot[bpar][:, sl], in0=osc[:, :],
                                    in1=bat[bpar][:, sl], op=ALU.add)
                    v.drain().then_inc(s_vg.h, 1)
                E("vector", f)
                p.ma["f2_oev"] = s_vg.inc(1)
            p.ma["f2_omm"] = s_mm.n
            p.ma["f2_cons"] = p.ma["f2_oev"]

            def f(g, tba=tba, bpar=bpar, ev=p.ma["f2_oev"]):
                g.wait_ge(s_vg.h, ev)
                g.dma_start(out=out_h[tba * 128:(tba + 1) * 128, :],
                            in_=ot[bpar][:]).then_inc(s_oo[bpar].h, 16)
            E("gpsimd", f)
            s_oo[bpar].inc(16)

    o0, o1 = s_oo[0].n, s_oo[1].n

    def f(g):
        g.wait_ge(s_oo[0].h, o0)
        g.wait_ge(s_oo[1].h, o1)
    E("gpsimd", f)


def bfly_host_layouts(up_w, dn_w):
    res = {}
    pairs = None
    for name, w in (("up", up_w), ("dn", dn_w)):
        M = np.zeros((7, 32, 128, 128), np.float32)
        A = np.zeros((5, 32, 128), np.float32)
        B = np.zeros((5, 32, 128), np.float32)
        prs = [[None] * 32 for _ in range(5)]
        for stage in range(12):
            s_ = 2 ** stage
            nb = 4096 // (2 * s_)
            idx = np.arange(4096).reshape(nb, 2, s_)
            ww = w[stage].reshape(nb, s_, 2, 2)
            a_idx = idx[:, 0, :]
            b_idx = idx[:, 1, :]
            if stage <= 6:
                for kb in range(nb):
                    for si in range(s_):
                        a = int(a_idx[kb, si]); b = int(b_idx[kb, si])
                        c = a // 128
                        M[stage, c, a % 128, a % 128] = ww[kb, si, 0, 0]
                        M[stage, c, a % 128, b % 128] = ww[kb, si, 0, 1]
                        M[stage, c, b % 128, a % 128] = ww[kb, si, 1, 0]
                        M[stage, c, b % 128, b % 128] = ww[kb, si, 1, 1]
            else:
                st = stage - 7
                aa = a_idx.ravel(); bb = b_idx.ravel()
                wf = ww.reshape(-1, 2, 2)
                ca = aa // 128; pa = aa % 128
                cb_ = bb // 128
                for i in range(len(aa)):
                    A[st, ca[i], pa[i]] = wf[i, 0, 0]
                    B[st, ca[i], pa[i]] = wf[i, 0, 1]
                    A[st, cb_[i], pa[i]] = wf[i, 1, 0]
                    B[st, cb_[i], pa[i]] = wf[i, 1, 1]
                    prs[st][ca[i]] = (int(ca[i]), int(cb_[i]))
                    prs[st][cb_[i]] = (int(ca[i]), int(cb_[i]))
        res[name + "M"] = M
        res[name + "A"] = np.ascontiguousarray(A.transpose(2, 0, 1))
        res[name + "B"] = np.ascontiguousarray(B.transpose(2, 0, 1))
        pairs = prs
    res["pairs"] = pairs
    return res


def build_l1f1(pairs):
    ctx = contextlib.ExitStack()
    with ctx:
        nc = bass.Bass("TRN2", target_bir_lowering=False, debug=False)
        p = Prog(ctx, nc)
        p.ma = {"t34": 0}
        p.bfly_pairs = pairs
        d = {}
        d["xn1f"] = nc.dram_tensor("xn1f", [128, 8, T], BF16, kind="ExternalInput").ap()
        for nm, shape in [("weT", [1024, 4096]), ("be", [128, 32]),
                          ("upM", [7, 32, 128, 128]), ("upA", [128, 5, 32]),
                          ("upB", [128, 5, 32]), ("ubias", [128, 32]),
                          ("identd", [128, 128]), ("ones1f", [1, 128])]:
            d[nm] = nc.dram_tensor(nm, shape, F32, kind="ExternalInput").ap()
        d["hgf"] = nc.dram_tensor("hgf", [128, 32, T], BF16, kind="ExternalOutput").ap()
        p.bar = p.sem("bar")
        p.tape_init()
        emit_l1f1(p, **d)
        with nc.Block() as block:
            p.block = block
            p.flush()
        return nc


def build_l1f2(pairs):
    ctx = contextlib.ExitStack()
    with ctx:
        nc = bass.Bass("TRN2", target_bir_lowering=False, debug=False)
        p = Prog(ctx, nc)
        p.ma = {"t34": 0}
        p.bfly_pairs_dn = pairs
        d = {}
        d["hgf"] = nc.dram_tensor("hgf", [128, 32, T], BF16, kind="ExternalInput").ap()
        for nm, shape in [("dnM", [7, 32, 128, 128]), ("dnA", [128, 5, 32]),
                          ("dnB", [128, 5, 32]), ("wcT", [4096, 1024]),
                          ("bcon", [1, 1024]), ("dbias", [1, 1024]),
                          ("x1h", [T, D]), ("brA1h", [T, D]),
                          ("identd", [128, 128]), ("ones1f", [1, 128])]:
            d[nm] = nc.dram_tensor(nm, shape, F32, kind="ExternalInput").ap()
        d["out_h"] = nc.dram_tensor("out", [T, D], F32, kind="ExternalOutput").ap()
        p.bar = p.sem("bar")
        p.tape_init()
        emit_l1f2(p, **d)
        with nc.Block() as block:
            p.block = block
            p.flush()
        return nc


def l1ffn_host_layouts(inp):
    bf = bfly_host_layouts(inp["L1_ffn_up_weights"], inp["L1_ffn_down_weights"])
    return dict(
        weT=np.ascontiguousarray(inp["L1_ffn_W_exp"]),
        be=np.ascontiguousarray(inp["L1_ffn_b_exp"].reshape(32, 128).T),
        upM=bf["upM"], upA=bf["upA"], upB=bf["upB"],
        dnM=bf["dnM"], dnA=bf["dnA"], dnB=bf["dnB"],
        ubias=np.ascontiguousarray(inp["L1_ffn_up_bias"].reshape(32, 128).T),
        wcT=np.ascontiguousarray(inp["L1_ffn_W_con"]),
        bcon=inp["L1_ffn_b_con"].reshape(1, -1).astype(np.float32),
        dbias=inp["L1_ffn_down_bias"].reshape(1, -1).astype(np.float32),
        identd=np.eye(128, dtype=np.float32),
        ones1f=np.ones((1, 128), np.float32),
        pairs=bf["pairs"],
    )


_BUILD_CACHE = {}


def _cached(key, fn):
    if key not in _BUILD_CACHE:
        _BUILD_CACHE[key] = fn()
    return _BUILD_CACHE[key]


def emit_pre_tape(p, x, xnf, xn2f, lnw_rep, lnb_rep, ident_hbm, mu_hbm,
                  rstd_hbm, affine=True):
    E = p.emit
    NQ = 2 if affine else 1
    NT = 8 * NQ
    tag = "pa" if affine else "pb"

    xb = [p.sbuf(f"{tag}_xb{i}", [128, D], F32) for i in range(2)]
    xc = p.sbuf(f"{tag}_xc", [128, D], F32)
    sq = p.sbuf(f"{tag}_sq", [128, D], F32)
    xn = p.sbuf(f"{tag}_xn", [128, D], F32)
    xnbf = p.sbuf(f"{tag}_xnbf", [128, D], BF16)
    xn2bf = p.sbuf(f"{tag}_xn2bf", [128, D], BF16)
    xc2 = p.sbuf(f"{tag}_xc2", [128, D], F32)
    st1 = p.sbuf(f"{tag}_st1", [128, 8], F32)
    epsc = p.sbuf(f"{tag}_eps", [128, 1], F32)
    mu_sb = p.sbuf(f"{tag}_mu", [128, NTB], F32)
    rstd_sb = p.sbuf(f"{tag}_rstd", [128, NTB], F32)
    lnw = p.sbuf(f"{tag}_lnw", [128, D], F32)
    lnb = p.sbuf(f"{tag}_lnb", [128, D], F32)
    ident = p.sbuf(f"{tag}_id", [128, 128], BF16)
    stg = p.sbuf(f"{tag}_stg", [128, 8, 128], BF16)
    stg2 = p.sbuf(f"{tag}_stg2", [128, 8, 128], BF16)
    ptb = [p.psum_bank(f"{tag}_pt{i}") for i in range(2)]
    pt = [b.ap().bitcast(BF16)[:, 0:128] for b in ptb]

    s_xld = [p.semc("pre_xld0"), p.semc("pre_xld1")]
    s_cld = p.semc("pre_cld")
    s_cldg = p.semc("pre_cldg")
    s_v2s = p.semc("pre_v2s")
    s_s2v = p.semc("pre_s2v")
    s_cast = p.semc("pre_cast")
    s_tr = p.semc("pre_tr")
    s_ev = p.semc("pre_ev")
    s_out = p.semc("pre_out")
    s_xcons = p.semc("pre_xcons")

    def f(g):
        g.dma_start(out=ident[:], in_=ident_hbm[:, :]).then_inc(s_cldg.h, 16)
    E("gpsimd", f)
    CLDG = s_cldg.inc(16)

    if affine:
        def f(s_):
            s_.dma_start(out=lnw[:], in_=lnw_rep[:, :]).then_inc(s_cld.h, 16)
            s_.dma_start(out=lnb[:], in_=lnb_rep[:, :]).then_inc(s_cld.h, 16)
        E("sync", f)
        s_cld.inc(32)
    CLD = s_cld.n

    xld_t = {}
    for tb in range(NTB):
        par = tb % 2
        need = s_xcons.n - 1

        def f(s_, tb=tb, par=par, need=need):
            if need > 0:
                s_.wait_ge(s_xcons.h, need)
            s_.dma_start(out=xb[par][:], in_=x[tb * 128:(tb + 1) * 128, :]
                         ).then_inc(s_xld[par].h, 16)
        E("sync", f)
        xld_t[tb] = s_xld[par].inc(16)

        v2s_1 = s_v2s.n + 1
        s2v_1 = s_s2v.n + 1
        cast_1 = s_cast.n + 1
        out_need = s_out.n

        def f(v, tb=tb, par=par, xl=xld_t[tb], s2v_1=s2v_1, on=out_need,
              tr0=s_tr.n):
            if tb == 0:
                v.memset(epsc[:], EPS)
                if affine:
                    v.wait_ge(s_cld.h, CLD)
            b = xb[par]
            v.wait_ge(s_xld[par].h, xl)
            v.drain()
            v.reduce_sum(out=st1[:, 0:1], in_=b[:], axis=AX.X)
            v.drain()
            v.tensor_scalar(out=st1[:, 0:1], in0=st1[:, 0:1], scalar1=-1.0 / D,
                            scalar2=None, op0=ALU.mult)
            v.drain()
            v.tensor_scalar(out=xc[:], in0=b[:], scalar1=st1[:, 0:1], scalar2=None,
                            op0=ALU.add).then_inc(s_v2s.h, 1)
            v.tensor_copy(out=mu_sb[:, tb:tb + 1], in_=st1[:, 0:1])
            v.drain().then_inc(s_xcons.h, 1)
            v.wait_ge(s_s2v.h, s2v_1)
            v.reciprocal(out=st1[:, 3:4], in_=st1[:, 2:3])
            v.drain()
            v.tensor_copy(out=rstd_sb[:, tb:tb + 1], in_=st1[:, 3:4])
            v.tensor_scalar(out=xn[:], in0=xc[:], scalar1=st1[:, 3:4], scalar2=None,
                            op0=ALU.mult)
            v.drain()
            v.tensor_copy(out=xnbf[:], in_=xn[:]).then_inc(s_cast.h, 1)
            if affine:
                v.reduce_sum(out=st1[:, 4:5], in_=xn[:], axis=AX.X)
                v.drain()
                v.tensor_scalar(out=st1[:, 4:5], in0=st1[:, 4:5], scalar1=-1.0 / D,
                                scalar2=None, op0=ALU.mult)
                v.drain()
                v.tensor_scalar(out=xc2[:], in0=xn[:], scalar1=st1[:, 4:5],
                                scalar2=None, op0=ALU.add).then_inc(s_v2s.h, 1)
                v.wait_ge(s_s2v.h, s2v_1 + 1)
                v.reciprocal(out=st1[:, 7:8], in_=st1[:, 6:7])
                v.drain()
                v.tensor_scalar(out=xc2[:], in0=xc2[:], scalar1=st1[:, 7:8],
                                scalar2=None, op0=ALU.mult)
                v.drain()
                v.tensor_tensor(out=xc2[:], in0=xc2[:], in1=lnw[:], op=ALU.mult)
                v.drain()
                v.tensor_tensor(out=xn2bf[:], in0=xc2[:], in1=lnb[:],
                                op=ALU.add).then_inc(s_cast.h, 1)
            if on > 0:
                v.wait_ge(s_out.h, on)
            for c in range(NT):
                v.wait_ge(s_tr.h, tr0 + c + 1)
                psrc = pt[c % 2]
                dst = stg if c < 8 else stg2
                cc = c if c < 8 else c - 8
                v.tensor_copy(out=dst[:, cc, :], in_=psrc[:]).then_inc(s_ev.h, 1)
        E("vector", f)
        s_v2s.inc(NQ)
        s_cast.inc(NQ)
        s_xcons.inc(1)

        def f(sc, v2s_1=v2s_1):
            sc.wait_ge(s_v2s.h, v2s_1)
            sc.activation(out=sq[:], in_=xc[:], func=AF.Square, accum_out=st1[:, 1:2])
            sc.drain()
            sc.activation(out=st1[:, 2:3], in_=st1[:, 1:2], func=AF.Sqrt,
                          scale=1.0 / D, bias=epsc[:, 0:1]).then_inc(s_s2v.h, 1)
            if affine:
                sc.wait_ge(s_v2s.h, v2s_1 + 1)
                sc.activation(out=sq[:], in_=xc2[:], func=AF.Square,
                              accum_out=st1[:, 5:6])
                sc.drain()
                sc.activation(out=st1[:, 6:7], in_=st1[:, 5:6], func=AF.Sqrt,
                              scale=1.0 / D, bias=epsc[:, 0:1]).then_inc(s_s2v.h, 1)
        E("scalar", f)
        s_s2v.inc(NQ)

        def f(t, tb=tb, cast_1=cast_1, ev0=s_ev.n, tr0=s_tr.n, cldg=CLDG):
            if tb == 0:
                t.wait_ge(s_cldg.h, cldg)
            for c in range(NT):
                t.wait_ge(s_cast.h, cast_1 + (0 if c < 8 else 1))
                if tb > 0 or c >= 2:
                    t.wait_ge(s_ev.h, ev0 + c - NT - 1 + NT)
                t.transpose(pt[c % 2][:], (xnbf if c < 8 else xn2bf)
                            [:, (c % 8) * 128:((c % 8) + 1) * 128],
                            ident[:]).then_inc(s_tr.h, 1)
        E("tensor", f)
        s_tr.inc(NT)
        ev_mid = s_ev.n + 8
        s_ev.inc(NT)
        ev_all = s_ev.n

        def f(g, tb=tb, em=ev_mid, ea=ev_all):
            g.wait_ge(s_ev.h, em)
            g.dma_start(out=xnf[:, :, tb * 128:(tb + 1) * 128],
                        in_=stg[:, :, :]).then_inc(s_out.h, 16)
            if affine:
                g.wait_ge(s_ev.h, ea)
                g.dma_start(out=xn2f[:, :, tb * 128:(tb + 1) * 128],
                            in_=stg2[:, :, :]).then_inc(s_out.h, 16)
        E("gpsimd", f)
        s_out.inc(16 * NQ)

    def f(g):
        g.dma_start(out=mu_hbm[:, :], in_=mu_sb[:]).then_inc(s_out.h, 16)
        g.dma_start(out=rstd_hbm[:, :], in_=rstd_sb[:]).then_inc(s_out.h, 16)
    E("gpsimd", f)
    s_out.inc(32)
    tot = s_out.n

    def f(g):
        g.wait_ge(s_out.h, tot)
    E("gpsimd", f)
    return mu_sb, rstd_sb


def build_fused(pairs):
    ctx = contextlib.ExitStack()
    with ctx:
        nc = bass.Bass("TRN2", target_bir_lowering=False, debug=False)
        p = Prog(ctx, nc)
        p.ma = {"t34": 0}
        p.bfly_pairs = pairs
        p.bfly_pairs_dn = pairs

        def EI(nm, shape, dt=F32):
            return nc.dram_tensor(nm, shape, dt, kind="ExternalInput").ap()

        def IN(nm, shape, dt):
            return nc.dram_tensor(nm, shape, dt, kind="Internal").ap()

        x = EI("x", [T, D])
        lnw = EI("lnw_rep", [128, D]); lnb = EI("lnb_rep", [128, D])
        ident = EI("ident", [128, 128])
        identb = EI("identb", [128, 128], BF16)
        ma = {nm: EI("m_" + nm, shape, dt) for nm, shape, dt in [
            ("wzT", [1024, 2048], BF16), ("wxT", [1024, 2304], BF16),
            ("wdT", [1024, 32], BF16),
            ("woT", [2048, 1024], BF16), ("convw", [128, 18, 4], F32),
            ("convb", [128, 18], F32),
            ("dtb4", [128, CPS * 32], F32), ("alog", [1, 32], F32),
            ("dvec", [1, 32], F32),
            ("normw", [128, 2048], BF16), ("triu_i", [128, 128], F32),
            ("tril_s", [128, 128], F32),
            ("ones128", [128, 128], F32), ("ones1f", [1, 128], F32)]}
        mo = {nm: EI("q_" + nm, shape, dt) for nm, shape, dt in [
            ("rwT", [1024, 8], BF16), ("rb", [1, 8], BF16),
            ("w1sT", [9, 4, 128, 8, 1024], BF16),
            ("w2sT", [9, 4, 128, 8, 1024], BF16), ("b1s", [9, 128, 32], F32),
            ("b2s", [9, 1, 1024], BF16)]}
        la = {nm: EI("a_" + nm, shape, dt) for nm, shape, dt in [
            ("wqT", [1024, 1024], BF16), ("wkT", [1024, 1024], BF16),
            ("wvT", [1024, 1024], BF16),
            ("woT", [1024, 1024], BF16), ("bq", [1, 1024], BF16),
            ("bk", [1, 1024], BF16),
            ("bv", [1, 1024], BF16), ("bo", [1, 1024], BF16),
            ("cosT", [128, 32, 32], F32),
            ("sinT", [128, 32, 32], F32), ("cmaskT", [128, 4, 512], BF16),
            ("colones", [128, 1], BF16)]}
        ff = {nm: EI("f_" + nm, shape, dt) for nm, shape, dt in [
            ("weT", [1024, 4096], BF16), ("be", [128, 32], F32),
            ("upM", [7, 32, 128, 128], BF16),
            ("upA", [128, 5, 32], F32), ("upB", [128, 5, 32], F32),
            ("ubias", [128, 32], F32),
            ("dnM", [7, 32, 128, 128], BF16), ("dnA", [128, 5, 32], F32),
            ("dnB", [128, 5, 32], F32),
            ("wcT", [4096, 1024], BF16), ("bcon", [1, 1024], BF16),
            ("dbias", [1, 1024], BF16)]}

        xnf = IN("i_xnf", [128, 8, T], BF16)
        xn2f = IN("i_xn2f", [128, 8, T], BF16)
        xn1f = IN("i_xn1f", [128, 8, T], BF16)
        mu1 = IN("i_mu1", [128, NTB], F32)
        rstd1 = IN("i_rstd1", [128, NTB], F32)
        mu2 = IN("i_mu2", [128, NTB], F32)
        rstd2 = IN("i_rstd2", [128, NTB], F32)
        brA = IN("i_brA", [T, D], F32)
        x1 = IN("i_x1", [T, D], F32)
        brA1 = IN("i_brA1", [T, D], F32)
        qf = IN("i_qf", [128, 8, T], BF16)
        kf = IN("i_kf", [128, 8, T], BF16)
        vf = IN("i_vf", [128, 32, 1024], BF16)
        aof = IN("i_aof", [128, 8, T], BF16)
        hgf = IN("i_hgf", [128, 32, T], BF16)
        ltdram = IN("i_lt", [32, 2, 32, 128], F32)
        out_h = nc.dram_tensor("out", [T, D], F32, kind="ExternalOutput").ap()

        p.bar = p.sem("bar")
        p.tape_init()

        p.push_phase()
        emit_pre_tape(p, x, xnf, xn2f, lnw, lnb, identb, mu1, rstd1, affine=True)
        p.tape_barrier()
        p.pop_phase()

        p.push_phase()
        emit_mamba(p, xn2f=xn2f, identf=ident, identd=identb, ltdram=ltdram,
                   brA_out=brA, **ma)
        p.tape_barrier()
        p.pop_phase()

        p.push_phase()
        emit_moe(p, xnf, x, brA, mu1, rstd1, mo["rwT"], mo["rb"], mo["w1sT"],
                 mo["w2sT"], mo["b1s"], mo["b2s"], ma["ones1f"], x1, n_exp=9)
        p.tape_barrier()
        p.pop_phase()

        p.push_phase()
        emit_pre_tape(p, x1, xn1f, None, lnw, lnb, identb, mu2, rstd2, affine=False)
        p.tape_barrier()
        p.pop_phase()

        p.push_phase()
        emit_l1attn(p, xn1f, la["wqT"], la["wkT"], la["wvT"], la["woT"], la["bq"],
                    la["bk"], la["bv"], la["bo"], la["cosT"], la["sinT"],
                    ma["ones1f"], identb, la["cmaskT"], la["colones"], qf, kf, vf,
                    aof, brA1)
        p.tape_barrier()
        p.pop_phase()

        p.push_phase()
        emit_l1f1(p, xn1f, ff["weT"], ff["be"], ff["upM"], ff["upA"], ff["upB"],
                  ff["ubias"], identb, ma["ones1f"], hgf)
        p.tape_barrier()
        p.pop_phase()

        p.push_phase()
        emit_l1f2(p, hgf, ff["dnM"], ff["dnA"], ff["dnB"], ff["wcT"], ff["bcon"],
                  ff["dbias"], x1, brA1, identb, ma["ones1f"], out_h)
        p.pop_phase()

        with nc.Block() as block:
            p.block = block
            p.flush()
        return nc


def fused_in_map(inp, b, lays=None):
    if lays is None:
        lays = {}
        lays["m"] = mamba_host_layouts(inp)
        lays["q"] = moe_host_layouts(inp)
        lays["a"] = l1attn_host_layouts(inp)
        lays["f"] = l1ffn_host_layouts(inp)
    m, q, a, f = lays["m"], lays["q"], lays["a"], lays["f"]
    im = {"x": np.ascontiguousarray(inp["L0_attn_x"][b])}
    im["lnw_rep"] = np.ascontiguousarray(np.broadcast_to(inp["L0_attn_ln_w"], (128, D))).astype(np.float32)
    im["lnb_rep"] = np.ascontiguousarray(np.broadcast_to(inp["L0_attn_ln_b"], (128, D))).astype(np.float32)
    im["ident"] = np.eye(128, dtype=np.float32)
    im["identb"] = np.eye(128, dtype=BF16NP)
    bf_m = ("wzT", "wxT", "wdT", "woT", "normw")
    bf_q = ("rwT", "rb", "w1sT", "w2sT", "b2s")
    bf_a = ("wqT", "wkT", "wvT", "woT", "bq", "bk", "bv", "bo", "cmaskT",
            "colones")
    bf_f = ("weT", "upM", "dnM", "wcT", "bcon", "dbias")

    def pick(dct, nm, bfset):
        arr = dct[nm]
        if nm in bfset and arr.dtype != BF16NP:
            arr = np.ascontiguousarray(arr).astype(BF16NP)
            dct[nm] = arr
        return arr

    for nm in ("wzT", "wxT", "wdT", "woT", "convw", "convb", "dtb4", "alog",
               "dvec", "normw", "triu_i", "tril_s", "ones128", "ones1f"):
        im["m_" + nm] = pick(m, nm, bf_m)
    for nm in ("rwT", "rb", "w1sT", "w2sT", "b1s", "b2s"):
        im["q_" + nm] = pick(q, nm, bf_q)
    for nm in ("wqT", "wkT", "wvT", "woT", "bq", "bk", "bv", "bo", "cosT",
               "sinT", "cmaskT", "colones"):
        im["a_" + nm] = pick(a, nm, bf_a)
    for nm in ("weT", "be", "upM", "upA", "upB", "ubias", "dnM", "dnA", "dnB",
               "wcT", "bcon", "dbias"):
        im["f_" + nm] = pick(f, nm, bf_f)
    return im, lays


def kernel(**inputs):
    lay_f = l1ffn_host_layouts(inputs)
    pairs = lay_f["pairs"]
    nc = build_fused(pairs)
    lays = None
    in_maps = []
    for b in range(8):
        im, lays = fused_in_map(inputs, b, lays)
        in_maps.append(im)
    res = run_bass_kernel_spmd(nc, in_maps, core_ids=list(range(N_CORES)))
    return np.stack([np.asarray(r["out"]) for r in res.results], 0).astype(np.float32)


# version 08
# speedup vs baseline: 1.0888x; 1.0524x over previous
import contextlib
import math
import numpy as np
import ml_dtypes
import concourse.bass as bass
import concourse.mybir as mybir
from concourse.bass_utils import run_bass_kernel_spmd

F32 = mybir.dt.float32
BF16 = mybir.dt.bfloat16
AF = mybir.ActivationFunctionType
ALU = mybir.AluOpType
AX = mybir.AxisListType

T = 4096
D = 1024
NTB = T // 128
EPS = 1e-5
BF16NP = ml_dtypes.bfloat16

N_CORES = 8



class Prog:

    def __init__(self, ctx, nc):
        self.ctx = ctx
        self.ctx0 = ctx
        self.nc = nc
        self.block = None
        self.bar = None
        self.nbar = 0
        self._semcs = {}
        self._banks = []
        self._bank_i = 0

    def sbuf(self, name, shape, dtype, persist=False):
        ctx = self.ctx0 if persist else self.ctx
        return ctx.enter_context(self.nc.sbuf_tensor(name, shape, dtype))

    def push_phase(self):
        st = contextlib.ExitStack()
        self._phase = st
        self.ctx = st

    def pop_phase(self):
        self._phase.close()
        self.ctx = self.ctx0

    def psum(self, name, shape, dtype):
        return self.ctx.enter_context(self.nc.psum_tensor(name, shape, dtype))

    def psum_bank(self, name):
        if len(self._banks) < 8:
            self._banks.append(self.ctx0.enter_context(
                self.nc.psum_tensor(f"bank{len(self._banks)}", [128, 512], F32)))
            self._bank_i = len(self._banks) - 1
            return self._banks[self._bank_i]
        self._bank_i = (self._bank_i + 1) % 8
        return self._banks[self._bank_i]

    def semc(self, name):
        if name not in self._semcs:
            self._semcs[name] = SemC(self.ctx0.enter_context(self.nc.semaphore(name)))
        return self._semcs[name]

    def sem(self, name):
        return self.ctx0.enter_context(self.nc.semaphore(name))

    def barrier(self):
        self.nbar += 1
        k = self.nbar
        b = self.block
        for section in (b.tensor, b.vector, b.scalar, b.gpsimd, b.sync):
            @section
            def _(e):
                e.nop().then_inc(self.bar, 1)
                e.wait_ge(self.bar, 5 * k)

    def tape_init(self):
        self.ops = {k: [] for k in ("tensor", "vector", "scalar", "gpsimd", "sync")}

    def emit(self, eng, fn):
        self.ops[eng].append(fn)

    def tape_barrier(self):
        self.nbar += 1
        k = self.nbar
        for eng in self.ops:
            def f(e, k=k):
                e.drain()
                e.nop().then_inc(self.bar, 1)
                e.wait_ge(self.bar, 5 * k)
            self.emit(eng, f)

    def flush(self):
        b = self.block
        for name, section in (("tensor", b.tensor), ("vector", b.vector),
                              ("scalar", b.scalar), ("gpsimd", b.gpsimd),
                              ("sync", b.sync)):
            ops = self.ops[name]

            @section
            def _(e, _ops=ops):
                for fn in _ops:
                    fn(e)


class SemC:

    def __init__(self, handle):
        self.h = handle
        self.n = 0

    def inc(self, k=1):
        self.n += k
        return self.n



def emit_pre(p, x, xnf, xn2f, lnw_rep, lnb_rep, ident_hbm, mu_hbm, rstd_hbm,
             affine=True):
    nc, block = p.nc, p.block
    NQ = 2 if affine else 1

    xb = [p.sbuf(f"pre_xb{i}", [128, D], F32) for i in range(2)]
    xc = p.sbuf("pre_xc", [128, D], F32)
    sq = p.sbuf("pre_sq", [128, D], F32)
    xn = p.sbuf("pre_xn", [128, D], F32)
    xnbf = p.sbuf("pre_xnbf", [128, D], BF16)
    xn2bf = p.sbuf("pre_xn2bf", [128, D], BF16)
    xc2 = p.sbuf("pre_xc2", [128, D], F32)
    st1 = p.sbuf("pre_st1", [128, 8], F32)
    epsc = p.sbuf("pre_eps", [128, 1], F32)
    mu_sb = p.sbuf("pre_mu", [128, NTB], F32)
    rstd_sb = p.sbuf("pre_rstd", [128, NTB], F32)
    lnw = p.sbuf("pre_lnw", [128, D], F32)
    lnb = p.sbuf("pre_lnb", [128, D], F32)
    ident = p.sbuf("ident_sb", [128, 128], BF16)
    stg = p.sbuf("pre_stg", [128, 8, 128], BF16)
    stg2 = p.sbuf("pre_stg2", [128, 8, 128], BF16)
    ptb = [p.psum_bank(f"pre_pt{i}") for i in range(2)]
    pt = [b.ap().bitcast(BF16)[:, 0:128] for b in ptb]

    s_xld = [p.sem("pre_xld0"), p.sem("pre_xld1")]
    s_cld = p.sem("pre_cld")
    s_cldg = p.sem("pre_cldg")
    s_v2s = p.sem("pre_v2s")
    s_s2v = p.sem("pre_s2v")
    s_cast = p.sem("pre_cast")
    s_tr = p.sem("pre_tr")
    s_ev = p.sem("pre_ev")
    s_out = p.sem("pre_out")
    s_xcons = p.sem("pre_xcons")

    NT = 8 * NQ

    @block.sync
    def _(s_):
        if affine:
            s_.dma_start(out=lnw[:], in_=lnw_rep[:, :]).then_inc(s_cld, 16)
            s_.dma_start(out=lnb[:], in_=lnb_rep[:, :]).then_inc(s_cld, 16)
        for tb in range(NTB):
            if tb >= 2:
                s_.wait_ge(s_xcons, tb - 1)
            s_.dma_start(out=xb[tb % 2][:], in_=x[tb * 128:(tb + 1) * 128, :]
                         ).then_inc(s_xld[tb % 2], 16)

    CLD = 32 if affine else 0

    @block.vector
    def _(v):
        def guard(instr):
            v.drain()

        v.memset(epsc[:], EPS)
        if affine:
            v.wait_ge(s_cld, CLD)
        for tb in range(NTB):
            b = xb[tb % 2]
            v.wait_ge(s_xld[tb % 2], 16 * (tb // 2 + 1))
            guard(v.reduce_sum(out=st1[:, 0:1], in_=b[:], axis=AX.X))
            v.tensor_scalar(out=st1[:, 0:1], in0=st1[:, 0:1], scalar1=-1.0 / D,
                            scalar2=None, op0=ALU.mult)
            v.drain()
            v.tensor_scalar(out=xc[:], in0=b[:], scalar1=st1[:, 0:1], scalar2=None,
                            op0=ALU.add).then_inc(s_v2s, 1)
            v.tensor_copy(out=mu_sb[:, tb:tb + 1], in_=st1[:, 0:1]).then_inc(s_xcons, 1)
            v.wait_ge(s_s2v, NQ * tb + 1)
            guard(v.reciprocal(out=st1[:, 3:4], in_=st1[:, 2:3]))
            v.tensor_copy(out=rstd_sb[:, tb:tb + 1], in_=st1[:, 3:4])
            v.tensor_scalar(out=xn[:], in0=xc[:], scalar1=st1[:, 3:4], scalar2=None,
                            op0=ALU.mult)
            v.tensor_copy(out=xnbf[:], in_=xn[:]).then_inc(s_cast, 1)
            if affine:
                guard(v.reduce_sum(out=st1[:, 4:5], in_=xn[:], axis=AX.X))
                v.tensor_scalar(out=st1[:, 4:5], in0=st1[:, 4:5], scalar1=-1.0 / D,
                                scalar2=None, op0=ALU.mult)
                v.drain()
                v.tensor_scalar(out=xc2[:], in0=xn[:], scalar1=st1[:, 4:5],
                                scalar2=None, op0=ALU.add).then_inc(s_v2s, 1)
                v.wait_ge(s_s2v, 2 * tb + 2)
                guard(v.reciprocal(out=st1[:, 7:8], in_=st1[:, 6:7]))
                v.tensor_scalar(out=xc2[:], in0=xc2[:], scalar1=st1[:, 7:8],
                                scalar2=None, op0=ALU.mult)
                v.drain()
                v.tensor_tensor(out=xc2[:], in0=xc2[:], in1=lnw[:], op=ALU.mult)
                v.drain()
                v.tensor_tensor(out=xn2bf[:], in0=xc2[:], in1=lnb[:],
                                op=ALU.add).then_inc(s_cast, 1)
            v.wait_ge(s_out, 16 * NQ * tb)
            for c in range(NT):
                v.wait_ge(s_tr, NT * tb + c + 1)
                psrc = pt[c % 2]
                dst = stg if c < 8 else stg2
                cc = c if c < 8 else c - 8
                v.tensor_copy(out=dst[:, cc, :], in_=psrc[:]).then_inc(s_ev, 1)

    @block.scalar
    def _(sc):
        for tb in range(NTB):
            sc.wait_ge(s_v2s, NQ * tb + 1)
            sc.activation(out=sq[:], in_=xc[:], func=AF.Square, accum_out=st1[:, 1:2])
            sc.drain()
            sc.activation(out=st1[:, 2:3], in_=st1[:, 1:2], func=AF.Sqrt,
                          scale=1.0 / D, bias=epsc[:, 0:1]).then_inc(s_s2v, 1)
            if affine:
                sc.wait_ge(s_v2s, 2 * tb + 2)
                sc.activation(out=sq[:], in_=xc2[:], func=AF.Square,
                              accum_out=st1[:, 5:6])
                sc.drain()
                sc.activation(out=st1[:, 6:7], in_=st1[:, 5:6], func=AF.Sqrt,
                              scale=1.0 / D, bias=epsc[:, 0:1]).then_inc(s_s2v, 1)

    @block.tensor
    def _(t):
        t.wait_ge(s_cldg, 16)
        for tb in range(NTB):
            for c in range(NT):
                t.wait_ge(s_cast, NQ * tb + (1 if c < 8 else 2))
                if tb > 0 or c >= 2:
                    t.wait_ge(s_ev, NT * tb + c - 1)
                srct = xnbf if c < 8 else xn2bf
                cc = c if c < 8 else c - 8
                t.transpose(pt[c % 2][:], srct[:, cc * 128:(cc + 1) * 128],
                            ident[:]).then_inc(s_tr, 1)

    @block.gpsimd
    def _(g):
        g.dma_start(out=ident[:], in_=ident_hbm[:, :]).then_inc(s_cldg, 16)
        for tb in range(NTB):
            g.wait_ge(s_ev, NT * tb + 8)
            g.dma_start(out=xnf[:, :, tb * 128:(tb + 1) * 128],
                        in_=stg[:, :, :]).then_inc(s_out, 16)
            if affine:
                g.wait_ge(s_ev, NT * tb + 16)
                g.dma_start(out=xn2f[:, :, tb * 128:(tb + 1) * 128],
                            in_=stg2[:, :, :]).then_inc(s_out, 16)
        g.dma_start(out=mu_hbm[:, :], in_=mu_sb[:]).then_inc(s_out, 16)
        g.dma_start(out=rstd_hbm[:, :], in_=rstd_sb[:]).then_inc(s_out, 16)
        g.wait_ge(s_out, 16 * (NQ * NTB + 2))


def build_pre(affine=True):
    ctx = contextlib.ExitStack()
    with ctx:
        nc = bass.Bass("TRN2", target_bir_lowering=False, debug=False)
        p = Prog(ctx, nc)
        x = nc.dram_tensor("x", [T, D], F32, kind="ExternalInput").ap()
        lnw = nc.dram_tensor("lnw_rep", [128, D], F32, kind="ExternalInput").ap()
        lnb = nc.dram_tensor("lnb_rep", [128, D], F32, kind="ExternalInput").ap()
        ident = nc.dram_tensor("ident", [128, 128], F32, kind="ExternalInput").ap()
        xnf = nc.dram_tensor("xnf", [128, 8, T], BF16, kind="ExternalOutput").ap()
        if affine:
            xn2f = nc.dram_tensor("xn2f", [128, 8, T], BF16, kind="ExternalOutput").ap()
        else:
            xn2f = None
        mu = nc.dram_tensor("mu", [128, NTB], F32, kind="ExternalOutput").ap()
        rstd = nc.dram_tensor("rstd", [128, NTB], F32, kind="ExternalOutput").ap()
        p.bar = p.sem("bar")
        with nc.Block() as block:
            p.block = block
            emit_pre(p, x, xnf, xn2f, lnw, lnb, ident, mu, rstd, affine=affine)
        return nc






def emit_moe(p, xnf, x, brA, mu_hbm, rstd_hbm, rwT, rb, w1sT, w2sT, b1s, b2s,
             ones1, x1_out, n_exp=9):
    nc = p.nc
    E = p.emit

    rwt = p.sbuf("moe_rwt", [128, 8, 8], BF16)
    rbt = p.sbuf("moe_rbt", [1, 8], BF16)
    ones_r = p.sbuf("moe_ones", [1, 128], BF16)
    mu_sb = p.sbuf("moe_mu", [128, NTB], F32)
    rstd_sb = p.sbuf("moe_rstd", [128, NTB], F32)
    comb = p.sbuf("moe_comb", [128, NTB, 8], F32)
    xnr = [p.sbuf(f"moe_xnr{i}", [128, 8, 128], BF16) for i in range(2)]
    lg = p.sbuf("moe_lg", [128, 8], F32)
    rst = p.sbuf("moe_rst", [128, 8], F32)
    eprob = p.sbuf("moe_eprob", [128, 8], F32)
    msk = p.sbuf("moe_msk", [128, 8], F32)
    w1q = [p.sbuf(f"moe_w1q{i}", [128, 8, 1024], BF16) for i in range(2)]
    w2q = [p.sbuf(f"moe_w2q{i}", [128, 8, 1024], BF16) for i in range(2)]
    b1t = [p.sbuf(f"moe_b1t{i}", [128, 32], F32) for i in range(2)]
    b2t = [p.sbuf(f"moe_b2t{i}", [1, 1024], BF16) for i in range(2)]
    xnt = [p.sbuf(f"moe_xnt{i}", [128, 8, 512], BF16) for i in range(2)]
    ht = [p.sbuf(f"moe_ht{i}", [128, 8, 512], BF16) for i in range(2)]
    acc = p.sbuf("moe_acc", [128, 16, 1024], F32)
    xcb = [p.sbuf(f"moe_xcb{i}", [128, D], F32) for i in range(2)]
    bab = [p.sbuf(f"moe_bab{i}", [128, D], F32) for i in range(2)]
    x1b = [p.sbuf(f"moe_x1b{i}", [128, D], F32) for i in range(2)]
    xn_s = p.sbuf("moe_xn_s", [128, D], F32)

    p_r = [p.psum_bank(f"moe_pr{i}") for i in range(2)]
    p_h = [p.psum_bank(f"moe_ph{i}") for i in range(2)]
    p_o = [p.psum_bank(f"moe_po{i}") for i in range(2)]

    s_cld = SemC(p.sem("moe_cld"))
    s_cld2 = SemC(p.sem("moe_cld2"))
    s_rxl = [SemC(p.sem("moe_rxl0")), SemC(p.sem("moe_rxl1"))]
    s_rmm = SemC(p.sem("moe_rmm"))
    s_rlev = SemC(p.sem("moe_rlev"))
    s_v2s = SemC(p.sem("moe_v2s"))
    s_s2v = SemC(p.sem("moe_s2v"))
    s_cdone = SemC(p.sem("moe_cdone"))
    s_wld = [SemC(p.sem("moe_wld0")), SemC(p.sem("moe_wld1"))]
    s_xld = [SemC(p.sem("moe_xld0")), SemC(p.sem("moe_xld1"))]
    s_hgrp = SemC(p.sem("moe_hgrp"))
    s_hev = SemC(p.sem("moe_hev"))
    s_ogrp = SemC(p.sem("moe_ogrp"))
    s_oev = SemC(p.sem("moe_oev"))
    s_cxl = [SemC(p.sem("moe_cxl0")), SemC(p.sem("moe_cxl1"))]
    s_cv = SemC(p.sem("moe_cv"))
    s_x1o = [SemC(p.sem("moe_x1o0")), SemC(p.sem("moe_x1o1"))]

    def f(g):
        g.dma_start(out=rwt[:], in_=rwT.rearrange("(c p) e -> p c e", p=128)).then_inc(s_cld.h, 16)
        g.dma_start(out=rbt[:], in_=rb[:, :]).then_inc(s_cld.h, 16)
        g.dma_start(out=ones_r[:], in_=ones1[:, :]).then_inc(s_cld.h, 16)
    E("gpsimd", f)
    s_cld.inc(48)

    def f(s_):
        s_.dma_start(out=mu_sb[:], in_=mu_hbm[:, :]).then_inc(s_cld2.h, 16)
        s_.dma_start(out=rstd_sb[:], in_=rstd_hbm[:, :]).then_inc(s_cld2.h, 16)
    E("sync", f)
    s_cld2.inc(32)
    CLD = s_cld.n
    CLD2 = s_cld2.n

    for tb in range(NTB):
        par = tb % 2
        need_cdone = s_cdone.n - 1
        xl = s_rxl[par].inc(16)

        def f(s_, tb=tb, par=par, need=need_cdone):
            s_.wait_ge(s_cld.h, CLD)
            if need > 0:
                s_.wait_ge(s_cdone.h, need)
            s_.dma_start(out=xnr[par][:],
                         in_=xnf[:, :, tb * 128:(tb + 1) * 128]).then_inc(s_rxl[par].h, 16)
        E("sync", f)

        need_rlev = s_rlev.n - 1
        mm = s_rmm.inc(1)

        def f(t, tb=tb, par=par, xl=xl, need=need_rlev):
            t.wait_ge(s_cld.h, CLD)
            t.wait_ge(s_rxl[par].h, xl)
            if need > 0:
                t.wait_ge(s_rlev.h, need)
            pr = p_r[par].ap()
            for c in range(8):
                t.matmul(pr[:, 0:8], xnr[par][:, c, :], rwt[:, c, :],
                         start=(c == 0), stop=False)
            t.matmul(pr[:, 0:8], ones_r[:], rbt[:, :], start=False,
                     stop=True).then_inc(s_rmm.h, 1)
        E("tensor", f)

        rlev = s_rlev.inc(1)
        v2s = s_v2s.inc(1)
        s2v_t = s_s2v.n + 1
        cd = s_cdone.inc(1)

        def f(v, tb=tb, par=par, mm=mm, s2v_t=s2v_t):
            v.wait_ge(s_rmm.h, mm)
            v.reduce_max(out=rst[:, 0:1], in_=p_r[par].ap()[:, 0:8], axis=AX.X)
            v.drain()
            v.tensor_scalar(out=rst[:, 1:2], in0=rst[:, 0:1], scalar1=-1.0,
                            scalar2=None, op0=ALU.mult)
            v.drain().then_inc(s_v2s.h, 1)
            v.wait_ge(s_s2v.h, s2v_t)
            v.tensor_scalar(out=msk[:], in0=eprob[:], scalar1=1.0, scalar2=None,
                            op0=ALU.is_ge)
            v.drain()
            v.scalar_tensor_tensor(out=msk[:], in0=msk[:], scalar=-2.0, in1=eprob[:],
                                   op0=ALU.mult, op1=ALU.add)
            v.drain()
            v.reduce_max(out=rst[:, 2:3], in_=msk[:], axis=AX.X)
            v.drain()
            v.tensor_scalar(out=msk[:], in0=eprob[:], scalar1=rst[:, 2:3],
                            scalar2=None, op0=ALU.is_ge)
            v.tensor_scalar(out=rst[:, 3:4], in0=rst[:, 2:3], scalar1=1.0,
                            scalar2=None, op0=ALU.add)
            v.drain()
            v.reciprocal(out=rst[:, 4:5], in_=rst[:, 3:4])
            v.drain()
            v.tensor_tensor(out=msk[:], in0=msk[:], in1=eprob[:], op=ALU.mult)
            v.drain()
            v.tensor_scalar(out=comb[:, tb, :], in0=msk[:], scalar1=rst[:, 4:5],
                            scalar2=None, op0=ALU.mult).then_inc(s_cdone.h, 1)
        E("vector", f)

        def f(sc, par=par, v2s=v2s):
            sc.wait_ge(s_v2s.h, v2s)
            sc.activation(out=eprob[:], in_=p_r[par].ap()[:, 0:8], func=AF.Exp,
                          bias=rst[:, 1:2]).then_inc(s_rlev.h, 1)
            sc.drain().then_inc(s_s2v.h, 1)
        E("scalar", f)
        s_s2v.inc(1)

    p.tape_barrier()

    TPH = 16
    n_eq = 0
    n_t = 0
    n_b = 0
    n_og = 0
    for H in range(2):
        eq_list = [(H, e, q) for e in range(n_exp) for q in range(4)]
        pending = None

        def emit_pending():
            nonlocal pending, n_og
            if pending is None:
                return
            key, it, og0 = pending
            _emit_moe_outgroups(p, key, it, ht, w1q, w2q, b1t, b2t, ones_r, p_o,
                                acc, comb, s_hev, s_oev, s_ogrp, og0)
            n_og += 8
            s_ogrp.inc(8)
            s_oev.inc(8)
            pending = None

        for (H_, e, q) in eq_list:
            idx_eq = n_eq
            n_eq += 1
            par = idx_eq % 2
            epar = e % 2
            need_og = (idx_eq - 1) * 32

            def f(g, e=e, q=q, par=par, epar=epar, need=need_og):
                if need > 0:
                    g.wait_ge(s_ogrp.h, need)
                g.dma_start(out=w1q[par][:],
                            in_=w1sT[e, q, :, :, :]).then_inc(s_wld[par].h, 16)
                if q == 0:
                    g.dma_start(out=b1t[epar][:], in_=b1s[e, :, :]).then_inc(s_wld[par].h, 16)
            E("gpsimd", f)

            def f(s_, e=e, q=q, par=par, epar=epar, need=need_og):
                if need > 0:
                    s_.wait_ge(s_ogrp.h, need)
                s_.dma_start(out=w2q[par][:],
                             in_=w2sT[e, q, :, :, :]).then_inc(s_wld[par].h, 16)
                if q == 0:
                    s_.dma_start(out=b2t[epar][:], in_=b2s[e, :, :]).then_inc(s_wld[par].h, 16)
            E("sync", f)
            s_wld[par].inc(32 + (32 if q == 0 else 0))
            wld_after = s_wld[par].n

            for tau in range(4):
                idx_t = n_t
                n_t += 1
                tpar = idx_t % 2
                need_hgrp = (idx_t - 1) * 8

                def f(s_, H=H, tau=tau, tpar=tpar, need=need_hgrp):
                    if need > 0:
                        s_.wait_ge(s_hgrp.h, need)
                    s_.dma_start(out=xnt[tpar][:],
                                 in_=xnf[:, :, H * 2048 + tau * 512:H * 2048 + (tau + 1) * 512]
                                 ).then_inc(s_xld[tpar].h, 16)
                E("sync", f)
                xld_after = s_xld[tpar].inc(16)

                def f(t, par=par, tpar=tpar, wld=wld_after, xld=xld_after, gb0=n_b):
                    t.wait_ge(s_wld[par].h, wld)
                    t.wait_ge(s_xld[tpar].h, xld)
                    for b in range(8):
                        gb = gb0 + b
                        if gb >= 2:
                            t.wait_ge(s_hev.h, gb - 1)
                        ph = p_h[gb % 2].ap()
                        for c in range(8):
                            mm = t.matmul(ph[:, 0:512], w1q[par][:, c, b * 128:(b + 1) * 128],
                                          xnt[tpar][:, c, :], start=(c == 0),
                                          stop=(c == 7))
                        mm.then_inc(s_hgrp.h, 1)
                E("tensor", f)
                gb0_this = n_b
                n_b += 8
                s_hgrp.inc(8)

                def f(sc, q=q, par=par, epar=epar, tpar=tpar, gb0=gb0_this, wld=wld_after,
                      need_og=(idx_t - 1) * 8):
                    sc.wait_ge(s_wld[par].h, wld)
                    if need_og > 0:
                        sc.wait_ge(s_ogrp.h, need_og)
                    for b in range(8):
                        gb = gb0 + b
                        sc.wait_ge(s_hgrp.h, gb + 1)
                        ph = p_h[gb % 2].ap()
                        sc.activation(out=ht[tpar][:, b, :], in_=ph[:, 0:512],
                                      func=AF.Gelu_apprx_tanh,
                                      bias=b1t[epar][:, q * 8 + b:q * 8 + b + 1]
                                      ).then_inc(s_hev.h, 1)
                E("scalar", f)
                s_hev.inc(8)

                emit_pending()
                pending = ((H, e, q, tau), idx_t, n_og)

        emit_pending()
        p.tape_barrier()

        for tl in range(TPH):
            tb = H * TPH + tl
            par = tl % 2
            need_cv = s_cv.n - 1

            def f(s_, tb=tb, par=par, need=need_cv):
                if need > 0:
                    s_.wait_ge(s_cv.h, need)
                s_.dma_start(out=xcb[par][:],
                             in_=x[tb * 128:(tb + 1) * 128, :]).then_inc(s_cxl[par].h, 16)
                s_.dma_start(out=bab[par][:],
                             in_=brA[tb * 128:(tb + 1) * 128, :]).then_inc(s_cxl[par].h, 16)
            E("sync", f)
            cxl = s_cxl[par].inc(32)
            need_out = s_x1o[par].n

            def f(v, tb=tb, tl=tl, par=par, cxl=cxl, need_out=need_out):
                v.wait_ge(s_cld2.h, CLD2)
                v.wait_ge(s_cxl[par].h, cxl)
                if need_out > 0:
                    v.wait_ge(s_x1o[par].h, need_out)
                v.tensor_scalar(out=xn_s[:], in0=xcb[par][:], scalar1=mu_sb[:, tb:tb + 1],
                                scalar2=rstd_sb[:, tb:tb + 1], op0=ALU.add, op1=ALU.mult)
                v.drain()
                v.tensor_tensor(out=xn_s[:], in0=xn_s[:], in1=acc[:, tl, :], op=ALU.add)
                v.drain()
                v.tensor_tensor(out=xn_s[:], in0=xn_s[:], in1=xcb[par][:], op=ALU.add)
                v.drain()
                v.tensor_tensor(out=x1b[par][:], in0=xn_s[:], in1=bab[par][:], op=ALU.add)
                v.drain().then_inc(s_cv.h, 1)
            E("vector", f)
            cv = s_cv.inc(1)

            def f(g, tb=tb, par=par, cv=cv):
                g.wait_ge(s_cv.h, cv)
                g.dma_start(out=x1_out[tb * 128:(tb + 1) * 128, :],
                            in_=x1b[par][:]).then_inc(s_x1o[par].h, 16)
            E("gpsimd", f)
            s_x1o[par].inc(16)

        p.tape_barrier()

    t0_, t1_ = s_x1o[0].n, s_x1o[1].n

    def f(g):
        g.wait_ge(s_x1o[0].h, t0_)
        g.wait_ge(s_x1o[1].h, t1_)
    E("gpsimd", f)


def _emit_moe_outgroups(p, key, idx_t, ht, w1q, w2q, b1t, b2t, ones_r, p_o,
                        acc, comb, s_hev, s_oev, s_ogrp, og0):
    (H, e, q, tau) = key
    idx_eq = idx_t // 4
    par = idx_eq % 2
    epar = e % 2
    tpar = idx_t % 2
    hev_need = (idx_t + 1) * 8

    def f(t, q=q, par=par, epar=epar, tpar=tpar, og0=og0, hev=hev_need):
        t.wait_ge(s_hev.h, hev)
        for og in range(8):
            tl_, dt = og // 2, og % 2
            gog = og0 + og
            if gog >= 2:
                t.wait_ge(s_oev.h, gog - 1)
            po = p_o[gog % 2].ap()
            for b in range(8):
                mm = t.matmul(po[:, 0:512],
                              ht[tpar][:, b, tl_ * 128:(tl_ + 1) * 128],
                              w2q[par][:, b, dt * 512:(dt + 1) * 512],
                              start=(b == 0), stop=(q != 3 and b == 7))
            if q == 3:
                mm = t.matmul(po[:, 0:512], ones_r[:], b2t[epar][:, dt * 512:(dt + 1) * 512],
                              start=False, stop=True)
            mm.then_inc(s_ogrp.h, 1)
    p.emit("tensor", f)

    ogrp_base = s_ogrp.n

    def f(v, e=e, q=q, tau=tau, H=H, og0=og0, ogrp_base=ogrp_base):
        for og in range(8):
            tl_, dt = og // 2, og % 2
            gog = og0 + og
            v.wait_ge(s_ogrp.h, ogrp_base + og + 1)
            po = p_o[gog % 2].ap()
            tl_abs = tau * 4 + tl_
            tb_abs = H * 16 + tl_abs
            dst = acc[:, tl_abs, dt * 512:(dt + 1) * 512]
            if e == 0 and q == 0:
                v.tensor_scalar(out=dst, in0=po[:, 0:512], scalar1=1.0,
                                scalar2=None, op0=ALU.mult).then_inc(s_oev.h, 1)
            elif e == 0:
                v.scalar_tensor_tensor(out=dst, in0=po[:, 0:512], scalar=1.0,
                                       in1=dst, op0=ALU.mult,
                                       op1=ALU.add).then_inc(s_oev.h, 1)
            else:
                v.scalar_tensor_tensor(out=dst, in0=po[:, 0:512],
                                       scalar=comb[:, tb_abs, e - 1:e],
                                       in1=dst, op0=ALU.mult,
                                       op1=ALU.add).then_inc(s_oev.h, 1)
    p.emit("vector", f)


def build_moe(n_exp=9):
    ctx = contextlib.ExitStack()
    with ctx:
        nc = bass.Bass("TRN2", target_bir_lowering=False, debug=False)
        p = Prog(ctx, nc)
        xnf = nc.dram_tensor("xnf", [128, 8, T], BF16, kind="ExternalInput").ap()
        x = nc.dram_tensor("x", [T, D], F32, kind="ExternalInput").ap()
        brA = nc.dram_tensor("brA", [T, D], F32, kind="ExternalInput").ap()
        mu = nc.dram_tensor("mu", [128, NTB], F32, kind="ExternalInput").ap()
        rstd = nc.dram_tensor("rstd", [128, NTB], F32, kind="ExternalInput").ap()
        rwT = nc.dram_tensor("rwT", [1024, 8], F32, kind="ExternalInput").ap()
        rb = nc.dram_tensor("rb", [1, 8], F32, kind="ExternalInput").ap()
        w1sT = nc.dram_tensor("w1sT", [9, 8, 128, 4096], F32, kind="ExternalInput").ap()
        w2sT = nc.dram_tensor("w2sT", [9, 32, 128, 1024], F32, kind="ExternalInput").ap()
        b1s = nc.dram_tensor("b1s", [9, 128, 32], F32, kind="ExternalInput").ap()
        b2s = nc.dram_tensor("b2s", [9, 1, 1024], F32, kind="ExternalInput").ap()
        ones1 = nc.dram_tensor("ones1", [1, 128], F32, kind="ExternalInput").ap()
        x1 = nc.dram_tensor("x1", [T, D], F32, kind="ExternalOutput").ap()
        p.bar = p.sem("bar")
        p.tape_init()
        emit_moe(p, xnf, x, brA, mu, rstd, rwT, rb, w1sT, w2sT, b1s, b2s, ones1,
                 x1, n_exp=n_exp)
        with nc.Block() as block:
            p.block = block
            p.flush()
        return nc


def moe_host_layouts(inp):
    w1 = np.concatenate([inp["L0_ffn_shared_w1"][None], inp["L0_ffn_spec_w1"]], 0)
    w2 = np.concatenate([inp["L0_ffn_shared_w2"][None], inp["L0_ffn_spec_w2"]], 0)
    b1 = np.concatenate([inp["L0_ffn_shared_b1"][None], inp["L0_ffn_spec_b1"]], 0)
    b2 = np.concatenate([inp["L0_ffn_shared_b2"][None], inp["L0_ffn_spec_b2"]], 0)
    w1sT = np.ascontiguousarray(
        w1.transpose(0, 2, 1).reshape(9, 8, 128, 4, 1024).transpose(0, 3, 2, 1, 4))
    w2sT = np.ascontiguousarray(
        w2.transpose(0, 2, 1).reshape(9, 4, 8, 128, 1024).transpose(0, 1, 3, 2, 4))
    b1s = np.ascontiguousarray(b1.reshape(9, 32, 128).transpose(0, 2, 1))
    b2s = np.ascontiguousarray(b2.reshape(9, 1, 1024))
    rwT = np.ascontiguousarray(inp["L0_ffn_router_w"].T)
    rb = np.ascontiguousarray(inp["L0_ffn_router_b"].reshape(1, 8))
    ones1 = np.ones((1, 128), np.float32)
    return dict(w1sT=w1sT, w2sT=w2sT, b1s=b1s, b2s=b2s, rwT=rwT, rb=rb, ones1=ones1)



NH = 32
HD = 64
DIN = 2048
SCW = 256
CPS = SCW // 128
NSC = T // SCW
NCH = 32


def emit_mamba(p, xn2f, wzT, wxT, wdT, woT, convw, convb, dtb4, alog, dvec,
               normw, triu_i, tril_s, ones128, ones1f, identf, identd,
               ltdram, brA_out, dbg=None):
    nc = p.nc
    E = p.emit

    wz = p.sbuf("ma_wz", [128, 8, 2048], BF16)
    wx = p.sbuf("ma_wx", [128, 8, 2304], BF16)
    wd = p.sbuf("ma_wd", [128, 8, 32], BF16)
    wo = p.sbuf("ma_wo", [128, 16, 1024], BF16)
    cw = p.sbuf("ma_cw", [128, 18, 4], F32)
    cb = p.sbuf("ma_cb", [128, 18], F32)
    dtbr = p.sbuf("ma_dtbr", [128, CPS * 32], F32)
    nwr = p.sbuf("ma_nwr", [128, 2048], BF16)
    tri1 = p.sbuf("ma_tri1", [128, 128], F32)
    tri2 = p.sbuf("ma_tri2", [128, 128], F32)
    on128 = p.sbuf("ma_on128", [128, 128], F32)
    on1f = p.sbuf("ma_on1f", [1, 128], F32)
    non1f = p.sbuf("ma_non1f", [1, 128], F32)
    idf = p.sbuf("ma_idf", [128, 128], F32)
    idb = p.sbuf("ma_idb", [128, 128], BF16)
    alog_s = p.sbuf("ma_alog", [1, 32], F32)
    arow = p.sbuf("ma_arow", [1, 128], F32)
    arep = p.sbuf("ma_arep", [128, 128], F32)
    dvec_s = p.sbuf("ma_dvec", [1, 32], F32)
    drow = p.sbuf("ma_drow", [1, 32], BF16)
    drep = p.sbuf("ma_drep", [128, 32], BF16)
    epsc = p.sbuf("ma_eps", [128, 1], F32)

    xn2t = p.sbuf("ma_xn2t", [128, 8, SCW], BF16)
    z_sil = p.sbuf("ma_zsil", [128, CPS, 2048], BF16)
    xbce = [p.sbuf(f"ma_xbce{i}", [128, SCW + 3], F32) for i in range(2)]
    xbcc = [p.sbuf(f"ma_xbcc{i}", [128, SCW], F32) for i in range(2)]
    halo = p.sbuf("ma_halo", [128, 18, 3], F32)
    xs_fm = p.sbuf("ma_xsfm", [128, 16, SCW], BF16)
    b_fm = p.sbuf("ma_bfm", [128, SCW], BF16)
    c_fm = p.sbuf("ma_cfm", [128, SCW], BF16)
    dt_raw = p.sbuf("ma_dtraw", [128, CPS, 32], F32)
    dtp = p.sbuf("ma_dtp", [128, CPS * 32], F32)
    dtp_bf = p.sbuf("ma_dtpbf", [128, CPS, 32], BF16)
    adt = p.sbuf("ma_adt", [128, CPS * 32], F32)
    sp1 = p.sbuf("ma_sp1", [128, CPS * 32], F32)
    sp2 = p.sbuf("ma_sp2", [128, CPS * 32], F32)
    ltok = p.sbuf("ma_ltok", [128, 32], F32)
    etok = p.sbuf("ma_etok", [128, CPS, 32], F32)
    lam = p.sbuf("ma_lam", [128, CPS, 32], F32)
    w2bf = p.sbuf("ma_w2bf", [128, CPS, 32], BF16)
    w2f = p.sbuf("ma_w2f", [128, 32], F32)
    ltt = p.sbuf("ma_ltt", [32, 128], F32)
    pstr = p.sbuf("ma_pstr", [1, 4096], F32)
    g2m = p.sbuf("ma_g2m", [128, 128], BF16)
    btok = p.sbuf("ma_btok", [128, 128], BF16)
    xtok = p.sbuf("ma_xtok", [128, 2048], BF16)
    xw1 = p.sbuf("ma_xw1", [128, 2048], BF16)
    xw2 = p.sbuf("ma_xw2", [128, 2048], BF16)
    decf = p.sbuf("ma_decf", [128, 512], F32)
    decb = p.sbuf("ma_decb", [128, 512], BF16)
    sct = [p.sbuf(f"ma_sct{i}", [128, 512], BF16) for i in range(2)]
    ych = p.sbuf("ma_ych", [128, 2048], F32)
    ssb = p.sbuf("ma_ssb", [128, 2048], F32)
    sbf = p.sbuf("ma_sbf", [128, 2048], BF16)
    ysq = p.sbuf("ma_ysq", [128, 2048], F32)
    gfm = p.sbuf("ma_gfm", [128, 16, 128], BF16)
    oout = [p.sbuf("ma_oout0", [128, 1024], F32)]

    pb = [p.psum_bank(f"ma_p{i}") for i in range(8)]
    pbb = [b.ap().bitcast(BF16) for b in pb]

    s_w = SemC(p.sem("ma_w"))
    s_ws = SemC(p.sem("ma_ws"))
    s_x = [SemC(p.sem(f"ma_x{i}")) for i in range(2)]
    s_pmm = SemC(p.sem("ma_pmm"))
    s_pev = SemC(p.sem("ma_pev"))
    s_v2s = SemC(p.sem("ma_v2s"))
    s_s2v = SemC(p.sem("ma_s2v"))
    s_vg = SemC(p.sem("ma_vg"))
    s_sg = SemC(p.sem("ma_sg"))
    s_pg = SemC(p.sem("ma_pg"))
    s_tg = SemC(p.sem("ma_tg"))
    s_lts = [SemC(p.sem(f"ma_lts{i}")) for i in range(2)]
    s_ltl = SemC(p.sem("ma_ltl"))
    s_oo = [SemC(p.sem(f"ma_oo{i}")) for i in range(2)]
    s_t34 = SemC(p.sem("ma_t34"))
    s_e34 = SemC(p.sem("ma_e34"))

    def tW(t, val):
        if val > 0:
            t.wait_ge(s_tg.h, val)

    def f(g):
        g.dma_start(out=wz[:], in_=wzT.rearrange("(c p) z -> p c z", p=128)).then_inc(s_w.h, 16)
        g.dma_start(out=wx[:], in_=wxT.rearrange("(c p) z -> p c z", p=128)).then_inc(s_w.h, 16)
        g.dma_start(out=wd[:], in_=wdT.rearrange("(c p) z -> p c z", p=128)).then_inc(s_w.h, 16)
        g.dma_start(out=wo[:], in_=woT.rearrange("(c p) z -> p c z", p=128)).then_inc(s_w.h, 16)
        g.dma_start(out=idb[:], in_=identd[:, :]).then_inc(s_w.h, 16)
        g.dma_start(out=drow[:], in_=dvec[:, :]).then_inc(s_w.h, 16)
        g.dma_start(out=nwr[:], in_=normw[:, :]).then_inc(s_w.h, 16)
    E("gpsimd", f)
    s_w.inc(112)
    W0 = s_w.n

    def f(s_):
        for dst, srct in ((cw, convw), (cb, convb), (dtbr, dtb4),
                          (tri1, triu_i), (tri2, tril_s), (on128, ones128),
                          (on1f, ones1f), (idf, identf), (alog_s, alog),
                          (dvec_s, dvec)):
            s_.dma_start(out=dst[:], in_=srct[:, :] if srct.ndim == 2 else srct[:, :, :]).then_inc(s_ws.h, 16)
    E("sync", f)
    s_ws.inc(16 * 10)
    WS0 = s_ws.n

    def f(sc):
        sc.wait_ge(s_ws.h, WS0)
        sc.activation(out=arow[:, 0:32], in_=alog_s[:], func=AF.Exp, scale=1.0)
        sc.drain()
        sc.mul(out=arow[:, 0:32], in_=arow[:, 0:32], mul=-1.0)
        sc.drain()
        for j in range(1, 4):
            sc.copy(out=arow[:, 32 * j:32 * (j + 1)], in_=arow[:, 0:32])
        sc.drain().then_inc(s_sg.h, 1)
    E("scalar", f)
    s_sg.inc(1)
    SG_A = s_sg.n

    def f(t):
        t.wait_ge(s_sg.h, SG_A)
        t.wait_ge(s_w.h, W0)
        t.matmul(pb[2].ap()[:, 0:128], on1f[:], arow[:], start=True,
                 stop=True).then_inc(s_tg.h, 1)
    E("tensor", f)
    s_tg.inc(1)
    TG_A = s_tg.n

    def f(v):
        v.memset(epsc[:], EPS)
        v.memset(ssb[:], 0.0)
        v.memset(sbf[:], 0.0)
        v.memset(halo[:], 0.0)
        v.wait_ge(s_tg.h, TG_A)
        v.tensor_copy(out=arep[:], in_=pb[2].ap()[:, 0:128])
        v.drain().then_inc(s_vg.h, 1)
    E("vector", f)
    s_vg.inc(1)
    VG_A = s_vg.n

    onesb = p.sbuf("ma_onesb", [1, 128], BF16)

    def f(v):
        v.tensor_copy(out=onesb[:], in_=on1f[:])
        v.drain().then_inc(s_vg.h, 1)
    E("vector", f)
    s_vg.inc(1)
    VG_OB = s_vg.n

    def f(t):
        t.wait_ge(s_vg.h, VG_OB)
        t.matmul(pb[7].ap()[:, 0:32], onesb[:], drow[:], start=True, stop=True).then_inc(s_tg.h, 1)
    E("tensor", f)
    s_tg.inc(1)
    TG_D = s_tg.n

    def f(v):
        v.wait_ge(s_tg.h, TG_D)
        v.tensor_copy(out=drep[:], in_=pb[7].ap()[:, 0:32])
        v.drain().then_inc(s_vg.h, 1)
    E("vector", f)
    s_vg.inc(1)

    p.tape_barrier()

    p.ma = dict(pgrp=0, t34=0, ldiff_done={}, sil={}, zs={}, dt_ready={},
                strip_ld={}, lblk={})
    ctxd = dict(locals())

    def f(v):
        v.tensor_scalar(out=non1f[:], in0=on1f[:], scalar1=-1.0, scalar2=None,
                        op0=ALU.mult)
        v.drain().then_inc(s_vg.h, 1)
    E("vector", f)
    s_vg.inc(1)

    for S in range(NSC):
        _mamba_stage1(p, S, ctxd)
        for cloc in range(CPS):
            g = CPS * S + cloc
            _mamba_scan_chunk(p, S, cloc, g, ctxd)
        p.tape_barrier()

    o0, o1 = s_oo[0].n, s_oo[1].n

    def f(s_):
        s_.wait_ge(s_oo[0].h, o0)
        s_.wait_ge(s_oo[1].h, o1)
    E("sync", f)


def _mamba_stage1(p, S, L):
    E = p.emit
    xn2t, wz, wx, wd = L["xn2t"], L["wz"], L["wx"], L["wd"]
    pb = L["pb"]
    z_sil, xbce, xbcc, halo = L["z_sil"], L["xbce"], L["xbcc"], L["halo"]
    xs_fm, b_fm, c_fm = L["xs_fm"], L["b_fm"], L["c_fm"]
    dt_raw, dtp, dtp_bf, adt = L["dt_raw"], L["dtp"], L["dtp_bf"], L["adt"]
    sp1, sp2 = L["sp1"], L["sp2"]
    ltok, etok, lam, w2bf, w2f = L["ltok"], L["etok"], L["lam"], L["w2bf"], L["w2f"]
    ltt = L["ltt"]
    cw, cb, dtbr, arep = L["cw"], L["cb"], L["dtbr"], L["arep"]
    tri1, tri2, on128, idf = L["tri1"], L["tri2"], L["on128"], L["idf"]
    xn2f, ltdram = L["xn2f"], L["ltdram"]
    s_x, s_pmm, s_pev = L["s_x"], L["s_pmm"], L["s_pev"]
    s_vg, s_sg, s_pg = L["s_vg"], L["s_sg"], L["s_pg"]
    s_v2s, s_s2v = L["s_v2s"], L["s_s2v"]
    s_tg = L["s_tg"]
    s_lts = L["s_lts"]
    s_w, s_ws = L["s_w"], L["s_ws"]
    W0, WS0 = L["W0"], L["WS0"]

    xl = s_x[S % 2].inc(16)

    def f(s_, S=S):
        s_.wait_ge(s_w.h, W0)
        s_.dma_start(out=xn2t[:], in_=xn2f[:, :, S * SCW:(S + 1) * SCW]
                     ).then_inc(s_x[S % 2].h, 16)
    E("sync", f)

    groups = ([("z", i // 4, i % 4) for i in range(4 * CPS)]
              + [("x", cb_, None) for cb_ in range(18)]
              + [("dt", tb, None) for tb in range(CPS)])
    pg0 = p.ma["pgrp"]
    for gi, (kind, a1, a2) in enumerate(groups):
        gidx = pg0 + gi
        bank = pb[gidx % 2]
        need_ev = gidx - 1

        def f(t, kind=kind, a1=a1, a2=a2, bank=bank, xl=xl, need=need_ev, S=S):
            t.wait_ge(s_x[S % 2].h, xl)
            if need > 0:
                t.wait_ge(s_pev.h, need)
            pa = bank.ap()
            if kind == "z":
                for c in range(8):
                    mm = t.matmul(pa[:, 0:512], xn2t[:, c, a1 * 128:(a1 + 1) * 128],
                                  wz[:, c, a2 * 512:(a2 + 1) * 512],
                                  start=(c == 0), stop=(c == 7))
            elif kind == "x":
                for c in range(8):
                    mm = t.matmul(pa[:, 0:SCW], wx[:, c, a1 * 128:(a1 + 1) * 128],
                                  xn2t[:, c, :], start=(c == 0), stop=(c == 7))
            else:
                for c in range(8):
                    mm = t.matmul(pa[:, 0:32], xn2t[:, c, a1 * 128:(a1 + 1) * 128],
                                  wd[:, c, :], start=(c == 0), stop=(c == 7))
            mm.then_inc(s_pmm.h, 1)
        E("tensor", f)
        pmm = s_pmm.inc(1)

        if kind == "z":
            def f(sc, a1=a1, a2=a2, bank=bank, pmm=pmm):
                sc.wait_ge(s_pmm.h, pmm)
                sc.activation(out=z_sil[:, a1, a2 * 512:(a2 + 1) * 512],
                              in_=bank.ap()[:, 0:512],
                              func=AF.Silu).then_inc(s_pev.h, 1)
            E("scalar", f)
            s_pev.inc(1)
        elif kind == "x":
            cc = a1
            par = cc % 2
            conv_prev = p.ma.get("conv_pg", {}).get(cc - 2, 0)
            sil_prev = p.ma.get("sil_sg", {}).get(cc - 2, 0)

            def f(v, cc=cc, par=par, bank=bank, pmm=pmm, cprev=conv_prev):
                v.wait_ge(s_pmm.h, pmm)
                if cprev > 0:
                    v.wait_ge(s_pg.h, cprev)
                v.tensor_copy(out=xbce[par][:, 0:3], in_=halo[:, cc, :])
                v.tensor_copy(out=xbce[par][:, 3:SCW + 3],
                              in_=bank.ap()[:, 0:SCW])
                v.drain()
                v.tensor_copy(out=halo[:, cc, :], in_=xbce[par][:, SCW:SCW + 3])
                v.drain().then_inc(s_pev.h, 1)
            E("vector", f)
            ev = s_pev.inc(1)

            def f(v, cc=cc, par=par, ev=ev, sprev=sil_prev):
                v.wait_ge(s_pev.h, ev)
                if sprev > 0:
                    v.wait_ge(s_sg.h, sprev)
                for k in range(4):
                    win = xbce[par][:, k:k + SCW]
                    wcol = cw[:, cc, k:k + 1]
                    if k == 0:
                        v.tensor_scalar(out=xbcc[par][:, :], in0=win, scalar1=wcol,
                                        scalar2=None, op0=ALU.mult)
                    else:
                        v.scalar_tensor_tensor(out=xbcc[par][:, :], in0=win,
                                               scalar=wcol, in1=xbcc[par][:, :],
                                               op0=ALU.mult, op1=ALU.add)
                    v.drain()
                v.nop().then_inc(s_pg.h, 1)
            E("vector", f)
            p.ma.setdefault("conv_pg", {})[cc] = s_pg.inc(1)

            def f(sc, cc=cc, par=par, pg=p.ma["conv_pg"][cc]):
                sc.wait_ge(s_pg.h, pg)
                sc.wait_ge(s_ws.h, WS0)
                if cc < 16:
                    dst = xs_fm[:, cc, :]
                elif cc == 16:
                    dst = b_fm[:, :]
                else:
                    dst = c_fm[:, :]
                sc.activation(out=dst, in_=xbcc[par][:, :], func=AF.Silu,
                              bias=cb[:, cc:cc + 1]).then_inc(s_sg.h, 1)
            E("scalar", f)
            p.ma.setdefault("sil_sg", {})[cc] = s_sg.inc(1)
        else:
            def f(v, a1=a1, bank=bank, pmm=pmm):
                v.wait_ge(s_pmm.h, pmm)
                v.tensor_copy(out=dt_raw[:, a1, :],
                              in_=bank.ap()[:, 0:32]).then_inc(s_pev.h, 1)
            E("vector", f)
            s_pev.inc(1)
    p.ma["pgrp"] += len(groups)
    pev_all = s_pev.n
    p.ma["sil"][S] = s_sg.n

    def f(v, pev=pev_all):
        v.wait_ge(s_pev.h, pev)
        v.tensor_tensor(out=sp1[:], in0=dt_raw[:].rearrange("p a b -> p (a b)"),
                        in1=dtbr[:], op=ALU.add)
        v.drain()
        v.tensor_scalar(out=dtp[:], in0=sp1[:], scalar1=0.0, scalar2=None,
                        op0=ALU.max)
        v.drain().then_inc(s_v2s.h, 1)
    E("vector", f)
    v2s = s_v2s.inc(1)

    def f(sc, v2s=v2s):
        sc.wait_ge(s_v2s.h, v2s)
        sc.activation(out=sp2[:], in_=sp1[:], func=AF.Abs)
        sc.drain()
        sc.activation(out=sp2[:], in_=sp2[:], func=AF.Exp, scale=-1.0)
        sc.drain()
        sc.activation(out=sp2[:], in_=sp2[:], func=AF.Ln, bias=1.0)
        sc.drain().then_inc(s_s2v.h, 1)
    E("scalar", f)
    s2v = s_s2v.inc(1)

    def f(v, s2v=s2v):
        v.wait_ge(s_s2v.h, s2v)
        v.tensor_tensor(out=dtp[:], in0=dtp[:], in1=sp2[:], op=ALU.add)
        v.drain()
        v.tensor_tensor(out=adt[:], in0=dtp[:], in1=arep[:, 0:CPS * 32], op=ALU.mult)
        v.tensor_copy(out=dtp_bf[:, :, :],
                      in_=dtp[:].rearrange("p (a b) -> p a b", a=CPS, b=32))
        v.drain().then_inc(s_vg.h, 1)
    E("vector", f)
    dt_vg = s_vg.inc(1)
    p.ma["dt_ready"][S] = dt_vg

    for cloc in range(CPS):
        gch = CPS * S + cloc

        def f(t, cloc=cloc, dt_vg=dt_vg, pv=p.ma.get("ltev_vg", 0),
              ps=p.ma.get("lexp_sg", 0)):
            t.wait_ge(s_vg.h, dt_vg)
            if pv > 0:
                t.wait_ge(s_vg.h, pv)
            if ps > 0:
                t.wait_ge(s_sg.h, ps)
            asl = adt[:, cloc * 32:(cloc + 1) * 32]
            t.matmul(pb[2].ap()[:, 0:32], tri1[:], asl, start=True,
                     stop=True).then_inc(s_tg.h, 1)
            t.matmul(pb[7].ap()[:, 0:32], tri2[:], asl, start=True, stop=False)
            t.matmul(pb[7].ap()[:, 32:64], on128[:], asl, start=False,
                     stop=True).then_inc(s_tg.h, 1)
        E("tensor", f)
        tg_l = s_tg.inc(2)

        def f(v, tg=tg_l - 1):
            v.wait_ge(s_tg.h, tg)
            v.tensor_copy(out=ltok[:], in_=pb[2].ap()[:, 0:32])
            v.drain().then_inc(s_vg.h, 1)
        E("vector", f)
        ltok_vg = s_vg.inc(1)

        def f(sc, cloc=cloc, tg=tg_l, w2prev=p.ma.get("w2_vg", 0)):
            sc.wait_ge(s_tg.h, tg - 1)
            sc.activation(out=etok[:, cloc, :], in_=pb[2].ap()[:, 0:32], func=AF.Exp)
            sc.drain()
            sc.wait_ge(s_tg.h, tg)
            if w2prev > 0:
                sc.wait_ge(s_vg.h, w2prev)
            sc.activation(out=w2f[:, :], in_=pb[7].ap()[:, 0:32], func=AF.Exp)
            sc.drain()
            sc.activation(out=lam[:, cloc, :], in_=pb[7].ap()[:, 32:64], func=AF.Exp)
            sc.drain().then_inc(s_sg.h, 1)
        E("scalar", f)
        lexp_sg = s_sg.inc(1)
        p.ma["lexp_sg"] = lexp_sg

        def f(v, cloc=cloc, sg=lexp_sg):
            v.wait_ge(s_sg.h, sg)
            v.tensor_tensor(out=w2bf[:, cloc, :], in0=w2f[:, :],
                            in1=dtp[:, cloc * 32:(cloc + 1) * 32], op=ALU.mult)
            v.drain().then_inc(s_vg.h, 1)
        E("vector", f)
        p.ma["w2_vg"] = s_vg.inc(1)

        def f(t, vg=ltok_vg, sg=lexp_sg):
            t.wait_ge(s_vg.h, vg)
            t.wait_ge(s_sg.h, sg)
            t.transpose(pb[2].ap()[0:32, 0:128], ltok[:], idf[:]).then_inc(s_tg.h, 1)
        E("tensor", f)
        tg_t = s_tg.inc(1)

        def f(v, tg=tg_t, pstore=p.ma.get("lt_store", None)):
            v.wait_ge(s_tg.h, tg)
            if pstore is not None:
                v.wait_ge(pstore[0], pstore[1])
            v.tensor_copy(out=ltt[:], in_=pb[2].ap()[0:32, 0:128])
            v.drain().then_inc(s_vg.h, 1)
        E("vector", f)
        ltev_vg = s_vg.inc(1)
        p.ma["ltev_vg"] = ltev_vg
        p.ma["lblk"][gch] = (tg_t, ltev_vg)

        def f(gp, gch=gch, vg=ltev_vg):
            gp.wait_ge(s_vg.h, vg)
            gp.dma_start(out=ltdram[gch, 0, :, :], in_=ltt[:]).then_inc(s_lts[gch % 2].h, 16)
        E("gpsimd", f)
        s_lts[gch % 2].inc(16)
        p.ma["strip_ld"][gch] = s_lts[gch % 2].n
        p.ma["lt_store"] = (s_lts[gch % 2].h, s_lts[gch % 2].n)


def _mamba_scan_chunk(p, S, cloc, g, L):
    E = p.emit
    pb, pbb = L["pb"], L["pbb"]
    xs_fm, b_fm, c_fm = L["xs_fm"], L["b_fm"], L["c_fm"]
    btok, xtok, xw1, xw2 = L["btok"], L["xtok"], L["xw1"], L["xw2"]
    dtp_bf, w2bf, etok, lam = L["dtp_bf"], L["w2bf"], L["etok"], L["lam"]
    pstr = L["pstr"]
    decf, decb, sct, g2m = L["decf"], L["decb"], L["sct"], L["g2m"]
    ych, ssb, sbf = L["ych"], L["ssb"], L["sbf"]
    ysq = L["ysq"]
    gfm, oout = L["gfm"], L["oout"]
    gbf = L["xw1"]
    z_sil, nwr, drep, epsc = L["z_sil"], L["nwr"], L["drep"], L["epsc"]
    tri1, idb, on1f, idf, wo = L["tri1"], L["idb"], L["on1f"], L["idf"], L["wo"]
    non1f = L["non1f"]
    ltdram, brA_out = L["ltdram"], L["brA_out"]
    s_tg, s_vg, s_sg = L["s_tg"], L["s_vg"], L["s_sg"]
    s_t34, s_e34 = L["s_t34"], L["s_e34"]
    s_v2s, s_s2v = L["s_v2s"], L["s_s2v"]
    s_lts, s_ltl, s_oo = L["s_lts"], L["s_ltl"], L["s_oo"]
    sil_sg = p.ma["sil"][S]
    dt_vg = p.ma["dt_ready"][S]
    par = g % 2

    need_ld = p.ma["ldiff_done"].get(g - 1, 0)
    st_cnt = p.ma["strip_ld"][g]

    def f(s_, g=g, par=par, need=need_ld, st=st_cnt):
        s_.wait_ge(s_lts[par].h, st)
        if need > 0:
            s_.wait_ge(s_t34.h, need)
        s_.dma_start(out=pstr[:, :],
                     in_=ltdram[g, 0, :, :].rearrange("a b -> (a b)").unsqueeze(0)
                     ).then_inc(s_ltl.h, 16)
    E("sync", f)
    ltl = s_ltl.inc(16)

    def t34_op(body, extra_waits=()):
        i = p.ma["t34"]
        p.ma["t34"] += 1
        bank = pb[3 + (i % 2)]
        bb = pbb[3 + (i % 2)]
        need = i - 1

        def f(t, bank=bank, bb=bb, need=need, ws=tuple(extra_waits)):
            for (sh, sv) in ws:
                t.wait_ge(sh, sv)
            if need > 0:
                t.wait_ge(s_e34.h, need)
            body(t, bank.ap(), bb)
        E("tensor", f)
        s_t34.inc(1)
        return s_t34.n, bank, bb

    def ev34(engine, body, tcount, extra_waits=()):
        def f(e, tcount=tcount, ws=tuple(extra_waits)):
            e.wait_ge(s_t34.h, tcount)
            for (sh, sv) in ws:
                e.wait_ge(sh, sv)
            body(e)
        E(engine, f)
        s_e34.inc(1)
        return s_e34.n

    def body(t, pa, bb):
        t.transpose(bb[:, 0:128], b_fm[:, cloc * 128:(cloc + 1) * 128],
                    idb[:]).then_inc(s_t34.h, 1)
    tc, _, bb_ = t34_op(body, extra_waits=[(s_sg.h, sil_sg)])

    def b(v, bb_=bb_, btfree=p.ma.get("xw_free", 0)):
        if btfree > 0:
            v.wait_ge(s_tg.h, btfree)
        v.tensor_copy(out=btok[:, :], in_=bb_[:, 0:128]).then_inc(s_e34.h, 1)
    ev34("vector", b, tc)

    for cc in range(16):
        def body(t, pa, bb, cc=cc):
            t.transpose(bb[:, 0:128], xs_fm[:, cc, cloc * 128:(cloc + 1) * 128],
                        idb[:]).then_inc(s_t34.h, 1)
        tc, _, bb_ = t34_op(body)

        def b(v, cc=cc, bb_=bb_):
            v.tensor_copy(out=xtok[:, cc * 128:(cc + 1) * 128],
                          in_=bb_[:, 0:128]).then_inc(s_e34.h, 1)
        ev34("vector", b, tc)
    xtok_ev = s_e34.n

    def f(v, cloc=cloc, ev=xtok_ev, dvg=dt_vg, w2vg=p.ma["w2_vg"],
          xwfree=p.ma.get("xw_free", 0)):
        v.wait_ge(s_e34.h, ev)
        if xwfree > 0:
            v.wait_ge(s_tg.h, xwfree)
        v.wait_ge(s_vg.h, max(dvg, w2vg))
        xv = xtok[:].rearrange("p (h d) -> p h d", h=32)
        v.tensor_tensor(out=xw1[:].rearrange("p (h d) -> p h d", h=32), in0=xv,
                        in1=dtp_bf[:, cloc, :].unsqueeze(2).broadcast_to([128, 32, 64]),
                        op=ALU.mult)
        v.tensor_tensor(out=xw2[:].rearrange("p (h d) -> p h d", h=32), in0=xv,
                        in1=w2bf[:, cloc, :].unsqueeze(2).broadcast_to([128, 32, 64]),
                        op=ALU.mult)
        v.drain().then_inc(s_vg.h, 1)
    E("vector", f)
    xw_vg = s_vg.inc(1)

    def f(t, cloc=cloc, sg=sil_sg, lblk=p.ma["lblk"][CPS * S + CPS - 1]):
        t.wait_ge(s_sg.h, sg)
        t.wait_ge(s_vg.h, lblk[1])
        t.matmul(pb[2].ap()[:, 0:128], b_fm[:, cloc * 128:(cloc + 1) * 128],
                 c_fm[:, cloc * 128:(cloc + 1) * 128], start=True,
                 stop=True).then_inc(s_tg.h, 1)
    E("tensor", f)
    g2_tg = s_tg.inc(1)

    def f(v, tg=g2_tg):
        v.wait_ge(s_tg.h, tg)
        v.tensor_tensor(out=g2m[:], in0=pb[2].ap()[:, 0:128], in1=tri1[:],
                        op=ALU.mult)
        v.drain().then_inc(s_vg.h, 1)
    E("vector", f)
    g2_vg = s_vg.inc(1)

    sbf_ready = p.ma.get("sbf_vg", 0)
    pb5_free0 = p.ma.get("ds_ev", 0)
    for j in range(4):
        evn = pb5_free0 if j == 0 else p.ma.get("yev", 0)

        def f(t, j=j, sr=sbf_ready, evn=evn, sg=sil_sg):
            t.wait_ge(s_sg.h, sg)
            if sr > 0:
                t.wait_ge(s_vg.h, sr)
            if evn > 0:
                t.wait_ge(s_vg.h, evn)
            t.matmul(pb[5].ap()[:, 0:512], c_fm[:, cloc * 128:(cloc + 1) * 128],
                     sbf[:, j * 512:(j + 1) * 512], start=True,
                     stop=True).then_inc(s_tg.h, 1)
        E("tensor", f)
        tgy = s_tg.inc(1)

        def f(v, j=j, tgy=tgy, cloc=cloc):
            v.wait_ge(s_tg.h, tgy)
            v.tensor_tensor(out=ych[:, j * 512:(j + 1) * 512]
                            .rearrange("p (a b) -> p a b", a=8),
                            in0=pb[5].ap()[:, 0:512].rearrange("p (a b) -> p a b", a=8),
                            in1=etok[:, cloc, j * 8:(j + 1) * 8].unsqueeze(2)
                            .broadcast_to([128, 8, 64]),
                            op=ALU.mult)
            v.drain().then_inc(s_vg.h, 1)
        E("vector", f)
        p.ma["yev"] = s_vg.inc(1)

    for hg in range(8):
        def body(t, pa, bb, hg=hg, ltl=ltl):
            t.wait_ge(s_ltl.h, ltl)
            for j in range(4):
                h = 4 * hg + j
                sl = slice(j * 128, (j + 1) * 128)
                t.matmul(pa[:, sl], on1f[:], pstr[:, h * 128:(h + 1) * 128],
                         start=True, stop=False)
                mm = t.matmul(pa[:, sl], pstr[:, h * 128:(h + 1) * 128],
                              non1f[:], start=False, stop=True)
            mm.then_inc(s_t34.h, 1)
        tc, bank, _ = t34_op(body)

        def b(v, bank=bank, prev_s2v=s_s2v.n):
            if prev_s2v > 0:
                v.wait_ge(s_s2v.h, prev_s2v)
            v.tensor_scalar(out=decf[:], in0=bank.ap()[:, 0:512], scalar1=0.0,
                            scalar2=None, op0=ALU.min)
            v.drain().then_inc(s_e34.h, 1)
        ev34("vector", b, tc)

        def f(v):
            v.nop().then_inc(s_v2s.h, 1)
        E("vector", f)
        v2s = s_v2s.inc(1)

        def f(sc, v2s=v2s, prev_sct=s_vg.n):
            sc.wait_ge(s_v2s.h, v2s)
            sc.wait_ge(s_vg.h, prev_sct)
            sc.activation(out=decb[:], in_=decf[:], func=AF.Exp)
            sc.drain().then_inc(s_s2v.h, 1)
        E("scalar", f)
        s2v = s_s2v.inc(1)

        sfree = p.ma.get("yintra_mm", {}).get(hg - 2 if hg >= 2 else -99,
                                              p.ma.get("yintra_last", 0))

        def f(v, hg=hg, s2v=s2v, gvg=g2_vg, sfree=sfree):
            v.wait_ge(s_s2v.h, s2v)
            v.wait_ge(s_vg.h, gvg)
            if sfree > 0:
                v.wait_ge(s_tg.h, sfree)
            v.tensor_tensor(out=sct[hg % 2][:].rearrange("p (a b) -> p a b", a=4),
                            in0=decb[:].rearrange("p (a b) -> p a b", a=4),
                            in1=g2m[:].unsqueeze(1).broadcast_to([128, 4, 128]),
                            op=ALU.mult)
            v.drain().then_inc(s_vg.h, 1)
        E("vector", f)
        sct_vg = s_vg.inc(1)

        j = hg // 2
        pb6_free = p.ma.get("yintra_ev", 0) if hg % 2 == 0 else 0

        def f(t, hg=hg, sv=sct_vg, xwv=xw_vg, pbf=pb6_free):
            t.wait_ge(s_vg.h, max(sv, xwv))
            if pbf > 0 and hg % 2 == 0:
                t.wait_ge(s_vg.h, pbf)
            base = (hg % 2) * 256
            for hh in range(4):
                h = 4 * hg + hh
                mm = t.matmul(pb[6].ap()[:, base + hh * 64:base + (hh + 1) * 64],
                              sct[hg % 2][:, hh * 128:(hh + 1) * 128],
                              xw1[:, h * 64:(h + 1) * 64],
                              start=(hg % 2 == 0 and hh == 0),
                              stop=(hg % 2 == 1 and hh == 3))
            mm.then_inc(s_tg.h, 1)
        E("tensor", f)
        p.ma.setdefault("yintra_mm", {})[hg] = s_tg.inc(1)

        if hg % 2 == 1:
            def f(v, j=j, tgi=p.ma["yintra_mm"][hg]):
                v.wait_ge(s_tg.h, tgi)
                v.scalar_tensor_tensor(out=ych[:, j * 512:(j + 1) * 512],
                                       in0=pb[6].ap()[:, 0:512], scalar=1.0,
                                       in1=ych[:, j * 512:(j + 1) * 512],
                                       op0=ALU.mult, op1=ALU.add)
                v.drain().then_inc(s_vg.h, 1)
            E("vector", f)
            p.ma["yintra_ev"] = s_vg.inc(1)
    y_vg = s_vg.n
    p.ma["yintra_last"] = p.ma["yintra_mm"][7]
    p.ma["ldiff_done"][g] = s_t34.n

    def f(v, ev=xtok_ev, yv=y_vg):
        v.wait_ge(s_vg.h, yv)
        v.tensor_tensor(out=ysq[:].rearrange("p (h d) -> p h d", h=32),
                        in0=xtok[:].rearrange("p (h d) -> p h d", h=32),
                        in1=drep[:].unsqueeze(2).broadcast_to([128, 32, 64]),
                        op=ALU.mult)
        v.drain()
        v.tensor_tensor(out=ych[:], in0=ych[:], in1=ysq[:], op=ALU.add)
        v.drain().then_inc(s_vg.h, 1)
    E("vector", f)
    yd_vg = s_vg.inc(1)

    for j in range(4):
        def f(t, j=j, xwv=xw_vg, yev=p.ma["yev"], dsev=p.ma.get("ds_ev", 0)):
            t.wait_ge(s_vg.h, xwv)
            t.wait_ge(s_vg.h, yev)
            if dsev > 0:
                t.wait_ge(s_vg.h, dsev)
            t.matmul(pb[5].ap()[:, 0:512], btok[:, :],
                     xw2[:, j * 512:(j + 1) * 512], start=True,
                     stop=True).then_inc(s_tg.h, 1)
        E("tensor", f)
        tgd = s_tg.inc(1)

        def f(v, j=j, tgd=tgd, cloc=cloc):
            v.wait_ge(s_tg.h, tgd)
            v.tensor_tensor(out=ssb[:, j * 512:(j + 1) * 512]
                            .rearrange("p (a b) -> p a b", a=8),
                            in0=ssb[:, j * 512:(j + 1) * 512]
                            .rearrange("p (a b) -> p a b", a=8),
                            in1=lam[:, cloc, j * 8:(j + 1) * 8].unsqueeze(2)
                            .broadcast_to([128, 8, 64]),
                            op=ALU.mult)
            v.drain()
            v.scalar_tensor_tensor(out=ssb[:, j * 512:(j + 1) * 512],
                                   in0=pb[5].ap()[:, 0:512], scalar=1.0,
                                   in1=ssb[:, j * 512:(j + 1) * 512],
                                   op0=ALU.mult, op1=ALU.add)
            v.drain().then_inc(s_vg.h, 1)
        E("vector", f)
        p.ma["ds_ev"] = s_vg.inc(1)
    p.ma["xw_free"] = s_tg.n

    def f(v):
        v.tensor_copy(out=sbf[:], in_=ssb[:])
        v.drain().then_inc(s_vg.h, 1)
    E("vector", f)
    p.ma["sbf_vg"] = s_vg.inc(1)

    def f(v, cloc=cloc, yv=yd_vg):
        v.wait_ge(s_vg.h, yv)
        v.tensor_tensor(out=ych[:], in0=ych[:], in1=z_sil[:, cloc, :], op=ALU.mult)
        v.drain().then_inc(s_v2s.h, 1)
    E("vector", f)
    v2s_g = s_v2s.inc(1)

    def f(sc, v2s=v2s_g):
        sc.wait_ge(s_v2s.h, v2s)
        sc.activation(out=ysq[:], in_=ych[:], func=AF.Square,
                      accum_out=L["w2f"][:, 0:1])
        sc.drain()
        sc.activation(out=L["w2f"][:, 1:2], in_=L["w2f"][:, 0:1], func=AF.Sqrt,
                      scale=1.0 / 2048, bias=epsc[:, 0:1])
        sc.drain().then_inc(s_s2v.h, 1)
    E("scalar", f)
    s2v_g = s_s2v.inc(1)

    def f(v, s2v=s2v_g):
        v.wait_ge(s_s2v.h, s2v)
        v.reciprocal(out=L["w2f"][:, 2:3], in_=L["w2f"][:, 1:2])
        v.drain()
        v.tensor_scalar(out=ych[:], in0=ych[:], scalar1=L["w2f"][:, 2:3],
                        scalar2=None, op0=ALU.mult)
        v.drain()
        v.tensor_tensor(out=gbf[:], in0=ych[:], in1=nwr[:], op=ALU.mult)
        v.drain().then_inc(s_vg.h, 1)
    E("vector", f)
    gbf_vg = s_vg.inc(1)

    for cc in range(16):
        def body(t, pa, bb, cc=cc, gv=gbf_vg):
            t.wait_ge(s_vg.h, gv)
            t.transpose(bb[:, 0:128], gbf[:, cc * 128:(cc + 1) * 128],
                        idb[:]).then_inc(s_t34.h, 1)
        tc, _, bb_ = t34_op(body)

        def b(v, cc=cc, bb_=bb_):
            v.tensor_copy(out=gfm[:, cc, :], in_=bb_[:, 0:128]).then_inc(s_e34.h, 1)
        ev34("vector", b, tc)
    gfm_ev = s_e34.n

    opar = 0
    for dt_ in range(2):
        def body(t, pa, bb, dt_=dt_, gv=gfm_ev, oo=p.ma.get("oo_done", {}).get((g - 2, dt_), 0)):
            t.wait_ge(s_e34.h, gv)
            for cc in range(16):
                mm = t.matmul(pa[:, 0:512], gfm[:, cc, :],
                              wo[:, cc, dt_ * 512:(dt_ + 1) * 512],
                              start=(cc == 0), stop=(cc == 15))
            mm.then_inc(s_t34.h, 1)
        tc, bank, _ = t34_op(body)

        def b(v, dt_=dt_, bank=bank, opar=opar, pre=s_oo[opar].n):
            if pre > 0:
                v.wait_ge(s_oo[opar].h, pre)
            v.tensor_copy(out=oout[opar][:, dt_ * 512:(dt_ + 1) * 512],
                          in_=bank.ap()[:, 0:512]).then_inc(s_e34.h, 1)
        ev34("vector", b, tc)
    oev = s_e34.n

    def f(s_, g=g, opar=opar, oev=oev):
        s_.wait_ge(s_e34.h, oev)
        s_.dma_start(out=brA_out[g * 128:(g + 1) * 128, :],
                     in_=oout[opar][:]).then_inc(s_oo[opar].h, 16)
    E("sync", f)
    s_oo[opar].inc(16)


def build_mamba():
    ctx = contextlib.ExitStack()
    with ctx:
        nc = bass.Bass("TRN2", target_bir_lowering=False, debug=False)
        p = Prog(ctx, nc)
        d = {}
        d["xn2f"] = nc.dram_tensor("xn2f", [128, 8, T], BF16, kind="ExternalInput").ap()
        for nm, shape in [("wzT", [1024, 2048]), ("wxT", [1024, 2304]),
                          ("wdT", [1024, 32]), ("woT", [2048, 1024]),
                          ("convw", [128, 18, 4]), ("convb", [128, 18]),
                          ("dtb4", [128, CPS * 32]), ("alog", [1, 32]),
                          ("dvec", [1, 32]), ("normw", [128, 2048]),
                          ("triu_i", [128, 128]), ("tril_s", [128, 128]),
                          ("ones128", [128, 128]), ("ones1f", [1, 128]),
                          ("identf", [128, 128]), ("identd", [128, 128])]:
            d[nm] = nc.dram_tensor(nm, shape, F32, kind="ExternalInput").ap()
        d["ltdram"] = nc.dram_tensor("ltdram", [32, 2, 32, 128], F32, kind="Internal").ap()
        d["brA_out"] = nc.dram_tensor("brA", [T, D], F32, kind="ExternalOutput").ap()
        p.bar = p.sem("bar")
        p.tape_init()
        emit_mamba(p, **d)
        with nc.Block() as block:
            p.block = block
            p.flush()
        return nc


def mamba_host_layouts(inp, pref="L0_attn_"):
    ipw = inp[pref + "in_proj_w"]
    e = np.eye(128, dtype=np.float32)
    return dict(
        wzT=np.ascontiguousarray(ipw[0:2048].T),
        wxT=np.ascontiguousarray(ipw[2048:4352].T),
        wdT=np.ascontiguousarray(ipw[4352:4384].T),
        woT=np.ascontiguousarray(inp[pref + "out_proj_w"].T),
        convw=np.ascontiguousarray(inp[pref + "conv_w"].reshape(18, 128, 4).transpose(1, 0, 2)),
        convb=np.ascontiguousarray(inp[pref + "conv_b"].reshape(18, 128).T),
        dtb4=np.ascontiguousarray(np.broadcast_to(np.tile(inp[pref + "dt_bias"], CPS), (128, CPS * 32))).astype(np.float32),
        alog=inp[pref + "A_log"].reshape(1, 32).astype(np.float32),
        dvec=inp[pref + "D"].reshape(1, 32).astype(np.float32),
        normw=np.ascontiguousarray(np.broadcast_to(inp[pref + "norm_w"], (128, 2048))).astype(np.float32),
        triu_i=np.triu(np.ones((128, 128), np.float32)),
        tril_s=np.tril(np.ones((128, 128), np.float32), -1),
        ones128=np.ones((128, 128), np.float32),
        ones1f=np.ones((1, 128), np.float32),
        identf=e, identd=e,
    )



def emit_l1attn(p, xn1f, wqT, wkT, wvT, woT, bq, bk, bv, bo, cosT, sinT,
                ones1f, identd, cmaskT, colones, qf, kf, vf, aof, brA1):
    nc = p.nc
    E = p.emit
    SCALE = 0.125

    wo = p.sbuf("la_wo", [128, 8, 1024], BF16)
    bqs = p.sbuf("la_bq", [1, 1024], BF16)
    bks = p.sbuf("la_bk", [1, 1024], BF16)
    bvs = p.sbuf("la_bv", [1, 1024], BF16)
    bos = p.sbuf("la_bo", [1, 1024], BF16)
    on1b = p.sbuf("la_on1b", [1, 128], BF16)
    onecol = p.sbuf("la_onecol", [64, 1], BF16)
    idb = p.sbuf("la_idb", [128, 128], BF16)
    cmask = p.sbuf("la_cmask", [128, 4, 512], BF16)
    onescol = p.sbuf("la_onescol", [128, 1], BF16)

    stA = contextlib.ExitStack()

    def sbA(nm, shape, dt):
        return stA.enter_context(nc.sbuf_tensor(nm, shape, dt))

    wq = sbA("la_wq", [128, 8, 1024], BF16)
    wk = sbA("la_wk", [128, 8, 1024], BF16)
    wv = sbA("la_wv", [128, 8, 1024], BF16)
    coss = sbA("la_cos", [128, 32, 32], F32)
    sins = sbA("la_sin", [128, 32, 32], F32)
    xt = [sbA(f"la_xt{i}", [128, 8, 128], BF16) for i in range(2)]
    qtok = sbA("la_qtok", [128, 1024], F32)
    rsc = sbA("la_rsc", [128, 4, 512], F32)
    rot = sbA("la_rot", [128, 1024], BF16)
    stg = sbA("la_stg", [128, 8, 128], BF16)

    pb = [p.psum_bank(f"la_p{i}") for i in range(8)]
    pbb = [b.ap().bitcast(BF16) for b in pb]

    s_w = SemC(p.sem("la_w"))
    s_ws = SemC(p.sem("la_ws"))
    s_x = [SemC(p.sem(f"la_x{i}")) for i in range(2)]
    s_mm = SemC(p.sem("la_mm"))
    s_ev = SemC(p.sem("la_ev"))
    s_vg = SemC(p.sem("la_vg"))
    s_sg = SemC(p.sem("la_sg"))
    s_st = {(w_, i): SemC(p.sem(f"la_st{w_}{i}")) for w_ in range(3) for i in range(2)}
    s_hl = SemC(p.sem("la_hl"))
    s_ao = SemC(p.sem("la_ao"))
    s_al = [SemC(p.sem(f"la_al{i}")) for i in range(2)]
    s_oo = [SemC(p.sem(f"la_oo{i}")) for i in range(2)]

    def f(g):
        for dst, srct in ((wq, wqT), (wk, wkT), (wv, wvT), (wo, woT)):
            g.dma_start(out=dst[:], in_=srct.rearrange("(c p) z -> p c z", p=128)
                        ).then_inc(s_w.h, 16)
        for dst, srct in ((bqs, bq), (bks, bk), (bvs, bv), (bos, bo)):
            g.dma_start(out=dst[:], in_=srct[:, :]).then_inc(s_w.h, 16)
        g.dma_start(out=idb[:], in_=identd[:, :]).then_inc(s_w.h, 16)
        g.dma_start(out=on1b[:], in_=ones1f[:, :]).then_inc(s_w.h, 16)
        g.dma_start(out=onecol[:], in_=ones1f[:, 0:64].rearrange("a b -> b a")
                    ).then_inc(s_w.h, 16)
        g.dma_start(out=cmask[:], in_=cmaskT[:, :, :]).then_inc(s_w.h, 16)
        g.dma_start(out=onescol[:], in_=colones[:, :]).then_inc(s_w.h, 16)
    E("gpsimd", f)
    s_w.inc(16 * 13)
    W0 = s_w.n

    def f(s_):
        s_.dma_start(out=coss[:], in_=cosT[:, :, :]).then_inc(s_ws.h, 16)
        s_.dma_start(out=sins[:], in_=sinT[:, :, :]).then_inc(s_ws.h, 16)
    E("sync", f)
    s_ws.inc(32)
    WS0 = s_ws.n

    s_tev = SemC(p.sem("la_tev"))
    ntr = [0]
    nqkv = [0]
    xt_used = {0: 0, 1: 0}
    last_store = [None]
    last_vstore = [None]

    for tb in range(NTB):
        par = tb % 2

        def f(s_, tb=tb, par=par, need=xt_used[par]):
            s_.wait_ge(s_w.h, W0)
            if need > 0:
                s_.wait_ge(s_mm.h, need)
            s_.dma_start(out=xt[par][:], in_=xn1f[:, :, tb * 128:(tb + 1) * 128]
                         ).then_inc(s_x[par].h, 16)
        E("sync", f)
        xl = s_x[par].inc(16)

        for wi, (wt, bt, dst_hbm) in enumerate(((wq, bqs, qf), (wk, bks, kf),
                                                (wv, bvs, vf))):
            for half in range(2):
                gi = nqkv[0]
                nqkv[0] += 1
                bank = gi % 4

                def f(t, wt=wt, bt=bt, par=par, half=half, xl=xl, bank=bank,
                      evn=s_ev.n - 3):
                    t.wait_ge(s_x[par].h, xl)
                    if evn > 0:
                        t.wait_ge(s_ev.h, evn)
                    pa = pb[bank].ap()
                    for c in range(8):
                        t.matmul(pa[:, 0:512], xt[par][:, c, :],
                                 wt[:, c, half * 512:(half + 1) * 512],
                                 start=(c == 0), stop=False)
                    t.matmul(pa[:, 0:512], on1b[:],
                             bt[:, half * 512:(half + 1) * 512],
                             start=False, stop=True).then_inc(s_mm.h, 1)
                E("tensor", f)
                mm = s_mm.inc(1)

                def f(v, half=half, mm=mm, bank=bank):
                    v.wait_ge(s_mm.h, mm)
                    v.tensor_copy(out=qtok[:, half * 512:(half + 1) * 512],
                                  in_=pb[bank].ap()[:, 0:512]).then_inc(s_ev.h, 1)
                E("vector", f)
                s_ev.inc(1)
            xt_used[par] = s_mm.n

            def f(v, wi=wi, tb=tb, ev=s_ev.n, lvs=last_vstore[0]):
                v.wait_ge(s_ws.h, WS0)
                v.wait_ge(s_ev.h, ev)
                if wi == 0 and lvs is not None:
                    v.wait_ge(lvs[0], lvs[1])
                if wi < 2:
                    qv = qtok[:].rearrange("p (h i two) -> p h i two", h=16, i=32)
                    cb_ = coss[:, tb, :].unsqueeze(1).broadcast_to([128, 16, 32])
                    sb_ = sins[:, tb, :].unsqueeze(1).broadcast_to([128, 16, 32])
                    A = rsc[:, 0, :].rearrange("p (h i) -> p h i", h=16)
                    B = rsc[:, 1, :].rearrange("p (h i) -> p h i", h=16)
                    C = rsc[:, 2, :].rearrange("p (h i) -> p h i", h=16)
                    Dd = rsc[:, 3, :].rearrange("p (h i) -> p h i", h=16)
                    rv = rot[:].rearrange("p (h i two) -> p h i two", h=16, i=32)
                    v.tensor_tensor(out=A, in0=qv[:, :, :, 0], in1=cb_, op=ALU.mult)
                    v.tensor_tensor(out=C, in0=qv[:, :, :, 0], in1=sb_, op=ALU.mult)
                    v.tensor_tensor(out=B, in0=qv[:, :, :, 1], in1=sb_, op=ALU.mult)
                    v.tensor_tensor(out=Dd, in0=qv[:, :, :, 1], in1=cb_, op=ALU.mult)
                    v.drain()
                    v.tensor_tensor(out=rv[:, :, :, 0], in0=A, in1=B, op=ALU.subtract)
                    v.tensor_tensor(out=rv[:, :, :, 1], in0=C, in1=Dd, op=ALU.add)
                else:
                    v.tensor_copy(out=rot[:], in_=qtok[:])
                v.drain().then_inc(s_vg.h, 1)
            E("vector", f)
            rope_vg = s_vg.inc(1)

            if wi < 2:
                for c in range(8):
                    i = ntr[0]
                    ntr[0] += 1
                    bank = 4 + (i % 2)

                    def f(t, c=c, bank=bank, vg=rope_vg, need=i - 1):
                        t.wait_ge(s_vg.h, vg)
                        if need > 0:
                            t.wait_ge(s_tev.h, need)
                        t.transpose(pbb[bank][:, 0:128], rot[:, c * 128:(c + 1) * 128],
                                    idb[:]).then_inc(s_mm.h, 1)
                    E("tensor", f)
                    trm = s_mm.inc(1)

                    def f(v, c=c, bank=bank, trm=trm, ls=last_store[0]):
                        v.wait_ge(s_mm.h, trm)
                        if c == 0 and ls is not None:
                            v.wait_ge(ls[0], ls[1])
                        v.tensor_copy(out=stg[:, c, :],
                                      in_=pbb[bank][:, 0:128]).then_inc(s_tev.h, 1)
                    E("vector", f)
                    s_tev.inc(1)
                stv = s_tev.n

                def f(g, tb=tb, dst=dst_hbm, stv=stv, wi=wi, par=par):
                    g.wait_ge(s_tev.h, stv)
                    g.dma_start(out=dst[:, :, tb * 128:(tb + 1) * 128],
                                in_=stg[:]).then_inc(s_st[(wi, par)].h, 16)
                E("gpsimd", f)
                s_st[(wi, par)].inc(16)
                last_store[0] = (s_st[(wi, par)].h, s_st[(wi, par)].n)
            else:
                def f(g, tb=tb, vg=rope_vg, par=par):
                    g.wait_ge(s_vg.h, vg)
                    g.dma_start(out=vf[:, tb, :],
                                in_=rot[:]).then_inc(s_st[(2, par)].h, 16)
                E("gpsimd", f)
                s_st[(2, par)].inc(16)
                last_vstore[0] = (s_st[(2, par)].h, s_st[(2, par)].n)

    p.tape_barrier()
    p.ma["t34"] = 0

    st_tot = [(s.h, s.n) for s in s_st.values()]

    def f(s_, st_tot=st_tot):
        for hh, nn in st_tot:
            s_.wait_ge(hh, nn)
    E("sync", f)
    p.tape_barrier()

    stA.close()
    stB = contextlib.ExitStack()

    def sbB(nm, shape, dt):
        return stB.enter_context(nc.sbuf_tensor(nm, shape, dt))

    qext = [sbB(f"la_qext{i}", [65, T], BF16) for i in range(2)]
    kext = [sbB(f"la_kext{i}", [65, T], BF16) for i in range(2)]
    sqf = sbB("la_sqf", [64, T], BF16)
    vsb = sbB("la_vsb", [128, 32, 16, 65], BF16)
    pstrip = sbB("la_pstrip", [128, 32, 512], BF16)
    r2row = sbB("la_r2", [1, T], BF16)
    brow = sbB("la_brow", [1, T], BF16)
    cells = sbB("la_cells", [1, 8], F32)
    rden = sbB("la_rden", [1, 512], F32)
    rdenb = sbB("la_rdenb", [1, 512], BF16)
    rdrep = sbB("la_rdrep", [64, 512], F32)
    aoh = [sbB(f"la_aoh{i}", [64, T], BF16) for i in range(2)]

    s_vl = s_ws
    vstg = sbB("la_vstg", [128, 2, 1024], BF16)
    vrep = []

    def f(v):
        for hh in range(16):
            v.memset(vsb[:, :, hh, 64:65], 1.0)
        v.drain().then_inc(s_vg.h, 1)
    E("vector", f)
    ones_vg = s_vg.inc(1)

    for i in range(16):
        used = p.ma.get("vstg_used", 0)

        def f(g, i=i, used=used):
            if used > 0:
                g.wait_ge(s_vg.h, used)
            g.dma_start(out=vstg[:], in_=vf[:, i * 2:(i + 1) * 2, :]
                        ).then_inc(s_vl.h, 16)
        E("gpsimd", f)
        vl = s_vl.inc(16)

        def f(v, i=i, vl=vl):
            v.wait_ge(s_vl.h, vl)
            for hh in range(16):
                v.tensor_copy(out=vsb[:, i * 2:(i + 1) * 2, hh, 0:64],
                              in_=vstg[:, :, hh * 64:(hh + 1) * 64])
            v.drain().then_inc(s_vg.h, 1)
        E("vector", f)
        p.ma["vstg_used"] = s_vg.inc(1)
        vrep.append(p.ma["vstg_used"])
    VL0 = s_vl.n

    s_se = SemC(p.sem("la_se"))
    QT = 8
    bnd = {}

    def emit_qk_dma(h):
        par = h % 2
        hp = (h % 2) * 64
        hc = h // 2
        need = p.ma.get(("qk_used", par), 0)

        def f(s_, hp=hp, hc=hc, par=par, need=need):
            if need > 0:
                s_.wait_ge(s_mm.h, need)
            s_.dma_start(out=qext[par][0:64, :], in_=qf[hp:hp + 64, hc, :]).then_inc(s_hl.h, 16)
            s_.dma_start(out=kext[par][0:64, :], in_=kf[hp:hp + 64, hc, :]).then_inc(s_hl.h, 16)
        E("sync", f)
        return s_hl.inc(32)

    def emit_bounds(h, hl):
        par = h % 2
        qe = qext[par]
        ke = kext[par]
        prev_mm = p.ma.get("bnd_mm", 0)

        def f(v, qe=qe, hl=hl, pm=prev_mm):
            v.wait_ge(s_hl.h, hl)
            if pm > 0:
                v.wait_ge(s_mm.h, pm)
            v.tensor_tensor(out=sqf[:, :], in0=qe[0:64, :], in1=qe[0:64, :],
                            op=ALU.mult)
            v.drain().then_inc(s_vg.h, 1)
        E("vector", f)
        sq_vg = s_vg.inc(1)

        for i in range(8):
            bank = 6 + (i % 2)

            def f(t, i=i, bank=bank, vg=sq_vg, evp=s_vg.n - 1):
                t.wait_ge(s_vg.h, vg)
                if i >= 2:
                    t.wait_ge(s_vg.h, evp)
                t.matmul(pb[bank].ap()[0:1, 0:512], onecol[:],
                         sqf[:, i * 512:(i + 1) * 512], start=True,
                         stop=True).then_inc(s_mm.h, 1)
            E("tensor", f)
            mmi = s_mm.inc(1)

            def f(v, i=i, bank=bank, mmi=mmi):
                v.wait_ge(s_mm.h, mmi)
                v.tensor_copy(out=r2row[:, i * 512:(i + 1) * 512],
                              in_=pb[bank].ap()[0:1, 0:512]).then_inc(s_vg.h, 1)
            E("vector", f)
            s_vg.inc(1)
        rq_ev = s_vg.n

        def f(v, ke=ke, ev=rq_ev):
            v.wait_ge(s_vg.h, ev)
            v.tensor_tensor(out=sqf[:, :], in0=ke[0:64, :], in1=ke[0:64, :],
                            op=ALU.mult)
            v.drain().then_inc(s_vg.h, 1)
        E("vector", f)
        sqk_vg = s_vg.inc(1)

        for i in range(8):
            bank = 6 + (i % 2)

            def f(t, i=i, bank=bank, vg=sqk_vg, evp=s_vg.n - 1):
                t.wait_ge(s_vg.h, vg)
                if i >= 2:
                    t.wait_ge(s_vg.h, evp)
                t.matmul(pb[bank].ap()[0:1, 0:512], onecol[:],
                         sqf[:, i * 512:(i + 1) * 512], start=True,
                         stop=True).then_inc(s_mm.h, 1)
            E("tensor", f)
            mmi = s_mm.inc(1)

            def f(v, i=i, bank=bank, mmi=mmi):
                v.wait_ge(s_mm.h, mmi)
                v.tensor_copy(out=brow[:, i * 512:(i + 1) * 512],
                              in_=pb[bank].ap()[0:1, 0:512]).then_inc(s_vg.h, 1)
            E("vector", f)
            s_vg.inc(1)
        p.ma["bnd_mm"] = s_mm.n

        def f(v):
            v.drain()
            v.reduce_max(out=cells[:, 0:1], in_=brow[:, :], axis=AX.X)
            v.drain().then_inc(s_vg.h, 1)
        E("vector", f)
        rk_ev = s_vg.inc(1)

        def f(sc, ev=rk_ev):
            sc.wait_ge(s_vg.h, ev)
            sc.activation(out=cells[:, 1:2], in_=cells[:, 0:1], func=AF.Sqrt)
            sc.drain()
            sc.mul(out=cells[:, 2:3], in_=cells[:, 1:2], mul=-1.0)
            sc.drain()
            sc.activation(out=brow[:, :], in_=r2row[:, :], func=AF.Sqrt)
            sc.drain().then_inc(s_sg.h, 1)
        E("scalar", f)
        b_sg = s_sg.inc(1)

        def f(v, qe=qe, ke=ke, sg=b_sg):
            v.wait_ge(s_sg.h, sg)
            v.tensor_scalar(out=qe[64:65, :], in0=brow[:, :], scalar1=cells[:, 2:3],
                            scalar2=None, op0=ALU.mult)
            v.memset(ke[64:65, :], 1.0)
            v.drain().then_inc(s_vg.h, 1)
        E("vector", f)
        bnd[h] = s_vg.inc(1)

    hl0 = emit_qk_dma(0)
    hl1 = emit_qk_dma(1)
    emit_bounds(0, hl0)
    emit_bounds(1, hl1)

    for h in range(16):
        par = h % 2
        hp = (h % 2) * 64
        hc = h // 2
        qe = qext[par]
        ke = kext[par]
        ao = aoh[par]
        bnd_vg = bnd[h]

        for qt in range(QT):
            khi = 4 * (qt + 1)
            for kb in range(khi):
                bank = p.ma.get("gstrip", 0) % 3
                p.ma["gstrip"] = p.ma.get("gstrip", 0) + 1

                def f(t, qe=qe, ke=ke, qt=qt, kb=kb, bank=bank, bvg=bnd_vg,
                      sen=s_se.n - 2):
                    t.wait_ge(s_vg.h, bvg)
                    if sen > 0:
                        t.wait_ge(s_se.h, sen)
                    t.matmul(pb[bank].ap()[:, 0:512], ke[:, kb * 128:(kb + 1) * 128],
                             qe[:, qt * 512:(qt + 1) * 512], start=True,
                             stop=True).then_inc(s_mm.h, 1)
                E("tensor", f)
                smm = s_mm.inc(1)

                def f(sc, kb=kb, bank=bank, smm=smm,
                      avp=p.ma.get("av_mm", 0)):
                    sc.wait_ge(s_mm.h, smm)
                    if kb == 0 and avp > 0:
                        sc.wait_ge(s_mm.h, avp)
                    sc.activation(out=pstrip[:, kb, :], in_=pb[bank].ap()[:, 0:512],
                                  func=AF.Exp, scale=SCALE).then_inc(s_se.h, 1)
                E("scalar", f)
                s_se.inc(1)
            se_all = s_se.n

            def f(v, qt=qt, khi=khi, se=se_all):
                v.wait_ge(s_se.h, se)
                for j in range(4):
                    kb = 4 * qt + j
                    v.tensor_tensor(out=pstrip[:, kb, :], in0=pstrip[:, kb, :],
                                    in1=cmask[:, j, :], op=ALU.mult)
                v.drain().then_inc(s_vg.h, 1)
            E("vector", f)
            msk_vg = s_vg.inc(1)

            def f(t, h=h, khi=khi, mv=msk_vg, avp=p.ma.get("av_ev", 0),
                  vr=vrep[min((khi - 1) // 2, 15)], ov=ones_vg):
                t.wait_ge(s_vg.h, ov)
                t.wait_ge(s_vg.h, vr)
                t.wait_ge(s_vg.h, mv)
                if avp > 0:
                    t.wait_ge(s_vg.h, avp)
                for kb in range(khi):
                    mm = t.matmul(pb[4].ap()[0:65, 0:512],
                                  vsb[:, kb, h, :],
                                  pstrip[:, kb, :], start=(kb == 0),
                                  stop=(kb == khi - 1))
                mm.then_inc(s_mm.h, 1)
            E("tensor", f)
            av_mm = s_mm.inc(1)
            p.ma["av_mm"] = av_mm

            def f(v, mm=av_mm):
                v.wait_ge(s_mm.h, mm)
                v.tensor_copy(out=rden[:, :], in_=pb[4].ap()[64:65, 0:512])
                v.drain()
                v.reciprocal(out=rden[:, :], in_=rden[:, :])
                v.drain()
                v.tensor_copy(out=rdenb[:, :], in_=rden[:, :])
                v.drain().then_inc(s_vg.h, 1)
            E("vector", f)
            p.ma["dn_ev"] = s_vg.inc(1)
            dn_ev = p.ma["dn_ev"]

            def f(t, dv=dn_ev, rp=p.ma.get("rdrep_ev", 0)):
                t.wait_ge(s_vg.h, dv)
                if rp > 0:
                    t.wait_ge(s_vg.h, rp)
                t.matmul(pb[3].ap()[0:64, 0:512], on1b[:, 0:64], rdenb[:, :],
                         start=True, stop=True).then_inc(s_mm.h, 1)
            E("tensor", f)
            rr_mm = s_mm.inc(1)

            def f(v, mm=rr_mm):
                v.wait_ge(s_mm.h, mm)
                v.tensor_copy(out=rdrep[:, :], in_=pb[3].ap()[0:64, 0:512])
                v.drain().then_inc(s_vg.h, 1)
            E("vector", f)
            p.ma["rdrep_ev"] = s_vg.inc(1)

            def f(v, ao=ao, qt=qt, mm=av_mm, rp=p.ma["rdrep_ev"],
                  aofree=p.ma.get(("aoh_store", par), 0)):
                v.wait_ge(s_mm.h, mm)
                v.wait_ge(s_vg.h, rp)
                if qt == 0 and aofree > 0:
                    v.wait_ge(s_ao.h, aofree)
                v.tensor_tensor(out=ao[:, qt * 512:(qt + 1) * 512],
                                in0=pb[4].ap()[0:64, 0:512], in1=rdrep[:, :],
                                op=ALU.mult)
                v.drain().then_inc(s_vg.h, 1)
            E("vector", f)
            p.ma["av_ev"] = s_vg.inc(1)

        def f(g, ao=ao, hp=hp, hc=hc, avl=p.ma["av_ev"]):
            g.wait_ge(s_vg.h, avl)
            g.dma_start(out=aof[hp:hp + 64, hc, :], in_=ao[:, :]).then_inc(s_ao.h, 16)
        E("gpsimd", f)
        s_ao.inc(16)
        p.ma[("aoh_store", par)] = s_ao.n
        p.ma[("qk_used", par)] = s_mm.n
        p.ma["head_done"] = s_ao.n
        if h + 2 < 16:
            hln = emit_qk_dma(h + 2)
            emit_bounds(h + 2, hln)
    p.tape_barrier()

    stB.close()
    aot = [p.sbuf(f"la_aot{i}", [128, 8, 128], BF16) for i in range(2)]
    oott = [p.sbuf(f"la_oot{i}", [128, 1024], F32) for i in range(2)]
    ao_done = s_ao.n
    for tb in range(NTB):
        par = tb % 2

        def f(s_, tb=tb, par=par, need=p.ma.get("oo_mm", 0), ad=ao_done):
            s_.wait_ge(s_ao.h, ad)
            if need > 0:
                s_.wait_ge(s_mm.h, need)
            s_.dma_start(out=aot[par][:], in_=aof[:, :, tb * 128:(tb + 1) * 128]
                         ).then_inc(s_al[par].h, 16)
        E("sync", f)
        al = s_al[par].inc(16)

        for dt_ in range(2):
            bank = (2 * tb + dt_) % 2

            def f(t, par=par, dt_=dt_, al=al, bank=bank, evn=p.ma.get("oo_ev", 0) - 1):
                t.wait_ge(s_al[par].h, al)
                if evn > 0:
                    t.wait_ge(s_vg.h, evn)
                pa = pb[bank].ap()
                for c in range(8):
                    t.matmul(pa[:, 0:512], aot[par][:, c, :],
                             wo[:, c, dt_ * 512:(dt_ + 1) * 512],
                             start=(c == 0), stop=False)
                t.matmul(pa[:, 0:512], on1b[:], bos[:, dt_ * 512:(dt_ + 1) * 512],
                         start=False, stop=True).then_inc(s_mm.h, 1)
            E("tensor", f)
            mm = s_mm.inc(1)

            def f(v, par=par, dt_=dt_, mm=mm, bank=bank,
                  oprev=s_oo[par].n if dt_ == 0 else 0):
                v.wait_ge(s_mm.h, mm)
                if dt_ == 0 and oprev > 0:
                    v.wait_ge(s_oo[par].h, oprev)
                v.tensor_copy(out=oott[par][:, dt_ * 512:(dt_ + 1) * 512],
                              in_=pb[bank].ap()[:, 0:512]).then_inc(s_vg.h, 1)
            E("vector", f)
            p.ma["oo_ev"] = s_vg.inc(1)
        p.ma["oo_mm"] = s_mm.n

        def f(g, tb=tb, par=par, ev=p.ma["oo_ev"]):
            g.wait_ge(s_vg.h, ev)
            g.dma_start(out=brA1[tb * 128:(tb + 1) * 128, :],
                        in_=oott[par][:]).then_inc(s_oo[par].h, 16)
        E("gpsimd", f)
        s_oo[par].inc(16)

    o0, o1 = s_oo[0].n, s_oo[1].n

    def f(g):
        g.wait_ge(s_oo[0].h, o0)
        g.wait_ge(s_oo[1].h, o1)
    E("gpsimd", f)


def build_l1attn():
    ctx = contextlib.ExitStack()
    with ctx:
        nc = bass.Bass("TRN2", target_bir_lowering=False, debug=False)
        p = Prog(ctx, nc)
        p.ma = {"t34": 0}
        d = {}
        d["xn1f"] = nc.dram_tensor("xn1f", [128, 8, T], BF16, kind="ExternalInput").ap()
        for nm, shape in [("wqT", [1024, 1024]), ("wkT", [1024, 1024]),
                          ("wvT", [1024, 1024]), ("woT", [1024, 1024]),
                          ("bq", [1, 1024]), ("bk", [1, 1024]), ("bv", [1, 1024]),
                          ("bo", [1, 1024]), ("cosT", [128, 32, 32]),
                          ("sinT", [128, 32, 32]), ("ones1f", [1, 128]),
                          ("identd", [128, 128]), ("cmaskT", [128, 4, 512]),
                          ("colones", [128, 1])]:
            d[nm] = nc.dram_tensor(nm, shape, F32, kind="ExternalInput").ap()
        d["qf"] = nc.dram_tensor("qf", [128, 8, T], BF16, kind="Internal").ap()
        d["kf"] = nc.dram_tensor("kf", [128, 8, T], BF16, kind="Internal").ap()
        d["vf"] = nc.dram_tensor("vf", [128, 32, 1024], BF16, kind="Internal").ap()
        d["aof"] = nc.dram_tensor("aof", [128, 8, T], BF16, kind="Internal").ap()
        d["brA1"] = nc.dram_tensor("brA1", [T, D], F32, kind="ExternalOutput").ap()
        p.bar = p.sem("bar")
        p.tape_init()
        emit_l1attn(p, **d)
        with nc.Block() as block:
            p.block = block
            p.flush()
        return nc


def l1attn_host_layouts(inp):
    t = np.arange(T, dtype=np.float64)
    inv = 1.0 / (10000.0 ** (np.arange(0, 64, 2, dtype=np.float64) / 64))
    ang = np.outer(t, inv)
    cosT = np.ascontiguousarray(np.cos(ang).astype(np.float32).reshape(32, 128, 32).transpose(1, 0, 2))
    sinT = np.ascontiguousarray(np.sin(ang).astype(np.float32).reshape(32, 128, 32).transpose(1, 0, 2))
    cm = np.zeros((4, 128, 512), np.float32)
    for j in range(4):
        s = np.arange(128)[:, None] + 128 * j
        tt = np.arange(512)[None, :]
        cm[j] = (s <= tt).astype(np.float32)
    cmaskT = np.ascontiguousarray(cm.transpose(1, 0, 2))
    return dict(
        wqT=np.ascontiguousarray(inp["L1_attn_Wq"].T),
        wkT=np.ascontiguousarray(inp["L1_attn_Wk"].T),
        wvT=np.ascontiguousarray(inp["L1_attn_Wv"].T),
        woT=np.ascontiguousarray(inp["L1_attn_Wo"].T),
        bq=inp["L1_attn_bq"].reshape(1, -1).astype(np.float32),
        bk=inp["L1_attn_bk"].reshape(1, -1).astype(np.float32),
        bv=inp["L1_attn_bv"].reshape(1, -1).astype(np.float32),
        bo=inp["L1_attn_bo"].reshape(1, -1).astype(np.float32),
        cosT=cosT, sinT=sinT, cmaskT=cmaskT,
        ones1f=np.ones((1, 128), np.float32),
        colones=np.ones((128, 1), np.float32),
        identd=np.eye(128, dtype=np.float32),
    )



def _compose_bfly(p, mats_hbm, R_sb, pb, pbb, idb, s_w, W0, s_mm, s_vg, tag):
    E = p.emit
    NMB = 2
    mbuf = [p.sbuf(f"bf_m{tag}{i}", [128, 128], BF16) for i in range(NMB)]
    s_ml = [p.semc(f"bf_ml{i}") for i in range(NMB)]
    base = s_vg.n
    mm_used = {}
    gi = 0
    for s in range(6, -1, -1):
        for c in range(32):
            par = gi % NMB
            used = mm_used.get(par, 0)

            def f(q, c=c, s=s, par=par, used=used):
                if used > 0:
                    q.wait_ge(s_mm.h, used)
                q.dma_start(out=mbuf[par][:], in_=mats_hbm[s, c, :, :]
                             ).then_inc(s_ml[par].h, 16)
            E("gpsimd" if par == 0 else "sync", f)
            ml = s_ml[par].inc(16)

            def f(t, c=c, s=s, par=par, ml=ml, gi=gi, base=base):
                t.wait_ge(s_w.h, W0)
                t.wait_ge(s_ml[par].h, ml)
                if gi >= 2:
                    t.wait_ge(s_vg.h, base + gi - 1)
                if s < 6:
                    t.wait_ge(s_vg.h, base + (5 - s) * 32 + c + 1)
                rhs = idb[:] if s == 6 else R_sb[:, c, :]
                t.matmul(pb[gi % 2].ap()[:, 0:128], mbuf[par][:], rhs, start=True,
                         stop=True).then_inc(s_mm.h, 1)
            E("tensor", f)
            mm = s_mm.inc(1)
            mm_used[par] = mm

            def f(v, c=c, gi=gi, mm=mm):
                v.wait_ge(s_mm.h, mm)
                v.tensor_copy(out=R_sb[:, c, :],
                              in_=pb[gi % 2].ap()[:, 0:128]).then_inc(s_vg.h, 1)
            E("vector", f)
            s_vg.inc(1)
            gi += 1


def _emit_cross_stages(p, hb, tbuf, coefA, coefB, pairs, s_vg, ws, pre=None,
                       outw=512):
    E = p.emit
    s_sgx = p.semc("bf_sgx")
    vdone = []
    gg = 0
    for st in range(5):
        src = hb[st % 2]
        dst = hb[(st + 1) % 2]
        prev_vg = s_vg.n
        for grp in range(4):
            base = (gg % 2) * 8
            tb_free = vdone[gg - 2] if gg >= 2 else 0

            def fs(sc, st=st, src=src, grp=grp, prev_vg=prev_vg, base=base,
                   tb_free=tb_free):
                sc.wait_ge(ws[0], ws[1])
                if grp == 0:
                    if st == 0 and pre is not None:
                        sc.wait_ge(pre[0], pre[1])
                    if prev_vg > 0:
                        sc.wait_ge(s_vg.h, prev_vg)
                if tb_free > 0:
                    sc.wait_ge(s_vg.h, tb_free)
                for c in range(grp * 8, grp * 8 + 8):
                    lo, hi = pairs[st][c]
                    sc.activation(out=tbuf[:, base + c - grp * 8, :],
                                  in_=src[:, hi, :],
                                  func=AF.Copy, scale=coefB[:, st, c:c + 1],
                                  bias=0.0)
                sc.drain().then_inc(s_sgx.h, 1)
            E("scalar", fs)
            sgx = s_sgx.inc(1)

            def fv(v, st=st, src=src, dst=dst, grp=grp, sgx=sgx, base=base):
                v.wait_ge(s_sgx.h, sgx)
                if grp == 0:
                    if st == 0 and pre is not None:
                        v.wait_ge(pre[0], pre[1])
                    v.wait_ge(ws[0], ws[1])
                for c in range(grp * 8, grp * 8 + 8):
                    lo, hi = pairs[st][c]
                    v.scalar_tensor_tensor(out=dst[:, c, :], in0=src[:, lo, :],
                                           scalar=coefA[:, st, c:c + 1],
                                           in1=tbuf[:, base + c - grp * 8, :],
                                           op0=ALU.mult,
                                           op1=ALU.add)
                v.drain().then_inc(s_vg.h, 1)
            E("vector", fv)
            vdone.append(s_vg.inc(1))
            gg += 1
    return (5 % 2)


def emit_l1f1(p, xn1f, weT, be, upM, upA, upB, ubias, identd, ones1f, hgf):
    nc = p.nc
    E = p.emit
    we = p.sbuf("f1_we", [128, 8, 4096], BF16)
    bes = p.sbuf("f1_be", [128, 32], F32)
    idb = p.sbuf("f1_idb", [128, 128], BF16)
    on1b = p.sbuf("f1_on1b", [1, 128], BF16)
    Rup = p.sbuf("f1_rup", [128, 32, 128], BF16)
    cA = p.sbuf("f1_ca", [128, 5, 32], F32)
    cB = p.sbuf("f1_cb", [128, 5, 32], F32)
    ub = p.sbuf("f1_ub", [128, 32], F32)
    xt = [p.sbuf(f"f1_xt{i}", [128, 8, 512], BF16) for i in range(2)]
    ha = [p.sbuf(f"f1_ha{i}", [128, 32, 512], BF16) for i in range(2)]
    hb2 = p.sbuf("f1_hb", [128, 32, 512], BF16)
    tbuf = p.sbuf("f1_tb", [128, 16, 512], BF16)

    pb = [p.psum_bank(f"f1_p{i}") for i in range(4)]
    pbb = [b.ap().bitcast(BF16) for b in pb]

    s_w = SemC(p.sem("f1_w"))
    s_ws = SemC(p.sem("f1_ws"))
    s_x = [SemC(p.sem(f"f1_x{i}")) for i in range(2)]
    s_mm = SemC(p.sem("f1_mm"))
    s_ev = SemC(p.sem("f1_ev"))
    s_vg = SemC(p.sem("f1_vg"))
    s_sg = SemC(p.sem("f1_sg"))
    s_hs = [SemC(p.sem(f"f1_hs{i}")) for i in range(2)]

    def f(g):
        g.dma_start(out=we[:], in_=weT.rearrange("(c p) z -> p c z", p=128)).then_inc(s_w.h, 16)
        g.dma_start(out=bes[:], in_=be[:, :]).then_inc(s_w.h, 16)
        g.dma_start(out=idb[:], in_=identd[:, :]).then_inc(s_w.h, 16)
        g.dma_start(out=on1b[:], in_=ones1f[:, :]).then_inc(s_w.h, 16)
    E("gpsimd", f)
    s_w.inc(64)
    W0 = s_w.n

    def f(s_):
        s_.dma_start(out=cA[:], in_=upA[:, :, :]).then_inc(s_ws.h, 16)
        s_.dma_start(out=cB[:], in_=upB[:, :, :]).then_inc(s_ws.h, 16)
        s_.dma_start(out=ub[:], in_=ubias[:, :]).then_inc(s_ws.h, 16)
    E("sync", f)
    s_ws.inc(48)
    WS0 = s_ws.n

    _compose_bfly(p, upM, Rup, pb, pbb, idb, s_w, W0, s_mm, s_vg, "u")
    comp_vg = s_vg.n
    p.tape_barrier()

    pairs = p.bfly_pairs
    s_wv = SemC(p.sem("f1_wv"))

    for tt in range(8):
        par = tt % 2

        def f(s_, tt=tt, par=par, need=p.ma.get("f1_xt", {}).get(par, 0)):
            s_.wait_ge(s_w.h, W0)
            if need > 0:
                s_.wait_ge(s_mm.h, need)
            s_.dma_start(out=xt[par][:], in_=xn1f[:, :, tt * 512:(tt + 1) * 512]
                         ).then_inc(s_x[par].h, 16)
        E("sync", f)
        xl = s_x[par].inc(16)

        prev_store = p.ma.get(("f1_hs", par), 0)
        for c in range(32):
            bank = c % 2

            def f(t, c=c, par=par, bank=bank, xl=xl, evn=s_ev.n - 1):
                t.wait_ge(s_x[par].h, xl)
                if evn > 0:
                    t.wait_ge(s_ev.h, evn)
                pa = pb[bank].ap()
                for d in range(8):
                    t.matmul(pa[:, 0:512], we[:, d, c * 128:(c + 1) * 128],
                             xt[par][:, d, :], start=(d == 0),
                             stop=(d == 7)).then_inc(s_mm.h, 1) if d == 7 else                         t.matmul(pa[:, 0:512], we[:, d, c * 128:(c + 1) * 128],
                                 xt[par][:, d, :], start=(d == 0), stop=False)
            E("tensor", f)
            mm = s_mm.inc(1)

            def f(v, c=c, par=par, bank=bank, mm=mm, ps=prev_store):
                v.wait_ge(s_mm.h, mm)
                v.wait_ge(s_ws.h, WS0)
                if ps > 0 and c == 0:
                    v.wait_ge(s_hs[par].h, ps)
                v.tensor_scalar(out=ha[par][:, c, :], in0=pb[bank].ap()[:, 0:512],
                                scalar1=bes[:, c:c + 1], scalar2=None,
                                op0=ALU.add).then_inc(s_ev.h, 1)
            E("vector", f)
            s_ev.inc(1)
        p.ma.setdefault("f1_xt", {})[par] = s_mm.n
        h0ev = s_ev.n

        for c in range(32):
            bank = 2 + (c % 2)

            def f(t, c=c, par=par, bank=bank, ev=h0ev, wvn=s_wv.n - 1):
                t.wait_ge(s_ev.h, ev)
                if wvn > 0:
                    t.wait_ge(s_wv.h, wvn)
                t.matmul(pb[bank].ap()[:, 0:512], Rup[:, c, :], ha[par][:, c, :],
                         start=True, stop=True).then_inc(s_mm.h, 1)
            E("tensor", f)
            mm = s_mm.inc(1)

            def f(v, c=c, bank=bank, mm=mm, cvp=p.ma.get("f1_cv", 0)):
                v.wait_ge(s_mm.h, mm)
                if c == 0 and cvp > 0:
                    v.wait_ge(s_vg.h, cvp)
                v.tensor_copy(out=hb2[:, c, :],
                              in_=pb[bank].ap()[:, 0:512]).then_inc(s_wv.h, 1)
            E("vector", f)
            s_wv.inc(1)

        _emit_cross_stages(p, [hb2, ha[par]], tbuf, cA, cB, pairs, s_vg,
                           (s_ws.h, WS0), pre=(s_wv.h, s_wv.n))
        cross_vg = s_vg.n
        p.ma["f1_cv"] = cross_vg

        def f(sc, par=par, cv=cross_vg):
            sc.wait_ge(s_vg.h, cv)
            for c in range(32):
                sc.activation(out=ha[par][:, c, :], in_=ha[par][:, c, :],
                              func=AF.Gelu, bias=ub[:, c:c + 1])
            sc.drain().then_inc(s_sg.h, 1)
        E("scalar", f)
        p.ma["f1_gelu"] = s_sg.inc(1)

        def f(g, tt=tt, par=par, sg=p.ma["f1_gelu"]):
            g.wait_ge(s_sg.h, sg)
            g.dma_start(out=hgf[:, :, tt * 512:(tt + 1) * 512],
                        in_=ha[par][:]).then_inc(s_hs[par].h, 16)
        E("gpsimd", f)
        p.ma[("f1_hs", par)] = s_hs[par].inc(16)

    h0_, h1_ = s_hs[0].n, s_hs[1].n

    def f(g):
        g.wait_ge(s_hs[0].h, h0_)
        g.wait_ge(s_hs[1].h, h1_)
    E("gpsimd", f)


def emit_l1f2(p, hgf, dnM, dnA, dnB, wcT, bcon, dbias, x1h, brA1h, identd,
              ones1f, out_h):
    nc = p.nc
    E = p.emit
    wc = p.sbuf("f2_wc", [128, 32, 1024], BF16)
    bcs = p.sbuf("f2_bc", [1, 1024], BF16)
    dbs = p.sbuf("f2_db", [1, 1024], BF16)
    idb = p.sbuf("f2_idb", [128, 128], BF16)
    on1b = p.sbuf("f2_on1b", [1, 128], BF16)
    Rdn = p.sbuf("f2_rdn", [128, 32, 128], BF16)
    cA = p.sbuf("f2_ca", [128, 5, 32], F32)
    cB = p.sbuf("f2_cb", [128, 5, 32], F32)
    ha = [p.sbuf(f"f2_ha{i}", [128, 32, 512], BF16) for i in range(2)]
    hb2 = p.sbuf("f2_hb", [128, 32, 512], BF16)
    tbuf = p.sbuf("f2_tb", [128, 16, 512], BF16)
    x1t = [p.sbuf(f"f2_x1t{i}", [128, 1024], F32) for i in range(2)]
    bat = [p.sbuf(f"f2_bat{i}", [128, 1024], F32) for i in range(2)]

    pb = [p.psum_bank(f"f2_p{i}") for i in range(4)]
    pbb = [b.ap().bitcast(BF16) for b in pb]

    s_w = SemC(p.sem("f2_w"))
    s_ws = SemC(p.sem("f2_ws"))
    s_h = [SemC(p.sem(f"f2_h{i}")) for i in range(2)]
    s_xb = [SemC(p.sem(f"f2_xb{i}")) for i in range(2)]
    s_mm = SemC(p.sem("f2_mm"))
    s_wv = SemC(p.sem("f2_wv"))
    s_vg = SemC(p.sem("f2_vg"))
    s_oo = [SemC(p.sem(f"f2_oo{i}")) for i in range(2)]

    def f(g):
        g.dma_start(out=wc[:], in_=wcT.rearrange("(c p) z -> p c z", p=128)).then_inc(s_w.h, 16)
        g.dma_start(out=bcs[:], in_=bcon[:, :]).then_inc(s_w.h, 16)
        g.dma_start(out=dbs[:], in_=dbias[:, :]).then_inc(s_w.h, 16)
        g.dma_start(out=idb[:], in_=identd[:, :]).then_inc(s_w.h, 16)
        g.dma_start(out=on1b[:], in_=ones1f[:, :]).then_inc(s_w.h, 16)
    E("gpsimd", f)
    s_w.inc(80)
    W0 = s_w.n

    def f(s_):
        s_.dma_start(out=cA[:], in_=dnA[:, :, :]).then_inc(s_ws.h, 16)
        s_.dma_start(out=cB[:], in_=dnB[:, :, :]).then_inc(s_ws.h, 16)
    E("sync", f)
    s_ws.inc(32)
    WS0 = s_ws.n

    _compose_bfly(p, dnM, Rdn, pb, pbb, idb, s_w, W0, s_mm, s_vg, "d")
    p.tape_barrier()

    pairs = p.bfly_pairs_dn

    for tt in range(8):
        par = tt % 2

        def f(s_, tt=tt, par=par, need=p.ma.get(("f2_omm", par), 0)):
            s_.wait_ge(s_w.h, W0)
            if need > 0:
                s_.wait_ge(s_mm.h, need)
            s_.dma_start(out=ha[par][:], in_=hgf[:, :, tt * 512:(tt + 1) * 512]
                         ).then_inc(s_h[par].h, 16)
        E("sync", f)
        hl = s_h[par].inc(16)

        prev_out = p.ma.get("f2_omm", 0)
        for c in range(32):
            bank = c % 2

            def f(t, c=c, par=par, bank=bank, hl=hl, wvn=s_wv.n - 1):
                t.wait_ge(s_h[par].h, hl)
                if wvn > 0:
                    t.wait_ge(s_wv.h, wvn)
                t.matmul(pb[bank].ap()[:, 0:512], Rdn[:, c, :], ha[par][:, c, :],
                         start=True, stop=True).then_inc(s_mm.h, 1)
            E("tensor", f)
            mm = s_mm.inc(1)

            def f(v, c=c, bank=bank, mm=mm, cvp=p.ma.get("f2_crossv", 0)):
                v.wait_ge(s_mm.h, mm)
                if c == 0 and cvp > 0:
                    v.wait_ge(s_vg.h, cvp)
                v.tensor_copy(out=hb2[:, c, :],
                              in_=pb[bank].ap()[:, 0:512]).then_inc(s_wv.h, 1)
            E("vector", f)
            s_wv.inc(1)

        _emit_cross_stages(p, [hb2, ha[par]], tbuf, cA, cB, pairs, s_vg,
                           (s_ws.h, WS0), pre=(s_wv.h, s_wv.n))
        cross_vg = s_vg.n
        p.ma["f2_crossv"] = cross_vg

        for tb in range(4):
            tba = tt * 4 + tb
            bpar = tba % 2

            def f(s_, tba=tba, bpar=bpar, need=p.ma.get("f2_cons", 0),
                  ost=p.ma.get(("f2_ost", bpar), 0)):
                if need > 0:
                    s_.wait_ge(s_vg.h, need)
                if ost > 0:
                    s_.wait_ge(s_oo[bpar].h, ost)
                s_.dma_start(out=x1t[bpar][:], in_=x1h[tba * 128:(tba + 1) * 128, :]
                             ).then_inc(s_xb[bpar].h, 16)
                s_.dma_start(out=bat[bpar][:], in_=brA1h[tba * 128:(tba + 1) * 128, :]
                             ).then_inc(s_xb[bpar].h, 16)
            E("sync", f)
            xbl = s_xb[bpar].inc(32)

            for dt_ in range(2):
                bank = 2 + ((2 * tba + dt_) % 2)

                def f(t, tb=tb, dt_=dt_, par=par, bank=bank, cv=cross_vg,
                      evn=p.ma.get("f2_oev", 0) - 1):
                    t.wait_ge(s_vg.h, cv)
                    if evn > 0:
                        t.wait_ge(s_vg.h, evn)
                    pa = pb[bank].ap()
                    for c in range(32):
                        t.matmul(pa[:, 0:512], ha[par][:, c, tb * 128:(tb + 1) * 128],
                                 wc[:, c, dt_ * 512:(dt_ + 1) * 512],
                                 start=(c == 0), stop=False)
                    t.matmul(pa[:, 0:512], on1b[:], bcs[:, dt_ * 512:(dt_ + 1) * 512],
                             start=False, stop=False)
                    t.matmul(pa[:, 0:512], on1b[:], dbs[:, dt_ * 512:(dt_ + 1) * 512],
                             start=False, stop=True).then_inc(s_mm.h, 1)
                E("tensor", f)
                mm = s_mm.inc(1)

                def f(v, dt_=dt_, bpar=bpar, bank=bank, mm=mm, xbl=xbl):
                    v.wait_ge(s_mm.h, mm)
                    v.wait_ge(s_xb[bpar].h, xbl)
                    sl = slice(dt_ * 512, (dt_ + 1) * 512)
                    v.scalar_tensor_tensor(out=x1t[bpar][:, sl],
                                           in0=x1t[bpar][:, sl],
                                           scalar=1.0, in1=pb[bank].ap()[:, 0:512],
                                           op0=ALU.mult, op1=ALU.add)
                    v.drain()
                    v.tensor_tensor(out=x1t[bpar][:, sl], in0=x1t[bpar][:, sl],
                                    in1=bat[bpar][:, sl], op=ALU.add)
                    v.drain().then_inc(s_vg.h, 1)
                E("vector", f)
                p.ma["f2_oev"] = s_vg.inc(1)
            p.ma[("f2_omm", par)] = s_mm.n
            p.ma["f2_cons"] = p.ma["f2_oev"]

            def f(g, tba=tba, bpar=bpar, ev=p.ma["f2_oev"]):
                g.wait_ge(s_vg.h, ev)
                g.dma_start(out=out_h[tba * 128:(tba + 1) * 128, :],
                            in_=x1t[bpar][:]).then_inc(s_oo[bpar].h, 16)
            E("gpsimd", f)
            s_oo[bpar].inc(16)
            p.ma[("f2_ost", bpar)] = s_oo[bpar].n

    o0, o1 = s_oo[0].n, s_oo[1].n

    def f(g):
        g.wait_ge(s_oo[0].h, o0)
        g.wait_ge(s_oo[1].h, o1)
    E("gpsimd", f)


def bfly_host_layouts(up_w, dn_w):
    res = {}
    pairs = None
    for name, w in (("up", up_w), ("dn", dn_w)):
        M = np.zeros((7, 32, 128, 128), np.float32)
        A = np.zeros((5, 32, 128), np.float32)
        B = np.zeros((5, 32, 128), np.float32)
        prs = [[None] * 32 for _ in range(5)]
        for stage in range(12):
            s_ = 2 ** stage
            nb = 4096 // (2 * s_)
            idx = np.arange(4096).reshape(nb, 2, s_)
            ww = w[stage].reshape(nb, s_, 2, 2)
            a_idx = idx[:, 0, :]
            b_idx = idx[:, 1, :]
            if stage <= 6:
                for kb in range(nb):
                    for si in range(s_):
                        a = int(a_idx[kb, si]); b = int(b_idx[kb, si])
                        c = a // 128
                        M[stage, c, a % 128, a % 128] = ww[kb, si, 0, 0]
                        M[stage, c, a % 128, b % 128] = ww[kb, si, 0, 1]
                        M[stage, c, b % 128, a % 128] = ww[kb, si, 1, 0]
                        M[stage, c, b % 128, b % 128] = ww[kb, si, 1, 1]
            else:
                st = stage - 7
                aa = a_idx.ravel(); bb = b_idx.ravel()
                wf = ww.reshape(-1, 2, 2)
                ca = aa // 128; pa = aa % 128
                cb_ = bb // 128
                for i in range(len(aa)):
                    A[st, ca[i], pa[i]] = wf[i, 0, 0]
                    B[st, ca[i], pa[i]] = wf[i, 0, 1]
                    A[st, cb_[i], pa[i]] = wf[i, 1, 0]
                    B[st, cb_[i], pa[i]] = wf[i, 1, 1]
                    prs[st][ca[i]] = (int(ca[i]), int(cb_[i]))
                    prs[st][cb_[i]] = (int(ca[i]), int(cb_[i]))
        res[name + "M"] = M
        res[name + "A"] = np.ascontiguousarray(A.transpose(2, 0, 1))
        res[name + "B"] = np.ascontiguousarray(B.transpose(2, 0, 1))
        pairs = prs
    res["pairs"] = pairs
    return res


def build_l1f1(pairs):
    ctx = contextlib.ExitStack()
    with ctx:
        nc = bass.Bass("TRN2", target_bir_lowering=False, debug=False)
        p = Prog(ctx, nc)
        p.ma = {"t34": 0}
        p.bfly_pairs = pairs
        d = {}
        d["xn1f"] = nc.dram_tensor("xn1f", [128, 8, T], BF16, kind="ExternalInput").ap()
        for nm, shape in [("weT", [1024, 4096]), ("be", [128, 32]),
                          ("upM", [7, 32, 128, 128]), ("upA", [128, 5, 32]),
                          ("upB", [128, 5, 32]), ("ubias", [128, 32]),
                          ("identd", [128, 128]), ("ones1f", [1, 128])]:
            d[nm] = nc.dram_tensor(nm, shape, F32, kind="ExternalInput").ap()
        d["hgf"] = nc.dram_tensor("hgf", [128, 32, T], BF16, kind="ExternalOutput").ap()
        p.bar = p.sem("bar")
        p.tape_init()
        emit_l1f1(p, **d)
        with nc.Block() as block:
            p.block = block
            p.flush()
        return nc


def build_l1f2(pairs):
    ctx = contextlib.ExitStack()
    with ctx:
        nc = bass.Bass("TRN2", target_bir_lowering=False, debug=False)
        p = Prog(ctx, nc)
        p.ma = {"t34": 0}
        p.bfly_pairs_dn = pairs
        d = {}
        d["hgf"] = nc.dram_tensor("hgf", [128, 32, T], BF16, kind="ExternalInput").ap()
        for nm, shape in [("dnM", [7, 32, 128, 128]), ("dnA", [128, 5, 32]),
                          ("dnB", [128, 5, 32]), ("wcT", [4096, 1024]),
                          ("bcon", [1, 1024]), ("dbias", [1, 1024]),
                          ("x1h", [T, D]), ("brA1h", [T, D]),
                          ("identd", [128, 128]), ("ones1f", [1, 128])]:
            d[nm] = nc.dram_tensor(nm, shape, F32, kind="ExternalInput").ap()
        d["out_h"] = nc.dram_tensor("out", [T, D], F32, kind="ExternalOutput").ap()
        p.bar = p.sem("bar")
        p.tape_init()
        emit_l1f2(p, **d)
        with nc.Block() as block:
            p.block = block
            p.flush()
        return nc


def l1ffn_host_layouts(inp):
    bf = bfly_host_layouts(inp["L1_ffn_up_weights"], inp["L1_ffn_down_weights"])
    return dict(
        weT=np.ascontiguousarray(inp["L1_ffn_W_exp"]),
        be=np.ascontiguousarray(inp["L1_ffn_b_exp"].reshape(32, 128).T),
        upM=bf["upM"], upA=bf["upA"], upB=bf["upB"],
        dnM=bf["dnM"], dnA=bf["dnA"], dnB=bf["dnB"],
        ubias=np.ascontiguousarray(inp["L1_ffn_up_bias"].reshape(32, 128).T),
        wcT=np.ascontiguousarray(inp["L1_ffn_W_con"]),
        bcon=inp["L1_ffn_b_con"].reshape(1, -1).astype(np.float32),
        dbias=inp["L1_ffn_down_bias"].reshape(1, -1).astype(np.float32),
        identd=np.eye(128, dtype=np.float32),
        ones1f=np.ones((1, 128), np.float32),
        pairs=bf["pairs"],
    )


_BUILD_CACHE = {}


def _cached(key, fn):
    if key not in _BUILD_CACHE:
        _BUILD_CACHE[key] = fn()
    return _BUILD_CACHE[key]


def emit_pre_tape(p, x, xnf, xn2f, lnw_rep, lnb_rep, ident_hbm, mu_hbm,
                  rstd_hbm, affine=True):
    E = p.emit
    NQ = 2 if affine else 1
    NT = 8 * NQ
    tag = "pa" if affine else "pb"

    xb = [p.sbuf(f"{tag}_xb{i}", [128, D], F32) for i in range(2)]
    xc = p.sbuf(f"{tag}_xc", [128, D], F32)
    sq = p.sbuf(f"{tag}_sq", [128, D], F32)
    xn = p.sbuf(f"{tag}_xn", [128, D], F32)
    xnbf = p.sbuf(f"{tag}_xnbf", [128, D], BF16)
    xn2bf = p.sbuf(f"{tag}_xn2bf", [128, D], BF16)
    xc2 = p.sbuf(f"{tag}_xc2", [128, D], F32)
    st1 = p.sbuf(f"{tag}_st1", [128, 8], F32)
    epsc = p.sbuf(f"{tag}_eps", [128, 1], F32)
    mu_sb = p.sbuf(f"{tag}_mu", [128, NTB], F32)
    rstd_sb = p.sbuf(f"{tag}_rstd", [128, NTB], F32)
    lnw = p.sbuf(f"{tag}_lnw", [128, D], F32)
    lnb = p.sbuf(f"{tag}_lnb", [128, D], F32)
    ident = p.sbuf(f"{tag}_id", [128, 128], BF16)
    stg = p.sbuf(f"{tag}_stg", [128, 8, 128], BF16)
    stg2 = p.sbuf(f"{tag}_stg2", [128, 8, 128], BF16)
    ptb = [p.psum_bank(f"{tag}_pt{i}") for i in range(2)]
    pt = [b.ap().bitcast(BF16)[:, 0:128] for b in ptb]

    s_xld = [p.semc("pre_xld0"), p.semc("pre_xld1")]
    s_cld = p.semc("pre_cld")
    s_cldg = p.semc("pre_cldg")
    s_v2s = p.semc("pre_v2s")
    s_s2v = p.semc("pre_s2v")
    s_cast = p.semc("pre_cast")
    s_tr = p.semc("pre_tr")
    s_ev = p.semc("pre_ev")
    s_out = p.semc("pre_out")
    s_xcons = p.semc("pre_xcons")

    def f(g):
        g.dma_start(out=ident[:], in_=ident_hbm[:, :]).then_inc(s_cldg.h, 16)
    E("gpsimd", f)
    CLDG = s_cldg.inc(16)

    if affine:
        def f(s_):
            s_.dma_start(out=lnw[:], in_=lnw_rep[:, :]).then_inc(s_cld.h, 16)
            s_.dma_start(out=lnb[:], in_=lnb_rep[:, :]).then_inc(s_cld.h, 16)
        E("sync", f)
        s_cld.inc(32)
    CLD = s_cld.n

    xld_t = {}
    for tb in range(NTB):
        par = tb % 2
        need = s_xcons.n - 1

        def f(s_, tb=tb, par=par, need=need):
            if need > 0:
                s_.wait_ge(s_xcons.h, need)
            s_.dma_start(out=xb[par][:], in_=x[tb * 128:(tb + 1) * 128, :]
                         ).then_inc(s_xld[par].h, 16)
        E("sync", f)
        xld_t[tb] = s_xld[par].inc(16)

        v2s_1 = s_v2s.n + 1
        s2v_1 = s_s2v.n + 1
        cast_1 = s_cast.n + 1
        out_need = s_out.n

        def f(v, tb=tb, par=par, xl=xld_t[tb], s2v_1=s2v_1, on=out_need,
              tr0=s_tr.n):
            if tb == 0:
                v.memset(epsc[:], EPS)
                if affine:
                    v.wait_ge(s_cld.h, CLD)
            b = xb[par]
            v.wait_ge(s_xld[par].h, xl)
            v.drain()
            v.reduce_sum(out=st1[:, 0:1], in_=b[:], axis=AX.X)
            v.drain()
            v.tensor_scalar(out=st1[:, 0:1], in0=st1[:, 0:1], scalar1=-1.0 / D,
                            scalar2=None, op0=ALU.mult)
            v.drain()
            v.tensor_scalar(out=xc[:], in0=b[:], scalar1=st1[:, 0:1], scalar2=None,
                            op0=ALU.add).then_inc(s_v2s.h, 1)
            v.tensor_copy(out=mu_sb[:, tb:tb + 1], in_=st1[:, 0:1])
            v.drain().then_inc(s_xcons.h, 1)
            v.wait_ge(s_s2v.h, s2v_1)
            v.reciprocal(out=st1[:, 3:4], in_=st1[:, 2:3])
            v.drain()
            v.tensor_copy(out=rstd_sb[:, tb:tb + 1], in_=st1[:, 3:4])
            v.tensor_scalar(out=xn[:], in0=xc[:], scalar1=st1[:, 3:4], scalar2=None,
                            op0=ALU.mult)
            v.drain()
            v.tensor_copy(out=xnbf[:], in_=xn[:]).then_inc(s_cast.h, 1)
            if affine:
                v.reduce_sum(out=st1[:, 4:5], in_=xn[:], axis=AX.X)
                v.drain()
                v.tensor_scalar(out=st1[:, 4:5], in0=st1[:, 4:5], scalar1=-1.0 / D,
                                scalar2=None, op0=ALU.mult)
                v.drain()
                v.tensor_scalar(out=xc2[:], in0=xn[:], scalar1=st1[:, 4:5],
                                scalar2=None, op0=ALU.add).then_inc(s_v2s.h, 1)
                v.wait_ge(s_s2v.h, s2v_1 + 1)
                v.reciprocal(out=st1[:, 7:8], in_=st1[:, 6:7])
                v.drain()
                v.tensor_scalar(out=xc2[:], in0=xc2[:], scalar1=st1[:, 7:8],
                                scalar2=None, op0=ALU.mult)
                v.drain()
                v.tensor_tensor(out=xc2[:], in0=xc2[:], in1=lnw[:], op=ALU.mult)
                v.drain()
                v.tensor_tensor(out=xn2bf[:], in0=xc2[:], in1=lnb[:],
                                op=ALU.add).then_inc(s_cast.h, 1)
            if on > 0:
                v.wait_ge(s_out.h, on)
            for c in range(NT):
                v.wait_ge(s_tr.h, tr0 + c + 1)
                psrc = pt[c % 2]
                dst = stg if c < 8 else stg2
                cc = c if c < 8 else c - 8
                v.tensor_copy(out=dst[:, cc, :], in_=psrc[:]).then_inc(s_ev.h, 1)
        E("vector", f)
        s_v2s.inc(NQ)
        s_cast.inc(NQ)
        s_xcons.inc(1)

        def f(sc, v2s_1=v2s_1):
            sc.wait_ge(s_v2s.h, v2s_1)
            sc.activation(out=sq[:], in_=xc[:], func=AF.Square, accum_out=st1[:, 1:2])
            sc.drain()
            sc.activation(out=st1[:, 2:3], in_=st1[:, 1:2], func=AF.Sqrt,
                          scale=1.0 / D, bias=epsc[:, 0:1]).then_inc(s_s2v.h, 1)
            if affine:
                sc.wait_ge(s_v2s.h, v2s_1 + 1)
                sc.activation(out=sq[:], in_=xc2[:], func=AF.Square,
                              accum_out=st1[:, 5:6])
                sc.drain()
                sc.activation(out=st1[:, 6:7], in_=st1[:, 5:6], func=AF.Sqrt,
                              scale=1.0 / D, bias=epsc[:, 0:1]).then_inc(s_s2v.h, 1)
        E("scalar", f)
        s_s2v.inc(NQ)

        def f(t, tb=tb, cast_1=cast_1, ev0=s_ev.n, tr0=s_tr.n, cldg=CLDG):
            if tb == 0:
                t.wait_ge(s_cldg.h, cldg)
            for c in range(NT):
                t.wait_ge(s_cast.h, cast_1 + (0 if c < 8 else 1))
                if tb > 0 or c >= 2:
                    t.wait_ge(s_ev.h, ev0 + c - NT - 1 + NT)
                t.transpose(pt[c % 2][:], (xnbf if c < 8 else xn2bf)
                            [:, (c % 8) * 128:((c % 8) + 1) * 128],
                            ident[:]).then_inc(s_tr.h, 1)
        E("tensor", f)
        s_tr.inc(NT)
        ev_mid = s_ev.n + 8
        s_ev.inc(NT)
        ev_all = s_ev.n

        def f(g, tb=tb, em=ev_mid, ea=ev_all):
            g.wait_ge(s_ev.h, em)
            g.dma_start(out=xnf[:, :, tb * 128:(tb + 1) * 128],
                        in_=stg[:, :, :]).then_inc(s_out.h, 16)
            if affine:
                g.wait_ge(s_ev.h, ea)
                g.dma_start(out=xn2f[:, :, tb * 128:(tb + 1) * 128],
                            in_=stg2[:, :, :]).then_inc(s_out.h, 16)
        E("gpsimd", f)
        s_out.inc(16 * NQ)

    def f(g):
        g.dma_start(out=mu_hbm[:, :], in_=mu_sb[:]).then_inc(s_out.h, 16)
        g.dma_start(out=rstd_hbm[:, :], in_=rstd_sb[:]).then_inc(s_out.h, 16)
    E("gpsimd", f)
    s_out.inc(32)
    tot = s_out.n

    def f(g):
        g.wait_ge(s_out.h, tot)
    E("gpsimd", f)
    return mu_sb, rstd_sb


def build_fused(pairs):
    ctx = contextlib.ExitStack()
    with ctx:
        nc = bass.Bass("TRN2", target_bir_lowering=False, debug=False)
        p = Prog(ctx, nc)
        p.ma = {"t34": 0}
        p.bfly_pairs = pairs
        p.bfly_pairs_dn = pairs

        def EI(nm, shape, dt=F32):
            return nc.dram_tensor(nm, shape, dt, kind="ExternalInput").ap()

        def IN(nm, shape, dt):
            return nc.dram_tensor(nm, shape, dt, kind="Internal").ap()

        x = EI("x", [T, D])
        lnw = EI("lnw_rep", [128, D]); lnb = EI("lnb_rep", [128, D])
        ident = EI("ident", [128, 128])
        identb = EI("identb", [128, 128], BF16)
        ma = {nm: EI("m_" + nm, shape, dt) for nm, shape, dt in [
            ("wzT", [1024, 2048], BF16), ("wxT", [1024, 2304], BF16),
            ("wdT", [1024, 32], BF16),
            ("woT", [2048, 1024], BF16), ("convw", [128, 18, 4], F32),
            ("convb", [128, 18], F32),
            ("dtb4", [128, CPS * 32], F32), ("alog", [1, 32], F32),
            ("dvec", [1, 32], F32),
            ("normw", [128, 2048], BF16), ("triu_i", [128, 128], F32),
            ("tril_s", [128, 128], F32),
            ("ones128", [128, 128], F32), ("ones1f", [1, 128], F32)]}
        mo = {nm: EI("q_" + nm, shape, dt) for nm, shape, dt in [
            ("rwT", [1024, 8], BF16), ("rb", [1, 8], BF16),
            ("w1sT", [9, 4, 128, 8, 1024], BF16),
            ("w2sT", [9, 4, 128, 8, 1024], BF16), ("b1s", [9, 128, 32], F32),
            ("b2s", [9, 1, 1024], BF16)]}
        la = {nm: EI("a_" + nm, shape, dt) for nm, shape, dt in [
            ("wqT", [1024, 1024], BF16), ("wkT", [1024, 1024], BF16),
            ("wvT", [1024, 1024], BF16),
            ("woT", [1024, 1024], BF16), ("bq", [1, 1024], BF16),
            ("bk", [1, 1024], BF16),
            ("bv", [1, 1024], BF16), ("bo", [1, 1024], BF16),
            ("cosT", [128, 32, 32], F32),
            ("sinT", [128, 32, 32], F32), ("cmaskT", [128, 4, 512], BF16),
            ("colones", [128, 1], BF16)]}
        ff = {nm: EI("f_" + nm, shape, dt) for nm, shape, dt in [
            ("weT", [1024, 4096], BF16), ("be", [128, 32], F32),
            ("upM", [7, 32, 128, 128], BF16),
            ("upA", [128, 5, 32], F32), ("upB", [128, 5, 32], F32),
            ("ubias", [128, 32], F32),
            ("dnM", [7, 32, 128, 128], BF16), ("dnA", [128, 5, 32], F32),
            ("dnB", [128, 5, 32], F32),
            ("wcT", [4096, 1024], BF16), ("bcon", [1, 1024], BF16),
            ("dbias", [1, 1024], BF16)]}

        xnf = IN("i_xnf", [128, 8, T], BF16)
        xn2f = IN("i_xn2f", [128, 8, T], BF16)
        xn1f = IN("i_xn1f", [128, 8, T], BF16)
        mu1 = IN("i_mu1", [128, NTB], F32)
        rstd1 = IN("i_rstd1", [128, NTB], F32)
        mu2 = IN("i_mu2", [128, NTB], F32)
        rstd2 = IN("i_rstd2", [128, NTB], F32)
        brA = IN("i_brA", [T, D], F32)
        x1 = IN("i_x1", [T, D], F32)
        brA1 = IN("i_brA1", [T, D], F32)
        qf = IN("i_qf", [128, 8, T], BF16)
        kf = IN("i_kf", [128, 8, T], BF16)
        vf = IN("i_vf", [128, 32, 1024], BF16)
        aof = IN("i_aof", [128, 8, T], BF16)
        hgf = IN("i_hgf", [128, 32, T], BF16)
        ltdram = IN("i_lt", [32, 2, 32, 128], F32)
        out_h = nc.dram_tensor("out", [T, D], F32, kind="ExternalOutput").ap()

        p.bar = p.sem("bar")
        p.tape_init()

        p.push_phase()
        emit_pre_tape(p, x, xnf, xn2f, lnw, lnb, identb, mu1, rstd1, affine=True)
        p.tape_barrier()
        p.pop_phase()

        p.push_phase()
        emit_mamba(p, xn2f=xn2f, identf=ident, identd=identb, ltdram=ltdram,
                   brA_out=brA, **ma)
        p.tape_barrier()
        p.pop_phase()

        p.push_phase()
        emit_moe(p, xnf, x, brA, mu1, rstd1, mo["rwT"], mo["rb"], mo["w1sT"],
                 mo["w2sT"], mo["b1s"], mo["b2s"], ma["ones1f"], x1, n_exp=9)
        p.tape_barrier()
        p.pop_phase()

        p.push_phase()
        emit_pre_tape(p, x1, xn1f, None, lnw, lnb, identb, mu2, rstd2, affine=False)
        p.tape_barrier()
        p.pop_phase()

        p.push_phase()
        emit_l1attn(p, xn1f, la["wqT"], la["wkT"], la["wvT"], la["woT"], la["bq"],
                    la["bk"], la["bv"], la["bo"], la["cosT"], la["sinT"],
                    ma["ones1f"], identb, la["cmaskT"], la["colones"], qf, kf, vf,
                    aof, brA1)
        p.tape_barrier()
        p.pop_phase()

        p.push_phase()
        emit_l1f1(p, xn1f, ff["weT"], ff["be"], ff["upM"], ff["upA"], ff["upB"],
                  ff["ubias"], identb, ma["ones1f"], hgf)
        p.tape_barrier()
        p.pop_phase()

        p.push_phase()
        emit_l1f2(p, hgf, ff["dnM"], ff["dnA"], ff["dnB"], ff["wcT"], ff["bcon"],
                  ff["dbias"], x1, brA1, identb, ma["ones1f"], out_h)
        p.pop_phase()

        with nc.Block() as block:
            p.block = block
            p.flush()
        return nc


def fused_in_map(inp, b, lays=None):
    if lays is None:
        lays = {}
        lays["m"] = mamba_host_layouts(inp)
        lays["q"] = moe_host_layouts(inp)
        lays["a"] = l1attn_host_layouts(inp)
        lays["f"] = l1ffn_host_layouts(inp)
    m, q, a, f = lays["m"], lays["q"], lays["a"], lays["f"]
    im = {"x": np.ascontiguousarray(inp["L0_attn_x"][b])}
    im["lnw_rep"] = np.ascontiguousarray(np.broadcast_to(inp["L0_attn_ln_w"], (128, D))).astype(np.float32)
    im["lnb_rep"] = np.ascontiguousarray(np.broadcast_to(inp["L0_attn_ln_b"], (128, D))).astype(np.float32)
    im["ident"] = np.eye(128, dtype=np.float32)
    im["identb"] = np.eye(128, dtype=BF16NP)
    bf_m = ("wzT", "wxT", "wdT", "woT", "normw")
    bf_q = ("rwT", "rb", "w1sT", "w2sT", "b2s")
    bf_a = ("wqT", "wkT", "wvT", "woT", "bq", "bk", "bv", "bo", "cmaskT",
            "colones")
    bf_f = ("weT", "upM", "dnM", "wcT", "bcon", "dbias")

    def pick(dct, nm, bfset):
        arr = dct[nm]
        if nm in bfset and arr.dtype != BF16NP:
            arr = np.ascontiguousarray(arr).astype(BF16NP)
            dct[nm] = arr
        return arr

    for nm in ("wzT", "wxT", "wdT", "woT", "convw", "convb", "dtb4", "alog",
               "dvec", "normw", "triu_i", "tril_s", "ones128", "ones1f"):
        im["m_" + nm] = pick(m, nm, bf_m)
    for nm in ("rwT", "rb", "w1sT", "w2sT", "b1s", "b2s"):
        im["q_" + nm] = pick(q, nm, bf_q)
    for nm in ("wqT", "wkT", "wvT", "woT", "bq", "bk", "bv", "bo", "cosT",
               "sinT", "cmaskT", "colones"):
        im["a_" + nm] = pick(a, nm, bf_a)
    for nm in ("weT", "be", "upM", "upA", "upB", "ubias", "dnM", "dnA", "dnB",
               "wcT", "bcon", "dbias"):
        im["f_" + nm] = pick(f, nm, bf_f)
    return im, lays


def kernel(**inputs):
    lay_f = l1ffn_host_layouts(inputs)
    pairs = lay_f["pairs"]
    nc = build_fused(pairs)
    lays = None
    in_maps = []
    for b in range(8):
        im, lays = fused_in_map(inputs, b, lays)
        in_maps.append(im)
    res = run_bass_kernel_spmd(nc, in_maps, core_ids=list(range(N_CORES)))
    return np.stack([np.asarray(r["out"]) for r in res.results], 0).astype(np.float32)
